# version 25
# baseline (speedup 1.0000x reference)
import numpy as np
from contextlib import ExitStack

import concourse.bass as bass
import concourse.mybir as mybir
from concourse.bass_utils import run_bass_kernel_spmd
from concourse.aot_env import allow_ici_all_to_all

F32 = mybir.dt.float32
BF16 = mybir.dt.bfloat16
AF = mybir.ActivationFunctionType
ALU = mybir.AluOpType
AX = mybir.AxisListType

B, S, E = 4, 4096, 1024
NCORES = 8
NT = 16
TOK = 2048
H, KVH, D = 16, 4, 64
QL, KVL = 64, 32
NEXP = 16
SHD = 2048
DHID = 4096
LN_EPS = 1e-5


class Sem:
    def __init__(self, nc, es, name):
        self.h = es.enter_context(nc.semaphore(name))
        self.n = 0

    def inc(self, inst, k=1):
        inst.then_inc(self.h, k)
        self.n += k
        return inst

    def wait(self, eng, lag=0):
        v = self.n - lag
        if v > 0:
            eng.wait_ge(self.h, v)

    def mark(self):
        return (self.h, self.n)


class Ctx:
    def __init__(self, nc, es, block):
        self.nc, self.es, self.block = nc, es, block
        self.T, self.V, self.S, self.Y, self.G = (
            block.tensor, block.vector, block.scalar, block.sync, block.gpsimd)
        self._sems = {}

    def gs(self, name):
        if name not in self._sems:
            self._sems[name] = Sem(self.nc, self.es, name)
        return self._sems[name]

    def sb(self, stack, name, shape, dt):
        return stack.enter_context(self.nc.sbuf_tensor(name, shape, dt))

    def psum(self, stack, name, shape, dt):
        return stack.enter_context(self.nc.psum_tensor(name, shape, dt))


def wait_gate(eng, gate):
    for (h, v) in gate:
        if v > 0:
            eng.wait_ge(h, v)



def emit_l0attn(C, io, xnT, attnT):
    nc, es = C.nc, C.es
    T, VE, SC, SY, G = C.T, C.V, C.S, C.Y, C.G
    x_own = io["x_own"]

    ph = ExitStack()
    sb = lambda n, s, d: C.sb(ph, n, s, d)
    KT = [sb(f"KT{i}", [128, S], BF16) for i in range(2)]
    Vt = sb("Vt", [128, 32 * 260], BF16)
    QT = [sb(f"QT{i}", [128, TOK], BF16) for i in range(8)]
    latT = sb("latT", [KVL, S], BF16)
    qlatT = sb("qlatT", [QL, TOK], BF16)
    wqa_sb = sb("wqa_sb", [128, 8 * QL], BF16)
    wqb_sb = sb("wqb_sb", [QL, E], BF16)
    wkva_sb = sb("wkva_sb", [128, 8 * KVL], BF16)
    wkvb_sb = sb("wkvb_sb", [KVL, 512], BF16)
    mask_sb = [sb(f"mask{i}_sb", [128, 128], BF16) for i in range(2)]
    ones64 = sb("ones64", [1, 64], F32)
    ropek_c = sb("ropek_c", [128, 32 * 32], F32)
    ropek_s = sb("ropek_s", [128, 32 * 32], F32)
    ropeq_c = sb("ropeq_c", [128, 16 * 32], F32)
    ropeq_s = sb("ropeq_s", [128, 16 * 32], F32)
    xbuf = [sb(f"xbuf{i}", [128, E], F32) for i in range(2)]
    xsq = [sb(f"xsq{i}", [128, E], F32) for i in range(2)]
    stat2 = [sb(f"stat{i}", [128, 8], F32) for i in range(2)]
    xn_tmp = [sb(f"xn_tmp{i}", [128, E], BF16) for i in range(2)]
    lat_own = sb("lat_own", [KVL, TOK], BF16)
    kq_tmp = [sb(f"kq_tmp{i}", [128, E], BF16) for i in range(2)]
    rtA = sb("rtA", [128, 512], F32)
    rtB = sb("rtB", [128, 512], F32)
    expS = [sb(f"expS{i}", [128, 512], BF16) for i in range(4)]
    recb = [sb(f"recb{i}", [64, 512], F32) for i in range(2)]
    rrow = [sb(f"rrow{i}", [1, 512], F32) for i in range(2)]

    gsn = lambda n: C.gs("a_" + n)
    vd = gsn("vd")
    wl, wls = gsn("wl"), gsn("wls")
    xl = [gsn("xl0"), gsn("xl1")]
    lnf, trp, evc = gsn("lnf"), gsn("trp"), gsn("evc")
    lns, sds, lnx = gsn("lns"), gsn("sds"), gsn("lnx")
    latm, late = gsn("latm"), gsn("late")
    lsto, agd, latl = gsn("lsto"), gsn("agd"), gsn("latl")
    kvm, roped, kqt, kqe = gsn("kvm"), gsn("roped"), gsn("kqt"), gsn("kqe")
    qlm, qle, qmm = gsn("qlm"), gsn("qle"), gsn("qmm")
    scm, exd, mkd, avd = gsn("scm"), gsn("exd"), gsn("mkd"), gsn("avd")
    nr1, nr2, nr3 = gsn("nr1"), gsn("nr2"), gsn("nr3")

    def g_loads(g):
        wl.inc(g.dma_start(out=wqa_sb[:], in_=io["wqa"].rearrange("(c p) n -> p c n", p=128)), 16)
        wl.inc(g.dma_start(out=wqb_sb[:], in_=io["wqb"]), 16)
        wl.inc(g.dma_start(out=wkva_sb[:], in_=io["wkva"].rearrange("(c p) n -> p c n", p=128)), 16)
        wl.inc(g.dma_start(out=wkvb_sb[:], in_=io["wkvb_kv"]), 16)
        wl.inc(g.dma_start(out=C.ident_bf[:], in_=io["ident"]), 16)
        wl.inc(g.dma_start(out=mask_sb[0][:], in_=io["mask0"]), 16)
        wl.inc(g.dma_start(out=mask_sb[1][:], in_=io["mask1"]), 16)
    G(g_loads)

    def sy_loads(sy):
        wls.inc(sy.dma_start(out=C.ident_f32[:], in_=io["ident"]), 16)
        wls.inc(sy.dma_start(out=ropek_c[:].rearrange("p (t f) -> p t f", f=32),
                             in_=io["ropek_cos"].rearrange("(t p) f -> p t f", p=128)), 16)
        wls.inc(sy.dma_start(out=ropek_s[:].rearrange("p (t f) -> p t f", f=32),
                             in_=io["ropek_sin"].rearrange("(t p) f -> p t f", p=128)), 16)
        wls.inc(sy.dma_start(out=ropeq_c[:].rearrange("p (t f) -> p t f", f=32),
                             in_=io["ropeq_cos"].rearrange("(t p) f -> p t f", p=128)), 16)
        wls.inc(sy.dma_start(out=ropeq_s[:].rearrange("p (t f) -> p t f", f=32),
                             in_=io["ropeq_sin"].rearrange("(t p) f -> p t f", p=128)), 16)
    SY(sy_loads)

    def v_init(v):
        wl.wait(v)
        wls.wait(v)
        v.memset(ones64[:], 1.0)
        lnf.inc(v.memset(
            Vt[:].rearrange("p (t z w) -> p t z w", z=4, w=65)[:, :, :, 64:65], 1.0))
    VE(v_init)

    def emit_ln_tile(src_ap, dst_bf_ap, ld_sem, bi, extra_war=None):
        stat = stat2[bi]
        xq = xsq[bi]

        def v_stats(v):
            ld_sem.wait(v)
            sds.wait(v, lag=1)
            v.tensor_tensor(xq[:], src_ap, src_ap, ALU.mult)
            v.tensor_reduce(stat[:, 0:1], src_ap, AX.X, ALU.add)
            vd.inc(v.tensor_reduce(stat[:, 1:2], xq[:], AX.X, ALU.add))
            vd.wait(v)
            vd.inc(v.tensor_scalar(stat[:, 2:3], stat[:, 0:1], 1.0 / E, None, ALU.mult))
            vd.wait(v)
            vd.inc(v.tensor_tensor(stat[:, 3:4], stat[:, 2:3], stat[:, 2:3], ALU.mult))
            v.tensor_scalar(stat[:, 4:5], stat[:, 1:2], 1.0 / E, LN_EPS, ALU.mult, ALU.add)
            vd.wait(v)
            lns.inc(v.tensor_tensor(stat[:, 4:5], stat[:, 4:5], stat[:, 3:4], ALU.subtract))
        VE(v_stats)

        def s_rstd(s):
            lns.wait(s)
            lnx.inc(s.activation(stat[:, 5:6], stat[:, 4:5], AF.Ln))
            lnx.wait(s)
            sds.inc(s.activation(stat[:, 6:7], stat[:, 5:6], AF.Exp, scale=-0.5))
        SC(s_rstd)

        def v_norm(v):
            sds.wait(v)
            if extra_war is not None:
                extra_war(v)
            lnf.inc(v.tensor_scalar(dst_bf_ap, src_ap, stat[:, 2:3], stat[:, 6:7],
                                    ALU.subtract, ALU.mult))
        VE(v_norm)

    psA = ExitStack()
    ps_tr = C.psum(psA, "ps_tr", [128, 1024], BF16)
    ps_lat = C.psum(psA, "ps_lat", [KVL, 128], F32)

    for a in range(NT):
        bi = a % 2

        def sy_x(sy, a=a, bi=bi):
            lnf.wait(sy, lag=1)
            xl[bi].inc(sy.dma_start(out=xbuf[bi][:], in_=x_own[128 * a:128 * (a + 1), :]), 16)
        SY(sy_x)

        emit_ln_tile(xbuf[bi][:], xn_tmp[bi][:], xl[bi], bi,
                     extra_war=(lambda v: trp.wait(v, lag=8)))

        def t_trans(tt, bi=bi):
            lnf.wait(tt)
            evc.wait(tt)
            for ct in range(8):
                trp.inc(tt.transpose(ps_tr[:, 128 * ct:128 * (ct + 1)],
                                     xn_tmp[bi][:, 128 * ct:128 * (ct + 1)], C.ident_bf[:]))
        T(t_trans)

        def v_evac(v, a=a):
            trp.wait(v)
            for ct in range(8):
                evc.inc(v.tensor_copy(xnT[ct][:, 128 * a:128 * (a + 1)],
                                      ps_tr[:, 128 * ct:128 * (ct + 1)]))
        VE(v_evac)

        def t_lat(tt, a=a):
            evc.wait(tt)
            late.wait(tt)
            for ct in range(8):
                latm.inc(tt.matmul(ps_lat[:, :], wkva_sb[:, KVL * ct:KVL * (ct + 1)],
                                   xnT[ct][:, 128 * a:128 * (a + 1)],
                                   start=(ct == 0), stop=(ct == 7)))
        T(t_lat)

        def v_lat(v, a=a):
            latm.wait(v)
            late.inc(v.tensor_copy(lat_own[:, 128 * a:128 * (a + 1)], ps_lat[:, :]))
        VE(v_lat)

    def sy_lat(sy):
        late.wait(sy)
        lsto.inc(sy.dma_start(out=io["lat_in"].ap(), in_=lat_own[:]), 16)
    SY(sy_lat)

    def g_ag(g):
        lsto.wait(g)
        with allow_ici_all_to_all():
            agd.inc(g.collective_compute(
                "AllGather", mybir.AluOpType.bypass,
                replica_groups=[[0, 1], [2, 3], [4, 5], [6, 7]],
                ins=[io["lat_in"].ap().opt()],
                outs=[io["lat_out"].ap().opt()],
            ))
    G(g_ag)

    def sy_latl(sy):
        agd.wait(sy)
        lo = io["lat_out"].ap()
        latl.inc(sy.dma_start(
            out=latT[:].rearrange("l (t two c) -> l t two c", two=2, c=128)[:, :, 0, :],
            in_=lo[0:KVL, :].rearrange("l (t c) -> l t c", c=128)), 16)
        latl.inc(sy.dma_start(
            out=latT[:].rearrange("l (t two c) -> l t two c", two=2, c=128)[:, :, 1, :],
            in_=lo[KVL:2 * KVL, :].rearrange("l (t c) -> l t c", c=128)), 16)
    SY(sy_latl)

    psA.close()
    psB = ExitStack()
    ps_ql = C.psum(psB, "ps_ql", [QL, 512], F32)
    ps_q = C.psum(psB, "ps_q", [128, 1024], F32)
    ps_qt = C.psum(psB, "ps_qt", [64, 256], BF16)

    for qc in range(4):
        def t_ql(tt, qc=qc):
            qle.wait(tt)
            for ct in range(8):
                qlm.inc(tt.matmul(ps_ql[:, :], wqa_sb[:, QL * ct:QL * (ct + 1)],
                                  xnT[ct][:, 512 * qc:512 * (qc + 1)],
                                  start=(ct == 0), stop=(ct == 7)))
        T(t_ql)

        def v_ql(v, qc=qc):
            qlm.wait(v)
            qle.inc(v.tensor_copy(qlatT[:, 512 * qc:512 * (qc + 1)], ps_ql[:, :]))
        VE(v_ql)

    for a in range(NT):
        bi = a % 2

        def t_q(tt, a=a):
            qle.wait(tt)
            roped.wait(tt)
            qmm.inc(tt.matmul(ps_q[:, 0:512], qlatT[:, 128 * a:128 * (a + 1)],
                              wqb_sb[:, 0:512], start=True, stop=True))
            qmm.inc(tt.matmul(ps_q[:, 512:1024], qlatT[:, 128 * a:128 * (a + 1)],
                              wqb_sb[:, 512:1024], start=True, stop=True))
        T(t_q)

        def v_ropeq(v, a=a, bi=bi):
            qmm.wait(v)
            kqt.wait(v, lag=16)
            e_ps = ps_q[:, :].rearrange("p (hh f2) -> p hh f2", hh=16)[:, :, 0::2]
            o_ps = ps_q[:, :].rearrange("p (hh f2) -> p hh f2", hh=16)[:, :, 1::2]
            cosb = ropeq_c[:, 32 * a:32 * (a + 1)].unsqueeze(1).broadcast_to((128, 16, 32))
            sinb = ropeq_s[:, 32 * a:32 * (a + 1)].unsqueeze(1).broadcast_to((128, 16, 32))
            rA = rtA[:, :].rearrange("p (hh f) -> p hh f", hh=16)
            rB = rtB[:, :].rearrange("p (hh f) -> p hh f", hh=16)
            qe = kq_tmp[bi][:, :].rearrange("p (hh f2) -> p hh f2", hh=16)[:, :, 0::2]
            qo = kq_tmp[bi][:, :].rearrange("p (hh f2) -> p hh f2", hh=16)[:, :, 1::2]
            v.tensor_tensor(rA, e_ps, cosb, ALU.mult)
            vd.inc(v.tensor_tensor(rB, o_ps, sinb, ALU.mult))
            vd.wait(v)
            vd.inc(v.tensor_tensor(qe, rA, rB, ALU.subtract))
            vd.wait(v)
            v.tensor_tensor(rA, e_ps, sinb, ALU.mult)
            vd.inc(v.tensor_tensor(rB, o_ps, cosb, ALU.mult))
            vd.wait(v)
            roped.inc(v.tensor_tensor(qo, rA, rB, ALU.add))
        VE(v_ropeq)

        for grp in range(8):
            def t_qt(tt, bi=bi, grp=grp):
                roped.wait(tt)
                kqe.wait(tt)
                h0, h1 = 2 * grp, 2 * grp + 1
                kqt.inc(tt.transpose(ps_qt[:, 0:128], kq_tmp[bi][:, 64 * h0:64 * (h0 + 1)],
                                     C.ident_bf[:]))
                kqt.inc(tt.transpose(ps_qt[:, 128:256], kq_tmp[bi][:, 64 * h1:64 * (h1 + 1)],
                                     C.ident_bf[:]))
            T(t_qt)

            def v_qte(v, a=a, grp=grp):
                kqt.wait(v)
                for k in range(2):
                    h_ = 2 * grp + k
                    p = (h_ % 4) + 4 * (h_ // 8)
                    up = (h_ // 4) % 2
                    kqe.inc(v.tensor_copy(QT[p][64 * up:64 * up + 64, 128 * a:128 * (a + 1)],
                                          ps_qt[:, 128 * k:128 * (k + 1)]))
            VE(v_qte)

    ps_kv = C.psum(psB, "ps_kv", [128, 512], F32)
    ps_kt = C.psum(psB, "ps_kt", [128, 256], BF16)

    for t in range(32):
        bi = t % 2

        def t_kv(tt, t=t):
            if t == 0:
                latl.wait(tt)
            roped.wait(tt)
            kvm.inc(tt.matmul(ps_kv[:, :], latT[:, 128 * t:128 * (t + 1)], wkvb_sb[:, :],
                              start=True, stop=True))
        T(t_kv)

        def v_ropek(v, t=t, bi=bi):
            kvm.wait(v)
            kqt.wait(v, lag=2)
            e_ps = ps_kv[:, 0:256].rearrange("p (k f2) -> p k f2", k=4)[:, :, 0::2]
            o_ps = ps_kv[:, 0:256].rearrange("p (k f2) -> p k f2", k=4)[:, :, 1::2]
            cosb = ropek_c[:, 32 * t:32 * (t + 1)].unsqueeze(1).broadcast_to((128, 4, 32))
            sinb = ropek_s[:, 32 * t:32 * (t + 1)].unsqueeze(1).broadcast_to((128, 4, 32))
            rA = rtA[:, 0:128].rearrange("p (k f) -> p k f", k=4)
            rB = rtB[:, 0:128].rearrange("p (k f) -> p k f", k=4)
            ke = kq_tmp[bi][:, 0:256].rearrange("p (k f2) -> p k f2", k=4)[:, :, 0::2]
            ko = kq_tmp[bi][:, 0:256].rearrange("p (k f2) -> p k f2", k=4)[:, :, 1::2]
            v.tensor_tensor(rA, e_ps, cosb, ALU.mult)
            vd.inc(v.tensor_tensor(rB, o_ps, sinb, ALU.mult))
            vd.wait(v)
            vd.inc(v.tensor_tensor(ke, rA, rB, ALU.subtract))
            vd.wait(v)
            v.tensor_tensor(rA, e_ps, sinb, ALU.mult)
            vd.inc(v.tensor_tensor(rB, o_ps, cosb, ALU.mult))
            vd.wait(v)
            v.tensor_tensor(ko, rA, rB, ALU.add)
            vdst = Vt[:, 260 * t:260 * t + 260].rearrange("p (z w) -> p z w", z=4)[:, :, 0:64]
            vsrc = ps_kv[:, 256:512].rearrange("p (z w) -> p z w", z=4)
            roped.inc(v.tensor_copy(vdst, vsrc))
        VE(v_ropek)

        def t_kt(tt, bi=bi):
            roped.wait(tt)
            kqe.wait(tt)
            kqt.inc(tt.transpose(ps_kt[:, 0:128], kq_tmp[bi][:, 0:128], C.ident_bf[:]))
            kqt.inc(tt.transpose(ps_kt[:, 128:256], kq_tmp[bi][:, 128:256], C.ident_bf[:]))
        T(t_kt)

        def v_kte(v, t=t):
            kqt.wait(v)
            v.tensor_copy(KT[0][:, 128 * t:128 * (t + 1)], ps_kt[:, 0:128])
            kqe.inc(v.tensor_copy(KT[1][:, 128 * t:128 * (t + 1)], ps_kt[:, 128:256]))
        VE(v_kte)

    psB.close()
    psC = ExitStack()
    ps_s = [C.psum(psC, f"ps_s{i}", [128, 512], F32) for i in range(4)]
    ps_o = [C.psum(psC, f"ps_o{i}", [65, 512], F32) for i in range(2)]
    ps_r = C.psum(psC, "ps_r", [64, 512], F32)

    state = {"first": True, "gsi": 0, "quad": 0}

    def quad(h_, q4):
        kvh = h_ // 4
        ktn = KT[h_ // 8]
        up = (h_ // 4) % 2
        qtn = QT[(h_ % 4) + 4 * (h_ // 8)]
        col0 = 512 * q4
        ob = state["quad"] % 2
        steps = []
        for kt in range(8 * q4 + 2):
            steps.append((kt, 0, 512, 0 if kt >= 8 * q4 else None))
        for m in range(1, 4):
            for kt in (8 * q4 + 2 * m, 8 * q4 + 2 * m + 1):
                steps.append((kt, 128 * m, 512 - 128 * m, m))
        nsteps = len(steps)
        pend = []
        exp_plan = []
        for si, (kt, off, n, mb) in enumerate(steps):
            G = state["gsi"]
            sl = G % 4
            state["gsi"] += 1

            def t_sc(tt, kt=kt, off=off, n=n, sl=sl, G=G):
                if state["first"]:
                    kqe.wait(tt)
                    roped.wait(tt)
                    qle.wait(tt)
                    state["first"] = False
                if G - 3 > 0:
                    tt.wait_ge(exd.h, G - 3)
                scm.inc(tt.matmul(ps_s[sl][:, 0:n],
                                  ktn[64 * up:64 * up + 64, 128 * kt:128 * (kt + 1)],
                                  qtn[64 * up:64 * up + 64, col0 + off:col0 + off + n],
                                  start=True, stop=True))
            T(t_sc)
            exp_plan.append((sl, n, scm.n, G))

            if mb is not None:
                def v_mask(v, kt=kt, sl=sl, G=G):
                    v.wait_ge(exd.h, G + 1)
                    mkd.inc(v.tensor_tensor(expS[sl][:, 0:128], expS[sl][:, 0:128],
                                            mask_sb[kt % 2][:], ALU.mult))
                VE(v_mask)
            mkd_goal = mkd.n

            def t_av(tt, kt=kt, off=off, n=n, sl=sl, si=si, G=G, mkd_goal=mkd_goal):
                tt.wait_ge(exd.h, G + 1)
                if mkd_goal > 0:
                    tt.wait_ge(mkd.h, mkd_goal)
                if si == 0:
                    nr3.wait(tt, lag=1)
                avd.inc(tt.matmul(ps_o[ob][:, off:off + n],
                                  Vt[:, 260 * kt + 65 * kvh:260 * kt + 65 * kvh + 65],
                                  expS[sl][:, 0:n],
                                  start=(si == 0), stop=(si == nsteps - 1)))
            pend.append(t_av)
            if len(pend) > 2:
                T(pend.pop(0))
        while pend:
            T(pend.pop(0))

        def s_exp_all(s, plan=tuple(exp_plan)):
            for (sl, n, scm_t, G) in plan:
                s.wait_ge(scm.h, scm_t)
                if G - 3 > 0:
                    s.wait_ge(avd.h, G - 3)
                exd.inc(s.activation(expS[sl][:, 0:n], ps_s[sl][:, 0:n],
                                     AF.Exp, scale=0.125))
        SC(s_exp_all)

        def v_rec(v):
            avd.wait(v)
            nr2.wait(v, lag=1)
            nr1.inc(v.reciprocal(rrow[ob][:, :], ps_o[ob][64:65, :]))
        VE(v_rec)

        def t_bc(tt):
            nr1.wait(tt)
            nr3.wait(tt)
            nr2.inc(tt.matmul(ps_r[:, :], ones64[:, :], rrow[ob][:, :], start=True, stop=True))
        T(t_bc)

        def v_fin(v, h_=h_, q4=q4):
            nr2.wait(v)
            vd.inc(v.tensor_copy(recb[ob][:, :], ps_r[:, :]))
            vd.wait(v)
            dst = attnT[h_ // 2][64 * (h_ % 2):64 * (h_ % 2) + 64, 512 * q4:512 * (q4 + 1)]
            nr3.inc(v.tensor_tensor(dst, ps_o[ob][0:64, :], recb[ob][:, :], ALU.mult))
        VE(v_fin)
        state["quad"] += 1

    for h_ in range(H):
        for q4 in range(4):
            quad(h_, q4)

    psC.close()
    ph.close()
    return [nr3.mark(), evc.mark(), kqe.mark(), roped.mark()]



def emit_l0ffn(C, io, xnT, attnT, attnT_stack, gate):
    nc = C.nc
    T, VE, SC, SY, G = C.T, C.V, C.S, C.Y, C.G

    ph = ExitStack()
    sb = lambda n, s, d: C.sb(ph, n, s, d)
    accT = [sb(f"f_accT{i}", [128, TOK], F32) for i in range(8)]
    dn_sb = [sb(f"f_dn{i}", [128, 1024], BF16) for i in range(8)]
    wbc = sb("f_wbc", [128, TOK], BF16)
    wT_sb = sb("f_wT", [NEXP, TOK], BF16)
    onehot_sb = sb("f_onehot", [NEXP, 2048], BF16)
    routerT_sb = sb("f_routerT", [128, 8 * NEXP], BF16)
    bout_sb = sb("f_bout", [128, 8], F32)
    logits = sb("f_logits", [128, NT * NEXP], F32)
    tcol = sb("f_tcol", [128, 4], F32)
    eqb = sb("f_eqb", [128, NEXP], F32)
    mkb = sb("f_mkb", [128, NEXP], F32)
    geb = sb("f_geb", [128, NEXP], F32)
    sgb = sb("f_sgb", [128, NEXP], F32)
    wfull = sb("f_wfull", [128, NEXP], F32)
    silu_sb = sb("f_silu", [128, TOK], BF16)
    wout_sb = [sb(f"f_wout{i}", [128, E], BF16) for i in range(2)]
    h_sb = io["h_sb"]

    gsn = lambda n: C.gs("f_" + n)
    vd = C.gs("a_vd")
    wlg, wls = gsn("wlg"), gsn("wls")
    wmm, wev = gsn("wmm"), gsn("wev")
    lgm, lgv, sgd, tpw, twe = gsn("lgm"), gsn("lgv"), gsn("sgd"), gsn("tpw"), gsn("twe")
    gul = [gsn(f"gul{i}") for i in range(2)]
    dnl8 = gsn("dnl")
    dnl = [dnl8] * 8
    wbm, wbe = gsn("wbm"), gsn("wbe")
    gmm, sil, umm, hbd, hcd = gsn("gmm"), gsn("sil"), gsn("umm"), gsn("hbd"), gsn("hcd")
    dnm, dne = gsn("dnm"), gsn("dne")
    sdl8 = gsn("sdl")
    sdl = [sdl8] * 8
    xld = [gsn("xld0"), gsn("xld1")]
    fint, finv = gsn("fint"), gsn("finv")
    sto = gsn("sto")

    wol = gsn("wol")

    def g_loads(g):
        wait_gate(g, gate)
        wlg.inc(g.dma_start(out=routerT_sb[:],
                            in_=io["routerT"].rearrange("(c p) n -> p c n", p=128)), 16)
        wlg.inc(g.dma_start(out=onehot_sb[:], in_=io["onehot"]), 16)
    G(g_loads)

    def sy_loads(sy):
        wait_gate(sy, gate)
        wls.inc(sy.dma_start(out=bout_sb[:], in_=io["bout"]), 16)
    SY(sy_loads)

    psP = ExitStack()
    ps_m2 = [C.psum(psP, f"f_ps_m{i}", [128, 512], F32) for i in range(2)]
    ps_t = C.psum(psP, "f_ps_t", [NEXP, 128], F32)
    ps_m = ps_m2[0]

    for qtr in range(4):
        def g_wo(g, qtr=qtr):
            if qtr == 0:
                wait_gate(g, gate)
            if qtr > 0:
                wmm.wait(g)
            for k in range(2):
                ci = 2 * qtr + k
                wol.inc(g.dma_start(out=wout_sb[k][:], in_=io["wout"][128 * ci:128 * (ci + 1), :]), 16)
        G(g_wo)

        for oc in range(8):
            for c4 in range(4):
                def t_wout(tt, qtr=qtr, oc=oc, c4=c4):
                    if qtr == 0 and oc == 0 and c4 == 0:
                        wait_gate(tt, gate)
                        wlg.wait(tt)
                    wol.wait(tt)
                    wev.wait(tt, lag=1)
                    mi = (qtr * 32 + oc * 4 + c4) % 2
                    for k in range(2):
                        ci = 2 * qtr + k
                        wmm.inc(tt.matmul(ps_m2[mi][:, :], wout_sb[k][:, 128 * oc:128 * (oc + 1)],
                                          attnT[ci][:, 512 * c4:512 * (c4 + 1)],
                                          start=(k == 0), stop=(k == 1)))
                T(t_wout)

                def v_wout(v, qtr=qtr, oc=oc, c4=c4):
                    wmm.wait(v)
                    mi = (qtr * 32 + oc * 4 + c4) % 2
                    if qtr == 0:
                        wls.wait(v)
                        wev.inc(v.tensor_scalar(accT[oc][:, 512 * c4:512 * (c4 + 1)],
                                                ps_m2[mi][:, :], bout_sb[:, oc:oc + 1], None, ALU.add))
                    else:
                        wev.inc(v.tensor_tensor(accT[oc][:, 512 * c4:512 * (c4 + 1)],
                                                accT[oc][:, 512 * c4:512 * (c4 + 1)],
                                                ps_m2[mi][:, :], ALU.add))
                VE(v_wout)

    for a in range(NT):
        def t_lg(tt, a=a):
            if a == 0:
                wev.wait(tt)
            lgv.wait(tt)
            for ci in range(8):
                lgm.inc(tt.matmul(ps_m[:, 0:NEXP], xnT[ci][:, 128 * a:128 * (a + 1)],
                                  routerT_sb[:, NEXP * ci:NEXP * (ci + 1)],
                                  start=(ci == 0), stop=(ci == 7)))
        T(t_lg)

        def v_lg(v, a=a):
            lgm.wait(v)
            lgv.inc(v.tensor_copy(logits[:, NEXP * a:NEXP * (a + 1)], ps_m[:, 0:NEXP]))
        VE(v_lg)

    for a in range(NT):
        la = logits[:, NEXP * a:NEXP * (a + 1)]

        def v_top2a(v, la=la):
            lgv.wait(v)
            vd.inc(v.tensor_reduce(tcol[:, 0:1], la, AX.X, ALU.max))
            vd.wait(v)
            vd.inc(v.tensor_scalar(eqb[:], la, tcol[:, 0:1], None, ALU.is_equal))
            vd.wait(v)
            vd.inc(v.scalar_tensor_tensor(mkb[:], eqb[:], -1e30, la, ALU.mult, ALU.add))
            vd.wait(v)
            vd.inc(v.tensor_reduce(tcol[:, 1:2], mkb[:], AX.X, ALU.max))
            vd.wait(v)
            sgd.inc(v.tensor_scalar(geb[:], la, tcol[:, 1:2], None, ALU.is_ge))
        VE(v_top2a)

        def s_sig(s, la=la):
            sgd.wait(s)
            tpw.wait(s, lag=1)
            sgd.inc(s.activation(sgb[:], la, AF.Sigmoid))
        SC(s_sig)

        def v_wfull(v):
            sgd.wait(v)
            tpw.inc(v.tensor_tensor(wfull[:], sgb[:], geb[:], ALU.mult))
        VE(v_wfull)

        def t_wT(tt):
            tpw.wait(tt)
            twe.wait(tt)
            tpw.inc(tt.transpose(ps_t[:, :], wfull[:, :], C.ident_f32[:]))
        T(t_wT)

        def v_wTe(v, a=a):
            tpw.wait(v)
            twe.inc(v.tensor_copy(wT_sb[:, 128 * a:128 * (a + 1)], ps_t[:, :]))
        VE(v_wTe)

    psP.close()

    esP3 = ExitStack()
    gu2 = [C.sb(esP3, f"f_gu2_{i}", [128, 2048], BF16) for i in range(2)]
    psX = ExitStack()
    ps_ga = C.psum(psX, "f_ps_ga", [128, 2048], F32)
    ps_gb = C.psum(psX, "f_ps_gb", [128, 2048], F32)

    def emit_expert(e):
        def g_dn(g, e=e):
            if e == 0:
                wmm.wait(g)
            dnm.wait(g)
            for ci in range(8):
                dnl[ci].inc(g.dma_start(out=dn_sb[ci][:],
                                        in_=io["dn_w"][e, 128 * ci:128 * (ci + 1), :]), 16)
        G(g_dn)

        for c4 in range(4):
            def t_wb(tt, e=e, c4=c4):
                if e == 0 and c4 == 0:
                    twe.wait(tt)
                    wev.wait(tt)
                hbd.wait(tt)
                wbe.wait(tt)
                dne.wait(tt)
                wbm.inc(tt.matmul(ps_gb[:, 512 * c4:512 * (c4 + 1)],
                                  onehot_sb[:, 128 * e:128 * (e + 1)],
                                  wT_sb[:, 512 * c4:512 * (c4 + 1)], start=True, stop=True))
            T(t_wb)

        def v_wb(v):
            wbm.wait(v)
            wbe.inc(v.tensor_copy(wbc[:], ps_gb[:, :]))
        VE(v_wb)

        for g in range(8):
            wi = g % 2

            def g_gu(g_, e=e, g=g, wi=wi):
                if e == 0 and g < 2:
                    wmm.wait(g_)
                umm.wait(g_, lag=1)
                gul[wi].inc(g_.dma_start(out=gu2[wi][:], in_=io["gup_w"][e, g]), 16)
            G(g_gu)

            def t_gate(tt, e=e, g=g, wi=wi):
                gul[wi].wait(tt)
                sil.wait(tt)
                if g == 0:
                    wbe.wait(tt)
                    dne.wait(tt)
                for ci in range(8):
                    for c4 in range(4):
                        gmm.inc(tt.matmul(ps_ga[:, 512 * c4:512 * (c4 + 1)],
                                          gu2[wi][:, 128 * ci:128 * (ci + 1)],
                                          xnT[ci][:, 512 * c4:512 * (c4 + 1)],
                                          start=(ci == 0), stop=(ci == 7)))
            T(t_gate)

            def s_silu(s, g=g):
                hbd.wait(s)
                for c4 in range(4):
                    gmm.wait(s, lag=3 - c4)
                    inst = s.activation(silu_sb[:, 512 * c4:512 * (c4 + 1)],
                                        ps_ga[:, 512 * c4:512 * (c4 + 1)], AF.Silu)
                sil.inc(inst)
            SC(s_silu)

            def t_up(tt, e=e, g=g, wi=wi):
                hbd.wait(tt)
                if g == 0:
                    wbe.wait(tt)
                for ci in range(8):
                    for c4 in range(4):
                        inst = tt.matmul(ps_gb[:, 512 * c4:512 * (c4 + 1)],
                                         gu2[wi][:, 1024 + 128 * ci:1024 + 128 * (ci + 1)],
                                         xnT[ci][:, 512 * c4:512 * (c4 + 1)],
                                         start=(ci == 0), stop=(ci == 7))
                umm.inc(inst)
            T(t_up)

            def v_h(v, e=e, g=g):
                umm.wait(v)
                sil.wait(v)
                wbe.wait(v)
                dnm.wait(v)
                hbd.inc(v.tensor_tensor(h_sb[g][:], ps_gb[:, :], silu_sb[:], ALU.mult))
                hbd.wait(v)
                hcd.inc(v.tensor_tensor(h_sb[g][:], h_sb[g][:], wbc[:], ALU.mult))
            VE(v_h)

        for oc in range(8):
            pbank = ps_ga if oc % 2 == 0 else ps_gb

            def t_dn(tt, e=e, oc=oc, pbank=pbank):
                hcd.wait(tt)
                dnl[0].wait(tt)
                if oc >= 2:
                    dne.wait(tt, lag=4)
                for hc in range(8):
                    for c4 in range(4):
                        inst = tt.matmul(pbank[:, 512 * c4:512 * (c4 + 1)],
                                         dn_sb[hc][:, 128 * oc:128 * (oc + 1)],
                                         h_sb[hc][:, 512 * c4:512 * (c4 + 1)],
                                         start=(hc == 0), stop=(hc == 7))
                        if hc == 7:
                            dnm.inc(inst)
            T(t_dn)

            for c4 in range(4):
                def v_dn(v, oc=oc, c4=c4, pbank=pbank):
                    dnm.wait(v, lag=3 - c4)
                    dne.inc(v.tensor_tensor(accT[oc][:, 512 * c4:512 * (c4 + 1)],
                                            accT[oc][:, 512 * c4:512 * (c4 + 1)],
                                            pbank[:, 512 * c4:512 * (c4 + 1)], ALU.add))
                VE(v_dn)

    for e in range(NEXP):
        emit_expert(e)

    for sh in range(2):
        for gl in range(8):
            gidx = 8 * sh + gl
            wi = gidx % 2

            def g_ws(g, gidx=gidx, wi=wi):
                umm.wait(g, lag=1)
                gul[wi].inc(g.dma_start(out=gu2[wi][:], in_=io["sgsu"][gidx]), 16)
            G(g_ws)

            def g_sd(g, gidx=gidx, gl=gl):
                dnm.wait(g)
                sdl[gl].inc(g.dma_start(out=dn_sb[gl][:],
                                        in_=io["sdwT"][128 * gidx:128 * (gidx + 1), :]), 16)
            G(g_sd)

            def t_sg(tt, wi=wi, sh=sh, gl=gl):
                gul[wi].wait(tt)
                sil.wait(tt)
                if sh == 0 and gl == 0:
                    dne.wait(tt)
                for ci in range(8):
                    for c4 in range(4):
                        gmm.inc(tt.matmul(ps_ga[:, 512 * c4:512 * (c4 + 1)],
                                          gu2[wi][:, 128 * ci:128 * (ci + 1)],
                                          xnT[ci][:, 512 * c4:512 * (c4 + 1)],
                                          start=(ci == 0), stop=(ci == 7)))
            T(t_sg)

            def s_ssilu(s):
                hbd.wait(s)
                for c4 in range(4):
                    gmm.wait(s, lag=3 - c4)
                    inst = s.activation(silu_sb[:, 512 * c4:512 * (c4 + 1)],
                                        ps_ga[:, 512 * c4:512 * (c4 + 1)], AF.Silu)
                sil.inc(inst)
            SC(s_ssilu)

            def t_su(tt, wi=wi, sh=sh, gl=gl):
                hbd.wait(tt)
                if sh == 0 and gl == 0:
                    dne.wait(tt)
                for ci in range(8):
                    for c4 in range(4):
                        inst = tt.matmul(ps_gb[:, 512 * c4:512 * (c4 + 1)],
                                         gu2[wi][:, 1024 + 128 * ci:1024 + 128 * (ci + 1)],
                                         xnT[ci][:, 512 * c4:512 * (c4 + 1)],
                                         start=(ci == 0), stop=(ci == 7))
                umm.inc(inst)
            T(t_su)

            def v_t(v, gl=gl):
                umm.wait(v)
                sil.wait(v)
                dnm.wait(v)
                hbd.inc(v.tensor_tensor(h_sb[gl][:], ps_gb[:, :], silu_sb[:], ALU.mult))
            VE(v_t)

        for oc in range(8):
            pbank = ps_ga if oc % 2 == 0 else ps_gb

            def t_sd(tt, oc=oc, pbank=pbank):
                hbd.wait(tt)
                sdl[0].wait(tt)
                if oc >= 2:
                    dne.wait(tt, lag=4)
                for gl in range(8):
                    for c4 in range(4):
                        inst = tt.matmul(pbank[:, 512 * c4:512 * (c4 + 1)],
                                         dn_sb[gl][:, 128 * oc:128 * (oc + 1)],
                                         h_sb[gl][:, 512 * c4:512 * (c4 + 1)],
                                         start=(gl == 0), stop=(gl == 7))
                        if gl == 7:
                            dnm.inc(inst)
            T(t_sd)

            for c4 in range(4):
                def v_sd(v, oc=oc, c4=c4, pbank=pbank):
                    dnm.wait(v, lag=3 - c4)
                    dne.inc(v.tensor_tensor(accT[oc][:, 512 * c4:512 * (c4 + 1)],
                                            accT[oc][:, 512 * c4:512 * (c4 + 1)],
                                            pbank[:, 512 * c4:512 * (c4 + 1)], ALU.add))
                VE(v_sd)

    psX.close()
    esP3.close()
    psF = ExitStack()
    ps_f = C.psum(psF, "f_ps_f", [128, 128], F32)
    xbuf = [h_sb[0][:].bitcast(F32), h_sb[1][:].bitcast(F32)]
    x1b = [h_sb[2][:].bitcast(F32), h_sb[3][:].bitcast(F32)]

    for a in range(NT):
        bi = a % 2

        def sy_xa(sy, a=a, bi=bi):
            if a < 2:
                dne.wait(sy)
            finv.wait(sy, lag=8)
            xld[bi].inc(sy.dma_start(out=xbuf[bi], in_=io["x_own"][128 * a:128 * (a + 1), :]), 16)
        SY(sy_xa)

        for ct in range(8):
            def t_f(tt, a=a, ct=ct):
                if a == 0 and ct == 0:
                    dne.wait(tt)
                    hbd.wait(tt)
                finv.wait(tt)
                fint.inc(tt.transpose(ps_f[:, :], accT[ct][:, 128 * a:128 * (a + 1)],
                                      C.ident_f32[:]))
            T(t_f)

            def v_f(v, a=a, bi=bi, ct=ct):
                fint.wait(v)
                xld[bi].wait(v)
                if ct == 0:
                    sto.wait(v, lag=16)
                finv.inc(v.tensor_tensor(x1b[bi][:, 128 * ct:128 * (ct + 1)],
                                         xbuf[bi][:, 128 * ct:128 * (ct + 1)],
                                         ps_f[:, :], ALU.add))
            VE(v_f)

        def sy_o(sy, a=a, bi=bi):
            finv.wait(sy)
            sto.inc(sy.dma_start(out=io["x1_scr"].ap()[128 * a:128 * (a + 1), :], in_=x1b[bi]), 16)
        SY(sy_o)

    psF.close()
    ph.close()
    return [sto.mark(), finv.mark()]



def emit_l1qkv(C, io, xn1T, gate):
    nc = C.nc
    T, VE, SC, SY, G = C.T, C.V, C.S, C.Y, C.G

    ph = ExitStack()
    sb = lambda n, s, d: C.sb(ph, n, s, d)
    w_sb = [sb(f"q_w{i}", [128, 4 * E], BF16) for i in range(8)]
    xbuf = [sb(f"q_xbuf{i}", [128, E], F32) for i in range(4)]
    xsq = [sb(f"q_xsq{i}", [128, E], F32) for i in range(4)]
    stat2 = [sb(f"q_stat{i}", [128, 8], F32) for i in range(4)]
    xn_tmp = [sb(f"q_xn{i}", [128, E], BF16) for i in range(4)]
    qkv_sb = [sb(f"q_qkv{i}", [128, 4 * E], BF16) for i in range(4)]

    gsn = lambda n: C.gs("q_" + n)
    vd = C.gs("a_vd")
    wlg = gsn("wlg")
    lnf, lns, sds, lnx = C.gs("a_lnf"), C.gs("a_lns"), C.gs("a_sds"), C.gs("a_lnx")
    xl = [C.gs("a_xl0"), C.gs("a_xl1")]
    trp, evc = C.gs("a_trp"), C.gs("a_evc")
    qmm, qev, sto = gsn("qmm"), gsn("qev"), gsn("sto")

    def g_loads(g):
        wait_gate(g, gate)
        for ci in range(8):
            wlg.inc(g.dma_start(out=w_sb[ci][:], in_=io["wqkv"][128 * ci:128 * (ci + 1), :]), 16)
    G(g_loads)

    ps_tr = C.psum(ph, "q_ps_tr", [128, 1024], BF16)
    ps_q = [C.psum(ph, f"q_ps_q{i}", [128, 512], F32) for i in range(2)]

    gq = {"n": 0}
    for a in range(NT):
        bi = a % 4
        xi = a % 2

        def sy_x(sy, a=a, bi=bi, xi=xi):
            if a < 2:
                wait_gate(sy, gate)
            lnf.wait(sy, lag=3)
            xl[xi].inc(sy.dma_start(out=xbuf[bi][:],
                                    in_=io["x1_scr"].ap()[128 * a:128 * (a + 1), :]), 16)
        SY(sy_x)

        def v_stats(v, bi=bi, xi=xi):
            stat = stat2[bi]
            xl[xi].wait(v)
            sds.wait(v, lag=3)
            v.tensor_tensor(xsq[bi][:], xbuf[bi][:], xbuf[bi][:], ALU.mult)
            v.tensor_reduce(stat[:, 0:1], xbuf[bi][:], AX.X, ALU.add)
            vd.inc(v.tensor_reduce(stat[:, 1:2], xsq[bi][:], AX.X, ALU.add))
            vd.wait(v)
            vd.inc(v.tensor_scalar(stat[:, 2:3], stat[:, 0:1], 1.0 / E, None, ALU.mult))
            vd.wait(v)
            vd.inc(v.tensor_tensor(stat[:, 3:4], stat[:, 2:3], stat[:, 2:3], ALU.mult))
            v.tensor_scalar(stat[:, 4:5], stat[:, 1:2], 1.0 / E, LN_EPS, ALU.mult, ALU.add)
            vd.wait(v)
            lns.inc(v.tensor_tensor(stat[:, 4:5], stat[:, 4:5], stat[:, 3:4], ALU.subtract))
        VE(v_stats)

        def s_rstd(s, bi=bi):
            stat = stat2[bi]
            lns.wait(s)
            lnx.inc(s.activation(stat[:, 5:6], stat[:, 4:5], AF.Ln))
            lnx.wait(s)
            sds.inc(s.activation(stat[:, 6:7], stat[:, 5:6], AF.Exp, scale=-0.5))
        SC(s_rstd)

        def v_norm(v, bi=bi):
            stat = stat2[bi]
            sds.wait(v)
            trp.wait(v, lag=24)
            lnf.inc(v.tensor_scalar(xn_tmp[bi][:], xbuf[bi][:], stat[:, 2:3], stat[:, 6:7],
                                    ALU.subtract, ALU.mult))
        VE(v_norm)

        def t_trans(tt, bi=bi):
            lnf.wait(tt)
            evc.wait(tt)
            for ct in range(8):
                trp.inc(tt.transpose(ps_tr[:, 128 * ct:128 * (ct + 1)],
                                     xn_tmp[bi][:, 128 * ct:128 * (ct + 1)], C.ident_bf[:]))
        T(t_trans)

        def v_evac(v, a=a):
            trp.wait(v)
            for ct in range(8):
                evc.inc(v.tensor_copy(xn1T[ct][:, 128 * a:128 * (a + 1)],
                                      ps_tr[:, 128 * ct:128 * (ct + 1)]))
        VE(v_evac)

        for oc8 in range(8):
            sl = gq["n"] % 2
            gq["n"] += 1

            def t_q(tt, a=a, oc8=oc8, sl=sl):
                if a == 0 and oc8 == 0:
                    wlg.wait(tt)
                evc.wait(tt)
                qev.wait(tt, lag=1)
                for ci in range(8):
                    qmm.inc(tt.matmul(ps_q[sl][:, :], xn1T[ci][:, 128 * a:128 * (a + 1)],
                                      w_sb[ci][:, 512 * oc8:512 * (oc8 + 1)],
                                      start=(ci == 0), stop=(ci == 7)))
            T(t_q)

            def v_q(v, a=a, bi=bi, oc8=oc8, sl=sl):
                qmm.wait(v)
                if oc8 == 0:
                    sto.wait(v, lag=48)
                qev.inc(v.tensor_copy(qkv_sb[bi][:, 512 * oc8:512 * (oc8 + 1)], ps_q[sl][:, :]))
            VE(v_q)

        def sy_q(sy, a=a, bi=bi):
            qev.wait(sy)
            sto.inc(sy.dma_start(out=io["qkv_scr"].ap()[128 * a:128 * (a + 1), :],
                                 in_=qkv_sb[bi][:]), 16)
        SY(sy_q)

    ph.close()
    return [sto.mark(), evc.mark(), qev.mark()]



def emit_l1fused(C, io, xn1T, gate):
    nc = C.nc
    T, VE, SC, SY, G = C.T, C.V, C.S, C.Y, C.G

    ph = ExitStack()
    sb = lambda n, s, d: C.sb(ph, n, s, d)
    accT = [sb(f"o_accT{i}", [128, TOK], BF16) for i in range(8)]
    scl = sb("o_scl", [128, 5 * 32], F32)
    kv_all = sb("w_kv", [128, 8192], BF16)
    qb = [sb(f"w_qb{i}", [128, 256], BF16) for i in range(2)]
    wb = [sb(f"w_wb{i}", [128, 256], BF16) for i in range(2)]
    wscr = sb("w_scr", [128, 4096], BF16)
    sc2 = [sb(f"w_sc{i}", [128, 1024], F32) for i in range(2)]
    P2 = [sb(f"w_P{i}", [128, 1024], BF16) for i in range(2)]
    scat4 = [sb(f"w_scat{i}", [128, 1024], BF16) for i in range(4)]
    sums = sb("w_sums", [128, 256], F32)
    rec = sb("w_rec", [128, 256], F32)
    ov = sb("w_ov", [128, 256], F32)
    t1 = sb("w_t1", [128, 256], F32)

    gsn = lambda n: C.gs("o_" + n)
    vd = C.gs("a_vd")
    wls = gsn("wls")
    a1l = [gsn("a1l0"), gsn("a1l1")]
    trp, evc = C.gs("a_trp"), C.gs("a_evc")
    wol = gsn("wol")
    wmm, wev = gsn("wmm"), gsn("wev")
    w12l = [gsn(f"w12l{i}") for i in range(2)]
    whl = [gsn(f"whl{i}") for i in range(2)]
    w3l = [gsn(f"w3l{i}") for i in range(2)]
    amm, sil, bmm, uev = gsn("amm"), gsn("sil"), gsn("bmm"), gsn("uev")
    hmm, hev = gsn("hmm"), gsn("hev")
    h2m, h2e = gsn("h2m"), gsn("h2e")
    w3m, w3e = gsn("w3m"), gsn("w3e")
    xld = [C.gs("a_xl0"), C.gs("a_xl1")]
    fint, finv, sto = gsn("fint"), gsn("finv"), gsn("sto")
    kvl, qbl, wbl = gsn("kvl"), gsn("qbl"), gsn("wbl")
    scr_d, exq, wind = gsn("w_scr_d"), gsn("w_exq"), gsn("w_wind")
    scw, scd, ccs = gsn("w_scw"), gsn("w_scd"), gsn("w_ccs")

    def sy_loads(sy):
        wait_gate(sy, gate)
        for k, t in enumerate(["w1s", "w2s", "h1s", "h2s"]):
            wls.inc(sy.dma_start(out=scl[:, 32 * k:32 * (k + 1)], in_=io[t]), 16)
        wls.inc(sy.dma_start(out=scl[:, 128:136], in_=io["w3s"]), 16)
    SY(sy_loads)

    psC = ExitStack()
    ps_a2 = [C.psum(psC, f"o_ps_a{i}", [128, 512], F32) for i in range(2)]
    ps_b2 = [C.psum(psC, f"o_ps_b{i}", [128, 512], F32) for i in range(2)]
    u_sb = [C.sb(psC, f"o_u{i}", [128, 512], BF16) for i in range(32)]
    v_sb = [C.sb(psC, f"o_v{i}", [128, 512], BF16) for i in range(32)]
    wbuf = [C.sb(psC, f"o_wbuf{i}", [128, 4096], BF16) for i in range(2)]
    silu2 = [C.sb(psC, f"o_silu{i}", [128, 512], BF16) for i in range(2)]

    qkv = io["qkv_scr"].ap()
    for qt in range(4):
        i = qt
        def sy_kv(sy, i=i):
            if i == 0:
                wait_gate(sy, gate)
            else:
                wind.wait(sy)
                sy.wait_ge(w3e.h, 8 * i)
            for a in range(NT):
                kvl.inc(sy.dma_start(out=kv_all[:, 512 * a:512 * (a + 1)],
                                     in_=qkv[128 * a:128 * (a + 1),
                                             1024 * i + 256:1024 * i + 768]), 16)
        SY(sy_kv)

        def sy_qb(sy, i=i):
            for a in range(NT):
                A = 16 * i + a
                if A - 1 > 0:
                    sy.wait_ge(scr_d.h, A - 1)
                qbl.inc(sy.dma_start(out=qb[a % 2][:],
                                     in_=qkv[128 * a:128 * (a + 1),
                                             1024 * i:1024 * i + 256]), 16)
        SY(sy_qb)

        def sy_wb(sy, i=i):
            for a in range(NT):
                A = 16 * i + a
                if A - 1 > 0:
                    sy.wait_ge(wind.h, A - 1)
                wbl.inc(sy.dma_start(out=wb[a % 2][:],
                                     in_=qkv[128 * a:128 * (a + 1),
                                             1024 * i + 768:1024 * (i + 1)]), 16)
        SY(sy_wb)

        amm_marks = []
        sil0, uev0 = sil.n, uev.n
        for g in range(32):
            wi = g % 2

            def g_w12(g_, qt=qt, g=g, wi=wi):
                if qt == 0 and g < 2:
                    wait_gate(g_, gate)
                if g < 2:
                    w3m.wait(g_)
                bmm.wait(g_, lag=8)
                w12l[wi].inc(g_.dma_start(out=wbuf[wi][:, 0:2048], in_=io["w12T"][g]), 16)
            G(g_w12)

            def t_w1(tt, qt=qt, g=g, wi=wi, sil0=sil0):
                w12l[wi].wait(tt)
                if sil0 + g - 1 > 0:
                    tt.wait_ge(sil.h, sil0 + g - 1)
                w3e.wait(tt)
                for ci in range(8):
                    amm.inc(tt.matmul(ps_a2[wi][:, :], wbuf[wi][:, 128 * ci:128 * (ci + 1)],
                                      xn1T[ci][:, 512 * qt:512 * (qt + 1)],
                                      start=(ci == 0), stop=(ci == 7)))
            T(t_w1)
            amm_marks.append(amm.n)

            def t_w2(tt, qt=qt, g=g, wi=wi, sil0=sil0, uev0=uev0):
                if sil0 + g > 0:
                    tt.wait_ge(sil.h, sil0 + g)
                if uev0 + g - 1 > 0:
                    tt.wait_ge(uev.h, uev0 + g - 1)
                for ci in range(8):
                    bmm.inc(tt.matmul(ps_b2[wi][:, :],
                                      wbuf[wi][:, 1024 + 128 * ci:1024 + 128 * (ci + 1)],
                                      xn1T[ci][:, 512 * qt:512 * (qt + 1)],
                                      start=(ci == 0), stop=(ci == 7)))
            T(t_w2)

            def v_u(v, g=g, wi=wi, sil0=sil0):
                bmm.wait(v)
                v.wait_ge(sil.h, sil0 + g + 1)
                hmm.wait(v)
                uev.inc(v.scalar_tensor_tensor(u_sb[g][:], ps_b2[wi][:, :],
                                               scl[:, 32 + g:33 + g],
                                               silu2[wi][:], ALU.mult, ALU.mult))
            VE(v_u)

        def s_si_all(s, qt=qt, marks=tuple(amm_marks), uev0=uev0):
            if qt == 0:
                wls.wait(s)
            for g in range(32):
                s.wait_ge(amm.h, marks[g])
                if uev0 + g - 1 > 0:
                    s.wait_ge(uev.h, uev0 + g - 1)
                sil.inc(s.activation(silu2[g % 2][:], ps_a2[g % 2][:, :], AF.Silu,
                                     scale=scl[:, g:g + 1]))
        SC(s_si_all)

        for a in range(NT):
            g4, al = a // 4, a % 4

            def v_sc(v, i=i, a=a, g4=g4, al=al):
                A = 16 * i + a
                b = g4 % 2
                kvl.wait(v)
                v.wait_ge(qbl.h, 16 * (A + 1))
                if al == 0 and 4 * i + g4 - 1 > 0:
                    v.wait_ge(exq.h, 4 * i + g4 - 1)
                q_ap = qb[a % 2][:].rearrange("p (u d) -> p u d", d=16) \
                    .unsqueeze(1).broadcast_to((128, 16, 16, 16))
                k_ap = kv_all[:].rearrange("p (ap c) -> p ap c", c=512)[:, :, 0:256] \
                    .rearrange("p ap (u d) -> p ap u d", d=16)
                o_ap = wscr[:].rearrange("p (ap u d) -> p ap u d", u=16, d=16)
                vd.inc(v.tensor_tensor(o_ap, q_ap, k_ap, ALU.mult))
                vd.wait(v)
                scr_d.inc(v.tensor_reduce(sc2[b][:, 256 * al:256 * (al + 1)],
                                          wscr[:].rearrange("p (au d) -> p au d", d=16),
                                          AX.X, ALU.add))
            VE(v_sc)

        hmm_marks = []
        hev0, h2m0 = hev.n, h2m.n
        for g in range(32):
            wi = g % 2

            def g_wh(g_, g=g, wi=wi):
                if g < 2:
                    bmm.wait(g_)
                hmm.wait(g_, lag=32)
                whl[wi].inc(g_.dma_start(out=wbuf[wi][:], in_=io["h1T"][g]), 16)
            G(g_wh)

            def t_h1(tt, g=g, wi=wi, hev0=hev0):
                whl[wi].wait(tt)
                uev.wait(tt)
                if hev0 + g - 1 > 0:
                    tt.wait_ge(hev.h, hev0 + g - 1)
                for ci in range(32):
                    hmm.inc(tt.matmul(ps_a2[wi][:, :], wbuf[wi][:, 128 * ci:128 * (ci + 1)],
                                      u_sb[ci][:, :], start=(ci == 0), stop=(ci == 31)))
            T(t_h1)
            hmm_marks.append(hmm.n)

        def s_g1_all(s, marks=tuple(hmm_marks), h2m0=h2m0):
            if h2m0 > 0:
                s.wait_ge(h2m.h, h2m0)
            for g in range(32):
                s.wait_ge(hmm.h, marks[g])
                hev.inc(s.activation(v_sb[g][:], ps_a2[g % 2][:, :], AF.Gelu,
                                     scale=scl[:, 64 + g:65 + g]))
        SC(s_g1_all)

        def s_expw(s, i=i):
            for g4 in range(4):
                b = g4 % 2
                s.wait_ge(scr_d.h, 16 * i + 4 * (g4 + 1))
                t = 16 * i + 4 * g4 - 4
                if t > 0:
                    s.wait_ge(wind.h, t)
                exq.inc(s.activation(P2[b][:], sc2[b][:], AF.Exp, scale=0.25))
        SC(s_expw)

        for a in range(NT):
            g4, al = a // 4, a % 4

            def v_av(v, i=i, a=a, g4=g4, al=al):
                A = 16 * i + a
                Gw = 4 * i + g4
                b = g4 % 2
                v.wait_ge(exq.h, 4 * i + g4 + 1)
                v.wait_ge(wbl.h, 16 * (A + 1))
                if al == 0 and 16 * (Gw - 3) > 0:
                    v.wait_ge(scw.h, 16 * (Gw - 3))
                pv = P2[b][:, 256 * al:256 * (al + 1)].rearrange("p (ap u) -> p u ap", ap=16)
                vd.inc(v.tensor_reduce(sums[:, 16 * a:16 * (a + 1)], pv, AX.X, ALU.add))
                vd.wait(v)
                vd.inc(v.reciprocal(rec[:, 16 * a:16 * (a + 1)], sums[:, 16 * a:16 * (a + 1)]))
                vd.wait(v)
                p_ap = pv.unsqueeze(2).broadcast_to((128, 16, 16, 16))
                v_ap = kv_all[:].rearrange("p (ap c) -> p ap c", c=512)[:, :, 256:512] \
                    .rearrange("p ap (u d) -> p u d ap", d=16)
                t_ap = wscr[:].rearrange("p (u d ap) -> p u d ap", d=16, ap=16)
                vd.inc(v.tensor_tensor(t_ap, p_ap, v_ap, ALU.mult))
                vd.wait(v)
                vd.inc(v.tensor_reduce(ov[:], wscr[:].rearrange("p (ud ap) -> p ud ap", ap=16),
                                       AX.X, ALU.add))
                vd.wait(v)
                r_ap = rec[:, 16 * a:16 * (a + 1)].unsqueeze(2).broadcast_to((128, 16, 16))
                vd.inc(v.tensor_tensor(t1[:].rearrange("p (u d) -> p u d", d=16),
                                       ov[:].rearrange("p (u d) -> p u d", d=16), r_ap, ALU.mult))
                vd.wait(v)
                wind.inc(v.tensor_tensor(scat4[g4][:, 256 * al:256 * (al + 1)], t1[:],
                                         wb[a % 2][:], ALU.mult))
            VE(v_av)

        h2m_marks = []
        h2e0, w3m0 = h2e.n, w3m.n
        for g in range(32):
            wi = g % 2

            def g_wh2(g_, g=g, wi=wi):
                if g < 2:
                    hmm.wait(g_)
                h2m.wait(g_, lag=32)
                whl[wi].inc(g_.dma_start(out=wbuf[wi][:], in_=io["h2T"][g]), 16)
            G(g_wh2)

            def t_h2(tt, g=g, wi=wi, h2e0=h2e0):
                whl[wi].wait(tt)
                hev.wait(tt)
                if h2e0 + g - 1 > 0:
                    tt.wait_ge(h2e.h, h2e0 + g - 1)
                for ci in range(32):
                    h2m.inc(tt.matmul(ps_b2[wi][:, :], wbuf[wi][:, 128 * ci:128 * (ci + 1)],
                                      v_sb[ci][:, :], start=(ci == 0), stop=(ci == 31)))
            T(t_h2)
            h2m_marks.append(h2m.n)

        def s_g2_all(s, marks=tuple(h2m_marks), w3m0=w3m0):
            if w3m0 > 0:
                s.wait_ge(w3m.h, w3m0)
            for g in range(32):
                s.wait_ge(h2m.h, marks[g])
                h2e.inc(s.activation(u_sb[g][:], ps_b2[g % 2][:, :], AF.Gelu,
                                     scale=scl[:, 96 + g:97 + g]))
        SC(s_g2_all)

        w3e0 = w3e.n
        for oc in range(8):
            wi = oc % 2

            def g_w3(g_, oc=oc, wi=wi):
                if oc < 2:
                    h2m.wait(g_)
                w3m.wait(g_, lag=32)
                w3l[wi].inc(g_.dma_start(out=wbuf[wi][:], in_=io["w3T"][oc]), 16)
            G(g_w3)

            def t_w3(tt, oc=oc, wi=wi, w3e0=w3e0):
                w3l[wi].wait(tt)
                h2e.wait(tt)
                if w3e0 + oc - 1 > 0:
                    tt.wait_ge(w3e.h, w3e0 + oc - 1)
                for ci in range(32):
                    w3m.inc(tt.matmul(ps_a2[wi][:, :], wbuf[wi][:, 128 * ci:128 * (ci + 1)],
                                      u_sb[ci][:, :], start=(ci == 0), stop=(ci == 31)))
            T(t_w3)

            def v_w3(v, qt=qt, oc=oc, wi=wi):
                w3m.wait(v)
                w3e.inc(v.tensor_scalar(accT[oc][:, 512 * qt:512 * (qt + 1)],
                                        ps_a2[wi][:, :], scl[:, 128 + oc:129 + oc],
                                        None, ALU.mult))
            VE(v_w3)

        for g4 in range(4):
            def sy_sc(sy, i=i, g4=g4):
                Gw = 4 * i + g4
                sy.wait_ge(wind.h, 16 * i + 4 * (g4 + 1))
                if 64 * (Gw - 3) > 0:
                    sy.wait_ge(scd.h, 64 * (Gw - 3))
                scw.inc(sy.dma_start(out=io["scr"].ap()[:, 1024 * g4:1024 * (g4 + 1)],
                                     in_=scat4[g4][:]), 16)
                sy.wait_ge(scw.h, 16 * (Gw + 1))
                for al in range(4):
                    a = 4 * g4 + al
                    blk = 2 * i + (a % 4) // 2
                    rowb = ((a // 4) * 2 + (a % 2)) * 32
                    dst = io["a2a_in"].ap()[256 * blk + rowb:256 * blk + rowb + 32, :] \
                        .rearrange("t (z w) -> t z w", z=4)
                    srcv = io["scr"].ap()[:, 256 * a:256 * (a + 1)].rearrange("(t z) w -> t z w", z=4)
                    scd.inc(sy.dma_start(out=dst, in_=srcv), 16)
            SY(sy_sc)

    def g_cc(g):
        scd.wait(g)
        with allow_ici_all_to_all():
            ccs.inc(g.collective_compute(
                "AllToAll", mybir.AluOpType.bypass,
                replica_groups=[list(range(NCORES))],
                ins=[io["a2a_in"].ap().opt()],
                outs=[io["a2a_out"].ap().opt()],
            ))
    G(g_cc)

    psC.close()
    esA = ExitStack()
    a1T = [C.sb(esA, f"o_a1T{i}", [128, TOK], BF16) for i in range(8)]
    abuf = [C.sb(esA, f"o_abuf{i}", [128, E], BF16) for i in range(2)]
    wo_sb = [C.sb(esA, f"o_wo{i}", [128, E], BF16) for i in range(2)]
    psA = ExitStack()
    ps_tr = C.psum(psA, "o_ps_tr", [128, 1024], BF16)

    for a in range(NT):
        bi = a % 2

        def sy_a1(sy, a=a, bi=bi):
            if a < 2:
                ccs.wait(sy)
                w3e.wait(sy)
            trp.wait(sy, lag=8)
            for al in range(2):
                for gm in range(2):
                    blk = 2 * (a // 4) + gm
                    rowb = ((a % 4) * 2 + al) * 32
                    a1l[bi].inc(sy.dma_start(
                        out=abuf[bi][64 * al + 32 * gm:64 * al + 32 * gm + 32, :],
                        in_=io["a2a_out"].ap()[256 * blk + rowb:256 * blk + rowb + 32, :]), 16)
        SY(sy_a1)

        def t_tr(tt, a=a, bi=bi):
            a1l[bi].wait(tt)
            evc.wait(tt)
            for ct in range(8):
                trp.inc(tt.transpose(ps_tr[:, 128 * ct:128 * (ct + 1)],
                                     abuf[bi][:, 128 * ct:128 * (ct + 1)], C.ident_bf[:]))
        T(t_tr)

        def v_ev(v, a=a):
            trp.wait(v)
            for ct in range(8):
                evc.inc(v.tensor_copy(a1T[ct][:, 128 * a:128 * (a + 1)],
                                      ps_tr[:, 128 * ct:128 * (ct + 1)]))
        VE(v_ev)

    psB = ExitStack()
    ps_m2 = [C.psum(psB, f"o_ps_m{i}", [128, 512], F32) for i in range(2)]

    for qtr in range(4):
        def g_wo(g, qtr=qtr):
            if qtr == 0:
                ccs.wait(g)
            if qtr > 0:
                wmm.wait(g)
            for k in range(2):
                ci = 2 * qtr + k
                wol.inc(g.dma_start(out=wo_sb[k][:], in_=io["w_out"][128 * ci:128 * (ci + 1), :]), 16)
        G(g_wo)

        for oc in range(8):
            for c4 in range(4):
                def t_wo(tt, qtr=qtr, oc=oc, c4=c4):
                    if qtr == 0 and oc == 0 and c4 == 0:
                        evc.wait(tt)
                    wol.wait(tt)
                    wev.wait(tt, lag=1)
                    mi = (qtr * 32 + oc * 4 + c4) % 2
                    for k in range(2):
                        ci = 2 * qtr + k
                        wmm.inc(tt.matmul(ps_m2[mi][:, :], wo_sb[k][:, 128 * oc:128 * (oc + 1)],
                                          a1T[ci][:, 512 * c4:512 * (c4 + 1)],
                                          start=(k == 0), stop=(k == 1)))
                T(t_wo)

                def v_wo(v, qtr=qtr, oc=oc, c4=c4):
                    wmm.wait(v)
                    if qtr == 0 and oc == 0 and c4 == 0:
                        w3e.wait(v)
                    mi = (qtr * 32 + oc * 4 + c4) % 2
                    wev.inc(v.tensor_tensor(accT[oc][:, 512 * c4:512 * (c4 + 1)],
                                            accT[oc][:, 512 * c4:512 * (c4 + 1)],
                                            ps_m2[mi][:, :], ALU.add))
                VE(v_wo)

    psB.close()
    psA.close()
    esA.close()
    psF = ExitStack()
    ps_f = C.psum(psF, "o_ps_f", [128, 128], BF16)
    xbuf = [C.sb(psF, f"o_xbuf{i}", [128, E], F32) for i in range(2)]
    x1b = [C.sb(psF, f"o_x1b{i}", [128, E], F32) for i in range(2)]

    for a in range(NT):
        bi = a % 2

        def sy_xa(sy, a=a, bi=bi):
            if a < 2:
                w3e.wait(sy)
                wev.wait(sy)
            finv.wait(sy, lag=8)
            xld[bi].inc(sy.dma_start(out=xbuf[bi][:],
                                     in_=io["x1_scr"].ap()[128 * a:128 * (a + 1), :]), 16)
        SY(sy_xa)

        for ct in range(8):
            def t_f(tt, a=a, ct=ct):
                if a == 0 and ct == 0:
                    w3e.wait(tt)
                    wev.wait(tt)
                finv.wait(tt)
                fint.inc(tt.transpose(ps_f[:, :], accT[ct][:, 128 * a:128 * (a + 1)],
                                      C.ident_bf[:]))
            T(t_f)

            def v_f(v, a=a, bi=bi, ct=ct):
                fint.wait(v)
                xld[bi].wait(v)
                if ct == 0:
                    sto.wait(v, lag=16)
                finv.inc(v.tensor_tensor(x1b[bi][:, 128 * ct:128 * (ct + 1)],
                                         xbuf[bi][:, 128 * ct:128 * (ct + 1)],
                                         ps_f[:, :], ALU.add))
            VE(v_f)

        def sy_o(sy, a=a, bi=bi):
            finv.wait(sy)
            sto.inc(sy.dma_start(out=io["out"][128 * a:128 * (a + 1), :], in_=x1b[bi][:]), 16)
        SY(sy_o)

    def sy_fin(sy):
        sto.wait(sy)
    SY(sy_fin)
    psF.close()
    ph.close()



def build_mega():
    nc = bass.Bass("TRN2", target_bir_lowering=False, detect_race_conditions=False)
    dt = lambda n, s, d: nc.dram_tensor(n, s, d, kind="ExternalInput").ap()
    io = dict(
        x_own=dt("x_own", [TOK, E], F32),
        wqa=dt("wqa", [E, QL], F32),
        wqb=dt("wqb", [QL, E], F32),
        wkva=dt("wkva", [E, KVL], F32),
        wkvb_kv=dt("wkvb_kv", [KVL, 512], F32),
        ropek_cos=dt("ropek_cos", [S, 32], F32),
        ropek_sin=dt("ropek_sin", [S, 32], F32),
        ropeq_cos=dt("ropeq_cos", [TOK, 32], F32),
        ropeq_sin=dt("ropeq_sin", [TOK, 32], F32),
        mask0=dt("mask0", [128, 128], F32),
        mask1=dt("mask1", [128, 128], F32),
        ident=dt("ident", [128, 128], F32),
        wout=dt("wout", [E, E], BF16),
        bout=dt("bout", [128, 8], F32),
        routerT=dt("routerT", [E, NEXP], F32),
        gup_w=dt("gup_w", [NEXP, 8, 128, 2048], BF16),
        dn_w=dt("dn_w", [NEXP, E, E], BF16),
        sgsu=dt("sgsu", [16, 128, 2048], BF16),
        sdwT=dt("sdwT", [SHD, E], BF16),
        onehot=dt("onehot", [NEXP, 2048], F32),
        wqkv=dt("wqkv", [E, 4 * E], BF16),
        w_out=dt("w_out", [E, E], BF16),
        w12T=dt("w12T", [32, 128, 2048], BF16),
        h1T=dt("h1T", [32, 128, 4096], BF16),
        h2T=dt("h2T", [32, 128, 4096], BF16),
        w3T=dt("w3T", [8, 128, 4096], BF16),
        w1s=dt("w1s", [128, 32], F32),
        w2s=dt("w2s", [128, 32], F32),
        h1s=dt("h1s", [128, 32], F32),
        h2s=dt("h2s", [128, 32], F32),
        w3s=dt("w3s", [128, 8], F32),
        out=nc.dram_tensor("out", [TOK, E], F32, kind="ExternalOutput").ap(),
        x1_scr=nc.dram_tensor("x1_scr", [TOK, E], F32),
        qkv_scr=nc.dram_tensor("qkv_scr", [TOK, 4 * E], BF16),
        scr=nc.dram_tensor("scr", [128, 4096], BF16),
        a2a_in=nc.dram_tensor("a2a_in", [TOK, E], BF16),
        a2a_out=nc.dram_tensor("a2a_out", [TOK, E], BF16),
        lat_in=nc.dram_tensor("lat_in", [KVL, TOK], BF16),
        lat_out=nc.dram_tensor("lat_out", [2 * KVL, TOK], BF16),
    )

    es = ExitStack()
    with es:
        block = es.enter_context(nc.Block())
        C = Ctx(nc, es, block)
        C.ident_bf = es.enter_context(nc.sbuf_tensor("ident_bf", [128, 128], BF16))
        C.ident_f32 = es.enter_context(nc.sbuf_tensor("ident_f32", [128, 128], F32))

        es_xn = ExitStack()
        xnT = [es_xn.enter_context(nc.sbuf_tensor(f"xnT{i}", [128, TOK], BF16)) for i in range(8)]
        es_at = ExitStack()
        attnT = [es_at.enter_context(nc.sbuf_tensor(f"attnT{i}", [128, TOK], BF16)) for i in range(8)]
        g1 = emit_l0attn(C, io, xnT, attnT)
        es_h = ExitStack()
        io["h_sb"] = [es_h.enter_context(nc.sbuf_tensor(f"h{i}", [128, TOK], BF16)) for i in range(8)]
        g2 = emit_l0ffn(C, io, xnT, attnT, es_at, g1)
        es_h.close()
        es_at.close()
        es_xn.close()

        es_xn1 = ExitStack()
        xn1T = [es_xn1.enter_context(nc.sbuf_tensor(f"xn1T{i}", [128, TOK], BF16)) for i in range(8)]
        g3 = emit_l1qkv(C, io, xn1T, g2)
        emit_l1fused(C, io, xn1T, g3)
        es_xn1.close()
    return nc



def _rope_tables():
    inv = 10000.0 ** (-np.arange(0, D, 2, dtype=np.float64) / D)
    pos = np.arange(S, dtype=np.float64)
    ang = np.outer(pos, inv)
    return np.cos(ang).astype(np.float32), np.sin(ang).astype(np.float32)


def _own_rows(core):
    b, h = core // 2, core % 2
    a = np.arange(NT)
    starts = b * S + 256 * a + 128 * h
    return (starts[:, None] + np.arange(128)[None, :]).reshape(-1)


def _inmaps(inputs):
    x2d = np.asarray(inputs["L0_attn_x"]).reshape(B * S, E)
    cosf, sinf = _rope_tables()
    ident = np.eye(128, dtype=np.float32)
    tri = np.triu(np.ones((128, 128), dtype=np.float32))
    ones = np.ones((128, 128), dtype=np.float32)
    zeros = np.zeros((128, 128), dtype=np.float32)
    wkvb = np.asarray(inputs["L0_attn_Wkvb"])
    wkvb_kv = np.ascontiguousarray(np.concatenate([wkvb[:, 0::2], wkvb[:, 1::2]], axis=1))
    onehot = np.zeros((NEXP, 2048), dtype=np.float32)
    for e in range(NEXP):
        onehot[e, 128 * e:128 * (e + 1)] = 1.0
    rs = lambda s: np.ascontiguousarray(np.asarray(s).reshape(-1, 128).T)
    import ml_dtypes
    bf = lambda a: np.ascontiguousarray(a).astype(ml_dtypes.bfloat16)
    common = dict(
        wqa=np.asarray(inputs["L0_attn_Wqa"]),
        wqb=np.asarray(inputs["L0_attn_Wqb"]),
        wkva=np.asarray(inputs["L0_attn_Wkva"]),
        wkvb_kv=wkvb_kv,
        ropek_cos=cosf, ropek_sin=sinf,
        ident=ident,
        wout=bf(np.asarray(inputs["L0_attn_Wout"])),
        bout=np.ascontiguousarray(np.asarray(inputs["L0_attn_bout"]).reshape(8, 128).T),
        routerT=np.ascontiguousarray(np.asarray(inputs["L0_ffn_router_w"]).T),
        gup_w=bf(np.asarray(inputs["L0_ffn_gate_up_w"])
                 .reshape(NEXP, 8, 128, 16, 128).transpose(0, 3, 2, 1, 4)
                 .reshape(NEXP, 2, 8, 128, 1024).transpose(0, 2, 3, 1, 4)
                 .reshape(NEXP, 8, 128, 2048)),
        dn_w=bf(np.asarray(inputs["L0_ffn_down_w"])),
        sgsu=bf(np.concatenate([
            np.asarray(inputs["L0_ffn_shared_gate_w"]).reshape(16, 128, 8, 128).transpose(0, 3, 2, 1).reshape(16, 128, 1024),
            np.asarray(inputs["L0_ffn_shared_up_w"]).reshape(16, 128, 8, 128).transpose(0, 3, 2, 1).reshape(16, 128, 1024),
        ], axis=2)),
        sdwT=bf(np.asarray(inputs["L0_ffn_shared_down_w"]).T),
        onehot=onehot,
        wqkv=bf(np.asarray(inputs["L1_attn_w_qkvw"])),
        w_out=bf(np.asarray(inputs["L1_attn_w_out"])),
        w12T=bf(np.concatenate([
            np.asarray(inputs["L1_ffn_w1_w"]).reshape(32, 128, 8, 128).transpose(0, 3, 2, 1).reshape(32, 128, 1024),
            np.asarray(inputs["L1_ffn_w2_w"]).reshape(32, 128, 8, 128).transpose(0, 3, 2, 1).reshape(32, 128, 1024),
        ], axis=2)),
        h1T=bf(np.asarray(inputs["L1_ffn_h1_w"])
               .reshape(32, 128, 32, 128).transpose(0, 3, 2, 1).reshape(32, 128, 4096)),
        h2T=bf(np.asarray(inputs["L1_ffn_h2_w"])
               .reshape(32, 128, 32, 128).transpose(0, 3, 2, 1).reshape(32, 128, 4096)),
        w3T=bf(np.asarray(inputs["L1_ffn_w3_w"])
               .reshape(8, 128, 32, 128).transpose(0, 3, 2, 1).reshape(8, 128, 4096)),
        w1s=rs(inputs["L1_ffn_w1_s"]), w2s=rs(inputs["L1_ffn_w2_s"]),
        h1s=rs(inputs["L1_ffn_h1_s"]), h2s=rs(inputs["L1_ffn_h2_s"]),
        w3s=rs(inputs["L1_ffn_w3_s"]),
    )
    maps = []
    for c in range(NCORES):
        b, h = c // 2, c % 2
        rows = _own_rows(c)
        m = dict(common)
        m.update(
            x_own=np.ascontiguousarray(x2d[rows]),
            ropeq_cos=np.ascontiguousarray(cosf[rows - b * S]),
            ropeq_sin=np.ascontiguousarray(sinf[rows - b * S]),
            mask0=(tri if h == 0 else ones),
            mask1=(zeros if h == 0 else tri),
        )
        maps.append(m)
    return maps


def kernel(**inputs):
    cores = list(range(NCORES))
    nc = build_mega()
    res = run_bass_kernel_spmd(nc, _inmaps(inputs), core_ids=cores)
    out = np.empty((B * S, E), dtype=np.float32)
    for c in cores:
        out[_own_rows(c)] = res.results[c]["out"]
    return out.reshape(B, S, E)


# version 26
# speedup vs baseline: 1.0182x; 1.0182x over previous
import numpy as np
from contextlib import ExitStack

import concourse.bass as bass
import concourse.mybir as mybir
from concourse.bass_utils import run_bass_kernel_spmd
from concourse.aot_env import allow_ici_all_to_all

F32 = mybir.dt.float32
BF16 = mybir.dt.bfloat16
AF = mybir.ActivationFunctionType
ALU = mybir.AluOpType
AX = mybir.AxisListType

B, S, E = 4, 4096, 1024
NCORES = 8
NT = 16
TOK = 2048
H, KVH, D = 16, 4, 64
QL, KVL = 64, 32
NEXP = 16
SHD = 2048
DHID = 4096
LN_EPS = 1e-5


class Sem:
    def __init__(self, nc, es, name):
        self.h = es.enter_context(nc.semaphore(name))
        self.n = 0

    def inc(self, inst, k=1):
        inst.then_inc(self.h, k)
        self.n += k
        return inst

    def wait(self, eng, lag=0):
        v = self.n - lag
        if v > 0:
            eng.wait_ge(self.h, v)

    def mark(self):
        return (self.h, self.n)


class Ctx:
    def __init__(self, nc, es, block):
        self.nc, self.es, self.block = nc, es, block
        self.T, self.V, self.S, self.Y, self.G = (
            block.tensor, block.vector, block.scalar, block.sync, block.gpsimd)
        self._sems = {}

    def gs(self, name):
        if name not in self._sems:
            self._sems[name] = Sem(self.nc, self.es, name)
        return self._sems[name]

    def sb(self, stack, name, shape, dt):
        return stack.enter_context(self.nc.sbuf_tensor(name, shape, dt))

    def psum(self, stack, name, shape, dt):
        return stack.enter_context(self.nc.psum_tensor(name, shape, dt))


def wait_gate(eng, gate):
    for (h, v) in gate:
        if v > 0:
            eng.wait_ge(h, v)



def emit_l0attn(C, io, xnT, attnT):
    nc, es = C.nc, C.es
    T, VE, SC, SY, G = C.T, C.V, C.S, C.Y, C.G
    x_own = io["x_own"]

    ph = ExitStack()
    sb = lambda n, s, d: C.sb(ph, n, s, d)
    KT = [sb(f"KT{i}", [128, S], BF16) for i in range(2)]
    Vt = sb("Vt", [128, 32 * 260], BF16)
    QT = [sb(f"QT{i}", [128, TOK], BF16) for i in range(8)]
    latT = sb("latT", [KVL, S], BF16)
    qlatT = sb("qlatT", [QL, TOK], BF16)
    wqa_sb = sb("wqa_sb", [128, 8 * QL], BF16)
    wqb_sb = sb("wqb_sb", [QL, E], BF16)
    wkva_sb = sb("wkva_sb", [128, 8 * KVL], BF16)
    wkvb_sb = sb("wkvb_sb", [KVL, 512], BF16)
    mask_sb = [sb(f"mask{i}_sb", [128, 128], BF16) for i in range(2)]
    ones64 = sb("ones64", [1, 64], F32)
    ropek_c = sb("ropek_c", [128, 32 * 32], F32)
    ropek_s = sb("ropek_s", [128, 32 * 32], F32)
    ropeq_c = sb("ropeq_c", [128, 16 * 32], F32)
    ropeq_s = sb("ropeq_s", [128, 16 * 32], F32)
    xbuf = [sb(f"xbuf{i}", [128, E], F32) for i in range(2)]
    xsq = [sb(f"xsq{i}", [128, E], F32) for i in range(2)]
    stat2 = [sb(f"stat{i}", [128, 8], F32) for i in range(2)]
    xn_tmp = [sb(f"xn_tmp{i}", [128, E], BF16) for i in range(2)]
    lat_own = sb("lat_own", [KVL, TOK], BF16)
    kq_tmp = [sb(f"kq_tmp{i}", [128, E], BF16) for i in range(2)]
    rtA = sb("rtA", [128, 512], F32)
    rtB = sb("rtB", [128, 512], F32)
    expS = [sb(f"expS{i}", [128, 512], BF16) for i in range(4)]
    recb = [sb(f"recb{i}", [64, 512], F32) for i in range(2)]
    rrow = [sb(f"rrow{i}", [1, 512], F32) for i in range(2)]

    gsn = lambda n: C.gs("a_" + n)
    vd = gsn("vd")
    wl, wls = gsn("wl"), gsn("wls")
    xl = [gsn("xl0"), gsn("xl1")]
    lnf, trp, evc = gsn("lnf"), gsn("trp"), gsn("evc")
    lns, sds, lnx = gsn("lns"), gsn("sds"), gsn("lnx")
    latm, late = gsn("latm"), gsn("late")
    lsto, agd, latl = gsn("lsto"), gsn("agd"), gsn("latl")
    kvm, roped, kqt, kqe = gsn("kvm"), gsn("roped"), gsn("kqt"), gsn("kqe")
    qlm, qle, qmm = gsn("qlm"), gsn("qle"), gsn("qmm")
    scm, exd, mkd, avd = gsn("scm"), gsn("exd"), gsn("mkd"), gsn("avd")
    nr1, nr2, nr3 = gsn("nr1"), gsn("nr2"), gsn("nr3")

    def g_loads(g):
        wl.inc(g.dma_start(out=wqa_sb[:], in_=io["wqa"].rearrange("(c p) n -> p c n", p=128)), 16)
        wl.inc(g.dma_start(out=wqb_sb[:], in_=io["wqb"]), 16)
        wl.inc(g.dma_start(out=wkva_sb[:], in_=io["wkva"].rearrange("(c p) n -> p c n", p=128)), 16)
        wl.inc(g.dma_start(out=wkvb_sb[:], in_=io["wkvb_kv"]), 16)
        wl.inc(g.dma_start(out=C.ident_bf[:], in_=io["ident"]), 16)
        wl.inc(g.dma_start(out=mask_sb[0][:], in_=io["mask0"]), 16)
        wl.inc(g.dma_start(out=mask_sb[1][:], in_=io["mask1"]), 16)
    G(g_loads)

    def sy_loads(sy):
        wls.inc(sy.dma_start(out=C.ident_f32[:], in_=io["ident"]), 16)
        wls.inc(sy.dma_start(out=ropek_c[:].rearrange("p (t f) -> p t f", f=32),
                             in_=io["ropek_cos"].rearrange("(t p) f -> p t f", p=128)), 16)
        wls.inc(sy.dma_start(out=ropek_s[:].rearrange("p (t f) -> p t f", f=32),
                             in_=io["ropek_sin"].rearrange("(t p) f -> p t f", p=128)), 16)
        wls.inc(sy.dma_start(out=ropeq_c[:].rearrange("p (t f) -> p t f", f=32),
                             in_=io["ropeq_cos"].rearrange("(t p) f -> p t f", p=128)), 16)
        wls.inc(sy.dma_start(out=ropeq_s[:].rearrange("p (t f) -> p t f", f=32),
                             in_=io["ropeq_sin"].rearrange("(t p) f -> p t f", p=128)), 16)
    SY(sy_loads)

    def v_init(v):
        wl.wait(v)
        wls.wait(v)
        v.memset(ones64[:], 1.0)
        lnf.inc(v.memset(
            Vt[:].rearrange("p (t z w) -> p t z w", z=4, w=65)[:, :, :, 64:65], 1.0))
    VE(v_init)

    def emit_ln_tile(src_ap, dst_bf_ap, ld_sem, bi, extra_war=None):
        stat = stat2[bi]
        xq = xsq[bi]

        def v_stats(v):
            ld_sem.wait(v)
            sds.wait(v, lag=1)
            v.tensor_tensor(xq[:], src_ap, src_ap, ALU.mult)
            v.tensor_reduce(stat[:, 0:1], src_ap, AX.X, ALU.add)
            vd.inc(v.tensor_reduce(stat[:, 1:2], xq[:], AX.X, ALU.add))
            vd.wait(v)
            vd.inc(v.tensor_scalar(stat[:, 2:3], stat[:, 0:1], 1.0 / E, None, ALU.mult))
            vd.wait(v)
            vd.inc(v.tensor_tensor(stat[:, 3:4], stat[:, 2:3], stat[:, 2:3], ALU.mult))
            v.tensor_scalar(stat[:, 4:5], stat[:, 1:2], 1.0 / E, LN_EPS, ALU.mult, ALU.add)
            vd.wait(v)
            lns.inc(v.tensor_tensor(stat[:, 4:5], stat[:, 4:5], stat[:, 3:4], ALU.subtract))
        VE(v_stats)

        def s_rstd(s):
            lns.wait(s)
            lnx.inc(s.activation(stat[:, 5:6], stat[:, 4:5], AF.Ln))
            lnx.wait(s)
            sds.inc(s.activation(stat[:, 6:7], stat[:, 5:6], AF.Exp, scale=-0.5))
        SC(s_rstd)

        def v_norm(v):
            sds.wait(v)
            if extra_war is not None:
                extra_war(v)
            lnf.inc(v.tensor_scalar(dst_bf_ap, src_ap, stat[:, 2:3], stat[:, 6:7],
                                    ALU.subtract, ALU.mult))
        VE(v_norm)

    psA = ExitStack()
    ps_tr = C.psum(psA, "ps_tr", [128, 1024], BF16)
    ps_lat = C.psum(psA, "ps_lat", [KVL, 128], F32)

    for a in range(NT):
        bi = a % 2

        def sy_x(sy, a=a, bi=bi):
            lnf.wait(sy, lag=1)
            xl[bi].inc(sy.dma_start(out=xbuf[bi][:], in_=x_own[128 * a:128 * (a + 1), :]), 16)
        SY(sy_x)

        emit_ln_tile(xbuf[bi][:], xn_tmp[bi][:], xl[bi], bi,
                     extra_war=(lambda v: trp.wait(v, lag=8)))

        def t_trans(tt, bi=bi):
            lnf.wait(tt)
            evc.wait(tt)
            for ct in range(8):
                trp.inc(tt.transpose(ps_tr[:, 128 * ct:128 * (ct + 1)],
                                     xn_tmp[bi][:, 128 * ct:128 * (ct + 1)], C.ident_bf[:]))
        T(t_trans)

        def v_evac(v, a=a):
            trp.wait(v)
            for ct in range(8):
                evc.inc(v.tensor_copy(xnT[ct][:, 128 * a:128 * (a + 1)],
                                      ps_tr[:, 128 * ct:128 * (ct + 1)]))
        VE(v_evac)

        def t_lat(tt, a=a):
            evc.wait(tt)
            late.wait(tt)
            for ct in range(8):
                latm.inc(tt.matmul(ps_lat[:, :], wkva_sb[:, KVL * ct:KVL * (ct + 1)],
                                   xnT[ct][:, 128 * a:128 * (a + 1)],
                                   start=(ct == 0), stop=(ct == 7)))
        T(t_lat)

        def v_lat(v, a=a):
            latm.wait(v)
            late.inc(v.tensor_copy(lat_own[:, 128 * a:128 * (a + 1)], ps_lat[:, :]))
        VE(v_lat)

    def sy_lat(sy):
        late.wait(sy)
        lsto.inc(sy.dma_start(out=io["lat_in"].ap(), in_=lat_own[:]), 16)
    SY(sy_lat)

    def g_ag(g):
        lsto.wait(g)
        with allow_ici_all_to_all():
            agd.inc(g.collective_compute(
                "AllGather", mybir.AluOpType.bypass,
                replica_groups=[[0, 1], [2, 3], [4, 5], [6, 7]],
                ins=[io["lat_in"].ap().opt()],
                outs=[io["lat_out"].ap().opt()],
            ))
    G(g_ag)

    def sy_latl(sy):
        agd.wait(sy)
        lo = io["lat_out"].ap()
        latl.inc(sy.dma_start(
            out=latT[:].rearrange("l (t two c) -> l t two c", two=2, c=128)[:, :, 0, :],
            in_=lo[0:KVL, :].rearrange("l (t c) -> l t c", c=128)), 16)
        latl.inc(sy.dma_start(
            out=latT[:].rearrange("l (t two c) -> l t two c", two=2, c=128)[:, :, 1, :],
            in_=lo[KVL:2 * KVL, :].rearrange("l (t c) -> l t c", c=128)), 16)
    SY(sy_latl)

    psA.close()
    psB = ExitStack()
    ps_ql = C.psum(psB, "ps_ql", [QL, 512], F32)
    ps_q = C.psum(psB, "ps_q", [128, 1024], F32)
    ps_qt = C.psum(psB, "ps_qt", [64, 256], BF16)

    for qc in range(4):
        def t_ql(tt, qc=qc):
            qle.wait(tt)
            for ct in range(8):
                qlm.inc(tt.matmul(ps_ql[:, :], wqa_sb[:, QL * ct:QL * (ct + 1)],
                                  xnT[ct][:, 512 * qc:512 * (qc + 1)],
                                  start=(ct == 0), stop=(ct == 7)))
        T(t_ql)

        def v_ql(v, qc=qc):
            qlm.wait(v)
            qle.inc(v.tensor_copy(qlatT[:, 512 * qc:512 * (qc + 1)], ps_ql[:, :]))
        VE(v_ql)

    for a in range(NT):
        bi = a % 2

        def t_q(tt, a=a):
            qle.wait(tt)
            roped.wait(tt)
            qmm.inc(tt.matmul(ps_q[:, 0:512], qlatT[:, 128 * a:128 * (a + 1)],
                              wqb_sb[:, 0:512], start=True, stop=True))
            qmm.inc(tt.matmul(ps_q[:, 512:1024], qlatT[:, 128 * a:128 * (a + 1)],
                              wqb_sb[:, 512:1024], start=True, stop=True))
        T(t_q)

        def v_ropeq(v, a=a, bi=bi):
            qmm.wait(v)
            kqt.wait(v, lag=16)
            e_ps = ps_q[:, :].rearrange("p (hh f2) -> p hh f2", hh=16)[:, :, 0::2]
            o_ps = ps_q[:, :].rearrange("p (hh f2) -> p hh f2", hh=16)[:, :, 1::2]
            cosb = ropeq_c[:, 32 * a:32 * (a + 1)].unsqueeze(1).broadcast_to((128, 16, 32))
            sinb = ropeq_s[:, 32 * a:32 * (a + 1)].unsqueeze(1).broadcast_to((128, 16, 32))
            rA = rtA[:, :].rearrange("p (hh f) -> p hh f", hh=16)
            rB = rtB[:, :].rearrange("p (hh f) -> p hh f", hh=16)
            qe = kq_tmp[bi][:, :].rearrange("p (hh f2) -> p hh f2", hh=16)[:, :, 0::2]
            qo = kq_tmp[bi][:, :].rearrange("p (hh f2) -> p hh f2", hh=16)[:, :, 1::2]
            v.tensor_tensor(rA, e_ps, cosb, ALU.mult)
            vd.inc(v.tensor_tensor(rB, o_ps, sinb, ALU.mult))
            vd.wait(v)
            vd.inc(v.tensor_tensor(qe, rA, rB, ALU.subtract))
            vd.wait(v)
            v.tensor_tensor(rA, e_ps, sinb, ALU.mult)
            vd.inc(v.tensor_tensor(rB, o_ps, cosb, ALU.mult))
            vd.wait(v)
            roped.inc(v.tensor_tensor(qo, rA, rB, ALU.add))
        VE(v_ropeq)

        for grp in range(8):
            def t_qt(tt, bi=bi, grp=grp):
                roped.wait(tt)
                kqe.wait(tt)
                h0, h1 = 2 * grp, 2 * grp + 1
                kqt.inc(tt.transpose(ps_qt[:, 0:128], kq_tmp[bi][:, 64 * h0:64 * (h0 + 1)],
                                     C.ident_bf[:]))
                kqt.inc(tt.transpose(ps_qt[:, 128:256], kq_tmp[bi][:, 64 * h1:64 * (h1 + 1)],
                                     C.ident_bf[:]))
            T(t_qt)

            def v_qte(v, a=a, grp=grp):
                kqt.wait(v)
                for k in range(2):
                    h_ = 2 * grp + k
                    p = (h_ % 4) + 4 * (h_ // 8)
                    up = (h_ // 4) % 2
                    kqe.inc(v.tensor_copy(QT[p][64 * up:64 * up + 64, 128 * a:128 * (a + 1)],
                                          ps_qt[:, 128 * k:128 * (k + 1)]))
            VE(v_qte)

    ps_kv = C.psum(psB, "ps_kv", [128, 512], F32)
    ps_kt = C.psum(psB, "ps_kt", [128, 256], BF16)

    for t in range(32):
        bi = t % 2

        def t_kv(tt, t=t):
            if t == 0:
                latl.wait(tt)
            roped.wait(tt)
            kvm.inc(tt.matmul(ps_kv[:, :], latT[:, 128 * t:128 * (t + 1)], wkvb_sb[:, :],
                              start=True, stop=True))
        T(t_kv)

        def v_ropek(v, t=t, bi=bi):
            kvm.wait(v)
            kqt.wait(v, lag=2)
            e_ps = ps_kv[:, 0:256].rearrange("p (k f2) -> p k f2", k=4)[:, :, 0::2]
            o_ps = ps_kv[:, 0:256].rearrange("p (k f2) -> p k f2", k=4)[:, :, 1::2]
            cosb = ropek_c[:, 32 * t:32 * (t + 1)].unsqueeze(1).broadcast_to((128, 4, 32))
            sinb = ropek_s[:, 32 * t:32 * (t + 1)].unsqueeze(1).broadcast_to((128, 4, 32))
            rA = rtA[:, 0:128].rearrange("p (k f) -> p k f", k=4)
            rB = rtB[:, 0:128].rearrange("p (k f) -> p k f", k=4)
            ke = kq_tmp[bi][:, 0:256].rearrange("p (k f2) -> p k f2", k=4)[:, :, 0::2]
            ko = kq_tmp[bi][:, 0:256].rearrange("p (k f2) -> p k f2", k=4)[:, :, 1::2]
            v.tensor_tensor(rA, e_ps, cosb, ALU.mult)
            vd.inc(v.tensor_tensor(rB, o_ps, sinb, ALU.mult))
            vd.wait(v)
            vd.inc(v.tensor_tensor(ke, rA, rB, ALU.subtract))
            vd.wait(v)
            v.tensor_tensor(rA, e_ps, sinb, ALU.mult)
            vd.inc(v.tensor_tensor(rB, o_ps, cosb, ALU.mult))
            vd.wait(v)
            v.tensor_tensor(ko, rA, rB, ALU.add)
            vdst = Vt[:, 260 * t:260 * t + 260].rearrange("p (z w) -> p z w", z=4)[:, :, 0:64]
            vsrc = ps_kv[:, 256:512].rearrange("p (z w) -> p z w", z=4)
            roped.inc(v.tensor_copy(vdst, vsrc))
        VE(v_ropek)

        def t_kt(tt, bi=bi):
            roped.wait(tt)
            kqe.wait(tt)
            kqt.inc(tt.transpose(ps_kt[:, 0:128], kq_tmp[bi][:, 0:128], C.ident_bf[:]))
            kqt.inc(tt.transpose(ps_kt[:, 128:256], kq_tmp[bi][:, 128:256], C.ident_bf[:]))
        T(t_kt)

        def v_kte(v, t=t):
            kqt.wait(v)
            v.tensor_copy(KT[0][:, 128 * t:128 * (t + 1)], ps_kt[:, 0:128])
            kqe.inc(v.tensor_copy(KT[1][:, 128 * t:128 * (t + 1)], ps_kt[:, 128:256]))
        VE(v_kte)

    psB.close()
    psC = ExitStack()
    ps_s = [C.psum(psC, f"ps_s{i}", [128, 512], F32) for i in range(4)]
    ps_o = [C.psum(psC, f"ps_o{i}", [65, 512], F32) for i in range(2)]
    ps_r = C.psum(psC, "ps_r", [64, 512], F32)

    state = {"first": True, "gsi": 0, "quad": 0}

    def quad(h_, q4):
        kvh = h_ // 4
        ktn = KT[h_ // 8]
        up = (h_ // 4) % 2
        qtn = QT[(h_ % 4) + 4 * (h_ // 8)]
        col0 = 512 * q4
        ob = state["quad"] % 2
        steps = []
        for kt in range(8 * q4 + 2):
            steps.append((kt, 0, 512, 0 if kt >= 8 * q4 else None))
        for m in range(1, 4):
            for kt in (8 * q4 + 2 * m, 8 * q4 + 2 * m + 1):
                steps.append((kt, 128 * m, 512 - 128 * m, m))
        nsteps = len(steps)
        pend = []
        exp_plan = []
        for si, (kt, off, n, mb) in enumerate(steps):
            G = state["gsi"]
            sl = G % 4
            state["gsi"] += 1

            def t_sc(tt, kt=kt, off=off, n=n, sl=sl, G=G):
                if state["first"]:
                    kqe.wait(tt)
                    roped.wait(tt)
                    qle.wait(tt)
                    state["first"] = False
                if G - 3 > 0:
                    tt.wait_ge(exd.h, G - 3)
                scm.inc(tt.matmul(ps_s[sl][:, 0:n],
                                  ktn[64 * up:64 * up + 64, 128 * kt:128 * (kt + 1)],
                                  qtn[64 * up:64 * up + 64, col0 + off:col0 + off + n],
                                  start=True, stop=True))
            T(t_sc)
            exp_plan.append((sl, n, scm.n, G))

            if mb is not None:
                def v_mask(v, kt=kt, sl=sl, G=G):
                    v.wait_ge(exd.h, G + 1)
                    mkd.inc(v.tensor_tensor(expS[sl][:, 0:128], expS[sl][:, 0:128],
                                            mask_sb[kt % 2][:], ALU.mult))
                VE(v_mask)
            mkd_goal = mkd.n

            def t_av(tt, kt=kt, off=off, n=n, sl=sl, si=si, G=G, mkd_goal=mkd_goal):
                tt.wait_ge(exd.h, G + 1)
                if mkd_goal > 0:
                    tt.wait_ge(mkd.h, mkd_goal)
                if si == 0:
                    nr3.wait(tt, lag=1)
                avd.inc(tt.matmul(ps_o[ob][:, off:off + n],
                                  Vt[:, 260 * kt + 65 * kvh:260 * kt + 65 * kvh + 65],
                                  expS[sl][:, 0:n],
                                  start=(si == 0), stop=(si == nsteps - 1)))
            pend.append(t_av)
            if len(pend) > 2:
                T(pend.pop(0))
        while pend:
            T(pend.pop(0))

        def s_exp_all(s, plan=tuple(exp_plan)):
            for (sl, n, scm_t, G) in plan:
                s.wait_ge(scm.h, scm_t)
                if G - 3 > 0:
                    s.wait_ge(avd.h, G - 3)
                exd.inc(s.activation(expS[sl][:, 0:n], ps_s[sl][:, 0:n],
                                     AF.Exp, scale=0.125))
        SC(s_exp_all)

        def v_rec(v):
            avd.wait(v)
            nr2.wait(v, lag=1)
            nr1.inc(v.reciprocal(rrow[ob][:, :], ps_o[ob][64:65, :]))
        VE(v_rec)

        def t_bc(tt):
            nr1.wait(tt)
            nr3.wait(tt)
            nr2.inc(tt.matmul(ps_r[:, :], ones64[:, :], rrow[ob][:, :], start=True, stop=True))
        T(t_bc)

        def v_fin(v, h_=h_, q4=q4):
            nr2.wait(v)
            vd.inc(v.tensor_copy(recb[ob][:, :], ps_r[:, :]))
            vd.wait(v)
            dst = attnT[h_ // 2][64 * (h_ % 2):64 * (h_ % 2) + 64, 512 * q4:512 * (q4 + 1)]
            nr3.inc(v.tensor_tensor(dst, ps_o[ob][0:64, :], recb[ob][:, :], ALU.mult))
        VE(v_fin)
        state["quad"] += 1

    for h_ in range(H):
        for q4 in range(4):
            quad(h_, q4)

    psC.close()
    ph.close()
    return [nr3.mark(), evc.mark(), kqe.mark(), roped.mark()]



def emit_l0ffn(C, io, xnT, attnT, attnT_stack, gate):
    nc = C.nc
    T, VE, SC, SY, G = C.T, C.V, C.S, C.Y, C.G

    ph = ExitStack()
    sb = lambda n, s, d: C.sb(ph, n, s, d)
    accT = [sb(f"f_accT{i}", [128, TOK], F32) for i in range(8)]
    dn_sb = [sb(f"f_dn{i}", [128, 1024], BF16) for i in range(8)]
    wbc = sb("f_wbc", [128, TOK], BF16)
    wT_sb = sb("f_wT", [NEXP, TOK], BF16)
    onehot_sb = sb("f_onehot", [NEXP, 2048], BF16)
    routerT_sb = sb("f_routerT", [128, 8 * NEXP], BF16)
    bout_sb = sb("f_bout", [128, 8], F32)
    logits = sb("f_logits", [128, NT * NEXP], F32)
    tcol = sb("f_tcol", [128, 4], F32)
    eqb = sb("f_eqb", [128, NEXP], F32)
    mkb = sb("f_mkb", [128, NEXP], F32)
    geb = sb("f_geb", [128, NEXP], F32)
    sgb = sb("f_sgb", [128, NEXP], F32)
    wfull = sb("f_wfull", [128, NEXP], F32)
    silu_sb = sb("f_silu", [128, TOK], BF16)
    wout_sb = [sb(f"f_wout{i}", [128, E], BF16) for i in range(2)]
    h_sb = io["h_sb"]

    gsn = lambda n: C.gs("f_" + n)
    vd = C.gs("a_vd")
    wlg, wls = gsn("wlg"), gsn("wls")
    wmm, wev = gsn("wmm"), gsn("wev")
    lgm, lgv, sgd, tpw, twe = gsn("lgm"), gsn("lgv"), gsn("sgd"), gsn("tpw"), gsn("twe")
    gul = [gsn(f"gul{i}") for i in range(2)]
    dnl8 = gsn("dnl")
    dnl = [dnl8] * 8
    wbm, wbe = gsn("wbm"), gsn("wbe")
    gmm, sil, umm, hbd, hcd = gsn("gmm"), gsn("sil"), gsn("umm"), gsn("hbd"), gsn("hcd")
    dnm, dne = gsn("dnm"), gsn("dne")
    sdl8 = gsn("sdl")
    sdl = [sdl8] * 8
    xld = [gsn("xld0"), gsn("xld1")]
    fint, finv = gsn("fint"), gsn("finv")
    sto = gsn("sto")

    wol = gsn("wol")

    def g_loads(g):
        wait_gate(g, gate)
        wlg.inc(g.dma_start(out=routerT_sb[:],
                            in_=io["routerT"].rearrange("(c p) n -> p c n", p=128)), 16)
        wlg.inc(g.dma_start(out=onehot_sb[:], in_=io["onehot"]), 16)
    G(g_loads)

    def sy_loads(sy):
        wait_gate(sy, gate)
        wls.inc(sy.dma_start(out=bout_sb[:], in_=io["bout"]), 16)
    SY(sy_loads)

    psP = ExitStack()
    ps_m2 = [C.psum(psP, f"f_ps_m{i}", [128, 512], F32) for i in range(2)]
    ps_t = C.psum(psP, "f_ps_t", [NEXP, 128], F32)
    ps_m = ps_m2[0]

    for qtr in range(4):
        def g_wo(g, qtr=qtr):
            if qtr == 0:
                wait_gate(g, gate)
            if qtr > 0:
                wmm.wait(g)
            for k in range(2):
                ci = 2 * qtr + k
                wol.inc(g.dma_start(out=wout_sb[k][:], in_=io["wout"][128 * ci:128 * (ci + 1), :]), 16)
        G(g_wo)

        for oc in range(8):
            for c4 in range(4):
                def t_wout(tt, qtr=qtr, oc=oc, c4=c4):
                    if qtr == 0 and oc == 0 and c4 == 0:
                        wait_gate(tt, gate)
                        wlg.wait(tt)
                    wol.wait(tt)
                    wev.wait(tt, lag=1)
                    mi = (qtr * 32 + oc * 4 + c4) % 2
                    for k in range(2):
                        ci = 2 * qtr + k
                        wmm.inc(tt.matmul(ps_m2[mi][:, :], wout_sb[k][:, 128 * oc:128 * (oc + 1)],
                                          attnT[ci][:, 512 * c4:512 * (c4 + 1)],
                                          start=(k == 0), stop=(k == 1)))
                T(t_wout)

                def v_wout(v, qtr=qtr, oc=oc, c4=c4):
                    wmm.wait(v)
                    mi = (qtr * 32 + oc * 4 + c4) % 2
                    if qtr == 0:
                        wls.wait(v)
                        wev.inc(v.tensor_scalar(accT[oc][:, 512 * c4:512 * (c4 + 1)],
                                                ps_m2[mi][:, :], bout_sb[:, oc:oc + 1], None, ALU.add))
                    else:
                        wev.inc(v.tensor_tensor(accT[oc][:, 512 * c4:512 * (c4 + 1)],
                                                accT[oc][:, 512 * c4:512 * (c4 + 1)],
                                                ps_m2[mi][:, :], ALU.add))
                VE(v_wout)

    for a in range(NT):
        def t_lg(tt, a=a):
            if a == 0:
                wev.wait(tt)
            lgv.wait(tt)
            for ci in range(8):
                lgm.inc(tt.matmul(ps_m[:, 0:NEXP], xnT[ci][:, 128 * a:128 * (a + 1)],
                                  routerT_sb[:, NEXP * ci:NEXP * (ci + 1)],
                                  start=(ci == 0), stop=(ci == 7)))
        T(t_lg)

        def v_lg(v, a=a):
            lgm.wait(v)
            lgv.inc(v.tensor_copy(logits[:, NEXP * a:NEXP * (a + 1)], ps_m[:, 0:NEXP]))
        VE(v_lg)

    for a in range(NT):
        la = logits[:, NEXP * a:NEXP * (a + 1)]

        def v_top2a(v, la=la):
            lgv.wait(v)
            vd.inc(v.tensor_reduce(tcol[:, 0:1], la, AX.X, ALU.max))
            vd.wait(v)
            vd.inc(v.tensor_scalar(eqb[:], la, tcol[:, 0:1], None, ALU.is_equal))
            vd.wait(v)
            vd.inc(v.scalar_tensor_tensor(mkb[:], eqb[:], -1e30, la, ALU.mult, ALU.add))
            vd.wait(v)
            vd.inc(v.tensor_reduce(tcol[:, 1:2], mkb[:], AX.X, ALU.max))
            vd.wait(v)
            sgd.inc(v.tensor_scalar(geb[:], la, tcol[:, 1:2], None, ALU.is_ge))
        VE(v_top2a)

        def s_sig(s, la=la):
            sgd.wait(s)
            tpw.wait(s, lag=1)
            sgd.inc(s.activation(sgb[:], la, AF.Sigmoid))
        SC(s_sig)

        def v_wfull(v):
            sgd.wait(v)
            tpw.inc(v.tensor_tensor(wfull[:], sgb[:], geb[:], ALU.mult))
        VE(v_wfull)

        def t_wT(tt):
            tpw.wait(tt)
            twe.wait(tt)
            tpw.inc(tt.transpose(ps_t[:, :], wfull[:, :], C.ident_f32[:]))
        T(t_wT)

        def v_wTe(v, a=a):
            tpw.wait(v)
            twe.inc(v.tensor_copy(wT_sb[:, 128 * a:128 * (a + 1)], ps_t[:, :]))
        VE(v_wTe)

    psP.close()

    esP3 = ExitStack()
    gu2 = [C.sb(esP3, f"f_gu2_{i}", [128, 2048], BF16) for i in range(2)]
    psX = ExitStack()
    ps_ga = C.psum(psX, "f_ps_ga", [128, 2048], F32)
    ps_gb = C.psum(psX, "f_ps_gb", [128, 2048], F32)

    def emit_expert(e):
        def g_dn(g, e=e):
            if e == 0:
                wmm.wait(g)
            dnm.wait(g)
            for ci in range(8):
                dnl[ci].inc(g.dma_start(out=dn_sb[ci][:],
                                        in_=io["dn_w"][e, 128 * ci:128 * (ci + 1), :]), 16)
        G(g_dn)

        for c4 in range(4):
            def t_wb(tt, e=e, c4=c4):
                if e == 0 and c4 == 0:
                    twe.wait(tt)
                    wev.wait(tt)
                hbd.wait(tt)
                wbe.wait(tt)
                dne.wait(tt)
                wbm.inc(tt.matmul(ps_gb[:, 512 * c4:512 * (c4 + 1)],
                                  onehot_sb[:, 128 * e:128 * (e + 1)],
                                  wT_sb[:, 512 * c4:512 * (c4 + 1)], start=True, stop=True))
            T(t_wb)

        def v_wb(v):
            wbm.wait(v)
            wbe.inc(v.tensor_copy(wbc[:], ps_gb[:, :]))
        VE(v_wb)

        for g in range(8):
            wi = g % 2

            def g_gu(g_, e=e, g=g, wi=wi):
                if e == 0 and g < 2:
                    wmm.wait(g_)
                umm.wait(g_, lag=1)
                gul[wi].inc(g_.dma_start(out=gu2[wi][:], in_=io["gup_w"][e, g]), 16)
            G(g_gu)

            def t_gate(tt, e=e, g=g, wi=wi):
                gul[wi].wait(tt)
                sil.wait(tt)
                if g == 0:
                    wbe.wait(tt)
                    dne.wait(tt)
                for ci in range(8):
                    for c4 in range(4):
                        gmm.inc(tt.matmul(ps_ga[:, 512 * c4:512 * (c4 + 1)],
                                          gu2[wi][:, 128 * ci:128 * (ci + 1)],
                                          xnT[ci][:, 512 * c4:512 * (c4 + 1)],
                                          start=(ci == 0), stop=(ci == 7)))
            T(t_gate)

            def s_silu(s, g=g):
                hbd.wait(s)
                for c4 in range(4):
                    gmm.wait(s, lag=3 - c4)
                    inst = s.activation(silu_sb[:, 512 * c4:512 * (c4 + 1)],
                                        ps_ga[:, 512 * c4:512 * (c4 + 1)], AF.Silu)
                sil.inc(inst)
            SC(s_silu)

            def t_up(tt, e=e, g=g, wi=wi):
                hbd.wait(tt)
                if g == 0:
                    wbe.wait(tt)
                for ci in range(8):
                    for c4 in range(4):
                        inst = tt.matmul(ps_gb[:, 512 * c4:512 * (c4 + 1)],
                                         gu2[wi][:, 1024 + 128 * ci:1024 + 128 * (ci + 1)],
                                         xnT[ci][:, 512 * c4:512 * (c4 + 1)],
                                         start=(ci == 0), stop=(ci == 7))
                umm.inc(inst)
            T(t_up)

            def v_h(v, e=e, g=g):
                umm.wait(v)
                sil.wait(v)
                wbe.wait(v)
                dnm.wait(v)
                hbd.inc(v.tensor_tensor(h_sb[g][:], ps_gb[:, :], silu_sb[:], ALU.mult))
                hbd.wait(v)
                hcd.inc(v.tensor_tensor(h_sb[g][:], h_sb[g][:], wbc[:], ALU.mult))
            VE(v_h)

        for oc in range(8):
            pbank = ps_ga if oc % 2 == 0 else ps_gb

            def t_dn(tt, e=e, oc=oc, pbank=pbank):
                hcd.wait(tt)
                dnl[0].wait(tt)
                if oc >= 2:
                    dne.wait(tt, lag=4)
                for hc in range(8):
                    for c4 in range(4):
                        inst = tt.matmul(pbank[:, 512 * c4:512 * (c4 + 1)],
                                         dn_sb[hc][:, 128 * oc:128 * (oc + 1)],
                                         h_sb[hc][:, 512 * c4:512 * (c4 + 1)],
                                         start=(hc == 0), stop=(hc == 7))
                        if hc == 7:
                            dnm.inc(inst)
            T(t_dn)

            for c4 in range(4):
                def v_dn(v, oc=oc, c4=c4, pbank=pbank):
                    dnm.wait(v, lag=3 - c4)
                    dne.inc(v.tensor_tensor(accT[oc][:, 512 * c4:512 * (c4 + 1)],
                                            accT[oc][:, 512 * c4:512 * (c4 + 1)],
                                            pbank[:, 512 * c4:512 * (c4 + 1)], ALU.add))
                VE(v_dn)

    for e in range(NEXP):
        emit_expert(e)

    for sh in range(2):
        for gl in range(8):
            gidx = 8 * sh + gl
            wi = gidx % 2

            def g_ws(g, gidx=gidx, wi=wi):
                umm.wait(g, lag=1)
                gul[wi].inc(g.dma_start(out=gu2[wi][:], in_=io["sgsu"][gidx]), 16)
            G(g_ws)

            def g_sd(g, gidx=gidx, gl=gl):
                dnm.wait(g)
                sdl[gl].inc(g.dma_start(out=dn_sb[gl][:],
                                        in_=io["sdwT"][128 * gidx:128 * (gidx + 1), :]), 16)
            G(g_sd)

            def t_sg(tt, wi=wi, sh=sh, gl=gl):
                gul[wi].wait(tt)
                sil.wait(tt)
                if sh == 0 and gl == 0:
                    dne.wait(tt)
                for ci in range(8):
                    for c4 in range(4):
                        gmm.inc(tt.matmul(ps_ga[:, 512 * c4:512 * (c4 + 1)],
                                          gu2[wi][:, 128 * ci:128 * (ci + 1)],
                                          xnT[ci][:, 512 * c4:512 * (c4 + 1)],
                                          start=(ci == 0), stop=(ci == 7)))
            T(t_sg)

            def s_ssilu(s):
                hbd.wait(s)
                for c4 in range(4):
                    gmm.wait(s, lag=3 - c4)
                    inst = s.activation(silu_sb[:, 512 * c4:512 * (c4 + 1)],
                                        ps_ga[:, 512 * c4:512 * (c4 + 1)], AF.Silu)
                sil.inc(inst)
            SC(s_ssilu)

            def t_su(tt, wi=wi, sh=sh, gl=gl):
                hbd.wait(tt)
                if sh == 0 and gl == 0:
                    dne.wait(tt)
                for ci in range(8):
                    for c4 in range(4):
                        inst = tt.matmul(ps_gb[:, 512 * c4:512 * (c4 + 1)],
                                         gu2[wi][:, 1024 + 128 * ci:1024 + 128 * (ci + 1)],
                                         xnT[ci][:, 512 * c4:512 * (c4 + 1)],
                                         start=(ci == 0), stop=(ci == 7))
                umm.inc(inst)
            T(t_su)

            def v_t(v, gl=gl):
                umm.wait(v)
                sil.wait(v)
                dnm.wait(v)
                hbd.inc(v.tensor_tensor(h_sb[gl][:], ps_gb[:, :], silu_sb[:], ALU.mult))
            VE(v_t)

        for oc in range(8):
            pbank = ps_ga if oc % 2 == 0 else ps_gb

            def t_sd(tt, oc=oc, pbank=pbank):
                hbd.wait(tt)
                sdl[0].wait(tt)
                if oc >= 2:
                    dne.wait(tt, lag=4)
                for gl in range(8):
                    for c4 in range(4):
                        inst = tt.matmul(pbank[:, 512 * c4:512 * (c4 + 1)],
                                         dn_sb[gl][:, 128 * oc:128 * (oc + 1)],
                                         h_sb[gl][:, 512 * c4:512 * (c4 + 1)],
                                         start=(gl == 0), stop=(gl == 7))
                        if gl == 7:
                            dnm.inc(inst)
            T(t_sd)

            for c4 in range(4):
                def v_sd(v, oc=oc, c4=c4, pbank=pbank):
                    dnm.wait(v, lag=3 - c4)
                    dne.inc(v.tensor_tensor(accT[oc][:, 512 * c4:512 * (c4 + 1)],
                                            accT[oc][:, 512 * c4:512 * (c4 + 1)],
                                            pbank[:, 512 * c4:512 * (c4 + 1)], ALU.add))
                VE(v_sd)

    psX.close()
    esP3.close()
    psF = ExitStack()
    ps_f = C.psum(psF, "f_ps_f", [128, 128], F32)
    xbuf = [h_sb[0][:].bitcast(F32), h_sb[1][:].bitcast(F32)]
    x1b = [h_sb[2][:].bitcast(F32), h_sb[3][:].bitcast(F32)]

    for a in range(NT):
        bi = a % 2

        def sy_xa(sy, a=a, bi=bi):
            if a < 2:
                dne.wait(sy)
            finv.wait(sy, lag=8)
            xld[bi].inc(sy.dma_start(out=xbuf[bi], in_=io["x_own"][128 * a:128 * (a + 1), :]), 16)
        SY(sy_xa)

        for ct in range(8):
            def t_f(tt, a=a, ct=ct):
                if a == 0 and ct == 0:
                    dne.wait(tt)
                    hbd.wait(tt)
                finv.wait(tt)
                fint.inc(tt.transpose(ps_f[:, :], accT[ct][:, 128 * a:128 * (a + 1)],
                                      C.ident_f32[:]))
            T(t_f)

            def v_f(v, a=a, bi=bi, ct=ct):
                fint.wait(v)
                xld[bi].wait(v)
                if ct == 0:
                    sto.wait(v, lag=16)
                finv.inc(v.tensor_tensor(x1b[bi][:, 128 * ct:128 * (ct + 1)],
                                         xbuf[bi][:, 128 * ct:128 * (ct + 1)],
                                         ps_f[:, :], ALU.add))
            VE(v_f)

        def sy_o(sy, a=a, bi=bi):
            finv.wait(sy)
            sto.inc(sy.dma_start(out=io["x1_scr"].ap()[128 * a:128 * (a + 1), :], in_=x1b[bi]), 16)
        SY(sy_o)

    psF.close()
    ph.close()
    return [sto.mark(), finv.mark()]



def emit_l1qkv(C, io, xn1T, gate):
    nc = C.nc
    T, VE, SC, SY, G = C.T, C.V, C.S, C.Y, C.G

    ph = ExitStack()
    sb = lambda n, s, d: C.sb(ph, n, s, d)
    w_sb = [sb(f"q_w{i}", [128, 4 * E], BF16) for i in range(8)]
    xbuf = [sb(f"q_xbuf{i}", [128, E], F32) for i in range(4)]
    xsq = [sb(f"q_xsq{i}", [128, E], F32) for i in range(4)]
    stat2 = [sb(f"q_stat{i}", [128, 8], F32) for i in range(4)]
    xn_tmp = [sb(f"q_xn{i}", [128, E], BF16) for i in range(4)]
    qkv_sb = [sb(f"q_qkv{i}", [128, 4 * E], BF16) for i in range(4)]

    gsn = lambda n: C.gs("q_" + n)
    vd = C.gs("a_vd")
    wlg = gsn("wlg")
    lnf, lns, sds, lnx = C.gs("a_lnf"), C.gs("a_lns"), C.gs("a_sds"), C.gs("a_lnx")
    xl = [C.gs("a_xl0"), C.gs("a_xl1")]
    trp, evc = C.gs("a_trp"), C.gs("a_evc")
    qmm, qev, sto = gsn("qmm"), gsn("qev"), gsn("sto")

    def g_loads(g):
        wait_gate(g, gate)
        for ci in range(8):
            wlg.inc(g.dma_start(out=w_sb[ci][:], in_=io["wqkv"][128 * ci:128 * (ci + 1), :]), 16)
    G(g_loads)

    ps_tr = C.psum(ph, "q_ps_tr", [128, 1024], BF16)
    ps_q = [C.psum(ph, f"q_ps_q{i}", [128, 512], F32) for i in range(2)]

    gq = {"n": 0}
    for a in range(NT):
        bi = a % 4
        xi = a % 2

        def sy_x(sy, a=a, bi=bi, xi=xi):
            if a < 2:
                wait_gate(sy, gate)
            lnf.wait(sy, lag=3)
            xl[xi].inc(sy.dma_start(out=xbuf[bi][:],
                                    in_=io["x1_scr"].ap()[128 * a:128 * (a + 1), :]), 16)
        SY(sy_x)

        def v_stats(v, bi=bi, xi=xi):
            stat = stat2[bi]
            xl[xi].wait(v)
            sds.wait(v, lag=3)
            v.tensor_tensor(xsq[bi][:], xbuf[bi][:], xbuf[bi][:], ALU.mult)
            v.tensor_reduce(stat[:, 0:1], xbuf[bi][:], AX.X, ALU.add)
            vd.inc(v.tensor_reduce(stat[:, 1:2], xsq[bi][:], AX.X, ALU.add))
            vd.wait(v)
            vd.inc(v.tensor_scalar(stat[:, 2:3], stat[:, 0:1], 1.0 / E, None, ALU.mult))
            vd.wait(v)
            vd.inc(v.tensor_tensor(stat[:, 3:4], stat[:, 2:3], stat[:, 2:3], ALU.mult))
            v.tensor_scalar(stat[:, 4:5], stat[:, 1:2], 1.0 / E, LN_EPS, ALU.mult, ALU.add)
            vd.wait(v)
            lns.inc(v.tensor_tensor(stat[:, 4:5], stat[:, 4:5], stat[:, 3:4], ALU.subtract))
        VE(v_stats)

        def s_rstd(s, bi=bi):
            stat = stat2[bi]
            lns.wait(s)
            lnx.inc(s.activation(stat[:, 5:6], stat[:, 4:5], AF.Ln))
            lnx.wait(s)
            sds.inc(s.activation(stat[:, 6:7], stat[:, 5:6], AF.Exp, scale=-0.5))
        SC(s_rstd)

        def v_norm(v, bi=bi):
            stat = stat2[bi]
            sds.wait(v)
            trp.wait(v, lag=24)
            lnf.inc(v.tensor_scalar(xn_tmp[bi][:], xbuf[bi][:], stat[:, 2:3], stat[:, 6:7],
                                    ALU.subtract, ALU.mult))
        VE(v_norm)

        def t_trans(tt, bi=bi):
            lnf.wait(tt)
            evc.wait(tt)
            for ct in range(8):
                trp.inc(tt.transpose(ps_tr[:, 128 * ct:128 * (ct + 1)],
                                     xn_tmp[bi][:, 128 * ct:128 * (ct + 1)], C.ident_bf[:]))
        T(t_trans)

        def v_evac(v, a=a):
            trp.wait(v)
            for ct in range(8):
                evc.inc(v.tensor_copy(xn1T[ct][:, 128 * a:128 * (a + 1)],
                                      ps_tr[:, 128 * ct:128 * (ct + 1)]))
        VE(v_evac)

        for oc8 in range(8):
            sl = gq["n"] % 2
            gq["n"] += 1

            def t_q(tt, a=a, oc8=oc8, sl=sl):
                if a == 0 and oc8 == 0:
                    wlg.wait(tt)
                evc.wait(tt)
                qev.wait(tt, lag=1)
                for ci in range(8):
                    qmm.inc(tt.matmul(ps_q[sl][:, :], xn1T[ci][:, 128 * a:128 * (a + 1)],
                                      w_sb[ci][:, 512 * oc8:512 * (oc8 + 1)],
                                      start=(ci == 0), stop=(ci == 7)))
            T(t_q)

            def v_q(v, a=a, bi=bi, oc8=oc8, sl=sl):
                qmm.wait(v)
                if oc8 == 0:
                    sto.wait(v, lag=48)
                qev.inc(v.tensor_copy(qkv_sb[bi][:, 512 * oc8:512 * (oc8 + 1)], ps_q[sl][:, :]))
            VE(v_q)

        def sy_q(sy, a=a, bi=bi):
            qev.wait(sy)
            sto.inc(sy.dma_start(out=io["qkv_scr"].ap()[128 * a:128 * (a + 1), :],
                                 in_=qkv_sb[bi][:]), 16)
        SY(sy_q)

    ph.close()
    return [sto.mark(), evc.mark(), qev.mark()]



def emit_l1fused(C, io, xn1T, gate):
    nc = C.nc
    T, VE, SC, SY, G = C.T, C.V, C.S, C.Y, C.G

    ph = ExitStack()
    sb = lambda n, s, d: C.sb(ph, n, s, d)
    accT = [sb(f"o_accT{i}", [128, TOK], BF16) for i in range(8)]
    scl = sb("o_scl", [128, 5 * 32], F32)
    kv_all = sb("w_kv", [128, 8192], BF16)
    qb = [sb(f"w_qb{i}", [128, 256], BF16) for i in range(2)]
    wb_all = sb("w_wball", [128, 4096], BF16)
    wscr = sb("w_scr", [128, 4096], BF16)
    sc2 = [sb(f"w_sc{i}", [128, 1024], F32) for i in range(2)]
    P2 = [sb(f"w_P{i}", [128, 1024], BF16) for i in range(2)]
    scat4 = [sb(f"w_scat{i}", [128, 1024], BF16) for i in range(4)]
    sums = sb("w_sums", [128, 16], F32)
    rec = sb("w_rec", [128, 16], F32)
    ov = sb("w_ov", [128, 256], F32)
    t1 = wscr[:].bitcast(F32)[:, 0:256]

    gsn = lambda n: C.gs("o_" + n)
    vd = C.gs("a_vd")
    wls = gsn("wls")
    a1l = [gsn("a1l0"), gsn("a1l1")]
    trp, evc = C.gs("a_trp"), C.gs("a_evc")
    wol = gsn("wol")
    wmm, wev = gsn("wmm"), gsn("wev")
    w12l = [gsn(f"w12l{i}") for i in range(2)]
    whl = [gsn(f"whl{i}") for i in range(2)]
    w3l = [gsn(f"w3l{i}") for i in range(2)]
    amm, sil, bmm, uev = gsn("amm"), gsn("sil"), gsn("bmm"), gsn("uev")
    hmm, hev = gsn("hmm"), gsn("hev")
    h2m, h2e = gsn("h2m"), gsn("h2e")
    w3m, w3e = gsn("w3m"), gsn("w3e")
    xld = [C.gs("a_xl0"), C.gs("a_xl1")]
    fint, finv, sto = gsn("fint"), gsn("finv"), gsn("sto")
    kvl, qbl, wbl = gsn("kvl"), gsn("qbl"), gsn("wbl")
    scr_d, exq, wind = gsn("w_scr_d"), gsn("w_exq"), gsn("w_wind")
    scw, scd, ccs = gsn("w_scw"), gsn("w_scd"), gsn("w_ccs")

    def sy_loads(sy):
        wait_gate(sy, gate)
        for k, t in enumerate(["w1s", "w2s", "h1s", "h2s"]):
            wls.inc(sy.dma_start(out=scl[:, 32 * k:32 * (k + 1)], in_=io[t]), 16)
        wls.inc(sy.dma_start(out=scl[:, 128:136], in_=io["w3s"]), 16)
    SY(sy_loads)

    psC = ExitStack()
    ps_a2 = [C.psum(psC, f"o_ps_a{i}", [128, 512], F32) for i in range(2)]
    ps_b2 = [C.psum(psC, f"o_ps_b{i}", [128, 512], F32) for i in range(2)]
    u_sb = [C.sb(psC, f"o_u{i}", [128, 512], BF16) for i in range(32)]
    v_sb = [C.sb(psC, f"o_v{i}", [128, 512], BF16) for i in range(32)]
    wbuf = [C.sb(psC, f"o_wbuf{i}", [128, 4096], BF16) for i in range(2)]
    silu2 = [C.sb(psC, f"o_silu{i}", [128, 512], BF16) for i in range(2)]

    qkv = io["qkv_scr"].ap()
    for qt in range(4):
        i = qt
        def sy_kv(sy, i=i):
            if i == 0:
                wait_gate(sy, gate)
            else:
                wind.wait(sy)
                sy.wait_ge(w3e.h, 8 * i)
            for a in range(NT):
                kvl.inc(sy.dma_start(out=kv_all[:, 512 * a:512 * (a + 1)],
                                     in_=qkv[128 * a:128 * (a + 1),
                                             1024 * i + 256:1024 * i + 768]), 16)
            for a in range(NT):
                wbl.inc(sy.dma_start(out=wb_all[:, 256 * a:256 * (a + 1)],
                                     in_=qkv[128 * a:128 * (a + 1),
                                             1024 * i + 768:1024 * (i + 1)]), 16)
        SY(sy_kv)

        def sy_qb(sy, i=i):
            for a in range(NT):
                A = 16 * i + a
                if A - 1 > 0:
                    sy.wait_ge(scr_d.h, A - 1)
                qbl.inc(sy.dma_start(out=qb[a % 2][:],
                                     in_=qkv[128 * a:128 * (a + 1),
                                             1024 * i:1024 * i + 256]), 16)
        SY(sy_qb)

        amm_marks = []
        sil0, uev0 = sil.n, uev.n
        for g in range(32):
            wi = g % 2

            def g_w12(g_, qt=qt, g=g, wi=wi):
                if qt == 0 and g < 2:
                    wait_gate(g_, gate)
                if g < 2:
                    w3m.wait(g_)
                bmm.wait(g_, lag=8)
                w12l[wi].inc(g_.dma_start(out=wbuf[wi][:, 0:2048], in_=io["w12T"][g]), 16)
            G(g_w12)

            def t_w1(tt, qt=qt, g=g, wi=wi, sil0=sil0):
                w12l[wi].wait(tt)
                if sil0 + g - 1 > 0:
                    tt.wait_ge(sil.h, sil0 + g - 1)
                w3e.wait(tt)
                for ci in range(8):
                    amm.inc(tt.matmul(ps_a2[wi][:, :], wbuf[wi][:, 128 * ci:128 * (ci + 1)],
                                      xn1T[ci][:, 512 * qt:512 * (qt + 1)],
                                      start=(ci == 0), stop=(ci == 7)))
            T(t_w1)
            amm_marks.append(amm.n)

            def t_w2(tt, qt=qt, g=g, wi=wi, sil0=sil0, uev0=uev0):
                if sil0 + g > 0:
                    tt.wait_ge(sil.h, sil0 + g)
                if uev0 + g - 1 > 0:
                    tt.wait_ge(uev.h, uev0 + g - 1)
                for ci in range(8):
                    bmm.inc(tt.matmul(ps_b2[wi][:, :],
                                      wbuf[wi][:, 1024 + 128 * ci:1024 + 128 * (ci + 1)],
                                      xn1T[ci][:, 512 * qt:512 * (qt + 1)],
                                      start=(ci == 0), stop=(ci == 7)))
            T(t_w2)

            def v_u(v, g=g, wi=wi, sil0=sil0):
                bmm.wait(v)
                v.wait_ge(sil.h, sil0 + g + 1)
                hmm.wait(v)
                uev.inc(v.scalar_tensor_tensor(u_sb[g][:], ps_b2[wi][:, :],
                                               scl[:, 32 + g:33 + g],
                                               silu2[wi][:], ALU.mult, ALU.mult))
            VE(v_u)

        def s_si_all(s, qt=qt, marks=tuple(amm_marks), uev0=uev0):
            if qt == 0:
                wls.wait(s)
            for g in range(32):
                s.wait_ge(amm.h, marks[g])
                if uev0 + g - 1 > 0:
                    s.wait_ge(uev.h, uev0 + g - 1)
                sil.inc(s.activation(silu2[g % 2][:], ps_a2[g % 2][:, :], AF.Silu,
                                     scale=scl[:, g:g + 1]))
        SC(s_si_all)

        for a in range(NT):
            g4, al = a // 4, a % 4

            def v_sc(v, i=i, a=a, g4=g4, al=al):
                A = 16 * i + a
                b = g4 % 2
                kvl.wait(v)
                v.wait_ge(qbl.h, 16 * (A + 1))
                if al == 0 and 4 * i + g4 - 1 > 0:
                    v.wait_ge(exq.h, 4 * i + g4 - 1)
                q_ap = qb[a % 2][:].rearrange("p (u d) -> p u d", d=16) \
                    .unsqueeze(1).broadcast_to((128, 16, 16, 16))
                k_ap = kv_all[:].rearrange("p (ap c) -> p ap c", c=512)[:, :, 0:256] \
                    .rearrange("p ap (u d) -> p ap u d", d=16)
                o_ap = wscr[:].rearrange("p (ap u d) -> p ap u d", u=16, d=16)
                vd.inc(v.tensor_tensor(o_ap, q_ap, k_ap, ALU.mult))
                vd.wait(v)
                scr_d.inc(v.tensor_reduce(sc2[b][:, 256 * al:256 * (al + 1)],
                                          wscr[:].rearrange("p (au d) -> p au d", d=16),
                                          AX.X, ALU.add))
            VE(v_sc)

        hmm_marks = []
        hev0, h2m0 = hev.n, h2m.n
        for g in range(32):
            wi = g % 2

            def g_wh(g_, g=g, wi=wi):
                if g < 2:
                    bmm.wait(g_)
                hmm.wait(g_, lag=32)
                whl[wi].inc(g_.dma_start(out=wbuf[wi][:], in_=io["h1T"][g]), 16)
            G(g_wh)

            def t_h1(tt, g=g, wi=wi, hev0=hev0):
                whl[wi].wait(tt)
                uev.wait(tt)
                if hev0 + g - 1 > 0:
                    tt.wait_ge(hev.h, hev0 + g - 1)
                for ci in range(32):
                    hmm.inc(tt.matmul(ps_a2[wi][:, :], wbuf[wi][:, 128 * ci:128 * (ci + 1)],
                                      u_sb[ci][:, :], start=(ci == 0), stop=(ci == 31)))
            T(t_h1)
            hmm_marks.append(hmm.n)

        def s_g1_all(s, marks=tuple(hmm_marks), h2m0=h2m0):
            if h2m0 > 0:
                s.wait_ge(h2m.h, h2m0)
            for g in range(32):
                s.wait_ge(hmm.h, marks[g])
                hev.inc(s.activation(v_sb[g][:], ps_a2[g % 2][:, :], AF.Gelu,
                                     scale=scl[:, 64 + g:65 + g]))
        SC(s_g1_all)

        def s_expw01(s, i=i):
            for g4 in range(2):
                b = g4 % 2
                s.wait_ge(scr_d.h, 16 * i + 4 * (g4 + 1))
                t = 16 * i + 4 * g4 - 4
                if t > 0:
                    s.wait_ge(wind.h, t)
                exq.inc(s.activation(P2[b][:], sc2[b][:], AF.Exp, scale=0.25))
        SC(s_expw01)

        for a in range(NT):
            g4, al = a // 4, a % 4

            def v_av(v, i=i, a=a, g4=g4, al=al):
                A = 16 * i + a
                Gw = 4 * i + g4
                b = g4 % 2
                v.wait_ge(exq.h, 4 * i + g4 + 1)
                v.wait_ge(wbl.h, 256 * (i + 1))
                if al == 0 and 16 * (Gw - 3) > 0:
                    v.wait_ge(scw.h, 16 * (Gw - 3))
                pv = P2[b][:, 256 * al:256 * (al + 1)].rearrange("p (ap u) -> p u ap", ap=16)
                vd.inc(v.tensor_reduce(sums[:], pv, AX.X, ALU.add))
                vd.wait(v)
                vd.inc(v.reciprocal(rec[:], sums[:]))
                vd.wait(v)
                p_ap = pv.unsqueeze(2).broadcast_to((128, 16, 16, 16))
                v_ap = kv_all[:].rearrange("p (ap c) -> p ap c", c=512)[:, :, 256:512] \
                    .rearrange("p ap (u d) -> p u d ap", d=16)
                t_ap = wscr[:].rearrange("p (u d ap) -> p u d ap", d=16, ap=16)
                vd.inc(v.tensor_tensor(t_ap, p_ap, v_ap, ALU.mult))
                vd.wait(v)
                vd.inc(v.tensor_reduce(ov[:], wscr[:].rearrange("p (ud ap) -> p ud ap", ap=16),
                                       AX.X, ALU.add))
                vd.wait(v)
                r_ap = rec[:].unsqueeze(2).broadcast_to((128, 16, 16))
                vd.inc(v.tensor_tensor(t1.rearrange("p (u d) -> p u d", d=16),
                                       ov[:].rearrange("p (u d) -> p u d", d=16), r_ap, ALU.mult))
                vd.wait(v)
                wind.inc(v.tensor_tensor(scat4[g4][:, 256 * al:256 * (al + 1)], t1,
                                         wb_all[:, 256 * a:256 * (a + 1)], ALU.mult))
            VE(v_av)

        h2m_marks = []
        h2e0, w3m0 = h2e.n, w3m.n
        for g in range(32):
            wi = g % 2

            def g_wh2(g_, g=g, wi=wi):
                if g < 2:
                    hmm.wait(g_)
                h2m.wait(g_, lag=32)
                whl[wi].inc(g_.dma_start(out=wbuf[wi][:], in_=io["h2T"][g]), 16)
            G(g_wh2)

            def t_h2(tt, g=g, wi=wi, h2e0=h2e0):
                whl[wi].wait(tt)
                hev.wait(tt)
                if h2e0 + g - 1 > 0:
                    tt.wait_ge(h2e.h, h2e0 + g - 1)
                for ci in range(32):
                    h2m.inc(tt.matmul(ps_b2[wi][:, :], wbuf[wi][:, 128 * ci:128 * (ci + 1)],
                                      v_sb[ci][:, :], start=(ci == 0), stop=(ci == 31)))
            T(t_h2)
            h2m_marks.append(h2m.n)

        def s_g2_all(s, marks=tuple(h2m_marks), w3m0=w3m0):
            if w3m0 > 0:
                s.wait_ge(w3m.h, w3m0)
            for g in range(32):
                s.wait_ge(h2m.h, marks[g])
                h2e.inc(s.activation(u_sb[g][:], ps_b2[g % 2][:, :], AF.Gelu,
                                     scale=scl[:, 96 + g:97 + g]))
        SC(s_g2_all)

        def s_expw23(s, i=i):
            for g4 in range(2, 4):
                b = g4 % 2
                s.wait_ge(scr_d.h, 16 * i + 4 * (g4 + 1))
                t = 16 * i + 4 * g4 - 4
                if t > 0:
                    s.wait_ge(wind.h, t)
                exq.inc(s.activation(P2[b][:], sc2[b][:], AF.Exp, scale=0.25))
        SC(s_expw23)

        w3e0 = w3e.n
        for oc in range(8):
            wi = oc % 2

            def g_w3(g_, oc=oc, wi=wi):
                if oc < 2:
                    h2m.wait(g_)
                w3m.wait(g_, lag=32)
                w3l[wi].inc(g_.dma_start(out=wbuf[wi][:], in_=io["w3T"][oc]), 16)
            G(g_w3)

            def t_w3(tt, oc=oc, wi=wi, w3e0=w3e0):
                w3l[wi].wait(tt)
                h2e.wait(tt)
                if w3e0 + oc - 1 > 0:
                    tt.wait_ge(w3e.h, w3e0 + oc - 1)
                for ci in range(32):
                    w3m.inc(tt.matmul(ps_a2[wi][:, :], wbuf[wi][:, 128 * ci:128 * (ci + 1)],
                                      u_sb[ci][:, :], start=(ci == 0), stop=(ci == 31)))
            T(t_w3)

            def v_w3(v, qt=qt, oc=oc, wi=wi):
                w3m.wait(v)
                w3e.inc(v.tensor_scalar(accT[oc][:, 512 * qt:512 * (qt + 1)],
                                        ps_a2[wi][:, :], scl[:, 128 + oc:129 + oc],
                                        None, ALU.mult))
            VE(v_w3)

        for g4 in range(4):
            def sy_sc(sy, i=i, g4=g4):
                Gw = 4 * i + g4
                sy.wait_ge(wind.h, 16 * i + 4 * (g4 + 1))
                if 64 * (Gw - 3) > 0:
                    sy.wait_ge(scd.h, 64 * (Gw - 3))
                scw.inc(sy.dma_start(out=io["scr"].ap()[:, 1024 * g4:1024 * (g4 + 1)],
                                     in_=scat4[g4][:]), 16)
                sy.wait_ge(scw.h, 16 * (Gw + 1))
                for al in range(4):
                    a = 4 * g4 + al
                    blk = 2 * i + (a % 4) // 2
                    rowb = ((a // 4) * 2 + (a % 2)) * 32
                    dst = io["a2a_in"].ap()[256 * blk + rowb:256 * blk + rowb + 32, :] \
                        .rearrange("t (z w) -> t z w", z=4)
                    srcv = io["scr"].ap()[:, 256 * a:256 * (a + 1)].rearrange("(t z) w -> t z w", z=4)
                    scd.inc(sy.dma_start(out=dst, in_=srcv), 16)
            SY(sy_sc)

    def g_cc(g):
        scd.wait(g)
        with allow_ici_all_to_all():
            ccs.inc(g.collective_compute(
                "AllToAll", mybir.AluOpType.bypass,
                replica_groups=[list(range(NCORES))],
                ins=[io["a2a_in"].ap().opt()],
                outs=[io["a2a_out"].ap().opt()],
            ))
    G(g_cc)

    psC.close()
    esA = ExitStack()
    a1T = [C.sb(esA, f"o_a1T{i}", [128, TOK], BF16) for i in range(8)]
    abuf = [C.sb(esA, f"o_abuf{i}", [128, E], BF16) for i in range(2)]
    wo_sb = [C.sb(esA, f"o_wo{i}", [128, E], BF16) for i in range(2)]
    psA = ExitStack()
    ps_tr = C.psum(psA, "o_ps_tr", [128, 1024], BF16)

    for a in range(NT):
        bi = a % 2

        def sy_a1(sy, a=a, bi=bi):
            if a < 2:
                ccs.wait(sy)
                w3e.wait(sy)
            trp.wait(sy, lag=8)
            for al in range(2):
                for gm in range(2):
                    blk = 2 * (a // 4) + gm
                    rowb = ((a % 4) * 2 + al) * 32
                    a1l[bi].inc(sy.dma_start(
                        out=abuf[bi][64 * al + 32 * gm:64 * al + 32 * gm + 32, :],
                        in_=io["a2a_out"].ap()[256 * blk + rowb:256 * blk + rowb + 32, :]), 16)
        SY(sy_a1)

        def t_tr(tt, a=a, bi=bi):
            a1l[bi].wait(tt)
            evc.wait(tt)
            for ct in range(8):
                trp.inc(tt.transpose(ps_tr[:, 128 * ct:128 * (ct + 1)],
                                     abuf[bi][:, 128 * ct:128 * (ct + 1)], C.ident_bf[:]))
        T(t_tr)

        def v_ev(v, a=a):
            trp.wait(v)
            for ct in range(8):
                evc.inc(v.tensor_copy(a1T[ct][:, 128 * a:128 * (a + 1)],
                                      ps_tr[:, 128 * ct:128 * (ct + 1)]))
        VE(v_ev)

    psB = ExitStack()
    ps_m2 = [C.psum(psB, f"o_ps_m{i}", [128, 512], F32) for i in range(2)]

    for qtr in range(4):
        def g_wo(g, qtr=qtr):
            if qtr == 0:
                ccs.wait(g)
            if qtr > 0:
                wmm.wait(g)
            for k in range(2):
                ci = 2 * qtr + k
                wol.inc(g.dma_start(out=wo_sb[k][:], in_=io["w_out"][128 * ci:128 * (ci + 1), :]), 16)
        G(g_wo)

        for oc in range(8):
            for c4 in range(4):
                def t_wo(tt, qtr=qtr, oc=oc, c4=c4):
                    if qtr == 0 and oc == 0 and c4 == 0:
                        evc.wait(tt)
                    wol.wait(tt)
                    wev.wait(tt, lag=1)
                    mi = (qtr * 32 + oc * 4 + c4) % 2
                    for k in range(2):
                        ci = 2 * qtr + k
                        wmm.inc(tt.matmul(ps_m2[mi][:, :], wo_sb[k][:, 128 * oc:128 * (oc + 1)],
                                          a1T[ci][:, 512 * c4:512 * (c4 + 1)],
                                          start=(k == 0), stop=(k == 1)))
                T(t_wo)

                def v_wo(v, qtr=qtr, oc=oc, c4=c4):
                    wmm.wait(v)
                    if qtr == 0 and oc == 0 and c4 == 0:
                        w3e.wait(v)
                    mi = (qtr * 32 + oc * 4 + c4) % 2
                    wev.inc(v.tensor_tensor(accT[oc][:, 512 * c4:512 * (c4 + 1)],
                                            accT[oc][:, 512 * c4:512 * (c4 + 1)],
                                            ps_m2[mi][:, :], ALU.add))
                VE(v_wo)

    psB.close()
    psA.close()
    esA.close()
    psF = ExitStack()
    ps_f = C.psum(psF, "o_ps_f", [128, 128], BF16)
    xbuf = [C.sb(psF, f"o_xbuf{i}", [128, E], F32) for i in range(2)]
    x1b = [C.sb(psF, f"o_x1b{i}", [128, E], F32) for i in range(2)]

    for a in range(NT):
        bi = a % 2

        def sy_xa(sy, a=a, bi=bi):
            if a < 2:
                w3e.wait(sy)
                wev.wait(sy)
            finv.wait(sy, lag=8)
            xld[bi].inc(sy.dma_start(out=xbuf[bi][:],
                                     in_=io["x1_scr"].ap()[128 * a:128 * (a + 1), :]), 16)
        SY(sy_xa)

        for ct in range(8):
            def t_f(tt, a=a, ct=ct):
                if a == 0 and ct == 0:
                    w3e.wait(tt)
                    wev.wait(tt)
                finv.wait(tt)
                fint.inc(tt.transpose(ps_f[:, :], accT[ct][:, 128 * a:128 * (a + 1)],
                                      C.ident_bf[:]))
            T(t_f)

            def v_f(v, a=a, bi=bi, ct=ct):
                fint.wait(v)
                xld[bi].wait(v)
                if ct == 0:
                    sto.wait(v, lag=16)
                finv.inc(v.tensor_tensor(x1b[bi][:, 128 * ct:128 * (ct + 1)],
                                         xbuf[bi][:, 128 * ct:128 * (ct + 1)],
                                         ps_f[:, :], ALU.add))
            VE(v_f)

        def sy_o(sy, a=a, bi=bi):
            finv.wait(sy)
            sto.inc(sy.dma_start(out=io["out"][128 * a:128 * (a + 1), :], in_=x1b[bi][:]), 16)
        SY(sy_o)

    def sy_fin(sy):
        sto.wait(sy)
    SY(sy_fin)
    psF.close()
    ph.close()



def build_mega():
    nc = bass.Bass("TRN2", target_bir_lowering=False, detect_race_conditions=False)
    dt = lambda n, s, d: nc.dram_tensor(n, s, d, kind="ExternalInput").ap()
    io = dict(
        x_own=dt("x_own", [TOK, E], F32),
        wqa=dt("wqa", [E, QL], F32),
        wqb=dt("wqb", [QL, E], F32),
        wkva=dt("wkva", [E, KVL], F32),
        wkvb_kv=dt("wkvb_kv", [KVL, 512], F32),
        ropek_cos=dt("ropek_cos", [S, 32], F32),
        ropek_sin=dt("ropek_sin", [S, 32], F32),
        ropeq_cos=dt("ropeq_cos", [TOK, 32], F32),
        ropeq_sin=dt("ropeq_sin", [TOK, 32], F32),
        mask0=dt("mask0", [128, 128], F32),
        mask1=dt("mask1", [128, 128], F32),
        ident=dt("ident", [128, 128], F32),
        wout=dt("wout", [E, E], BF16),
        bout=dt("bout", [128, 8], F32),
        routerT=dt("routerT", [E, NEXP], F32),
        gup_w=dt("gup_w", [NEXP, 8, 128, 2048], BF16),
        dn_w=dt("dn_w", [NEXP, E, E], BF16),
        sgsu=dt("sgsu", [16, 128, 2048], BF16),
        sdwT=dt("sdwT", [SHD, E], BF16),
        onehot=dt("onehot", [NEXP, 2048], F32),
        wqkv=dt("wqkv", [E, 4 * E], BF16),
        w_out=dt("w_out", [E, E], BF16),
        w12T=dt("w12T", [32, 128, 2048], BF16),
        h1T=dt("h1T", [32, 128, 4096], BF16),
        h2T=dt("h2T", [32, 128, 4096], BF16),
        w3T=dt("w3T", [8, 128, 4096], BF16),
        w1s=dt("w1s", [128, 32], F32),
        w2s=dt("w2s", [128, 32], F32),
        h1s=dt("h1s", [128, 32], F32),
        h2s=dt("h2s", [128, 32], F32),
        w3s=dt("w3s", [128, 8], F32),
        out=nc.dram_tensor("out", [TOK, E], F32, kind="ExternalOutput").ap(),
        x1_scr=nc.dram_tensor("x1_scr", [TOK, E], F32),
        qkv_scr=nc.dram_tensor("qkv_scr", [TOK, 4 * E], BF16),
        scr=nc.dram_tensor("scr", [128, 4096], BF16),
        a2a_in=nc.dram_tensor("a2a_in", [TOK, E], BF16),
        a2a_out=nc.dram_tensor("a2a_out", [TOK, E], BF16),
        lat_in=nc.dram_tensor("lat_in", [KVL, TOK], BF16),
        lat_out=nc.dram_tensor("lat_out", [2 * KVL, TOK], BF16),
    )

    es = ExitStack()
    with es:
        block = es.enter_context(nc.Block())
        C = Ctx(nc, es, block)
        C.ident_bf = es.enter_context(nc.sbuf_tensor("ident_bf", [128, 128], BF16))
        C.ident_f32 = es.enter_context(nc.sbuf_tensor("ident_f32", [128, 128], F32))

        es_xn = ExitStack()
        xnT = [es_xn.enter_context(nc.sbuf_tensor(f"xnT{i}", [128, TOK], BF16)) for i in range(8)]
        es_at = ExitStack()
        attnT = [es_at.enter_context(nc.sbuf_tensor(f"attnT{i}", [128, TOK], BF16)) for i in range(8)]
        g1 = emit_l0attn(C, io, xnT, attnT)
        es_h = ExitStack()
        io["h_sb"] = [es_h.enter_context(nc.sbuf_tensor(f"h{i}", [128, TOK], BF16)) for i in range(8)]
        g2 = emit_l0ffn(C, io, xnT, attnT, es_at, g1)
        es_h.close()
        es_at.close()
        es_xn.close()

        es_xn1 = ExitStack()
        xn1T = [es_xn1.enter_context(nc.sbuf_tensor(f"xn1T{i}", [128, TOK], BF16)) for i in range(8)]
        g3 = emit_l1qkv(C, io, xn1T, g2)
        emit_l1fused(C, io, xn1T, g3)
        es_xn1.close()
    return nc



def _rope_tables():
    inv = 10000.0 ** (-np.arange(0, D, 2, dtype=np.float64) / D)
    pos = np.arange(S, dtype=np.float64)
    ang = np.outer(pos, inv)
    return np.cos(ang).astype(np.float32), np.sin(ang).astype(np.float32)


def _own_rows(core):
    b, h = core // 2, core % 2
    a = np.arange(NT)
    starts = b * S + 256 * a + 128 * h
    return (starts[:, None] + np.arange(128)[None, :]).reshape(-1)


def _inmaps(inputs):
    x2d = np.asarray(inputs["L0_attn_x"]).reshape(B * S, E)
    cosf, sinf = _rope_tables()
    ident = np.eye(128, dtype=np.float32)
    tri = np.triu(np.ones((128, 128), dtype=np.float32))
    ones = np.ones((128, 128), dtype=np.float32)
    zeros = np.zeros((128, 128), dtype=np.float32)
    wkvb = np.asarray(inputs["L0_attn_Wkvb"])
    wkvb_kv = np.ascontiguousarray(np.concatenate([wkvb[:, 0::2], wkvb[:, 1::2]], axis=1))
    onehot = np.zeros((NEXP, 2048), dtype=np.float32)
    for e in range(NEXP):
        onehot[e, 128 * e:128 * (e + 1)] = 1.0
    rs = lambda s: np.ascontiguousarray(np.asarray(s).reshape(-1, 128).T)
    import ml_dtypes
    bf = lambda a: np.ascontiguousarray(a).astype(ml_dtypes.bfloat16)
    common = dict(
        wqa=np.asarray(inputs["L0_attn_Wqa"]),
        wqb=np.asarray(inputs["L0_attn_Wqb"]),
        wkva=np.asarray(inputs["L0_attn_Wkva"]),
        wkvb_kv=wkvb_kv,
        ropek_cos=cosf, ropek_sin=sinf,
        ident=ident,
        wout=bf(np.asarray(inputs["L0_attn_Wout"])),
        bout=np.ascontiguousarray(np.asarray(inputs["L0_attn_bout"]).reshape(8, 128).T),
        routerT=np.ascontiguousarray(np.asarray(inputs["L0_ffn_router_w"]).T),
        gup_w=bf(np.asarray(inputs["L0_ffn_gate_up_w"])
                 .reshape(NEXP, 8, 128, 16, 128).transpose(0, 3, 2, 1, 4)
                 .reshape(NEXP, 2, 8, 128, 1024).transpose(0, 2, 3, 1, 4)
                 .reshape(NEXP, 8, 128, 2048)),
        dn_w=bf(np.asarray(inputs["L0_ffn_down_w"])),
        sgsu=bf(np.concatenate([
            np.asarray(inputs["L0_ffn_shared_gate_w"]).reshape(16, 128, 8, 128).transpose(0, 3, 2, 1).reshape(16, 128, 1024),
            np.asarray(inputs["L0_ffn_shared_up_w"]).reshape(16, 128, 8, 128).transpose(0, 3, 2, 1).reshape(16, 128, 1024),
        ], axis=2)),
        sdwT=bf(np.asarray(inputs["L0_ffn_shared_down_w"]).T),
        onehot=onehot,
        wqkv=bf(np.asarray(inputs["L1_attn_w_qkvw"])),
        w_out=bf(np.asarray(inputs["L1_attn_w_out"])),
        w12T=bf(np.concatenate([
            np.asarray(inputs["L1_ffn_w1_w"]).reshape(32, 128, 8, 128).transpose(0, 3, 2, 1).reshape(32, 128, 1024),
            np.asarray(inputs["L1_ffn_w2_w"]).reshape(32, 128, 8, 128).transpose(0, 3, 2, 1).reshape(32, 128, 1024),
        ], axis=2)),
        h1T=bf(np.asarray(inputs["L1_ffn_h1_w"])
               .reshape(32, 128, 32, 128).transpose(0, 3, 2, 1).reshape(32, 128, 4096)),
        h2T=bf(np.asarray(inputs["L1_ffn_h2_w"])
               .reshape(32, 128, 32, 128).transpose(0, 3, 2, 1).reshape(32, 128, 4096)),
        w3T=bf(np.asarray(inputs["L1_ffn_w3_w"])
               .reshape(8, 128, 32, 128).transpose(0, 3, 2, 1).reshape(8, 128, 4096)),
        w1s=rs(inputs["L1_ffn_w1_s"]), w2s=rs(inputs["L1_ffn_w2_s"]),
        h1s=rs(inputs["L1_ffn_h1_s"]), h2s=rs(inputs["L1_ffn_h2_s"]),
        w3s=rs(inputs["L1_ffn_w3_s"]),
    )
    maps = []
    for c in range(NCORES):
        b, h = c // 2, c % 2
        rows = _own_rows(c)
        m = dict(common)
        m.update(
            x_own=np.ascontiguousarray(x2d[rows]),
            ropeq_cos=np.ascontiguousarray(cosf[rows - b * S]),
            ropeq_sin=np.ascontiguousarray(sinf[rows - b * S]),
            mask0=(tri if h == 0 else ones),
            mask1=(zeros if h == 0 else tri),
        )
        maps.append(m)
    return maps


def kernel(**inputs):
    cores = list(range(NCORES))
    nc = build_mega()
    res = run_bass_kernel_spmd(nc, _inmaps(inputs), core_ids=cores)
    out = np.empty((B * S, E), dtype=np.float32)
    for c in cores:
        out[_own_rows(c)] = res.results[c]["out"]
    return out.reshape(B, S, E)


# version 27
# speedup vs baseline: 1.0186x; 1.0004x over previous
import numpy as np
from contextlib import ExitStack

import concourse.bass as bass
import concourse.mybir as mybir
from concourse.bass_utils import run_bass_kernel_spmd
from concourse.aot_env import allow_ici_all_to_all

F32 = mybir.dt.float32
BF16 = mybir.dt.bfloat16
AF = mybir.ActivationFunctionType
ALU = mybir.AluOpType
AX = mybir.AxisListType

B, S, E = 4, 4096, 1024
NCORES = 8
NT = 16
TOK = 2048
H, KVH, D = 16, 4, 64
QL, KVL = 64, 32
NEXP = 16
SHD = 2048
DHID = 4096
LN_EPS = 1e-5


class Sem:
    def __init__(self, nc, es, name):
        self.h = es.enter_context(nc.semaphore(name))
        self.n = 0

    def inc(self, inst, k=1):
        inst.then_inc(self.h, k)
        self.n += k
        return inst

    def wait(self, eng, lag=0):
        v = self.n - lag
        if v > 0:
            eng.wait_ge(self.h, v)

    def mark(self):
        return (self.h, self.n)


class Ctx:
    def __init__(self, nc, es, block):
        self.nc, self.es, self.block = nc, es, block
        self.T, self.V, self.S, self.Y, self.G = (
            block.tensor, block.vector, block.scalar, block.sync, block.gpsimd)
        self._sems = {}

    def gs(self, name):
        if name not in self._sems:
            self._sems[name] = Sem(self.nc, self.es, name)
        return self._sems[name]

    def sb(self, stack, name, shape, dt):
        return stack.enter_context(self.nc.sbuf_tensor(name, shape, dt))

    def psum(self, stack, name, shape, dt):
        return stack.enter_context(self.nc.psum_tensor(name, shape, dt))


def wait_gate(eng, gate):
    for (h, v) in gate:
        if v > 0:
            eng.wait_ge(h, v)



def emit_l0attn(C, io, xnT, attnT):
    nc, es = C.nc, C.es
    T, VE, SC, SY, G = C.T, C.V, C.S, C.Y, C.G
    x_own = io["x_own"]

    ph = ExitStack()
    sb = lambda n, s, d: C.sb(ph, n, s, d)
    KT = [sb(f"KT{i}", [128, S], BF16) for i in range(2)]
    Vt = sb("Vt", [128, 32 * 260], BF16)
    QT = [sb(f"QT{i}", [128, TOK], BF16) for i in range(8)]
    latT = sb("latT", [KVL, S], BF16)
    qlatT = sb("qlatT", [QL, TOK], BF16)
    wqa_sb = sb("wqa_sb", [128, 8 * QL], BF16)
    wqb_sb = sb("wqb_sb", [QL, E], BF16)
    wkva_sb = sb("wkva_sb", [128, 8 * KVL], BF16)
    wkvb_sb = sb("wkvb_sb", [KVL, 512], BF16)
    mask_sb = [sb(f"mask{i}_sb", [128, 128], BF16) for i in range(2)]
    ones64 = sb("ones64", [1, 64], F32)
    ropek_c = sb("ropek_c", [128, 32 * 32], F32)
    ropek_s = sb("ropek_s", [128, 32 * 32], F32)
    ropeq_c = sb("ropeq_c", [128, 16 * 32], F32)
    ropeq_s = sb("ropeq_s", [128, 16 * 32], F32)
    xbuf = [sb(f"xbuf{i}", [128, E], F32) for i in range(2)]
    xsq = [sb(f"xsq{i}", [128, E], F32) for i in range(2)]
    stat2 = [sb(f"stat{i}", [128, 8], F32) for i in range(2)]
    xn_tmp = [sb(f"xn_tmp{i}", [128, E], BF16) for i in range(2)]
    lat_own = sb("lat_own", [KVL, TOK], BF16)
    kq_tmp = [sb(f"kq_tmp{i}", [128, E], BF16) for i in range(2)]
    rtA = sb("rtA", [128, 512], F32)
    rtB = sb("rtB", [128, 512], F32)
    expS = [sb(f"expS{i}", [128, 512], BF16) for i in range(4)]
    recb = [sb(f"recb{i}", [64, 512], F32) for i in range(2)]
    rrow = [sb(f"rrow{i}", [1, 512], F32) for i in range(2)]

    gsn = lambda n: C.gs("a_" + n)
    vd = gsn("vd")
    wl, wls = gsn("wl"), gsn("wls")
    xl = [gsn("xl0"), gsn("xl1")]
    lnf, trp, evc = gsn("lnf"), gsn("trp"), gsn("evc")
    lns, sds, lnx = gsn("lns"), gsn("sds"), gsn("lnx")
    latm, late = gsn("latm"), gsn("late")
    lsto, agd, latl = gsn("lsto"), gsn("agd"), gsn("latl")
    kvm, roped, kqt, kqe = gsn("kvm"), gsn("roped"), gsn("kqt"), gsn("kqe")
    qlm, qle, qmm = gsn("qlm"), gsn("qle"), gsn("qmm")
    scm, exd, mkd, avd = gsn("scm"), gsn("exd"), gsn("mkd"), gsn("avd")
    nr1, nr2, nr3 = gsn("nr1"), gsn("nr2"), gsn("nr3")

    def g_loads(g):
        wl.inc(g.dma_start(out=wqa_sb[:], in_=io["wqa"].rearrange("(c p) n -> p c n", p=128)), 16)
        wl.inc(g.dma_start(out=wqb_sb[:], in_=io["wqb"]), 16)
        wl.inc(g.dma_start(out=wkva_sb[:], in_=io["wkva"].rearrange("(c p) n -> p c n", p=128)), 16)
        wl.inc(g.dma_start(out=wkvb_sb[:], in_=io["wkvb_kv"]), 16)
        wl.inc(g.dma_start(out=C.ident_bf[:], in_=io["ident"]), 16)
        wl.inc(g.dma_start(out=mask_sb[0][:], in_=io["mask0"]), 16)
        wl.inc(g.dma_start(out=mask_sb[1][:], in_=io["mask1"]), 16)
    G(g_loads)

    def sy_loads(sy):
        wls.inc(sy.dma_start(out=C.ident_f32[:], in_=io["ident"]), 16)
        wls.inc(sy.dma_start(out=ropek_c[:].rearrange("p (t f) -> p t f", f=32),
                             in_=io["ropek_cos"].rearrange("(t p) f -> p t f", p=128)), 16)
        wls.inc(sy.dma_start(out=ropek_s[:].rearrange("p (t f) -> p t f", f=32),
                             in_=io["ropek_sin"].rearrange("(t p) f -> p t f", p=128)), 16)
        wls.inc(sy.dma_start(out=ropeq_c[:].rearrange("p (t f) -> p t f", f=32),
                             in_=io["ropeq_cos"].rearrange("(t p) f -> p t f", p=128)), 16)
        wls.inc(sy.dma_start(out=ropeq_s[:].rearrange("p (t f) -> p t f", f=32),
                             in_=io["ropeq_sin"].rearrange("(t p) f -> p t f", p=128)), 16)
    SY(sy_loads)

    def v_init(v):
        wl.wait(v)
        wls.wait(v)
        v.memset(ones64[:], 1.0)
        lnf.inc(v.memset(
            Vt[:].rearrange("p (t z w) -> p t z w", z=4, w=65)[:, :, :, 64:65], 1.0))
    VE(v_init)

    def emit_ln_tile(src_ap, dst_bf_ap, ld_sem, bi, extra_war=None):
        stat = stat2[bi]
        xq = xsq[bi]

        def v_stats(v):
            ld_sem.wait(v)
            sds.wait(v, lag=1)
            v.tensor_tensor(xq[:], src_ap, src_ap, ALU.mult)
            v.tensor_reduce(stat[:, 0:1], src_ap, AX.X, ALU.add)
            vd.inc(v.tensor_reduce(stat[:, 1:2], xq[:], AX.X, ALU.add))
            vd.wait(v)
            vd.inc(v.tensor_scalar(stat[:, 2:3], stat[:, 0:1], 1.0 / E, None, ALU.mult))
            vd.wait(v)
            vd.inc(v.tensor_tensor(stat[:, 3:4], stat[:, 2:3], stat[:, 2:3], ALU.mult))
            v.tensor_scalar(stat[:, 4:5], stat[:, 1:2], 1.0 / E, LN_EPS, ALU.mult, ALU.add)
            vd.wait(v)
            lns.inc(v.tensor_tensor(stat[:, 4:5], stat[:, 4:5], stat[:, 3:4], ALU.subtract))
        VE(v_stats)

        def s_rstd(s):
            lns.wait(s)
            lnx.inc(s.activation(stat[:, 5:6], stat[:, 4:5], AF.Ln))
            lnx.wait(s)
            sds.inc(s.activation(stat[:, 6:7], stat[:, 5:6], AF.Exp, scale=-0.5))
        SC(s_rstd)

        def v_norm(v):
            sds.wait(v)
            if extra_war is not None:
                extra_war(v)
            lnf.inc(v.tensor_scalar(dst_bf_ap, src_ap, stat[:, 2:3], stat[:, 6:7],
                                    ALU.subtract, ALU.mult))
        VE(v_norm)

    psA = ExitStack()
    ps_tr = C.psum(psA, "ps_tr", [128, 1024], BF16)
    ps_lat = C.psum(psA, "ps_lat", [KVL, 128], F32)

    for a in range(NT):
        bi = a % 2

        def sy_x(sy, a=a, bi=bi):
            lnf.wait(sy, lag=1)
            xl[bi].inc(sy.dma_start(out=xbuf[bi][:], in_=x_own[128 * a:128 * (a + 1), :]), 16)
        SY(sy_x)

        emit_ln_tile(xbuf[bi][:], xn_tmp[bi][:], xl[bi], bi,
                     extra_war=(lambda v: trp.wait(v, lag=8)))

        def t_trans(tt, bi=bi):
            lnf.wait(tt)
            evc.wait(tt)
            for ct in range(8):
                trp.inc(tt.transpose(ps_tr[:, 128 * ct:128 * (ct + 1)],
                                     xn_tmp[bi][:, 128 * ct:128 * (ct + 1)], C.ident_bf[:]))
        T(t_trans)

        def v_evac(v, a=a):
            trp.wait(v)
            for ct in range(8):
                evc.inc(v.tensor_copy(xnT[ct][:, 128 * a:128 * (a + 1)],
                                      ps_tr[:, 128 * ct:128 * (ct + 1)]))
        VE(v_evac)

        def t_lat(tt, a=a):
            evc.wait(tt)
            late.wait(tt)
            for ct in range(8):
                latm.inc(tt.matmul(ps_lat[:, :], wkva_sb[:, KVL * ct:KVL * (ct + 1)],
                                   xnT[ct][:, 128 * a:128 * (a + 1)],
                                   start=(ct == 0), stop=(ct == 7)))
        T(t_lat)

        def v_lat(v, a=a):
            latm.wait(v)
            late.inc(v.tensor_copy(lat_own[:, 128 * a:128 * (a + 1)], ps_lat[:, :]))
        VE(v_lat)

    def sy_lat(sy):
        late.wait(sy)
        lsto.inc(sy.dma_start(out=io["lat_in"].ap(), in_=lat_own[:]), 16)
    SY(sy_lat)

    def g_ag(g):
        lsto.wait(g)
        with allow_ici_all_to_all():
            agd.inc(g.collective_compute(
                "AllGather", mybir.AluOpType.bypass,
                replica_groups=[[0, 1], [2, 3], [4, 5], [6, 7]],
                ins=[io["lat_in"].ap().opt()],
                outs=[io["lat_out"].ap().opt()],
            ))
    G(g_ag)

    def sy_latl(sy):
        agd.wait(sy)
        lo = io["lat_out"].ap()
        latl.inc(sy.dma_start(
            out=latT[:].rearrange("l (t two c) -> l t two c", two=2, c=128)[:, :, 0, :],
            in_=lo[0:KVL, :].rearrange("l (t c) -> l t c", c=128)), 16)
        latl.inc(sy.dma_start(
            out=latT[:].rearrange("l (t two c) -> l t two c", two=2, c=128)[:, :, 1, :],
            in_=lo[KVL:2 * KVL, :].rearrange("l (t c) -> l t c", c=128)), 16)
    SY(sy_latl)

    psA.close()
    psB = ExitStack()
    ps_ql = C.psum(psB, "ps_ql", [QL, 512], F32)
    ps_q = C.psum(psB, "ps_q", [128, 1024], F32)
    ps_qt = C.psum(psB, "ps_qt", [64, 256], BF16)

    for qc in range(4):
        def t_ql(tt, qc=qc):
            qle.wait(tt)
            for ct in range(8):
                qlm.inc(tt.matmul(ps_ql[:, :], wqa_sb[:, QL * ct:QL * (ct + 1)],
                                  xnT[ct][:, 512 * qc:512 * (qc + 1)],
                                  start=(ct == 0), stop=(ct == 7)))
        T(t_ql)

        def v_ql(v, qc=qc):
            qlm.wait(v)
            qle.inc(v.tensor_copy(qlatT[:, 512 * qc:512 * (qc + 1)], ps_ql[:, :]))
        VE(v_ql)

    for a in range(NT):
        bi = a % 2

        def t_q(tt, a=a):
            qle.wait(tt)
            roped.wait(tt)
            qmm.inc(tt.matmul(ps_q[:, 0:512], qlatT[:, 128 * a:128 * (a + 1)],
                              wqb_sb[:, 0:512], start=True, stop=True))
            qmm.inc(tt.matmul(ps_q[:, 512:1024], qlatT[:, 128 * a:128 * (a + 1)],
                              wqb_sb[:, 512:1024], start=True, stop=True))
        T(t_q)

        def v_ropeq(v, a=a, bi=bi):
            qmm.wait(v)
            kqt.wait(v, lag=16)
            e_ps = ps_q[:, :].rearrange("p (hh f2) -> p hh f2", hh=16)[:, :, 0::2]
            o_ps = ps_q[:, :].rearrange("p (hh f2) -> p hh f2", hh=16)[:, :, 1::2]
            cosb = ropeq_c[:, 32 * a:32 * (a + 1)].unsqueeze(1).broadcast_to((128, 16, 32))
            sinb = ropeq_s[:, 32 * a:32 * (a + 1)].unsqueeze(1).broadcast_to((128, 16, 32))
            rA = rtA[:, :].rearrange("p (hh f) -> p hh f", hh=16)
            rB = rtB[:, :].rearrange("p (hh f) -> p hh f", hh=16)
            qe = kq_tmp[bi][:, :].rearrange("p (hh f2) -> p hh f2", hh=16)[:, :, 0::2]
            qo = kq_tmp[bi][:, :].rearrange("p (hh f2) -> p hh f2", hh=16)[:, :, 1::2]
            v.tensor_tensor(rA, e_ps, cosb, ALU.mult)
            vd.inc(v.tensor_tensor(rB, o_ps, sinb, ALU.mult))
            vd.wait(v)
            vd.inc(v.tensor_tensor(qe, rA, rB, ALU.subtract))
            vd.wait(v)
            v.tensor_tensor(rA, e_ps, sinb, ALU.mult)
            vd.inc(v.tensor_tensor(rB, o_ps, cosb, ALU.mult))
            vd.wait(v)
            roped.inc(v.tensor_tensor(qo, rA, rB, ALU.add))
        VE(v_ropeq)

        for grp in range(8):
            def t_qt(tt, bi=bi, grp=grp):
                roped.wait(tt)
                kqe.wait(tt)
                h0, h1 = 2 * grp, 2 * grp + 1
                kqt.inc(tt.transpose(ps_qt[:, 0:128], kq_tmp[bi][:, 64 * h0:64 * (h0 + 1)],
                                     C.ident_bf[:]))
                kqt.inc(tt.transpose(ps_qt[:, 128:256], kq_tmp[bi][:, 64 * h1:64 * (h1 + 1)],
                                     C.ident_bf[:]))
            T(t_qt)

            def v_qte(v, a=a, grp=grp):
                kqt.wait(v)
                for k in range(2):
                    h_ = 2 * grp + k
                    p = (h_ % 4) + 4 * (h_ // 8)
                    up = (h_ // 4) % 2
                    kqe.inc(v.tensor_copy(QT[p][64 * up:64 * up + 64, 128 * a:128 * (a + 1)],
                                          ps_qt[:, 128 * k:128 * (k + 1)]))
            VE(v_qte)

    ps_kv = C.psum(psB, "ps_kv", [128, 512], F32)
    ps_kt = C.psum(psB, "ps_kt", [128, 256], BF16)

    for t in range(32):
        bi = t % 2

        def t_kv(tt, t=t):
            if t == 0:
                latl.wait(tt)
            roped.wait(tt)
            kvm.inc(tt.matmul(ps_kv[:, :], latT[:, 128 * t:128 * (t + 1)], wkvb_sb[:, :],
                              start=True, stop=True))
        T(t_kv)

        def v_ropek(v, t=t, bi=bi):
            kvm.wait(v)
            kqt.wait(v, lag=2)
            e_ps = ps_kv[:, 0:256].rearrange("p (k f2) -> p k f2", k=4)[:, :, 0::2]
            o_ps = ps_kv[:, 0:256].rearrange("p (k f2) -> p k f2", k=4)[:, :, 1::2]
            cosb = ropek_c[:, 32 * t:32 * (t + 1)].unsqueeze(1).broadcast_to((128, 4, 32))
            sinb = ropek_s[:, 32 * t:32 * (t + 1)].unsqueeze(1).broadcast_to((128, 4, 32))
            rA = rtA[:, 0:128].rearrange("p (k f) -> p k f", k=4)
            rB = rtB[:, 0:128].rearrange("p (k f) -> p k f", k=4)
            ke = kq_tmp[bi][:, 0:256].rearrange("p (k f2) -> p k f2", k=4)[:, :, 0::2]
            ko = kq_tmp[bi][:, 0:256].rearrange("p (k f2) -> p k f2", k=4)[:, :, 1::2]
            v.tensor_tensor(rA, e_ps, cosb, ALU.mult)
            vd.inc(v.tensor_tensor(rB, o_ps, sinb, ALU.mult))
            vd.wait(v)
            vd.inc(v.tensor_tensor(ke, rA, rB, ALU.subtract))
            vd.wait(v)
            v.tensor_tensor(rA, e_ps, sinb, ALU.mult)
            vd.inc(v.tensor_tensor(rB, o_ps, cosb, ALU.mult))
            vd.wait(v)
            v.tensor_tensor(ko, rA, rB, ALU.add)
            vdst = Vt[:, 260 * t:260 * t + 260].rearrange("p (z w) -> p z w", z=4)[:, :, 0:64]
            vsrc = ps_kv[:, 256:512].rearrange("p (z w) -> p z w", z=4)
            roped.inc(v.tensor_copy(vdst, vsrc))
        VE(v_ropek)

        def t_kt(tt, bi=bi):
            roped.wait(tt)
            kqe.wait(tt)
            kqt.inc(tt.transpose(ps_kt[:, 0:128], kq_tmp[bi][:, 0:128], C.ident_bf[:]))
            kqt.inc(tt.transpose(ps_kt[:, 128:256], kq_tmp[bi][:, 128:256], C.ident_bf[:]))
        T(t_kt)

        def v_kte(v, t=t):
            kqt.wait(v)
            v.tensor_copy(KT[0][:, 128 * t:128 * (t + 1)], ps_kt[:, 0:128])
            kqe.inc(v.tensor_copy(KT[1][:, 128 * t:128 * (t + 1)], ps_kt[:, 128:256]))
        VE(v_kte)

    psB.close()
    psC = ExitStack()
    ps_s = [C.psum(psC, f"ps_s{i}", [128, 512], F32) for i in range(4)]
    ps_o = [C.psum(psC, f"ps_o{i}", [65, 512], F32) for i in range(2)]
    ps_r = C.psum(psC, "ps_r", [64, 512], F32)

    state = {"first": True, "gsi": 0, "quad": 0}

    def quad(h_, q4):
        kvh = h_ // 4
        ktn = KT[h_ // 8]
        up = (h_ // 4) % 2
        qtn = QT[(h_ % 4) + 4 * (h_ // 8)]
        col0 = 512 * q4
        ob = state["quad"] % 2
        steps = []
        for kt in range(8 * q4 + 2):
            steps.append((kt, 0, 512, 0 if kt >= 8 * q4 else None))
        for m in range(1, 4):
            for kt in (8 * q4 + 2 * m, 8 * q4 + 2 * m + 1):
                steps.append((kt, 128 * m, 512 - 128 * m, m))
        nsteps = len(steps)
        pend = []
        exp_plan = []
        for si, (kt, off, n, mb) in enumerate(steps):
            G = state["gsi"]
            sl = G % 4
            state["gsi"] += 1

            def t_sc(tt, kt=kt, off=off, n=n, sl=sl, G=G):
                if state["first"]:
                    kqe.wait(tt)
                    roped.wait(tt)
                    qle.wait(tt)
                    state["first"] = False
                if G - 3 > 0:
                    tt.wait_ge(exd.h, G - 3)
                scm.inc(tt.matmul(ps_s[sl][:, 0:n],
                                  ktn[64 * up:64 * up + 64, 128 * kt:128 * (kt + 1)],
                                  qtn[64 * up:64 * up + 64, col0 + off:col0 + off + n],
                                  start=True, stop=True))
            T(t_sc)
            exp_plan.append((sl, n, scm.n, G))

            if mb is not None:
                def v_mask(v, kt=kt, sl=sl, G=G):
                    v.wait_ge(exd.h, G + 1)
                    mkd.inc(v.tensor_tensor(expS[sl][:, 0:128], expS[sl][:, 0:128],
                                            mask_sb[kt % 2][:], ALU.mult))
                VE(v_mask)
            mkd_goal = mkd.n

            def t_av(tt, kt=kt, off=off, n=n, sl=sl, si=si, G=G, mkd_goal=mkd_goal):
                tt.wait_ge(exd.h, G + 1)
                if mkd_goal > 0:
                    tt.wait_ge(mkd.h, mkd_goal)
                if si == 0:
                    nr3.wait(tt, lag=1)
                avd.inc(tt.matmul(ps_o[ob][:, off:off + n],
                                  Vt[:, 260 * kt + 65 * kvh:260 * kt + 65 * kvh + 65],
                                  expS[sl][:, 0:n],
                                  start=(si == 0), stop=(si == nsteps - 1)))
            pend.append(t_av)
            if len(pend) > 2:
                T(pend.pop(0))
        while pend:
            T(pend.pop(0))

        def s_exp_all(s, plan=tuple(exp_plan)):
            for (sl, n, scm_t, G) in plan:
                s.wait_ge(scm.h, scm_t)
                if G - 3 > 0:
                    s.wait_ge(avd.h, G - 3)
                exd.inc(s.activation(expS[sl][:, 0:n], ps_s[sl][:, 0:n],
                                     AF.Exp, scale=0.125))
        SC(s_exp_all)

        def v_rec(v):
            avd.wait(v)
            nr2.wait(v, lag=1)
            nr1.inc(v.reciprocal(rrow[ob][:, :], ps_o[ob][64:65, :]))
        VE(v_rec)

        def t_bc(tt):
            nr1.wait(tt)
            nr3.wait(tt)
            nr2.inc(tt.matmul(ps_r[:, :], ones64[:, :], rrow[ob][:, :], start=True, stop=True))
        T(t_bc)

        def v_fin(v, h_=h_, q4=q4):
            nr2.wait(v)
            vd.inc(v.tensor_copy(recb[ob][:, :], ps_r[:, :]))
            vd.wait(v)
            dst = attnT[h_ // 2][64 * (h_ % 2):64 * (h_ % 2) + 64, 512 * q4:512 * (q4 + 1)]
            nr3.inc(v.tensor_tensor(dst, ps_o[ob][0:64, :], recb[ob][:, :], ALU.mult))
        VE(v_fin)
        state["quad"] += 1

    for h_ in range(H):
        for q4 in range(4):
            quad(h_, q4)

    psC.close()
    ph.close()
    return [nr3.mark(), evc.mark(), kqe.mark(), roped.mark()]



def emit_l0ffn(C, io, xnT, attnT, attnT_stack, gate):
    nc = C.nc
    T, VE, SC, SY, G = C.T, C.V, C.S, C.Y, C.G

    ph = ExitStack()
    sb = lambda n, s, d: C.sb(ph, n, s, d)
    accT = [sb(f"f_accT{i}", [128, TOK], F32) for i in range(8)]
    dn_sb = [sb(f"f_dn{i}", [128, 1024], BF16) for i in range(8)]
    wbc = sb("f_wbc", [128, TOK], BF16)
    wT_sb = sb("f_wT", [NEXP, TOK], BF16)
    onehot_sb = sb("f_onehot", [NEXP, 2048], BF16)
    routerT_sb = sb("f_routerT", [128, 8 * NEXP], BF16)
    bout_sb = sb("f_bout", [128, 8], F32)
    logits = sb("f_logits", [128, NT * NEXP], F32)
    tcol = sb("f_tcol", [128, 4], F32)
    eqb = sb("f_eqb", [128, NEXP], F32)
    mkb = sb("f_mkb", [128, NEXP], F32)
    geb = sb("f_geb", [128, NEXP], F32)
    sgb = sb("f_sgb", [128, NEXP], F32)
    wfull = sb("f_wfull", [128, NEXP], F32)
    silu_sb = sb("f_silu", [128, TOK], BF16)
    wout_sb = [sb(f"f_wout{i}", [128, E], BF16) for i in range(2)]
    h_sb = io["h_sb"]

    gsn = lambda n: C.gs("f_" + n)
    vd = C.gs("a_vd")
    wlg, wls = gsn("wlg"), gsn("wls")
    wmm, wev = gsn("wmm"), gsn("wev")
    lgm, lgv, sgd, tpw, twe = gsn("lgm"), gsn("lgv"), gsn("sgd"), gsn("tpw"), gsn("twe")
    gul = [gsn(f"gul{i}") for i in range(2)]
    dnl8 = gsn("dnl")
    dnl = [dnl8] * 8
    wbm, wbe = gsn("wbm"), gsn("wbe")
    gmm, sil, umm, hbd, hcd = gsn("gmm"), gsn("sil"), gsn("umm"), gsn("hbd"), gsn("hcd")
    dnm, dne = gsn("dnm"), gsn("dne")
    sdl8 = gsn("sdl")
    sdl = [sdl8] * 8
    xld = [gsn("xld0"), gsn("xld1")]
    fint, finv = gsn("fint"), gsn("finv")
    sto = gsn("sto")

    wol = gsn("wol")

    def g_loads(g):
        wait_gate(g, gate)
        wlg.inc(g.dma_start(out=routerT_sb[:],
                            in_=io["routerT"].rearrange("(c p) n -> p c n", p=128)), 16)
        wlg.inc(g.dma_start(out=onehot_sb[:], in_=io["onehot"]), 16)
    G(g_loads)

    def sy_loads(sy):
        wait_gate(sy, gate)
        wls.inc(sy.dma_start(out=bout_sb[:], in_=io["bout"]), 16)
    SY(sy_loads)

    psP = ExitStack()
    ps_m2 = [C.psum(psP, f"f_ps_m{i}", [128, 512], F32) for i in range(2)]
    ps_t = C.psum(psP, "f_ps_t", [NEXP, 128], F32)
    ps_m = ps_m2[0]

    for qtr in range(4):
        def g_wo(g, qtr=qtr):
            if qtr == 0:
                wait_gate(g, gate)
            if qtr > 0:
                wmm.wait(g)
            for k in range(2):
                ci = 2 * qtr + k
                wol.inc(g.dma_start(out=wout_sb[k][:], in_=io["wout"][128 * ci:128 * (ci + 1), :]), 16)
        G(g_wo)

        for oc in range(8):
            for c4 in range(4):
                def t_wout(tt, qtr=qtr, oc=oc, c4=c4):
                    if qtr == 0 and oc == 0 and c4 == 0:
                        wait_gate(tt, gate)
                        wlg.wait(tt)
                    wol.wait(tt)
                    wev.wait(tt, lag=1)
                    mi = (qtr * 32 + oc * 4 + c4) % 2
                    for k in range(2):
                        ci = 2 * qtr + k
                        wmm.inc(tt.matmul(ps_m2[mi][:, :], wout_sb[k][:, 128 * oc:128 * (oc + 1)],
                                          attnT[ci][:, 512 * c4:512 * (c4 + 1)],
                                          start=(k == 0), stop=(k == 1)))
                T(t_wout)

                def v_wout(v, qtr=qtr, oc=oc, c4=c4):
                    wmm.wait(v)
                    mi = (qtr * 32 + oc * 4 + c4) % 2
                    if qtr == 0:
                        wls.wait(v)
                        wev.inc(v.tensor_scalar(accT[oc][:, 512 * c4:512 * (c4 + 1)],
                                                ps_m2[mi][:, :], bout_sb[:, oc:oc + 1], None, ALU.add))
                    else:
                        wev.inc(v.tensor_tensor(accT[oc][:, 512 * c4:512 * (c4 + 1)],
                                                accT[oc][:, 512 * c4:512 * (c4 + 1)],
                                                ps_m2[mi][:, :], ALU.add))
                VE(v_wout)

    for a in range(NT):
        def t_lg(tt, a=a):
            if a == 0:
                wev.wait(tt)
            lgv.wait(tt)
            for ci in range(8):
                lgm.inc(tt.matmul(ps_m[:, 0:NEXP], xnT[ci][:, 128 * a:128 * (a + 1)],
                                  routerT_sb[:, NEXP * ci:NEXP * (ci + 1)],
                                  start=(ci == 0), stop=(ci == 7)))
        T(t_lg)

        def v_lg(v, a=a):
            lgm.wait(v)
            lgv.inc(v.tensor_copy(logits[:, NEXP * a:NEXP * (a + 1)], ps_m[:, 0:NEXP]))
        VE(v_lg)

    for a in range(NT):
        la = logits[:, NEXP * a:NEXP * (a + 1)]

        def v_top2a(v, la=la):
            lgv.wait(v)
            vd.inc(v.tensor_reduce(tcol[:, 0:1], la, AX.X, ALU.max))
            vd.wait(v)
            vd.inc(v.tensor_scalar(eqb[:], la, tcol[:, 0:1], None, ALU.is_equal))
            vd.wait(v)
            vd.inc(v.scalar_tensor_tensor(mkb[:], eqb[:], -1e30, la, ALU.mult, ALU.add))
            vd.wait(v)
            vd.inc(v.tensor_reduce(tcol[:, 1:2], mkb[:], AX.X, ALU.max))
            vd.wait(v)
            sgd.inc(v.tensor_scalar(geb[:], la, tcol[:, 1:2], None, ALU.is_ge))
        VE(v_top2a)

        def s_sig(s, la=la):
            sgd.wait(s)
            tpw.wait(s, lag=1)
            sgd.inc(s.activation(sgb[:], la, AF.Sigmoid))
        SC(s_sig)

        def v_wfull(v):
            sgd.wait(v)
            tpw.inc(v.tensor_tensor(wfull[:], sgb[:], geb[:], ALU.mult))
        VE(v_wfull)

        def t_wT(tt):
            tpw.wait(tt)
            twe.wait(tt)
            tpw.inc(tt.transpose(ps_t[:, :], wfull[:, :], C.ident_f32[:]))
        T(t_wT)

        def v_wTe(v, a=a):
            tpw.wait(v)
            twe.inc(v.tensor_copy(wT_sb[:, 128 * a:128 * (a + 1)], ps_t[:, :]))
        VE(v_wTe)

    psP.close()

    esP3 = ExitStack()
    gu2 = [C.sb(esP3, f"f_gu2_{i}", [128, 2048], BF16) for i in range(2)]
    psX = ExitStack()
    ps_ga = C.psum(psX, "f_ps_ga", [128, 2048], F32)
    ps_gb = C.psum(psX, "f_ps_gb", [128, 2048], F32)

    def emit_expert(e):
        def g_dn(g, e=e):
            if e == 0:
                wmm.wait(g)
            dnm.wait(g)
            for ci in range(8):
                dnl[ci].inc(g.dma_start(out=dn_sb[ci][:],
                                        in_=io["dn_w"][e, 128 * ci:128 * (ci + 1), :]), 16)
        G(g_dn)

        for c4 in range(4):
            def t_wb(tt, e=e, c4=c4):
                if e == 0 and c4 == 0:
                    twe.wait(tt)
                    wev.wait(tt)
                hbd.wait(tt)
                wbe.wait(tt)
                dne.wait(tt)
                wbm.inc(tt.matmul(ps_gb[:, 512 * c4:512 * (c4 + 1)],
                                  onehot_sb[:, 128 * e:128 * (e + 1)],
                                  wT_sb[:, 512 * c4:512 * (c4 + 1)], start=True, stop=True))
            T(t_wb)

        def v_wb(v):
            wbm.wait(v)
            wbe.inc(v.tensor_copy(wbc[:], ps_gb[:, :]))
        VE(v_wb)

        for g in range(8):
            wi = g % 2

            def g_gu(g_, e=e, g=g, wi=wi):
                if e == 0 and g < 2:
                    wmm.wait(g_)
                umm.wait(g_, lag=1)
                gul[wi].inc(g_.dma_start(out=gu2[wi][:], in_=io["gup_w"][e, g]), 16)
            G(g_gu)

            def t_gate(tt, e=e, g=g, wi=wi):
                gul[wi].wait(tt)
                sil.wait(tt)
                if g == 0:
                    wbe.wait(tt)
                    dne.wait(tt)
                for ci in range(8):
                    for c4 in range(4):
                        gmm.inc(tt.matmul(ps_ga[:, 512 * c4:512 * (c4 + 1)],
                                          gu2[wi][:, 128 * ci:128 * (ci + 1)],
                                          xnT[ci][:, 512 * c4:512 * (c4 + 1)],
                                          start=(ci == 0), stop=(ci == 7)))
            T(t_gate)

            def s_silu(s, g=g):
                hbd.wait(s)
                for c4 in range(4):
                    gmm.wait(s, lag=3 - c4)
                    inst = s.activation(silu_sb[:, 512 * c4:512 * (c4 + 1)],
                                        ps_ga[:, 512 * c4:512 * (c4 + 1)], AF.Silu)
                sil.inc(inst)
            SC(s_silu)

            def t_up(tt, e=e, g=g, wi=wi):
                hbd.wait(tt)
                if g == 0:
                    wbe.wait(tt)
                for ci in range(8):
                    for c4 in range(4):
                        inst = tt.matmul(ps_gb[:, 512 * c4:512 * (c4 + 1)],
                                         gu2[wi][:, 1024 + 128 * ci:1024 + 128 * (ci + 1)],
                                         xnT[ci][:, 512 * c4:512 * (c4 + 1)],
                                         start=(ci == 0), stop=(ci == 7))
                umm.inc(inst)
            T(t_up)

            def v_h(v, e=e, g=g):
                umm.wait(v)
                sil.wait(v)
                wbe.wait(v)
                dnm.wait(v)
                hbd.inc(v.tensor_tensor(h_sb[g][:], ps_gb[:, :], silu_sb[:], ALU.mult))
                hbd.wait(v)
                hcd.inc(v.tensor_tensor(h_sb[g][:], h_sb[g][:], wbc[:], ALU.mult))
            VE(v_h)

        for oc in range(8):
            pbank = ps_ga if oc % 2 == 0 else ps_gb

            def t_dn(tt, e=e, oc=oc, pbank=pbank):
                hcd.wait(tt)
                dnl[0].wait(tt)
                if oc >= 2:
                    dne.wait(tt, lag=4)
                for hc in range(8):
                    for c4 in range(4):
                        inst = tt.matmul(pbank[:, 512 * c4:512 * (c4 + 1)],
                                         dn_sb[hc][:, 128 * oc:128 * (oc + 1)],
                                         h_sb[hc][:, 512 * c4:512 * (c4 + 1)],
                                         start=(hc == 0), stop=(hc == 7))
                        if hc == 7:
                            dnm.inc(inst)
            T(t_dn)

            for c4 in range(4):
                def v_dn(v, oc=oc, c4=c4, pbank=pbank):
                    dnm.wait(v, lag=3 - c4)
                    dne.inc(v.tensor_tensor(accT[oc][:, 512 * c4:512 * (c4 + 1)],
                                            accT[oc][:, 512 * c4:512 * (c4 + 1)],
                                            pbank[:, 512 * c4:512 * (c4 + 1)], ALU.add))
                VE(v_dn)

    for e in range(NEXP):
        emit_expert(e)

    for sh in range(2):
        for gl in range(8):
            gidx = 8 * sh + gl
            wi = gidx % 2

            def g_ws(g, gidx=gidx, wi=wi):
                umm.wait(g, lag=1)
                gul[wi].inc(g.dma_start(out=gu2[wi][:], in_=io["sgsu"][gidx]), 16)
            G(g_ws)

            def g_sd(g, gidx=gidx, gl=gl):
                dnm.wait(g)
                sdl[gl].inc(g.dma_start(out=dn_sb[gl][:],
                                        in_=io["sdwT"][128 * gidx:128 * (gidx + 1), :]), 16)
            G(g_sd)

            def t_sg(tt, wi=wi, sh=sh, gl=gl):
                gul[wi].wait(tt)
                sil.wait(tt)
                if sh == 0 and gl == 0:
                    dne.wait(tt)
                for ci in range(8):
                    for c4 in range(4):
                        gmm.inc(tt.matmul(ps_ga[:, 512 * c4:512 * (c4 + 1)],
                                          gu2[wi][:, 128 * ci:128 * (ci + 1)],
                                          xnT[ci][:, 512 * c4:512 * (c4 + 1)],
                                          start=(ci == 0), stop=(ci == 7)))
            T(t_sg)

            def s_ssilu(s):
                hbd.wait(s)
                for c4 in range(4):
                    gmm.wait(s, lag=3 - c4)
                    inst = s.activation(silu_sb[:, 512 * c4:512 * (c4 + 1)],
                                        ps_ga[:, 512 * c4:512 * (c4 + 1)], AF.Silu)
                sil.inc(inst)
            SC(s_ssilu)

            def t_su(tt, wi=wi, sh=sh, gl=gl):
                hbd.wait(tt)
                if sh == 0 and gl == 0:
                    dne.wait(tt)
                for ci in range(8):
                    for c4 in range(4):
                        inst = tt.matmul(ps_gb[:, 512 * c4:512 * (c4 + 1)],
                                         gu2[wi][:, 1024 + 128 * ci:1024 + 128 * (ci + 1)],
                                         xnT[ci][:, 512 * c4:512 * (c4 + 1)],
                                         start=(ci == 0), stop=(ci == 7))
                umm.inc(inst)
            T(t_su)

            def v_t(v, gl=gl):
                umm.wait(v)
                sil.wait(v)
                dnm.wait(v)
                hbd.inc(v.tensor_tensor(h_sb[gl][:], ps_gb[:, :], silu_sb[:], ALU.mult))
            VE(v_t)

        for oc in range(8):
            pbank = ps_ga if oc % 2 == 0 else ps_gb

            def t_sd(tt, oc=oc, pbank=pbank):
                hbd.wait(tt)
                sdl[0].wait(tt)
                if oc >= 2:
                    dne.wait(tt, lag=4)
                for gl in range(8):
                    for c4 in range(4):
                        inst = tt.matmul(pbank[:, 512 * c4:512 * (c4 + 1)],
                                         dn_sb[gl][:, 128 * oc:128 * (oc + 1)],
                                         h_sb[gl][:, 512 * c4:512 * (c4 + 1)],
                                         start=(gl == 0), stop=(gl == 7))
                        if gl == 7:
                            dnm.inc(inst)
            T(t_sd)

            for c4 in range(4):
                def v_sd(v, oc=oc, c4=c4, pbank=pbank):
                    dnm.wait(v, lag=3 - c4)
                    dne.inc(v.tensor_tensor(accT[oc][:, 512 * c4:512 * (c4 + 1)],
                                            accT[oc][:, 512 * c4:512 * (c4 + 1)],
                                            pbank[:, 512 * c4:512 * (c4 + 1)], ALU.add))
                VE(v_sd)

    psX.close()
    esP3.close()
    psF = ExitStack()
    ps_f = C.psum(psF, "f_ps_f", [128, 128], F32)
    xbuf = [h_sb[0][:].bitcast(F32), h_sb[1][:].bitcast(F32)]
    x1b = [h_sb[2][:].bitcast(F32), h_sb[3][:].bitcast(F32)]

    for a in range(NT):
        bi = a % 2

        def sy_xa(sy, a=a, bi=bi):
            if a < 2:
                dne.wait(sy)
            finv.wait(sy, lag=8)
            xld[bi].inc(sy.dma_start(out=xbuf[bi], in_=io["x_own"][128 * a:128 * (a + 1), :]), 16)
        SY(sy_xa)

        for ct in range(8):
            def t_f(tt, a=a, ct=ct):
                if a == 0 and ct == 0:
                    dne.wait(tt)
                    hbd.wait(tt)
                finv.wait(tt)
                fint.inc(tt.transpose(ps_f[:, :], accT[ct][:, 128 * a:128 * (a + 1)],
                                      C.ident_f32[:]))
            T(t_f)

            def v_f(v, a=a, bi=bi, ct=ct):
                fint.wait(v)
                xld[bi].wait(v)
                if ct == 0:
                    sto.wait(v, lag=16)
                finv.inc(v.tensor_tensor(x1b[bi][:, 128 * ct:128 * (ct + 1)],
                                         xbuf[bi][:, 128 * ct:128 * (ct + 1)],
                                         ps_f[:, :], ALU.add))
            VE(v_f)

        def sy_o(sy, a=a, bi=bi):
            finv.wait(sy)
            sto.inc(sy.dma_start(out=io["x1_scr"].ap()[128 * a:128 * (a + 1), :], in_=x1b[bi]), 16)
        SY(sy_o)

    psF.close()
    ph.close()
    return [sto.mark(), finv.mark()]



def emit_l1qkv(C, io, xn1T, gate):
    nc = C.nc
    T, VE, SC, SY, G = C.T, C.V, C.S, C.Y, C.G

    ph = ExitStack()
    sb = lambda n, s, d: C.sb(ph, n, s, d)
    w_sb = [sb(f"q_w{i}", [128, 4 * E], BF16) for i in range(8)]
    xbuf = [sb(f"q_xbuf{i}", [128, E], F32) for i in range(4)]
    xsq = [sb(f"q_xsq{i}", [128, E], F32) for i in range(4)]
    stat2 = [sb(f"q_stat{i}", [128, 8], F32) for i in range(4)]
    xn_tmp = [sb(f"q_xn{i}", [128, E], BF16) for i in range(4)]
    qkv_sb = [sb(f"q_qkv{i}", [128, 4 * E], BF16) for i in range(4)]

    gsn = lambda n: C.gs("q_" + n)
    vd = C.gs("a_vd")
    wlg = gsn("wlg")
    lnf, lns, sds, lnx = C.gs("a_lnf"), C.gs("a_lns"), C.gs("a_sds"), C.gs("a_lnx")
    xl = [C.gs("a_xl0"), C.gs("a_xl1")]
    trp, evc = C.gs("a_trp"), C.gs("a_evc")
    qmm, qev, sto = gsn("qmm"), gsn("qev"), gsn("sto")

    def g_loads(g):
        wait_gate(g, gate)
        for ci in range(8):
            wlg.inc(g.dma_start(out=w_sb[ci][:], in_=io["wqkv"][128 * ci:128 * (ci + 1), :]), 16)
    G(g_loads)

    ps_tr = C.psum(ph, "q_ps_tr", [128, 1024], BF16)
    ps_q = [C.psum(ph, f"q_ps_q{i}", [128, 512], F32) for i in range(2)]

    gq = {"n": 0}
    for a in range(NT):
        bi = a % 4
        xi = a % 2

        def sy_x(sy, a=a, bi=bi, xi=xi):
            if a < 2:
                wait_gate(sy, gate)
            lnf.wait(sy, lag=3)
            xl[xi].inc(sy.dma_start(out=xbuf[bi][:],
                                    in_=io["x1_scr"].ap()[128 * a:128 * (a + 1), :]), 16)
        SY(sy_x)

        def v_stats(v, bi=bi, xi=xi):
            stat = stat2[bi]
            xl[xi].wait(v)
            sds.wait(v, lag=3)
            v.tensor_tensor(xsq[bi][:], xbuf[bi][:], xbuf[bi][:], ALU.mult)
            v.tensor_reduce(stat[:, 0:1], xbuf[bi][:], AX.X, ALU.add)
            vd.inc(v.tensor_reduce(stat[:, 1:2], xsq[bi][:], AX.X, ALU.add))
            vd.wait(v)
            vd.inc(v.tensor_scalar(stat[:, 2:3], stat[:, 0:1], 1.0 / E, None, ALU.mult))
            vd.wait(v)
            vd.inc(v.tensor_tensor(stat[:, 3:4], stat[:, 2:3], stat[:, 2:3], ALU.mult))
            v.tensor_scalar(stat[:, 4:5], stat[:, 1:2], 1.0 / E, LN_EPS, ALU.mult, ALU.add)
            vd.wait(v)
            lns.inc(v.tensor_tensor(stat[:, 4:5], stat[:, 4:5], stat[:, 3:4], ALU.subtract))
        VE(v_stats)

        def s_rstd(s, bi=bi):
            stat = stat2[bi]
            lns.wait(s)
            lnx.inc(s.activation(stat[:, 5:6], stat[:, 4:5], AF.Ln))
            lnx.wait(s)
            sds.inc(s.activation(stat[:, 6:7], stat[:, 5:6], AF.Exp, scale=-0.5))
        SC(s_rstd)

        def v_norm(v, bi=bi):
            stat = stat2[bi]
            sds.wait(v)
            trp.wait(v, lag=24)
            lnf.inc(v.tensor_scalar(xn_tmp[bi][:], xbuf[bi][:], stat[:, 2:3], stat[:, 6:7],
                                    ALU.subtract, ALU.mult))
        VE(v_norm)

        def t_trans(tt, bi=bi):
            lnf.wait(tt)
            evc.wait(tt)
            for ct in range(8):
                trp.inc(tt.transpose(ps_tr[:, 128 * ct:128 * (ct + 1)],
                                     xn_tmp[bi][:, 128 * ct:128 * (ct + 1)], C.ident_bf[:]))
        T(t_trans)

        def v_evac(v, a=a):
            trp.wait(v)
            for ct in range(8):
                evc.inc(v.tensor_copy(xn1T[ct][:, 128 * a:128 * (a + 1)],
                                      ps_tr[:, 128 * ct:128 * (ct + 1)]))
        VE(v_evac)

        for oc8 in range(8):
            sl = gq["n"] % 2
            gq["n"] += 1

            def t_q(tt, a=a, oc8=oc8, sl=sl):
                if a == 0 and oc8 == 0:
                    wlg.wait(tt)
                evc.wait(tt)
                qev.wait(tt, lag=1)
                for ci in range(8):
                    qmm.inc(tt.matmul(ps_q[sl][:, :], xn1T[ci][:, 128 * a:128 * (a + 1)],
                                      w_sb[ci][:, 512 * oc8:512 * (oc8 + 1)],
                                      start=(ci == 0), stop=(ci == 7)))
            T(t_q)

            def v_q(v, a=a, bi=bi, oc8=oc8, sl=sl):
                qmm.wait(v)
                if oc8 == 0:
                    sto.wait(v, lag=48)
                qev.inc(v.tensor_copy(qkv_sb[bi][:, 512 * oc8:512 * (oc8 + 1)], ps_q[sl][:, :]))
            VE(v_q)

        def sy_q(sy, a=a, bi=bi):
            qev.wait(sy)
            sto.inc(sy.dma_start(out=io["qkv_scr"].ap()[128 * a:128 * (a + 1), :],
                                 in_=qkv_sb[bi][:]), 16)
        SY(sy_q)

    ph.close()
    return [sto.mark(), evc.mark(), qev.mark()]



def emit_l1fused(C, io, xn1T, gate):
    nc = C.nc
    T, VE, SC, SY, G = C.T, C.V, C.S, C.Y, C.G

    ph = ExitStack()
    sb = lambda n, s, d: C.sb(ph, n, s, d)
    accT = [sb(f"o_accT{i}", [128, TOK], BF16) for i in range(8)]
    scl = sb("o_scl", [128, 5 * 32], F32)
    kv_all = sb("w_kv", [128, 8192], BF16)
    qb = [sb(f"w_qb{i}", [128, 256], BF16) for i in range(2)]
    wb_all = sb("w_wball", [128, 4096], BF16)
    wscr = sb("w_scr", [128, 4096], BF16)
    sc2 = [sb(f"w_sc{i}", [128, 1024], F32) for i in range(2)]
    P2 = [sb(f"w_P{i}", [128, 1024], BF16) for i in range(2)]
    scat4 = [sb(f"w_scat{i}", [128, 1024], BF16) for i in range(4)]
    sums = sb("w_sums", [128, 16], F32)
    rec = sb("w_rec", [128, 16], F32)
    ov = sb("w_ov", [128, 256], F32)
    t1 = wscr[:].bitcast(F32)[:, 0:256]

    gsn = lambda n: C.gs("o_" + n)
    vd = C.gs("a_vd")
    wls = gsn("wls")
    a1l = [gsn("a1l0"), gsn("a1l1")]
    trp, evc = C.gs("a_trp"), C.gs("a_evc")
    wol = gsn("wol")
    wmm, wev = gsn("wmm"), gsn("wev")
    w12l = [gsn(f"w12l{i}") for i in range(2)]
    whl = [gsn(f"whl{i}") for i in range(2)]
    w3l = [gsn(f"w3l{i}") for i in range(2)]
    amm, sil, bmm, uev = gsn("amm"), gsn("sil"), gsn("bmm"), gsn("uev")
    hmm, hev = gsn("hmm"), gsn("hev")
    h2m, h2e = gsn("h2m"), gsn("h2e")
    w3m, w3e = gsn("w3m"), gsn("w3e")
    xld = [C.gs("a_xl0"), C.gs("a_xl1")]
    fint, finv, sto = gsn("fint"), gsn("finv"), gsn("sto")
    kvl, qbl, wbl = gsn("kvl"), gsn("qbl"), gsn("wbl")
    scr_d, exq, wind = gsn("w_scr_d"), gsn("w_exq"), gsn("w_wind")
    scw, scd, ccs = gsn("w_scw"), gsn("w_scd"), gsn("w_ccs")

    def sy_loads(sy):
        wait_gate(sy, gate)
        for k, t in enumerate(["w1s", "w2s", "h1s", "h2s"]):
            wls.inc(sy.dma_start(out=scl[:, 32 * k:32 * (k + 1)], in_=io[t]), 16)
        wls.inc(sy.dma_start(out=scl[:, 128:136], in_=io["w3s"]), 16)
    SY(sy_loads)

    psC = ExitStack()
    ps_a2 = [C.psum(psC, f"o_ps_a{i}", [128, 512], F32) for i in range(2)]
    ps_b2 = [C.psum(psC, f"o_ps_b{i}", [128, 512], F32) for i in range(2)]
    u_sb = [C.sb(psC, f"o_u{i}", [128, 512], BF16) for i in range(32)]
    v_sb = [C.sb(psC, f"o_v{i}", [128, 512], BF16) for i in range(32)]
    wbuf = [C.sb(psC, f"o_wbuf{i}", [128, 4096], BF16) for i in range(2)]
    silu2 = [C.sb(psC, f"o_silu{i}", [128, 512], BF16) for i in range(2)]

    qkv = io["qkv_scr"].ap()
    for qt in range(4):
        i = qt
        def sy_kv(sy, i=i):
            if i == 0:
                wait_gate(sy, gate)
            else:
                wind.wait(sy)
                sy.wait_ge(w3e.h, 8 * i)
            for a in range(NT):
                kvl.inc(sy.dma_start(out=kv_all[:, 512 * a:512 * (a + 1)],
                                     in_=qkv[128 * a:128 * (a + 1),
                                             1024 * i + 256:1024 * i + 768]), 16)
            for a in range(NT):
                wbl.inc(sy.dma_start(out=wb_all[:, 256 * a:256 * (a + 1)],
                                     in_=qkv[128 * a:128 * (a + 1),
                                             1024 * i + 768:1024 * (i + 1)]), 16)
        SY(sy_kv)

        def sy_qb(sy, i=i):
            for a in range(NT):
                A = 16 * i + a
                if A - 1 > 0:
                    sy.wait_ge(scr_d.h, A - 1)
                qbl.inc(sy.dma_start(out=qb[a % 2][:],
                                     in_=qkv[128 * a:128 * (a + 1),
                                             1024 * i:1024 * i + 256]), 16)
        SY(sy_qb)

        amm_marks = []
        sil0, uev0 = sil.n, uev.n
        for g in range(32):
            wi = g % 2

            def g_w12(g_, qt=qt, g=g, wi=wi):
                if qt == 0 and g < 2:
                    wait_gate(g_, gate)
                if g < 2:
                    w3m.wait(g_)
                bmm.wait(g_, lag=8)
                if g < 2:
                    for h in range(2):
                        w12l[wi].inc(g_.dma_start(out=wbuf[wi][:, 1024 * h:1024 * (h + 1)],
                                                  in_=io["w12T"][g, :, 1024 * h:1024 * (h + 1)]), 16)
                else:
                    w12l[wi].inc(g_.dma_start(out=wbuf[wi][:, 0:2048], in_=io["w12T"][g]), 16)
            G(g_w12)

            def t_w1(tt, qt=qt, g=g, wi=wi, sil0=sil0):
                w12l[wi].wait(tt)
                if sil0 + g - 1 > 0:
                    tt.wait_ge(sil.h, sil0 + g - 1)
                w3e.wait(tt)
                for ci in range(8):
                    amm.inc(tt.matmul(ps_a2[wi][:, :], wbuf[wi][:, 128 * ci:128 * (ci + 1)],
                                      xn1T[ci][:, 512 * qt:512 * (qt + 1)],
                                      start=(ci == 0), stop=(ci == 7)))
            T(t_w1)
            amm_marks.append(amm.n)

            def t_w2(tt, qt=qt, g=g, wi=wi, sil0=sil0, uev0=uev0):
                if sil0 + g > 0:
                    tt.wait_ge(sil.h, sil0 + g)
                if uev0 + g - 1 > 0:
                    tt.wait_ge(uev.h, uev0 + g - 1)
                for ci in range(8):
                    bmm.inc(tt.matmul(ps_b2[wi][:, :],
                                      wbuf[wi][:, 1024 + 128 * ci:1024 + 128 * (ci + 1)],
                                      xn1T[ci][:, 512 * qt:512 * (qt + 1)],
                                      start=(ci == 0), stop=(ci == 7)))
            T(t_w2)

            def v_u(v, g=g, wi=wi, sil0=sil0):
                bmm.wait(v)
                v.wait_ge(sil.h, sil0 + g + 1)
                hmm.wait(v)
                uev.inc(v.scalar_tensor_tensor(u_sb[g][:], ps_b2[wi][:, :],
                                               scl[:, 32 + g:33 + g],
                                               silu2[wi][:], ALU.mult, ALU.mult))
            VE(v_u)

        def s_si_all(s, qt=qt, marks=tuple(amm_marks), uev0=uev0):
            if qt == 0:
                wls.wait(s)
            for g in range(32):
                s.wait_ge(amm.h, marks[g])
                if uev0 + g - 1 > 0:
                    s.wait_ge(uev.h, uev0 + g - 1)
                sil.inc(s.activation(silu2[g % 2][:], ps_a2[g % 2][:, :], AF.Silu,
                                     scale=scl[:, g:g + 1]))
        SC(s_si_all)

        for a in range(NT):
            g4, al = a // 4, a % 4

            def v_sc(v, i=i, a=a, g4=g4, al=al):
                A = 16 * i + a
                b = g4 % 2
                kvl.wait(v)
                v.wait_ge(qbl.h, 16 * (A + 1))
                if al == 0 and 4 * i + g4 - 1 > 0:
                    v.wait_ge(exq.h, 4 * i + g4 - 1)
                q_ap = qb[a % 2][:].rearrange("p (u d) -> p u d", d=16) \
                    .unsqueeze(1).broadcast_to((128, 16, 16, 16))
                k_ap = kv_all[:].rearrange("p (ap c) -> p ap c", c=512)[:, :, 0:256] \
                    .rearrange("p ap (u d) -> p ap u d", d=16)
                o_ap = wscr[:].rearrange("p (ap u d) -> p ap u d", u=16, d=16)
                vd.inc(v.tensor_tensor(o_ap, q_ap, k_ap, ALU.mult))
                vd.wait(v)
                scr_d.inc(v.tensor_reduce(sc2[b][:, 256 * al:256 * (al + 1)],
                                          wscr[:].rearrange("p (au d) -> p au d", d=16),
                                          AX.X, ALU.add))
            VE(v_sc)

        hmm_marks = []
        hev0, h2m0 = hev.n, h2m.n
        for g in range(32):
            wi = g % 2

            def g_wh(g_, g=g, wi=wi):
                if g < 2:
                    bmm.wait(g_)
                hmm.wait(g_, lag=32)
                if g < 2:
                    for h in range(2):
                        whl[wi].inc(g_.dma_start(out=wbuf[wi][:, 2048 * h:2048 * (h + 1)],
                                                 in_=io["h1T"][g, :, 2048 * h:2048 * (h + 1)]), 16)
                else:
                    whl[wi].inc(g_.dma_start(out=wbuf[wi][:], in_=io["h1T"][g]), 16)
            G(g_wh)

            def t_h1(tt, g=g, wi=wi, hev0=hev0):
                whl[wi].wait(tt)
                uev.wait(tt)
                if hev0 + g - 1 > 0:
                    tt.wait_ge(hev.h, hev0 + g - 1)
                for ci in range(32):
                    hmm.inc(tt.matmul(ps_a2[wi][:, :], wbuf[wi][:, 128 * ci:128 * (ci + 1)],
                                      u_sb[ci][:, :], start=(ci == 0), stop=(ci == 31)))
            T(t_h1)
            hmm_marks.append(hmm.n)

        def s_g1_all(s, marks=tuple(hmm_marks), h2m0=h2m0):
            if h2m0 > 0:
                s.wait_ge(h2m.h, h2m0)
            for g in range(32):
                s.wait_ge(hmm.h, marks[g])
                hev.inc(s.activation(v_sb[g][:], ps_a2[g % 2][:, :], AF.Gelu,
                                     scale=scl[:, 64 + g:65 + g]))
        SC(s_g1_all)

        def s_expw01(s, i=i):
            for g4 in range(2):
                b = g4 % 2
                s.wait_ge(scr_d.h, 16 * i + 4 * (g4 + 1))
                t = 16 * i + 4 * g4 - 4
                if t > 0:
                    s.wait_ge(wind.h, t)
                exq.inc(s.activation(P2[b][:], sc2[b][:], AF.Exp, scale=0.25))
        SC(s_expw01)

        for a in range(NT):
            g4, al = a // 4, a % 4

            def v_av(v, i=i, a=a, g4=g4, al=al):
                A = 16 * i + a
                Gw = 4 * i + g4
                b = g4 % 2
                v.wait_ge(exq.h, 4 * i + g4 + 1)
                v.wait_ge(wbl.h, 256 * (i + 1))
                if al == 0 and 16 * (Gw - 3) > 0:
                    v.wait_ge(scw.h, 16 * (Gw - 3))
                pv = P2[b][:, 256 * al:256 * (al + 1)].rearrange("p (ap u) -> p u ap", ap=16)
                vd.inc(v.tensor_reduce(sums[:], pv, AX.X, ALU.add))
                vd.wait(v)
                vd.inc(v.reciprocal(rec[:], sums[:]))
                vd.wait(v)
                p_ap = pv.unsqueeze(2).broadcast_to((128, 16, 16, 16))
                v_ap = kv_all[:].rearrange("p (ap c) -> p ap c", c=512)[:, :, 256:512] \
                    .rearrange("p ap (u d) -> p u d ap", d=16)
                t_ap = wscr[:].rearrange("p (u d ap) -> p u d ap", d=16, ap=16)
                vd.inc(v.tensor_tensor(t_ap, p_ap, v_ap, ALU.mult))
                vd.wait(v)
                vd.inc(v.tensor_reduce(ov[:], wscr[:].rearrange("p (ud ap) -> p ud ap", ap=16),
                                       AX.X, ALU.add))
                vd.wait(v)
                r_ap = rec[:].unsqueeze(2).broadcast_to((128, 16, 16))
                vd.inc(v.tensor_tensor(t1.rearrange("p (u d) -> p u d", d=16),
                                       ov[:].rearrange("p (u d) -> p u d", d=16), r_ap, ALU.mult))
                vd.wait(v)
                wind.inc(v.tensor_tensor(scat4[g4][:, 256 * al:256 * (al + 1)], t1,
                                         wb_all[:, 256 * a:256 * (a + 1)], ALU.mult))
            VE(v_av)

        h2m_marks = []
        h2e0, w3m0 = h2e.n, w3m.n
        for g in range(32):
            wi = g % 2

            def g_wh2(g_, g=g, wi=wi):
                if g < 2:
                    hmm.wait(g_)
                h2m.wait(g_, lag=32)
                if g < 2:
                    for h in range(2):
                        whl[wi].inc(g_.dma_start(out=wbuf[wi][:, 2048 * h:2048 * (h + 1)],
                                                 in_=io["h2T"][g, :, 2048 * h:2048 * (h + 1)]), 16)
                else:
                    whl[wi].inc(g_.dma_start(out=wbuf[wi][:], in_=io["h2T"][g]), 16)
            G(g_wh2)

            def t_h2(tt, g=g, wi=wi, h2e0=h2e0):
                whl[wi].wait(tt)
                hev.wait(tt)
                if h2e0 + g - 1 > 0:
                    tt.wait_ge(h2e.h, h2e0 + g - 1)
                for ci in range(32):
                    h2m.inc(tt.matmul(ps_b2[wi][:, :], wbuf[wi][:, 128 * ci:128 * (ci + 1)],
                                      v_sb[ci][:, :], start=(ci == 0), stop=(ci == 31)))
            T(t_h2)
            h2m_marks.append(h2m.n)

        def s_g2_all(s, marks=tuple(h2m_marks), w3m0=w3m0):
            if w3m0 > 0:
                s.wait_ge(w3m.h, w3m0)
            for g in range(32):
                s.wait_ge(h2m.h, marks[g])
                h2e.inc(s.activation(u_sb[g][:], ps_b2[g % 2][:, :], AF.Gelu,
                                     scale=scl[:, 96 + g:97 + g]))
        SC(s_g2_all)

        def s_expw23(s, i=i):
            for g4 in range(2, 4):
                b = g4 % 2
                s.wait_ge(scr_d.h, 16 * i + 4 * (g4 + 1))
                t = 16 * i + 4 * g4 - 4
                if t > 0:
                    s.wait_ge(wind.h, t)
                exq.inc(s.activation(P2[b][:], sc2[b][:], AF.Exp, scale=0.25))
        SC(s_expw23)

        w3e0 = w3e.n
        for oc in range(8):
            wi = oc % 2

            def g_w3(g_, oc=oc, wi=wi):
                if oc < 2:
                    h2m.wait(g_)
                w3m.wait(g_, lag=32)
                if oc < 2:
                    for h in range(2):
                        w3l[wi].inc(g_.dma_start(out=wbuf[wi][:, 2048 * h:2048 * (h + 1)],
                                                 in_=io["w3T"][oc, :, 2048 * h:2048 * (h + 1)]), 16)
                else:
                    w3l[wi].inc(g_.dma_start(out=wbuf[wi][:], in_=io["w3T"][oc]), 16)
            G(g_w3)

            def t_w3(tt, oc=oc, wi=wi, w3e0=w3e0):
                w3l[wi].wait(tt)
                h2e.wait(tt)
                if w3e0 + oc - 1 > 0:
                    tt.wait_ge(w3e.h, w3e0 + oc - 1)
                for ci in range(32):
                    w3m.inc(tt.matmul(ps_a2[wi][:, :], wbuf[wi][:, 128 * ci:128 * (ci + 1)],
                                      u_sb[ci][:, :], start=(ci == 0), stop=(ci == 31)))
            T(t_w3)

            def v_w3(v, qt=qt, oc=oc, wi=wi):
                w3m.wait(v)
                w3e.inc(v.tensor_scalar(accT[oc][:, 512 * qt:512 * (qt + 1)],
                                        ps_a2[wi][:, :], scl[:, 128 + oc:129 + oc],
                                        None, ALU.mult))
            VE(v_w3)

        for g4 in range(4):
            def sy_sc(sy, i=i, g4=g4):
                Gw = 4 * i + g4
                sy.wait_ge(wind.h, 16 * i + 4 * (g4 + 1))
                if 64 * (Gw - 3) > 0:
                    sy.wait_ge(scd.h, 64 * (Gw - 3))
                scw.inc(sy.dma_start(out=io["scr"].ap()[:, 1024 * g4:1024 * (g4 + 1)],
                                     in_=scat4[g4][:]), 16)
                sy.wait_ge(scw.h, 16 * (Gw + 1))
                for al in range(4):
                    a = 4 * g4 + al
                    blk = 2 * i + (a % 4) // 2
                    rowb = ((a // 4) * 2 + (a % 2)) * 32
                    dst = io["a2a_in"].ap()[256 * blk + rowb:256 * blk + rowb + 32, :] \
                        .rearrange("t (z w) -> t z w", z=4)
                    srcv = io["scr"].ap()[:, 256 * a:256 * (a + 1)].rearrange("(t z) w -> t z w", z=4)
                    scd.inc(sy.dma_start(out=dst, in_=srcv), 16)
            SY(sy_sc)

    def g_cc(g):
        scd.wait(g)
        with allow_ici_all_to_all():
            ccs.inc(g.collective_compute(
                "AllToAll", mybir.AluOpType.bypass,
                replica_groups=[list(range(NCORES))],
                ins=[io["a2a_in"].ap().opt()],
                outs=[io["a2a_out"].ap().opt()],
            ))
    G(g_cc)

    psC.close()
    esA = ExitStack()
    a1T = [C.sb(esA, f"o_a1T{i}", [128, TOK], BF16) for i in range(8)]
    abuf = [C.sb(esA, f"o_abuf{i}", [128, E], BF16) for i in range(2)]
    wo_sb = [C.sb(esA, f"o_wo{i}", [128, E], BF16) for i in range(2)]
    psA = ExitStack()
    ps_tr = C.psum(psA, "o_ps_tr", [128, 1024], BF16)

    for a in range(NT):
        bi = a % 2

        def sy_a1(sy, a=a, bi=bi):
            if a < 2:
                ccs.wait(sy)
                w3e.wait(sy)
            trp.wait(sy, lag=8)
            for al in range(2):
                for gm in range(2):
                    blk = 2 * (a // 4) + gm
                    rowb = ((a % 4) * 2 + al) * 32
                    a1l[bi].inc(sy.dma_start(
                        out=abuf[bi][64 * al + 32 * gm:64 * al + 32 * gm + 32, :],
                        in_=io["a2a_out"].ap()[256 * blk + rowb:256 * blk + rowb + 32, :]), 16)
        SY(sy_a1)

        def t_tr(tt, a=a, bi=bi):
            a1l[bi].wait(tt)
            evc.wait(tt)
            for ct in range(8):
                trp.inc(tt.transpose(ps_tr[:, 128 * ct:128 * (ct + 1)],
                                     abuf[bi][:, 128 * ct:128 * (ct + 1)], C.ident_bf[:]))
        T(t_tr)

        def v_ev(v, a=a):
            trp.wait(v)
            for ct in range(8):
                evc.inc(v.tensor_copy(a1T[ct][:, 128 * a:128 * (a + 1)],
                                      ps_tr[:, 128 * ct:128 * (ct + 1)]))
        VE(v_ev)

    psB = ExitStack()
    ps_m2 = [C.psum(psB, f"o_ps_m{i}", [128, 512], F32) for i in range(2)]

    for qtr in range(4):
        def g_wo(g, qtr=qtr):
            if qtr == 0:
                ccs.wait(g)
            if qtr > 0:
                wmm.wait(g)
            for k in range(2):
                ci = 2 * qtr + k
                wol.inc(g.dma_start(out=wo_sb[k][:], in_=io["w_out"][128 * ci:128 * (ci + 1), :]), 16)
        G(g_wo)

        for oc in range(8):
            for c4 in range(4):
                def t_wo(tt, qtr=qtr, oc=oc, c4=c4):
                    if qtr == 0 and oc == 0 and c4 == 0:
                        evc.wait(tt)
                    wol.wait(tt)
                    wev.wait(tt, lag=1)
                    mi = (qtr * 32 + oc * 4 + c4) % 2
                    for k in range(2):
                        ci = 2 * qtr + k
                        wmm.inc(tt.matmul(ps_m2[mi][:, :], wo_sb[k][:, 128 * oc:128 * (oc + 1)],
                                          a1T[ci][:, 512 * c4:512 * (c4 + 1)],
                                          start=(k == 0), stop=(k == 1)))
                T(t_wo)

                def v_wo(v, qtr=qtr, oc=oc, c4=c4):
                    wmm.wait(v)
                    if qtr == 0 and oc == 0 and c4 == 0:
                        w3e.wait(v)
                    mi = (qtr * 32 + oc * 4 + c4) % 2
                    wev.inc(v.tensor_tensor(accT[oc][:, 512 * c4:512 * (c4 + 1)],
                                            accT[oc][:, 512 * c4:512 * (c4 + 1)],
                                            ps_m2[mi][:, :], ALU.add))
                VE(v_wo)

    psB.close()
    psA.close()
    esA.close()
    psF = ExitStack()
    ps_f = C.psum(psF, "o_ps_f", [128, 128], BF16)
    xbuf = [C.sb(psF, f"o_xbuf{i}", [128, E], F32) for i in range(2)]
    x1b = [C.sb(psF, f"o_x1b{i}", [128, E], F32) for i in range(2)]

    for a in range(NT):
        bi = a % 2

        def sy_xa(sy, a=a, bi=bi):
            if a < 2:
                w3e.wait(sy)
                wev.wait(sy)
            finv.wait(sy, lag=8)
            xld[bi].inc(sy.dma_start(out=xbuf[bi][:],
                                     in_=io["x1_scr"].ap()[128 * a:128 * (a + 1), :]), 16)
        SY(sy_xa)

        for ct in range(8):
            def t_f(tt, a=a, ct=ct):
                if a == 0 and ct == 0:
                    w3e.wait(tt)
                    wev.wait(tt)
                finv.wait(tt)
                fint.inc(tt.transpose(ps_f[:, :], accT[ct][:, 128 * a:128 * (a + 1)],
                                      C.ident_bf[:]))
            T(t_f)

            def v_f(v, a=a, bi=bi, ct=ct):
                fint.wait(v)
                xld[bi].wait(v)
                if ct == 0:
                    sto.wait(v, lag=16)
                finv.inc(v.tensor_tensor(x1b[bi][:, 128 * ct:128 * (ct + 1)],
                                         xbuf[bi][:, 128 * ct:128 * (ct + 1)],
                                         ps_f[:, :], ALU.add))
            VE(v_f)

        def sy_o(sy, a=a, bi=bi):
            finv.wait(sy)
            sto.inc(sy.dma_start(out=io["out"][128 * a:128 * (a + 1), :], in_=x1b[bi][:]), 16)
        SY(sy_o)

    def sy_fin(sy):
        sto.wait(sy)
    SY(sy_fin)
    psF.close()
    ph.close()



def build_mega():
    nc = bass.Bass("TRN2", target_bir_lowering=False, detect_race_conditions=False)
    dt = lambda n, s, d: nc.dram_tensor(n, s, d, kind="ExternalInput").ap()
    io = dict(
        x_own=dt("x_own", [TOK, E], F32),
        wqa=dt("wqa", [E, QL], F32),
        wqb=dt("wqb", [QL, E], F32),
        wkva=dt("wkva", [E, KVL], F32),
        wkvb_kv=dt("wkvb_kv", [KVL, 512], F32),
        ropek_cos=dt("ropek_cos", [S, 32], F32),
        ropek_sin=dt("ropek_sin", [S, 32], F32),
        ropeq_cos=dt("ropeq_cos", [TOK, 32], F32),
        ropeq_sin=dt("ropeq_sin", [TOK, 32], F32),
        mask0=dt("mask0", [128, 128], F32),
        mask1=dt("mask1", [128, 128], F32),
        ident=dt("ident", [128, 128], F32),
        wout=dt("wout", [E, E], BF16),
        bout=dt("bout", [128, 8], F32),
        routerT=dt("routerT", [E, NEXP], F32),
        gup_w=dt("gup_w", [NEXP, 8, 128, 2048], BF16),
        dn_w=dt("dn_w", [NEXP, E, E], BF16),
        sgsu=dt("sgsu", [16, 128, 2048], BF16),
        sdwT=dt("sdwT", [SHD, E], BF16),
        onehot=dt("onehot", [NEXP, 2048], F32),
        wqkv=dt("wqkv", [E, 4 * E], BF16),
        w_out=dt("w_out", [E, E], BF16),
        w12T=dt("w12T", [32, 128, 2048], BF16),
        h1T=dt("h1T", [32, 128, 4096], BF16),
        h2T=dt("h2T", [32, 128, 4096], BF16),
        w3T=dt("w3T", [8, 128, 4096], BF16),
        w1s=dt("w1s", [128, 32], F32),
        w2s=dt("w2s", [128, 32], F32),
        h1s=dt("h1s", [128, 32], F32),
        h2s=dt("h2s", [128, 32], F32),
        w3s=dt("w3s", [128, 8], F32),
        out=nc.dram_tensor("out", [TOK, E], F32, kind="ExternalOutput").ap(),
        x1_scr=nc.dram_tensor("x1_scr", [TOK, E], F32),
        qkv_scr=nc.dram_tensor("qkv_scr", [TOK, 4 * E], BF16),
        scr=nc.dram_tensor("scr", [128, 4096], BF16),
        a2a_in=nc.dram_tensor("a2a_in", [TOK, E], BF16),
        a2a_out=nc.dram_tensor("a2a_out", [TOK, E], BF16),
        lat_in=nc.dram_tensor("lat_in", [KVL, TOK], BF16),
        lat_out=nc.dram_tensor("lat_out", [2 * KVL, TOK], BF16),
    )

    es = ExitStack()
    with es:
        block = es.enter_context(nc.Block())
        C = Ctx(nc, es, block)
        C.ident_bf = es.enter_context(nc.sbuf_tensor("ident_bf", [128, 128], BF16))
        C.ident_f32 = es.enter_context(nc.sbuf_tensor("ident_f32", [128, 128], F32))

        es_xn = ExitStack()
        xnT = [es_xn.enter_context(nc.sbuf_tensor(f"xnT{i}", [128, TOK], BF16)) for i in range(8)]
        es_at = ExitStack()
        attnT = [es_at.enter_context(nc.sbuf_tensor(f"attnT{i}", [128, TOK], BF16)) for i in range(8)]
        g1 = emit_l0attn(C, io, xnT, attnT)
        es_h = ExitStack()
        io["h_sb"] = [es_h.enter_context(nc.sbuf_tensor(f"h{i}", [128, TOK], BF16)) for i in range(8)]
        g2 = emit_l0ffn(C, io, xnT, attnT, es_at, g1)
        es_h.close()
        es_at.close()
        es_xn.close()

        es_xn1 = ExitStack()
        xn1T = [es_xn1.enter_context(nc.sbuf_tensor(f"xn1T{i}", [128, TOK], BF16)) for i in range(8)]
        g3 = emit_l1qkv(C, io, xn1T, g2)
        emit_l1fused(C, io, xn1T, g3)
        es_xn1.close()
    return nc



def _rope_tables():
    inv = 10000.0 ** (-np.arange(0, D, 2, dtype=np.float64) / D)
    pos = np.arange(S, dtype=np.float64)
    ang = np.outer(pos, inv)
    return np.cos(ang).astype(np.float32), np.sin(ang).astype(np.float32)


def _own_rows(core):
    b, h = core // 2, core % 2
    a = np.arange(NT)
    starts = b * S + 256 * a + 128 * h
    return (starts[:, None] + np.arange(128)[None, :]).reshape(-1)


def _inmaps(inputs):
    x2d = np.asarray(inputs["L0_attn_x"]).reshape(B * S, E)
    cosf, sinf = _rope_tables()
    ident = np.eye(128, dtype=np.float32)
    tri = np.triu(np.ones((128, 128), dtype=np.float32))
    ones = np.ones((128, 128), dtype=np.float32)
    zeros = np.zeros((128, 128), dtype=np.float32)
    wkvb = np.asarray(inputs["L0_attn_Wkvb"])
    wkvb_kv = np.ascontiguousarray(np.concatenate([wkvb[:, 0::2], wkvb[:, 1::2]], axis=1))
    onehot = np.zeros((NEXP, 2048), dtype=np.float32)
    for e in range(NEXP):
        onehot[e, 128 * e:128 * (e + 1)] = 1.0
    rs = lambda s: np.ascontiguousarray(np.asarray(s).reshape(-1, 128).T)
    import ml_dtypes
    bf = lambda a: np.ascontiguousarray(a).astype(ml_dtypes.bfloat16)
    common = dict(
        wqa=np.asarray(inputs["L0_attn_Wqa"]),
        wqb=np.asarray(inputs["L0_attn_Wqb"]),
        wkva=np.asarray(inputs["L0_attn_Wkva"]),
        wkvb_kv=wkvb_kv,
        ropek_cos=cosf, ropek_sin=sinf,
        ident=ident,
        wout=bf(np.asarray(inputs["L0_attn_Wout"])),
        bout=np.ascontiguousarray(np.asarray(inputs["L0_attn_bout"]).reshape(8, 128).T),
        routerT=np.ascontiguousarray(np.asarray(inputs["L0_ffn_router_w"]).T),
        gup_w=bf(np.asarray(inputs["L0_ffn_gate_up_w"])
                 .reshape(NEXP, 8, 128, 16, 128).transpose(0, 3, 2, 1, 4)
                 .reshape(NEXP, 2, 8, 128, 1024).transpose(0, 2, 3, 1, 4)
                 .reshape(NEXP, 8, 128, 2048)),
        dn_w=bf(np.asarray(inputs["L0_ffn_down_w"])),
        sgsu=bf(np.concatenate([
            np.asarray(inputs["L0_ffn_shared_gate_w"]).reshape(16, 128, 8, 128).transpose(0, 3, 2, 1).reshape(16, 128, 1024),
            np.asarray(inputs["L0_ffn_shared_up_w"]).reshape(16, 128, 8, 128).transpose(0, 3, 2, 1).reshape(16, 128, 1024),
        ], axis=2)),
        sdwT=bf(np.asarray(inputs["L0_ffn_shared_down_w"]).T),
        onehot=onehot,
        wqkv=bf(np.asarray(inputs["L1_attn_w_qkvw"])),
        w_out=bf(np.asarray(inputs["L1_attn_w_out"])),
        w12T=bf(np.concatenate([
            np.asarray(inputs["L1_ffn_w1_w"]).reshape(32, 128, 8, 128).transpose(0, 3, 2, 1).reshape(32, 128, 1024),
            np.asarray(inputs["L1_ffn_w2_w"]).reshape(32, 128, 8, 128).transpose(0, 3, 2, 1).reshape(32, 128, 1024),
        ], axis=2)),
        h1T=bf(np.asarray(inputs["L1_ffn_h1_w"])
               .reshape(32, 128, 32, 128).transpose(0, 3, 2, 1).reshape(32, 128, 4096)),
        h2T=bf(np.asarray(inputs["L1_ffn_h2_w"])
               .reshape(32, 128, 32, 128).transpose(0, 3, 2, 1).reshape(32, 128, 4096)),
        w3T=bf(np.asarray(inputs["L1_ffn_w3_w"])
               .reshape(8, 128, 32, 128).transpose(0, 3, 2, 1).reshape(8, 128, 4096)),
        w1s=rs(inputs["L1_ffn_w1_s"]), w2s=rs(inputs["L1_ffn_w2_s"]),
        h1s=rs(inputs["L1_ffn_h1_s"]), h2s=rs(inputs["L1_ffn_h2_s"]),
        w3s=rs(inputs["L1_ffn_w3_s"]),
    )
    maps = []
    for c in range(NCORES):
        b, h = c // 2, c % 2
        rows = _own_rows(c)
        m = dict(common)
        m.update(
            x_own=np.ascontiguousarray(x2d[rows]),
            ropeq_cos=np.ascontiguousarray(cosf[rows - b * S]),
            ropeq_sin=np.ascontiguousarray(sinf[rows - b * S]),
            mask0=(tri if h == 0 else ones),
            mask1=(zeros if h == 0 else tri),
        )
        maps.append(m)
    return maps


def kernel(**inputs):
    cores = list(range(NCORES))
    nc = build_mega()
    res = run_bass_kernel_spmd(nc, _inmaps(inputs), core_ids=cores)
    out = np.empty((B * S, E), dtype=np.float32)
    for c in cores:
        out[_own_rows(c)] = res.results[c]["out"]
    return out.reshape(B, S, E)


# version 28
# speedup vs baseline: 1.0400x; 1.0210x over previous
import numpy as np
from contextlib import ExitStack

import concourse.bass as bass
import concourse.mybir as mybir
from concourse.bass_utils import run_bass_kernel_spmd
from concourse.aot_env import allow_ici_all_to_all

F32 = mybir.dt.float32
BF16 = mybir.dt.bfloat16
AF = mybir.ActivationFunctionType
ALU = mybir.AluOpType
AX = mybir.AxisListType

B, S, E = 4, 4096, 1024
NCORES = 8
NT = 16
TOK = 2048
H, KVH, D = 16, 4, 64
QL, KVL = 64, 32
NEXP = 16
SHD = 2048
DHID = 4096
LN_EPS = 1e-5


class Sem:
    def __init__(self, nc, es, name):
        self.h = es.enter_context(nc.semaphore(name))
        self.n = 0

    def inc(self, inst, k=1):
        inst.then_inc(self.h, k)
        self.n += k
        return inst

    def wait(self, eng, lag=0):
        v = self.n - lag
        if v > 0:
            eng.wait_ge(self.h, v)

    def mark(self):
        return (self.h, self.n)


class Ctx:
    def __init__(self, nc, es, block):
        self.nc, self.es, self.block = nc, es, block
        self.T, self.V, self.S, self.Y, self.G = (
            block.tensor, block.vector, block.scalar, block.sync, block.gpsimd)
        self._sems = {}

    def gs(self, name):
        if name not in self._sems:
            self._sems[name] = Sem(self.nc, self.es, name)
        return self._sems[name]

    def sb(self, stack, name, shape, dt):
        return stack.enter_context(self.nc.sbuf_tensor(name, shape, dt))

    def psum(self, stack, name, shape, dt):
        return stack.enter_context(self.nc.psum_tensor(name, shape, dt))


def wait_gate(eng, gate):
    for (h, v) in gate:
        if v > 0:
            eng.wait_ge(h, v)



def emit_l0attn(C, io, xnT, attnT):
    nc, es = C.nc, C.es
    T, VE, SC, SY, G = C.T, C.V, C.S, C.Y, C.G
    x_own = io["x_own"]

    ph = ExitStack()
    sb = lambda n, s, d: C.sb(ph, n, s, d)
    KT = [sb(f"KT{i}", [128, S], BF16) for i in range(2)]
    Vt = sb("Vt", [128, 32 * 260], BF16)
    QT = [sb(f"QT{i}", [128, TOK], BF16) for i in range(8)]
    latT = sb("latT", [KVL, S], BF16)
    qlatT = sb("qlatT", [QL, TOK], BF16)
    wqa_sb = sb("wqa_sb", [128, 8 * QL], BF16)
    wqb_sb = sb("wqb_sb", [QL, E], BF16)
    wkva_sb = sb("wkva_sb", [128, 8 * KVL], BF16)
    wkvb_sb = sb("wkvb_sb", [KVL, 512], BF16)
    mask_sb = [sb(f"mask{i}_sb", [128, 128], BF16) for i in range(2)]
    ones64 = sb("ones64", [1, 64], F32)
    ropek_c = sb("ropek_c", [128, 32 * 32], F32)
    ropek_s = sb("ropek_s", [128, 32 * 32], F32)
    ropeq_c = sb("ropeq_c", [128, 16 * 32], F32)
    ropeq_s = sb("ropeq_s", [128, 16 * 32], F32)
    xbuf = [sb(f"xbuf{i}", [128, E], F32) for i in range(2)]
    xsq = [sb(f"xsq{i}", [128, E], F32) for i in range(2)]
    stat2 = [sb(f"stat{i}", [128, 8], F32) for i in range(2)]
    xn_tmp = [sb(f"xn_tmp{i}", [128, E], BF16) for i in range(2)]
    lat_own = sb("lat_own", [KVL, TOK], BF16)
    kq_tmp = [sb(f"kq_tmp{i}", [128, E], BF16) for i in range(2)]
    rtA = sb("rtA", [128, 512], F32)
    rtB = sb("rtB", [128, 512], F32)
    expS = [sb(f"expS{i}", [128, 512], BF16) for i in range(4)]
    recb = [sb(f"recb{i}", [64, 512], F32) for i in range(2)]
    rrow = [sb(f"rrow{i}", [1, 512], F32) for i in range(2)]

    gsn = lambda n: C.gs("a_" + n)
    vd = gsn("vd")
    wl, wls = gsn("wl"), gsn("wls")
    xl = [gsn("xl0"), gsn("xl1")]
    lnf, trp, evc = gsn("lnf"), gsn("trp"), gsn("evc")
    lns, sds, lnx = gsn("lns"), gsn("sds"), gsn("lnx")
    latm, late = gsn("latm"), gsn("late")
    lsto, agd, latl = gsn("lsto"), gsn("agd"), gsn("latl")
    kvm, roped, kqt, kqe = gsn("kvm"), gsn("roped"), gsn("kqt"), gsn("kqe")
    qlm, qle, qmm = gsn("qlm"), gsn("qle"), gsn("qmm")
    scm, exd, mkd, avd = gsn("scm"), gsn("exd"), gsn("mkd"), gsn("avd")
    nr1, nr2, nr3 = gsn("nr1"), gsn("nr2"), gsn("nr3")

    def g_loads(g):
        wl.inc(g.dma_start(out=wqa_sb[:], in_=io["wqa"].rearrange("(c p) n -> p c n", p=128)), 16)
        wl.inc(g.dma_start(out=wqb_sb[:], in_=io["wqb"]), 16)
        wl.inc(g.dma_start(out=wkva_sb[:], in_=io["wkva"].rearrange("(c p) n -> p c n", p=128)), 16)
        wl.inc(g.dma_start(out=wkvb_sb[:], in_=io["wkvb_kv"]), 16)
        wl.inc(g.dma_start(out=C.ident_bf[:], in_=io["ident"]), 16)
        wl.inc(g.dma_start(out=mask_sb[0][:], in_=io["mask0"]), 16)
        wl.inc(g.dma_start(out=mask_sb[1][:], in_=io["mask1"]), 16)
    G(g_loads)

    def sy_loads(sy):
        wls.inc(sy.dma_start(out=C.ident_f32[:], in_=io["ident"]), 16)
        wls.inc(sy.dma_start(out=ropek_c[:].rearrange("p (t f) -> p t f", f=32),
                             in_=io["ropek_cos"].rearrange("(t p) f -> p t f", p=128)), 16)
        wls.inc(sy.dma_start(out=ropek_s[:].rearrange("p (t f) -> p t f", f=32),
                             in_=io["ropek_sin"].rearrange("(t p) f -> p t f", p=128)), 16)
        wls.inc(sy.dma_start(out=ropeq_c[:].rearrange("p (t f) -> p t f", f=32),
                             in_=io["ropeq_cos"].rearrange("(t p) f -> p t f", p=128)), 16)
        wls.inc(sy.dma_start(out=ropeq_s[:].rearrange("p (t f) -> p t f", f=32),
                             in_=io["ropeq_sin"].rearrange("(t p) f -> p t f", p=128)), 16)
    SY(sy_loads)

    def v_init(v):
        wl.wait(v)
        wls.wait(v)
        v.memset(ones64[:], 1.0)
        lnf.inc(v.memset(
            Vt[:].rearrange("p (t z w) -> p t z w", z=4, w=65)[:, :, :, 64:65], 1.0))
    VE(v_init)

    def emit_ln_tile(src_ap, dst_bf_ap, ld_sem, bi, extra_war=None):
        stat = stat2[bi]
        xq = xsq[bi]

        def v_stats(v):
            ld_sem.wait(v)
            sds.wait(v, lag=1)
            v.tensor_tensor(xq[:], src_ap, src_ap, ALU.mult)
            v.tensor_reduce(stat[:, 0:1], src_ap, AX.X, ALU.add)
            vd.inc(v.tensor_reduce(stat[:, 1:2], xq[:], AX.X, ALU.add))
            vd.wait(v)
            vd.inc(v.tensor_scalar(stat[:, 2:3], stat[:, 0:1], 1.0 / E, None, ALU.mult))
            vd.wait(v)
            vd.inc(v.tensor_tensor(stat[:, 3:4], stat[:, 2:3], stat[:, 2:3], ALU.mult))
            v.tensor_scalar(stat[:, 4:5], stat[:, 1:2], 1.0 / E, LN_EPS, ALU.mult, ALU.add)
            vd.wait(v)
            lns.inc(v.tensor_tensor(stat[:, 4:5], stat[:, 4:5], stat[:, 3:4], ALU.subtract))
        VE(v_stats)

        def s_rstd(s):
            lns.wait(s)
            lnx.inc(s.activation(stat[:, 5:6], stat[:, 4:5], AF.Ln))
            lnx.wait(s)
            sds.inc(s.activation(stat[:, 6:7], stat[:, 5:6], AF.Exp, scale=-0.5))
        SC(s_rstd)

        def v_norm(v):
            sds.wait(v)
            if extra_war is not None:
                extra_war(v)
            lnf.inc(v.tensor_scalar(dst_bf_ap, src_ap, stat[:, 2:3], stat[:, 6:7],
                                    ALU.subtract, ALU.mult))
        VE(v_norm)

    psA = ExitStack()
    ps_tr = C.psum(psA, "ps_tr", [128, 1024], BF16)
    ps_lat = C.psum(psA, "ps_lat", [KVL, 128], F32)

    for a in range(NT):
        bi = a % 2

        def sy_x(sy, a=a, bi=bi):
            lnf.wait(sy, lag=1)
            xl[bi].inc(sy.dma_start(out=xbuf[bi][:], in_=x_own[128 * a:128 * (a + 1), :]), 16)
        SY(sy_x)

        emit_ln_tile(xbuf[bi][:], xn_tmp[bi][:], xl[bi], bi,
                     extra_war=(lambda v: trp.wait(v, lag=8)))

        def t_trans(tt, bi=bi):
            lnf.wait(tt)
            evc.wait(tt)
            for ct in range(8):
                trp.inc(tt.transpose(ps_tr[:, 128 * ct:128 * (ct + 1)],
                                     xn_tmp[bi][:, 128 * ct:128 * (ct + 1)], C.ident_bf[:]))
        T(t_trans)

        def v_evac(v, a=a):
            trp.wait(v)
            for ct in range(8):
                evc.inc(v.tensor_copy(xnT[ct][:, 128 * a:128 * (a + 1)],
                                      ps_tr[:, 128 * ct:128 * (ct + 1)]))
        VE(v_evac)

        def t_lat(tt, a=a):
            evc.wait(tt)
            late.wait(tt)
            for ct in range(8):
                latm.inc(tt.matmul(ps_lat[:, :], wkva_sb[:, KVL * ct:KVL * (ct + 1)],
                                   xnT[ct][:, 128 * a:128 * (a + 1)],
                                   start=(ct == 0), stop=(ct == 7)))
        T(t_lat)

        def v_lat(v, a=a):
            latm.wait(v)
            late.inc(v.tensor_copy(lat_own[:, 128 * a:128 * (a + 1)], ps_lat[:, :]))
        VE(v_lat)

    def sy_lat(sy):
        late.wait(sy)
        lsto.inc(sy.dma_start(out=io["lat_in"].ap(), in_=lat_own[:]), 16)
    SY(sy_lat)

    def g_ag(g):
        lsto.wait(g)
        with allow_ici_all_to_all():
            agd.inc(g.collective_compute(
                "AllGather", mybir.AluOpType.bypass,
                replica_groups=[[0, 1], [2, 3], [4, 5], [6, 7]],
                ins=[io["lat_in"].ap().opt()],
                outs=[io["lat_out"].ap().opt()],
            ))
    G(g_ag)

    def sy_latl(sy):
        agd.wait(sy)
        lo = io["lat_out"].ap()
        latl.inc(sy.dma_start(
            out=latT[:].rearrange("l (t two c) -> l t two c", two=2, c=128)[:, :, 0, :],
            in_=lo[0:KVL, :].rearrange("l (t c) -> l t c", c=128)), 16)
        latl.inc(sy.dma_start(
            out=latT[:].rearrange("l (t two c) -> l t two c", two=2, c=128)[:, :, 1, :],
            in_=lo[KVL:2 * KVL, :].rearrange("l (t c) -> l t c", c=128)), 16)
    SY(sy_latl)

    psA.close()
    psB = ExitStack()
    ps_ql = C.psum(psB, "ps_ql", [QL, 512], F32)
    ps_q = C.psum(psB, "ps_q", [128, 1024], F32)
    ps_qt = C.psum(psB, "ps_qt", [64, 256], BF16)

    for qc in range(4):
        def t_ql(tt, qc=qc):
            qle.wait(tt)
            for ct in range(8):
                qlm.inc(tt.matmul(ps_ql[:, :], wqa_sb[:, QL * ct:QL * (ct + 1)],
                                  xnT[ct][:, 512 * qc:512 * (qc + 1)],
                                  start=(ct == 0), stop=(ct == 7)))
        T(t_ql)

        def v_ql(v, qc=qc):
            qlm.wait(v)
            qle.inc(v.tensor_copy(qlatT[:, 512 * qc:512 * (qc + 1)], ps_ql[:, :]))
        VE(v_ql)

    for a in range(NT):
        bi = a % 2

        def t_q(tt, a=a):
            qle.wait(tt)
            roped.wait(tt)
            qmm.inc(tt.matmul(ps_q[:, 0:512], qlatT[:, 128 * a:128 * (a + 1)],
                              wqb_sb[:, 0:512], start=True, stop=True))
            qmm.inc(tt.matmul(ps_q[:, 512:1024], qlatT[:, 128 * a:128 * (a + 1)],
                              wqb_sb[:, 512:1024], start=True, stop=True))
        T(t_q)

        def v_ropeq(v, a=a, bi=bi):
            qmm.wait(v)
            kqt.wait(v, lag=16)
            e_ps = ps_q[:, :].rearrange("p (hh f2) -> p hh f2", hh=16)[:, :, 0::2]
            o_ps = ps_q[:, :].rearrange("p (hh f2) -> p hh f2", hh=16)[:, :, 1::2]
            cosb = ropeq_c[:, 32 * a:32 * (a + 1)].unsqueeze(1).broadcast_to((128, 16, 32))
            sinb = ropeq_s[:, 32 * a:32 * (a + 1)].unsqueeze(1).broadcast_to((128, 16, 32))
            rA = rtA[:, :].rearrange("p (hh f) -> p hh f", hh=16)
            rB = rtB[:, :].rearrange("p (hh f) -> p hh f", hh=16)
            qe = kq_tmp[bi][:, :].rearrange("p (hh f2) -> p hh f2", hh=16)[:, :, 0::2]
            qo = kq_tmp[bi][:, :].rearrange("p (hh f2) -> p hh f2", hh=16)[:, :, 1::2]
            v.tensor_tensor(rA, e_ps, cosb, ALU.mult)
            vd.inc(v.tensor_tensor(rB, o_ps, sinb, ALU.mult))
            vd.wait(v)
            vd.inc(v.tensor_tensor(qe, rA, rB, ALU.subtract))
            vd.wait(v)
            v.tensor_tensor(rA, e_ps, sinb, ALU.mult)
            vd.inc(v.tensor_tensor(rB, o_ps, cosb, ALU.mult))
            vd.wait(v)
            roped.inc(v.tensor_tensor(qo, rA, rB, ALU.add))
        VE(v_ropeq)

        for grp in range(8):
            def t_qt(tt, bi=bi, grp=grp):
                roped.wait(tt)
                kqe.wait(tt)
                h0, h1 = 2 * grp, 2 * grp + 1
                kqt.inc(tt.transpose(ps_qt[:, 0:128], kq_tmp[bi][:, 64 * h0:64 * (h0 + 1)],
                                     C.ident_bf[:]))
                kqt.inc(tt.transpose(ps_qt[:, 128:256], kq_tmp[bi][:, 64 * h1:64 * (h1 + 1)],
                                     C.ident_bf[:]))
            T(t_qt)

            def v_qte(v, a=a, grp=grp):
                kqt.wait(v)
                for k in range(2):
                    h_ = 2 * grp + k
                    p = (h_ % 4) + 4 * (h_ // 8)
                    up = (h_ // 4) % 2
                    kqe.inc(v.tensor_copy(QT[p][64 * up:64 * up + 64, 128 * a:128 * (a + 1)],
                                          ps_qt[:, 128 * k:128 * (k + 1)]))
            VE(v_qte)

    ps_kv = C.psum(psB, "ps_kv", [128, 512], F32)
    ps_kt = C.psum(psB, "ps_kt", [128, 256], BF16)

    for t in range(32):
        bi = t % 2

        def t_kv(tt, t=t):
            if t == 0:
                latl.wait(tt)
            roped.wait(tt)
            kvm.inc(tt.matmul(ps_kv[:, :], latT[:, 128 * t:128 * (t + 1)], wkvb_sb[:, :],
                              start=True, stop=True))
        T(t_kv)

        def v_ropek(v, t=t, bi=bi):
            kvm.wait(v)
            kqt.wait(v, lag=2)
            e_ps = ps_kv[:, 0:256].rearrange("p (k f2) -> p k f2", k=4)[:, :, 0::2]
            o_ps = ps_kv[:, 0:256].rearrange("p (k f2) -> p k f2", k=4)[:, :, 1::2]
            cosb = ropek_c[:, 32 * t:32 * (t + 1)].unsqueeze(1).broadcast_to((128, 4, 32))
            sinb = ropek_s[:, 32 * t:32 * (t + 1)].unsqueeze(1).broadcast_to((128, 4, 32))
            rA = rtA[:, 0:128].rearrange("p (k f) -> p k f", k=4)
            rB = rtB[:, 0:128].rearrange("p (k f) -> p k f", k=4)
            ke = kq_tmp[bi][:, 0:256].rearrange("p (k f2) -> p k f2", k=4)[:, :, 0::2]
            ko = kq_tmp[bi][:, 0:256].rearrange("p (k f2) -> p k f2", k=4)[:, :, 1::2]
            v.tensor_tensor(rA, e_ps, cosb, ALU.mult)
            vd.inc(v.tensor_tensor(rB, o_ps, sinb, ALU.mult))
            vd.wait(v)
            vd.inc(v.tensor_tensor(ke, rA, rB, ALU.subtract))
            vd.wait(v)
            v.tensor_tensor(rA, e_ps, sinb, ALU.mult)
            vd.inc(v.tensor_tensor(rB, o_ps, cosb, ALU.mult))
            vd.wait(v)
            v.tensor_tensor(ko, rA, rB, ALU.add)
            vdst = Vt[:, 260 * t:260 * t + 260].rearrange("p (z w) -> p z w", z=4)[:, :, 0:64]
            vsrc = ps_kv[:, 256:512].rearrange("p (z w) -> p z w", z=4)
            roped.inc(v.tensor_copy(vdst, vsrc))
        VE(v_ropek)

        def t_kt(tt, bi=bi):
            roped.wait(tt)
            kqe.wait(tt)
            kqt.inc(tt.transpose(ps_kt[:, 0:128], kq_tmp[bi][:, 0:128], C.ident_bf[:]))
            kqt.inc(tt.transpose(ps_kt[:, 128:256], kq_tmp[bi][:, 128:256], C.ident_bf[:]))
        T(t_kt)

        def v_kte(v, t=t):
            kqt.wait(v)
            v.tensor_copy(KT[0][:, 128 * t:128 * (t + 1)], ps_kt[:, 0:128])
            kqe.inc(v.tensor_copy(KT[1][:, 128 * t:128 * (t + 1)], ps_kt[:, 128:256]))
        VE(v_kte)

    psB.close()
    psC = ExitStack()
    ps_s = [C.psum(psC, f"ps_s{i}", [128, 512], F32) for i in range(4)]
    ps_o = [C.psum(psC, f"ps_o{i}", [65, 512], F32) for i in range(2)]
    ps_r = C.psum(psC, "ps_r", [64, 512], F32)

    state = {"first": True, "gsi": 0, "quad": 0}

    def quad(h_, q4):
        kvh = h_ // 4
        ktn = KT[h_ // 8]
        up = (h_ // 4) % 2
        qtn = QT[(h_ % 4) + 4 * (h_ // 8)]
        col0 = 512 * q4
        ob = state["quad"] % 2
        steps = []
        for kt in range(8 * q4 + 2):
            steps.append((kt, 0, 512, 0 if kt >= 8 * q4 else None))
        for m in range(1, 4):
            for kt in (8 * q4 + 2 * m, 8 * q4 + 2 * m + 1):
                steps.append((kt, 128 * m, 512 - 128 * m, m))
        nsteps = len(steps)
        pend = []
        exp_plan = []
        for si, (kt, off, n, mb) in enumerate(steps):
            G = state["gsi"]
            sl = G % 4
            state["gsi"] += 1

            def t_sc(tt, kt=kt, off=off, n=n, sl=sl, G=G):
                if state["first"]:
                    kqe.wait(tt)
                    roped.wait(tt)
                    qle.wait(tt)
                    state["first"] = False
                if G - 3 > 0:
                    tt.wait_ge(exd.h, G - 3)
                scm.inc(tt.matmul(ps_s[sl][:, 0:n],
                                  ktn[64 * up:64 * up + 64, 128 * kt:128 * (kt + 1)],
                                  qtn[64 * up:64 * up + 64, col0 + off:col0 + off + n],
                                  start=True, stop=True))
            T(t_sc)
            exp_plan.append((sl, n, scm.n, G))

            if mb is not None:
                def v_mask(v, kt=kt, sl=sl, G=G):
                    v.wait_ge(exd.h, G + 1)
                    mkd.inc(v.tensor_tensor(expS[sl][:, 0:128], expS[sl][:, 0:128],
                                            mask_sb[kt % 2][:], ALU.mult))
                VE(v_mask)
            mkd_goal = mkd.n

            def t_av(tt, kt=kt, off=off, n=n, sl=sl, si=si, G=G, mkd_goal=mkd_goal):
                tt.wait_ge(exd.h, G + 1)
                if mkd_goal > 0:
                    tt.wait_ge(mkd.h, mkd_goal)
                if si == 0:
                    nr3.wait(tt, lag=1)
                avd.inc(tt.matmul(ps_o[ob][:, off:off + n],
                                  Vt[:, 260 * kt + 65 * kvh:260 * kt + 65 * kvh + 65],
                                  expS[sl][:, 0:n],
                                  start=(si == 0), stop=(si == nsteps - 1)))
            pend.append(t_av)
            if len(pend) > 2:
                T(pend.pop(0))
        while pend:
            T(pend.pop(0))

        def s_exp_all(s, plan=tuple(exp_plan)):
            for (sl, n, scm_t, G) in plan:
                s.wait_ge(scm.h, scm_t)
                if G - 3 > 0:
                    s.wait_ge(avd.h, G - 3)
                exd.inc(s.activation(expS[sl][:, 0:n], ps_s[sl][:, 0:n],
                                     AF.Exp, scale=0.125))
        SC(s_exp_all)

        def v_rec(v):
            avd.wait(v)
            nr2.wait(v, lag=1)
            nr1.inc(v.reciprocal(rrow[ob][:, :], ps_o[ob][64:65, :]))
        VE(v_rec)

        def t_bc(tt):
            nr1.wait(tt)
            nr3.wait(tt)
            nr2.inc(tt.matmul(ps_r[:, :], ones64[:, :], rrow[ob][:, :], start=True, stop=True))
        T(t_bc)

        def v_fin(v, h_=h_, q4=q4):
            nr2.wait(v)
            vd.inc(v.tensor_copy(recb[ob][:, :], ps_r[:, :]))
            vd.wait(v)
            dst = attnT[h_ // 2][64 * (h_ % 2):64 * (h_ % 2) + 64, 512 * q4:512 * (q4 + 1)]
            nr3.inc(v.tensor_tensor(dst, ps_o[ob][0:64, :], recb[ob][:, :], ALU.mult))
        VE(v_fin)
        state["quad"] += 1

    for h_ in range(H):
        for q4 in range(4):
            quad(h_, q4)

    psC.close()
    ph.close()
    return [nr3.mark(), evc.mark(), kqe.mark(), roped.mark()]



def emit_l0ffn(C, io, xnT, attnT, attnT_stack, gate):
    nc = C.nc
    T, VE, SC, SY, G = C.T, C.V, C.S, C.Y, C.G

    ph = ExitStack()
    sb = lambda n, s, d: C.sb(ph, n, s, d)
    accT = [sb(f"f_accT{i}", [128, TOK], F32) for i in range(8)]
    dn_sb = [sb(f"f_dn{i}", [128, 1024], BF16) for i in range(8)]
    wbc = sb("f_wbc", [128, TOK], BF16)
    wT_sb = sb("f_wT", [NEXP, TOK], BF16)
    onehot_sb = sb("f_onehot", [NEXP, 2048], BF16)
    routerT_sb = sb("f_routerT", [128, 8 * NEXP], BF16)
    bout_sb = sb("f_bout", [128, 8], F32)
    logits = sb("f_logits", [128, NT * NEXP], F32)
    tcol = sb("f_tcol", [128, 4], F32)
    eqb = sb("f_eqb", [128, NEXP], F32)
    mkb = sb("f_mkb", [128, NEXP], F32)
    geb = sb("f_geb", [128, NEXP], F32)
    sgb = sb("f_sgb", [128, NEXP], F32)
    wfull = sb("f_wfull", [128, NEXP], F32)
    silu_sb = sb("f_silu", [128, TOK], BF16)
    wout_sb = [sb(f"f_wout{i}", [128, E], BF16) for i in range(2)]
    h_sb = io["h_sb"]

    gsn = lambda n: C.gs("f_" + n)
    vd = C.gs("a_vd")
    wlg, wls = gsn("wlg"), gsn("wls")
    wmm, wev = gsn("wmm"), gsn("wev")
    lgm, lgv, sgd, tpw, twe = gsn("lgm"), gsn("lgv"), gsn("sgd"), gsn("tpw"), gsn("twe")
    gul = [gsn(f"gul{i}") for i in range(2)]
    dnl8 = gsn("dnl")
    dnl = [dnl8] * 8
    wbm, wbe = gsn("wbm"), gsn("wbe")
    gmm, sil, umm, hbd, hcd = gsn("gmm"), gsn("sil"), gsn("umm"), gsn("hbd"), gsn("hcd")
    dnm, dne = gsn("dnm"), gsn("dne")
    sdl8 = gsn("sdl")
    sdl = [sdl8] * 8
    xld = [gsn("xld0"), gsn("xld1")]
    fint, finv = gsn("fint"), gsn("finv")
    sto = gsn("sto")

    wol = gsn("wol")

    def g_loads(g):
        wait_gate(g, gate)
        wlg.inc(g.dma_start(out=routerT_sb[:],
                            in_=io["routerT"].rearrange("(c p) n -> p c n", p=128)), 16)
        wlg.inc(g.dma_start(out=onehot_sb[:], in_=io["onehot"]), 16)
    G(g_loads)

    def sy_loads(sy):
        wait_gate(sy, gate)
        wls.inc(sy.dma_start(out=bout_sb[:], in_=io["bout"]), 16)
    SY(sy_loads)

    psP = ExitStack()
    ps_m2 = [C.psum(psP, f"f_ps_m{i}", [128, 512], F32) for i in range(2)]
    ps_t = C.psum(psP, "f_ps_t", [NEXP, 128], F32)
    ps_m = ps_m2[0]

    for qtr in range(4):
        def g_wo(g, qtr=qtr):
            if qtr == 0:
                wait_gate(g, gate)
            if qtr > 0:
                wmm.wait(g)
            for k in range(2):
                ci = 2 * qtr + k
                wol.inc(g.dma_start(out=wout_sb[k][:], in_=io["wout"][128 * ci:128 * (ci + 1), :]), 16)
        G(g_wo)

        for oc in range(8):
            for c4 in range(4):
                def t_wout(tt, qtr=qtr, oc=oc, c4=c4):
                    if qtr == 0 and oc == 0 and c4 == 0:
                        wait_gate(tt, gate)
                        wlg.wait(tt)
                    wol.wait(tt)
                    wev.wait(tt, lag=1)
                    mi = (qtr * 32 + oc * 4 + c4) % 2
                    for k in range(2):
                        ci = 2 * qtr + k
                        wmm.inc(tt.matmul(ps_m2[mi][:, :], wout_sb[k][:, 128 * oc:128 * (oc + 1)],
                                          attnT[ci][:, 512 * c4:512 * (c4 + 1)],
                                          start=(k == 0), stop=(k == 1)))
                T(t_wout)

                def v_wout(v, qtr=qtr, oc=oc, c4=c4):
                    wmm.wait(v)
                    mi = (qtr * 32 + oc * 4 + c4) % 2
                    if qtr == 0:
                        wls.wait(v)
                        wev.inc(v.tensor_scalar(accT[oc][:, 512 * c4:512 * (c4 + 1)],
                                                ps_m2[mi][:, :], bout_sb[:, oc:oc + 1], None, ALU.add))
                    else:
                        wev.inc(v.tensor_tensor(accT[oc][:, 512 * c4:512 * (c4 + 1)],
                                                accT[oc][:, 512 * c4:512 * (c4 + 1)],
                                                ps_m2[mi][:, :], ALU.add))
                VE(v_wout)

    for a in range(NT):
        def t_lg(tt, a=a):
            if a == 0:
                wev.wait(tt)
            lgv.wait(tt)
            for ci in range(8):
                lgm.inc(tt.matmul(ps_m[:, 0:NEXP], xnT[ci][:, 128 * a:128 * (a + 1)],
                                  routerT_sb[:, NEXP * ci:NEXP * (ci + 1)],
                                  start=(ci == 0), stop=(ci == 7)))
        T(t_lg)

        def v_lg(v, a=a):
            lgm.wait(v)
            lgv.inc(v.tensor_copy(logits[:, NEXP * a:NEXP * (a + 1)], ps_m[:, 0:NEXP]))
        VE(v_lg)

    for a in range(NT):
        la = logits[:, NEXP * a:NEXP * (a + 1)]

        def v_top2a(v, la=la):
            lgv.wait(v)
            vd.inc(v.tensor_reduce(tcol[:, 0:1], la, AX.X, ALU.max))
            vd.wait(v)
            vd.inc(v.tensor_scalar(eqb[:], la, tcol[:, 0:1], None, ALU.is_equal))
            vd.wait(v)
            vd.inc(v.scalar_tensor_tensor(mkb[:], eqb[:], -1e30, la, ALU.mult, ALU.add))
            vd.wait(v)
            vd.inc(v.tensor_reduce(tcol[:, 1:2], mkb[:], AX.X, ALU.max))
            vd.wait(v)
            sgd.inc(v.tensor_scalar(geb[:], la, tcol[:, 1:2], None, ALU.is_ge))
        VE(v_top2a)

        def s_sig(s, la=la):
            sgd.wait(s)
            tpw.wait(s, lag=1)
            sgd.inc(s.activation(sgb[:], la, AF.Sigmoid))
        SC(s_sig)

        def v_wfull(v):
            sgd.wait(v)
            tpw.inc(v.tensor_tensor(wfull[:], sgb[:], geb[:], ALU.mult))
        VE(v_wfull)

        def t_wT(tt):
            tpw.wait(tt)
            twe.wait(tt)
            tpw.inc(tt.transpose(ps_t[:, :], wfull[:, :], C.ident_f32[:]))
        T(t_wT)

        def v_wTe(v, a=a):
            tpw.wait(v)
            twe.inc(v.tensor_copy(wT_sb[:, 128 * a:128 * (a + 1)], ps_t[:, :]))
        VE(v_wTe)

    psP.close()

    esP3 = ExitStack()
    gu2 = [C.sb(esP3, f"f_gu2_{i}", [128, 2048], BF16) for i in range(2)]
    psX = ExitStack()
    ps_ga = C.psum(psX, "f_ps_ga", [128, 2048], F32)
    ps_gb = C.psum(psX, "f_ps_gb", [128, 2048], F32)

    def emit_expert(e):
        def g_dn(g, e=e):
            if e == 0:
                wmm.wait(g)
            dnm.wait(g)
            for ci in range(8):
                dnl[ci].inc(g.dma_start(out=dn_sb[ci][:],
                                        in_=io["dn_w"][e, 128 * ci:128 * (ci + 1), :]), 16)
        G(g_dn)

        for c4 in range(4):
            def t_wb(tt, e=e, c4=c4):
                if e == 0 and c4 == 0:
                    twe.wait(tt)
                    wev.wait(tt)
                hbd.wait(tt)
                wbe.wait(tt)
                dne.wait(tt)
                wbm.inc(tt.matmul(ps_gb[:, 512 * c4:512 * (c4 + 1)],
                                  onehot_sb[:, 128 * e:128 * (e + 1)],
                                  wT_sb[:, 512 * c4:512 * (c4 + 1)], start=True, stop=True))
            T(t_wb)

        def v_wb(v):
            wbm.wait(v)
            wbe.inc(v.tensor_copy(wbc[:], ps_gb[:, :]))
        VE(v_wb)

        for g in range(8):
            wi = g % 2

            def g_gu(g_, e=e, g=g, wi=wi):
                if e == 0 and g < 2:
                    wmm.wait(g_)
                umm.wait(g_, lag=1)
                gul[wi].inc(g_.dma_start(out=gu2[wi][:], in_=io["gup_w"][e, g]), 16)
            G(g_gu)

            def t_gate(tt, e=e, g=g, wi=wi):
                gul[wi].wait(tt)
                sil.wait(tt)
                if g == 0:
                    wbe.wait(tt)
                    dne.wait(tt)
                for ci in range(8):
                    for c4 in range(4):
                        gmm.inc(tt.matmul(ps_ga[:, 512 * c4:512 * (c4 + 1)],
                                          gu2[wi][:, 128 * ci:128 * (ci + 1)],
                                          xnT[ci][:, 512 * c4:512 * (c4 + 1)],
                                          start=(ci == 0), stop=(ci == 7)))
            T(t_gate)

            def s_silu(s, g=g):
                hbd.wait(s)
                for c4 in range(4):
                    gmm.wait(s, lag=3 - c4)
                    inst = s.activation(silu_sb[:, 512 * c4:512 * (c4 + 1)],
                                        ps_ga[:, 512 * c4:512 * (c4 + 1)], AF.Silu)
                sil.inc(inst)
            SC(s_silu)

            def t_up(tt, e=e, g=g, wi=wi):
                hbd.wait(tt)
                if g == 0:
                    wbe.wait(tt)
                for ci in range(8):
                    for c4 in range(4):
                        inst = tt.matmul(ps_gb[:, 512 * c4:512 * (c4 + 1)],
                                         gu2[wi][:, 1024 + 128 * ci:1024 + 128 * (ci + 1)],
                                         xnT[ci][:, 512 * c4:512 * (c4 + 1)],
                                         start=(ci == 0), stop=(ci == 7))
                umm.inc(inst)
            T(t_up)

            def v_h(v, e=e, g=g):
                umm.wait(v)
                sil.wait(v)
                wbe.wait(v)
                dnm.wait(v)
                hbd.inc(v.tensor_tensor(h_sb[g][:], ps_gb[:, :], silu_sb[:], ALU.mult))
                hbd.wait(v)
                hcd.inc(v.tensor_tensor(h_sb[g][:], h_sb[g][:], wbc[:], ALU.mult))
            VE(v_h)

        for oc in range(8):
            pbank = ps_ga if oc % 2 == 0 else ps_gb

            def t_dn(tt, e=e, oc=oc, pbank=pbank):
                hcd.wait(tt)
                dnl[0].wait(tt)
                if oc >= 2:
                    dne.wait(tt, lag=4)
                for hc in range(8):
                    for c4 in range(4):
                        inst = tt.matmul(pbank[:, 512 * c4:512 * (c4 + 1)],
                                         dn_sb[hc][:, 128 * oc:128 * (oc + 1)],
                                         h_sb[hc][:, 512 * c4:512 * (c4 + 1)],
                                         start=(hc == 0), stop=(hc == 7))
                        if hc == 7:
                            dnm.inc(inst)
            T(t_dn)

            for c4 in range(4):
                def v_dn(v, oc=oc, c4=c4, pbank=pbank):
                    dnm.wait(v, lag=3 - c4)
                    dne.inc(v.tensor_tensor(accT[oc][:, 512 * c4:512 * (c4 + 1)],
                                            accT[oc][:, 512 * c4:512 * (c4 + 1)],
                                            pbank[:, 512 * c4:512 * (c4 + 1)], ALU.add))
                VE(v_dn)

    for e in range(NEXP):
        emit_expert(e)

    for sh in range(2):
        for gl in range(8):
            gidx = 8 * sh + gl
            wi = gidx % 2

            def g_ws(g, gidx=gidx, wi=wi):
                umm.wait(g, lag=1)
                gul[wi].inc(g.dma_start(out=gu2[wi][:], in_=io["sgsu"][gidx]), 16)
            G(g_ws)

            def g_sd(g, gidx=gidx, gl=gl):
                dnm.wait(g)
                sdl[gl].inc(g.dma_start(out=dn_sb[gl][:],
                                        in_=io["sdwT"][128 * gidx:128 * (gidx + 1), :]), 16)
            G(g_sd)

            def t_sg(tt, wi=wi, sh=sh, gl=gl):
                gul[wi].wait(tt)
                sil.wait(tt)
                if sh == 0 and gl == 0:
                    dne.wait(tt)
                for ci in range(8):
                    for c4 in range(4):
                        gmm.inc(tt.matmul(ps_ga[:, 512 * c4:512 * (c4 + 1)],
                                          gu2[wi][:, 128 * ci:128 * (ci + 1)],
                                          xnT[ci][:, 512 * c4:512 * (c4 + 1)],
                                          start=(ci == 0), stop=(ci == 7)))
            T(t_sg)

            def s_ssilu(s):
                hbd.wait(s)
                for c4 in range(4):
                    gmm.wait(s, lag=3 - c4)
                    inst = s.activation(silu_sb[:, 512 * c4:512 * (c4 + 1)],
                                        ps_ga[:, 512 * c4:512 * (c4 + 1)], AF.Silu)
                sil.inc(inst)
            SC(s_ssilu)

            def t_su(tt, wi=wi, sh=sh, gl=gl):
                hbd.wait(tt)
                if sh == 0 and gl == 0:
                    dne.wait(tt)
                for ci in range(8):
                    for c4 in range(4):
                        inst = tt.matmul(ps_gb[:, 512 * c4:512 * (c4 + 1)],
                                         gu2[wi][:, 1024 + 128 * ci:1024 + 128 * (ci + 1)],
                                         xnT[ci][:, 512 * c4:512 * (c4 + 1)],
                                         start=(ci == 0), stop=(ci == 7))
                umm.inc(inst)
            T(t_su)

            def v_t(v, gl=gl):
                umm.wait(v)
                sil.wait(v)
                dnm.wait(v)
                hbd.inc(v.tensor_tensor(h_sb[gl][:], ps_gb[:, :], silu_sb[:], ALU.mult))
            VE(v_t)

        for oc in range(8):
            pbank = ps_ga if oc % 2 == 0 else ps_gb

            def t_sd(tt, oc=oc, pbank=pbank):
                hbd.wait(tt)
                sdl[0].wait(tt)
                if oc >= 2:
                    dne.wait(tt, lag=4)
                for gl in range(8):
                    for c4 in range(4):
                        inst = tt.matmul(pbank[:, 512 * c4:512 * (c4 + 1)],
                                         dn_sb[gl][:, 128 * oc:128 * (oc + 1)],
                                         h_sb[gl][:, 512 * c4:512 * (c4 + 1)],
                                         start=(gl == 0), stop=(gl == 7))
                        if gl == 7:
                            dnm.inc(inst)
            T(t_sd)

            for c4 in range(4):
                def v_sd(v, oc=oc, c4=c4, pbank=pbank):
                    dnm.wait(v, lag=3 - c4)
                    dne.inc(v.tensor_tensor(accT[oc][:, 512 * c4:512 * (c4 + 1)],
                                            accT[oc][:, 512 * c4:512 * (c4 + 1)],
                                            pbank[:, 512 * c4:512 * (c4 + 1)], ALU.add))
                VE(v_sd)

    psX.close()
    esP3.close()
    psF = ExitStack()
    ps_f = C.psum(psF, "f_ps_f", [128, 128], F32)
    xbuf = [h_sb[0][:].bitcast(F32), h_sb[1][:].bitcast(F32)]
    x1b = [h_sb[2][:].bitcast(F32), h_sb[3][:].bitcast(F32)]

    for a in range(NT):
        bi = a % 2

        def sy_xa(sy, a=a, bi=bi):
            if a < 2:
                dne.wait(sy)
            finv.wait(sy, lag=8)
            xld[bi].inc(sy.dma_start(out=xbuf[bi], in_=io["x_own"][128 * a:128 * (a + 1), :]), 16)
        SY(sy_xa)

        for ct in range(8):
            def t_f(tt, a=a, ct=ct):
                if a == 0 and ct == 0:
                    dne.wait(tt)
                    hbd.wait(tt)
                finv.wait(tt)
                fint.inc(tt.transpose(ps_f[:, :], accT[ct][:, 128 * a:128 * (a + 1)],
                                      C.ident_f32[:]))
            T(t_f)

            def v_f(v, a=a, bi=bi, ct=ct):
                fint.wait(v)
                xld[bi].wait(v)
                if ct == 0:
                    sto.wait(v, lag=16)
                finv.inc(v.tensor_tensor(x1b[bi][:, 128 * ct:128 * (ct + 1)],
                                         xbuf[bi][:, 128 * ct:128 * (ct + 1)],
                                         ps_f[:, :], ALU.add))
            VE(v_f)

        def sy_o(sy, a=a, bi=bi):
            finv.wait(sy)
            sto.inc(sy.dma_start(out=io["x1_scr"].ap()[128 * a:128 * (a + 1), :], in_=x1b[bi]), 16)
        SY(sy_o)

    psF.close()
    ph.close()
    return [sto.mark(), finv.mark()]



def emit_l1qkv(C, io, xn1T, gate):
    nc = C.nc
    T, VE, SC, SY, G = C.T, C.V, C.S, C.Y, C.G

    ph = ExitStack()
    sb = lambda n, s, d: C.sb(ph, n, s, d)
    w_sb = [sb(f"q_w{i}", [128, 4 * E], BF16) for i in range(8)]
    xbuf = [sb(f"q_xbuf{i}", [128, E], F32) for i in range(4)]
    xsq = [sb(f"q_xsq{i}", [128, E], F32) for i in range(4)]
    stat2 = [sb(f"q_stat{i}", [128, 8], F32) for i in range(4)]
    xn_tmp = [sb(f"q_xn{i}", [128, E], BF16) for i in range(4)]
    qkv_sb = [sb(f"q_qkv{i}", [128, 4 * E], BF16) for i in range(4)]

    gsn = lambda n: C.gs("q_" + n)
    vd = C.gs("a_vd")
    wlg = gsn("wlg")
    lnf, lns, sds, lnx = C.gs("a_lnf"), C.gs("a_lns"), C.gs("a_sds"), C.gs("a_lnx")
    xl = [C.gs("a_xl0"), C.gs("a_xl1")]
    trp, evc = C.gs("a_trp"), C.gs("a_evc")
    qmm, qev, sto = gsn("qmm"), gsn("qev"), gsn("sto")

    def g_loads(g):
        wait_gate(g, gate)
        for ci in range(8):
            wlg.inc(g.dma_start(out=w_sb[ci][:], in_=io["wqkv"][128 * ci:128 * (ci + 1), :]), 16)
    G(g_loads)

    ps_tr = C.psum(ph, "q_ps_tr", [128, 1024], BF16)
    ps_q = [C.psum(ph, f"q_ps_q{i}", [128, 512], F32) for i in range(2)]

    gq = {"n": 0}
    for a in range(NT):
        bi = a % 4
        xi = a % 2

        def sy_x(sy, a=a, bi=bi, xi=xi):
            if a < 2:
                wait_gate(sy, gate)
            lnf.wait(sy, lag=3)
            xl[xi].inc(sy.dma_start(out=xbuf[bi][:],
                                    in_=io["x1_scr"].ap()[128 * a:128 * (a + 1), :]), 16)
        SY(sy_x)

        def v_stats(v, bi=bi, xi=xi):
            stat = stat2[bi]
            xl[xi].wait(v)
            sds.wait(v, lag=3)
            v.tensor_tensor(xsq[bi][:], xbuf[bi][:], xbuf[bi][:], ALU.mult)
            v.tensor_reduce(stat[:, 0:1], xbuf[bi][:], AX.X, ALU.add)
            vd.inc(v.tensor_reduce(stat[:, 1:2], xsq[bi][:], AX.X, ALU.add))
            vd.wait(v)
            vd.inc(v.tensor_scalar(stat[:, 2:3], stat[:, 0:1], 1.0 / E, None, ALU.mult))
            vd.wait(v)
            vd.inc(v.tensor_tensor(stat[:, 3:4], stat[:, 2:3], stat[:, 2:3], ALU.mult))
            v.tensor_scalar(stat[:, 4:5], stat[:, 1:2], 1.0 / E, LN_EPS, ALU.mult, ALU.add)
            vd.wait(v)
            lns.inc(v.tensor_tensor(stat[:, 4:5], stat[:, 4:5], stat[:, 3:4], ALU.subtract))
        VE(v_stats)

        def s_rstd(s, bi=bi):
            stat = stat2[bi]
            lns.wait(s)
            lnx.inc(s.activation(stat[:, 5:6], stat[:, 4:5], AF.Ln))
            lnx.wait(s)
            sds.inc(s.activation(stat[:, 6:7], stat[:, 5:6], AF.Exp, scale=-0.5))
        SC(s_rstd)

        def v_norm(v, bi=bi):
            stat = stat2[bi]
            sds.wait(v)
            trp.wait(v, lag=24)
            lnf.inc(v.tensor_scalar(xn_tmp[bi][:], xbuf[bi][:], stat[:, 2:3], stat[:, 6:7],
                                    ALU.subtract, ALU.mult))
        VE(v_norm)

        def t_trans(tt, bi=bi):
            lnf.wait(tt)
            evc.wait(tt)
            for ct in range(8):
                trp.inc(tt.transpose(ps_tr[:, 128 * ct:128 * (ct + 1)],
                                     xn_tmp[bi][:, 128 * ct:128 * (ct + 1)], C.ident_bf[:]))
        T(t_trans)

        def v_evac(v, a=a):
            trp.wait(v)
            for ct in range(8):
                evc.inc(v.tensor_copy(xn1T[ct][:, 128 * a:128 * (a + 1)],
                                      ps_tr[:, 128 * ct:128 * (ct + 1)]))
        VE(v_evac)

        for oc8 in range(8):
            sl = gq["n"] % 2
            gq["n"] += 1

            def t_q(tt, a=a, oc8=oc8, sl=sl):
                if a == 0 and oc8 == 0:
                    wlg.wait(tt)
                evc.wait(tt)
                qev.wait(tt, lag=1)
                for ci in range(8):
                    qmm.inc(tt.matmul(ps_q[sl][:, :], xn1T[ci][:, 128 * a:128 * (a + 1)],
                                      w_sb[ci][:, 512 * oc8:512 * (oc8 + 1)],
                                      start=(ci == 0), stop=(ci == 7)))
            T(t_q)

            def v_q(v, a=a, bi=bi, oc8=oc8, sl=sl):
                qmm.wait(v)
                if oc8 == 0:
                    sto.wait(v, lag=48)
                qev.inc(v.tensor_copy(qkv_sb[bi][:, 512 * oc8:512 * (oc8 + 1)], ps_q[sl][:, :]))
            VE(v_q)

        def sy_q(sy, a=a, bi=bi):
            qev.wait(sy)
            sto.inc(sy.dma_start(out=io["qkv_scr"].ap()[128 * a:128 * (a + 1), :],
                                 in_=qkv_sb[bi][:]), 16)
        SY(sy_q)

    ph.close()
    return [sto.mark(), evc.mark(), qev.mark()]



def emit_l1fused(C, io, xn1T, gate):
    nc = C.nc
    T, VE, SC, SY, G = C.T, C.V, C.S, C.Y, C.G

    ph = ExitStack()
    sb = lambda n, s, d: C.sb(ph, n, s, d)
    accT = [sb(f"o_accT{i}", [128, TOK], BF16) for i in range(8)]
    scl = sb("o_scl", [128, 5 * 32], F32)
    kv_all = sb("w_kv", [128, 8192], BF16)
    qb = [sb(f"w_qb{i}", [128, 256], BF16) for i in range(2)]
    wb_all = sb("w_wball", [128, 4096], BF16)
    wscr = sb("w_scr", [128, 4096], BF16)
    sc2 = [sb(f"w_sc{i}", [128, 1024], F32) for i in range(2)]
    P4 = [sb(f"w_P{i}", [128, 1024], BF16) for i in range(4)]
    scat4 = [sb(f"w_scat{i}", [128, 1024], BF16) for i in range(4)]
    sums = sb("w_sums", [128, 16], F32)
    rec = sb("w_rec", [128, 16], F32)
    ov = sb("w_ov", [128, 256], F32)
    t1 = wscr[:].bitcast(F32)[:, 0:256]

    gsn = lambda n: C.gs("o_" + n)
    vd = C.gs("a_vd")
    wls = gsn("wls")
    a1l = [gsn("a1l0"), gsn("a1l1")]
    trp, evc = C.gs("a_trp"), C.gs("a_evc")
    wol = gsn("wol")
    wmm, wev = gsn("wmm"), gsn("wev")
    w12l = [gsn(f"w12l{i}") for i in range(2)]
    whl = [gsn(f"whl{i}") for i in range(2)]
    w3l = [gsn(f"w3l{i}") for i in range(2)]
    amm, sil, bmm, uev = gsn("amm"), gsn("sil"), gsn("bmm"), gsn("uev")
    hmm, hev = gsn("hmm"), gsn("hev")
    h2m, h2e = gsn("h2m"), gsn("h2e")
    w3m, w3e = gsn("w3m"), gsn("w3e")
    xld = [C.gs("a_xl0"), C.gs("a_xl1")]
    fint, finv, sto = gsn("fint"), gsn("finv"), gsn("sto")
    kvl, qbl, wbl = gsn("kvl"), gsn("qbl"), gsn("wbl")
    scr_d, exq, wind = gsn("w_scr_d"), gsn("w_exq"), gsn("w_wind")
    scw, scd, ccs = gsn("w_scw"), gsn("w_scd"), gsn("w_ccs")

    def sy_loads(sy):
        wait_gate(sy, gate)
        for k, t in enumerate(["w1s", "w2s", "h1s", "h2s"]):
            wls.inc(sy.dma_start(out=scl[:, 32 * k:32 * (k + 1)], in_=io[t]), 16)
        wls.inc(sy.dma_start(out=scl[:, 128:136], in_=io["w3s"]), 16)
    SY(sy_loads)

    psC = ExitStack()
    ps_a2 = [C.psum(psC, f"o_ps_a{i}", [128, 512], F32) for i in range(2)]
    ps_b2 = [C.psum(psC, f"o_ps_b{i}", [128, 512], F32) for i in range(2)]
    u_sb = [C.sb(psC, f"o_u{i}", [128, 512], BF16) for i in range(32)]
    v_sb = [C.sb(psC, f"o_v{i}", [128, 512], BF16) for i in range(32)]
    wbuf = [C.sb(psC, f"o_wbuf{i}", [128, 4096], BF16) for i in range(2)]
    silu2 = [C.sb(psC, f"o_silu{i}", [128, 512], BF16) for i in range(2)]

    qkv = io["qkv_scr"].ap()
    for qt in range(4):
        i = qt
        def sy_kv(sy, i=i):
            if i == 0:
                wait_gate(sy, gate)
            else:
                wind.wait(sy)
                sy.wait_ge(w3e.h, 8 * i)
            for a in range(NT):
                kvl.inc(sy.dma_start(out=kv_all[:, 512 * a:512 * (a + 1)],
                                     in_=qkv[128 * a:128 * (a + 1),
                                             1024 * i + 256:1024 * i + 768]), 16)
            for a in range(NT):
                wbl.inc(sy.dma_start(out=wb_all[:, 256 * a:256 * (a + 1)],
                                     in_=qkv[128 * a:128 * (a + 1),
                                             1024 * i + 768:1024 * (i + 1)]), 16)
        SY(sy_kv)

        def sy_qb(sy, i=i):
            for a in range(NT):
                A = 16 * i + a
                if A - 1 > 0:
                    sy.wait_ge(scr_d.h, A - 1)
                qbl.inc(sy.dma_start(out=qb[a % 2][:],
                                     in_=qkv[128 * a:128 * (a + 1),
                                             1024 * i:1024 * i + 256]), 16)
        SY(sy_qb)

        amm_marks = []
        sil0, uev0 = sil.n, uev.n
        for g in range(32):
            wi = g % 2

            def g_w12(g_, qt=qt, g=g, wi=wi):
                if qt == 0 and g < 2:
                    wait_gate(g_, gate)
                if g < 2:
                    w3m.wait(g_)
                bmm.wait(g_, lag=8)
                if g < 2:
                    for h in range(2):
                        w12l[wi].inc(g_.dma_start(out=wbuf[wi][:, 1024 * h:1024 * (h + 1)],
                                                  in_=io["w12T"][g, :, 1024 * h:1024 * (h + 1)]), 16)
                else:
                    w12l[wi].inc(g_.dma_start(out=wbuf[wi][:, 0:2048], in_=io["w12T"][g]), 16)
            G(g_w12)

            def t_w1(tt, qt=qt, g=g, wi=wi, sil0=sil0):
                w12l[wi].wait(tt)
                if sil0 + g - 1 > 0:
                    tt.wait_ge(sil.h, sil0 + g - 1)
                w3e.wait(tt)
                for ci in range(8):
                    amm.inc(tt.matmul(ps_a2[wi][:, :], wbuf[wi][:, 128 * ci:128 * (ci + 1)],
                                      xn1T[ci][:, 512 * qt:512 * (qt + 1)],
                                      start=(ci == 0), stop=(ci == 7)))
            T(t_w1)
            amm_marks.append(amm.n)

            def t_w2(tt, qt=qt, g=g, wi=wi, sil0=sil0, uev0=uev0):
                if sil0 + g > 0:
                    tt.wait_ge(sil.h, sil0 + g)
                if uev0 + g - 1 > 0:
                    tt.wait_ge(uev.h, uev0 + g - 1)
                for ci in range(8):
                    bmm.inc(tt.matmul(ps_b2[wi][:, :],
                                      wbuf[wi][:, 1024 + 128 * ci:1024 + 128 * (ci + 1)],
                                      xn1T[ci][:, 512 * qt:512 * (qt + 1)],
                                      start=(ci == 0), stop=(ci == 7)))
            T(t_w2)

            def v_u(v, g=g, wi=wi, sil0=sil0):
                bmm.wait(v)
                v.wait_ge(sil.h, sil0 + g + 1)
                hmm.wait(v)
                uev.inc(v.scalar_tensor_tensor(u_sb[g][:], ps_b2[wi][:, :],
                                               scl[:, 32 + g:33 + g],
                                               silu2[wi][:], ALU.mult, ALU.mult))
            VE(v_u)

        def s_si_all(s, qt=qt, marks=tuple(amm_marks), uev0=uev0):
            if qt == 0:
                wls.wait(s)
            for g in range(32):
                s.wait_ge(amm.h, marks[g])
                if uev0 + g - 1 > 0:
                    s.wait_ge(uev.h, uev0 + g - 1)
                sil.inc(s.activation(silu2[g % 2][:], ps_a2[g % 2][:, :], AF.Silu,
                                     scale=scl[:, g:g + 1]))
        SC(s_si_all)

        for a in range(NT):
            g4, al = a // 4, a % 4

            def v_sc(v, i=i, a=a, g4=g4, al=al):
                A = 16 * i + a
                b = g4 % 2
                kvl.wait(v)
                v.wait_ge(qbl.h, 16 * (A + 1))
                if al == 0 and 4 * i + g4 - 1 > 0:
                    v.wait_ge(exq.h, 4 * i + g4 - 1)
                q_ap = qb[a % 2][:].rearrange("p (u d) -> p u d", d=16) \
                    .unsqueeze(1).broadcast_to((128, 16, 16, 16))
                k_ap = kv_all[:].rearrange("p (ap c) -> p ap c", c=512)[:, :, 0:256] \
                    .rearrange("p ap (u d) -> p ap u d", d=16)
                o_ap = wscr[:].rearrange("p (ap u d) -> p ap u d", u=16, d=16)
                vd.inc(v.tensor_tensor(o_ap, q_ap, k_ap, ALU.mult))
                vd.wait(v)
                scr_d.inc(v.tensor_reduce(sc2[b][:, 256 * al:256 * (al + 1)],
                                          wscr[:].rearrange("p (au d) -> p au d", d=16),
                                          AX.X, ALU.add))
            VE(v_sc)

        hmm_marks = []
        hev0, h2m0 = hev.n, h2m.n
        for g in range(32):
            wi = g % 2

            def g_wh(g_, g=g, wi=wi):
                if g < 2:
                    bmm.wait(g_)
                hmm.wait(g_, lag=32)
                if g < 2:
                    for h in range(2):
                        whl[wi].inc(g_.dma_start(out=wbuf[wi][:, 2048 * h:2048 * (h + 1)],
                                                 in_=io["h1T"][g, :, 2048 * h:2048 * (h + 1)]), 16)
                else:
                    whl[wi].inc(g_.dma_start(out=wbuf[wi][:], in_=io["h1T"][g]), 16)
            G(g_wh)

            def t_h1(tt, g=g, wi=wi, hev0=hev0):
                whl[wi].wait(tt)
                uev.wait(tt)
                if hev0 + g - 1 > 0:
                    tt.wait_ge(hev.h, hev0 + g - 1)
                for ci in range(32):
                    hmm.inc(tt.matmul(ps_a2[wi][:, :], wbuf[wi][:, 128 * ci:128 * (ci + 1)],
                                      u_sb[ci][:, :], start=(ci == 0), stop=(ci == 31)))
            T(t_h1)
            hmm_marks.append(hmm.n)

        def s_g1_all(s, marks=tuple(hmm_marks), h2m0=h2m0):
            if h2m0 > 0:
                s.wait_ge(h2m.h, h2m0)
            for g in range(32):
                s.wait_ge(hmm.h, marks[g])
                hev.inc(s.activation(v_sb[g][:], ps_a2[g % 2][:, :], AF.Gelu,
                                     scale=scl[:, 64 + g:65 + g]))
        SC(s_g1_all)

        def s_expw(s, i=i):
            for g4 in range(4):
                s.wait_ge(scr_d.h, 16 * i + 4 * (g4 + 1))
                t = 16 * (i - 1) + 4 * (g4 + 1)
                if i > 0 and t > 0:
                    s.wait_ge(wind.h, t)
                exq.inc(s.activation(P4[g4][:], sc2[g4 % 2][:], AF.Exp, scale=0.25))
        SC(s_expw)

        for a in range(NT):
            g4, al = a // 4, a % 4

            def v_av(v, i=i, a=a, g4=g4, al=al):
                A = 16 * i + a
                Gw = 4 * i + g4
                b = g4 % 2
                v.wait_ge(exq.h, 4 * i + g4 + 1)
                v.wait_ge(wbl.h, 256 * (i + 1))
                if al == 0 and 16 * (Gw - 3) > 0:
                    v.wait_ge(scw.h, 16 * (Gw - 3))
                pv = P4[g4][:, 256 * al:256 * (al + 1)].rearrange("p (ap u) -> p u ap", ap=16)
                vd.inc(v.tensor_reduce(sums[:], pv, AX.X, ALU.add))
                vd.wait(v)
                vd.inc(v.reciprocal(rec[:], sums[:]))
                vd.wait(v)
                p_ap = pv.unsqueeze(2).broadcast_to((128, 16, 16, 16))
                v_ap = kv_all[:].rearrange("p (ap c) -> p ap c", c=512)[:, :, 256:512] \
                    .rearrange("p ap (u d) -> p u d ap", d=16)
                t_ap = wscr[:].rearrange("p (u d ap) -> p u d ap", d=16, ap=16)
                vd.inc(v.tensor_tensor(t_ap, p_ap, v_ap, ALU.mult))
                vd.wait(v)
                vd.inc(v.tensor_reduce(ov[:], wscr[:].rearrange("p (ud ap) -> p ud ap", ap=16),
                                       AX.X, ALU.add))
                vd.wait(v)
                r_ap = rec[:].unsqueeze(2).broadcast_to((128, 16, 16))
                vd.inc(v.tensor_tensor(t1.rearrange("p (u d) -> p u d", d=16),
                                       ov[:].rearrange("p (u d) -> p u d", d=16), r_ap, ALU.mult))
                vd.wait(v)
                wind.inc(v.tensor_tensor(scat4[g4][:, 256 * al:256 * (al + 1)], t1,
                                         wb_all[:, 256 * a:256 * (a + 1)], ALU.mult))
            VE(v_av)

        h2m_marks = []
        h2e0, w3m0 = h2e.n, w3m.n
        for g in range(32):
            wi = g % 2

            def g_wh2(g_, g=g, wi=wi):
                if g < 2:
                    hmm.wait(g_)
                h2m.wait(g_, lag=32)
                if g < 2:
                    for h in range(2):
                        whl[wi].inc(g_.dma_start(out=wbuf[wi][:, 2048 * h:2048 * (h + 1)],
                                                 in_=io["h2T"][g, :, 2048 * h:2048 * (h + 1)]), 16)
                else:
                    whl[wi].inc(g_.dma_start(out=wbuf[wi][:], in_=io["h2T"][g]), 16)
            G(g_wh2)

            def t_h2(tt, g=g, wi=wi, h2e0=h2e0):
                whl[wi].wait(tt)
                hev.wait(tt)
                if h2e0 + g - 1 > 0:
                    tt.wait_ge(h2e.h, h2e0 + g - 1)
                for ci in range(32):
                    h2m.inc(tt.matmul(ps_b2[wi][:, :], wbuf[wi][:, 128 * ci:128 * (ci + 1)],
                                      v_sb[ci][:, :], start=(ci == 0), stop=(ci == 31)))
            T(t_h2)
            h2m_marks.append(h2m.n)

        def s_g2_all(s, marks=tuple(h2m_marks), w3m0=w3m0):
            if w3m0 > 0:
                s.wait_ge(w3m.h, w3m0)
            for g in range(32):
                s.wait_ge(h2m.h, marks[g])
                h2e.inc(s.activation(u_sb[g][:], ps_b2[g % 2][:, :], AF.Gelu,
                                     scale=scl[:, 96 + g:97 + g]))
        SC(s_g2_all)


        w3e0 = w3e.n
        for oc in range(8):
            wi = oc % 2

            def g_w3(g_, oc=oc, wi=wi):
                if oc < 2:
                    h2m.wait(g_)
                w3m.wait(g_, lag=32)
                if oc < 2:
                    for h in range(2):
                        w3l[wi].inc(g_.dma_start(out=wbuf[wi][:, 2048 * h:2048 * (h + 1)],
                                                 in_=io["w3T"][oc, :, 2048 * h:2048 * (h + 1)]), 16)
                else:
                    w3l[wi].inc(g_.dma_start(out=wbuf[wi][:], in_=io["w3T"][oc]), 16)
            G(g_w3)

            def t_w3(tt, oc=oc, wi=wi, w3e0=w3e0):
                w3l[wi].wait(tt)
                h2e.wait(tt)
                if w3e0 + oc - 1 > 0:
                    tt.wait_ge(w3e.h, w3e0 + oc - 1)
                for ci in range(32):
                    w3m.inc(tt.matmul(ps_a2[wi][:, :], wbuf[wi][:, 128 * ci:128 * (ci + 1)],
                                      u_sb[ci][:, :], start=(ci == 0), stop=(ci == 31)))
            T(t_w3)

            def v_w3(v, qt=qt, oc=oc, wi=wi):
                w3m.wait(v)
                w3e.inc(v.tensor_scalar(accT[oc][:, 512 * qt:512 * (qt + 1)],
                                        ps_a2[wi][:, :], scl[:, 128 + oc:129 + oc],
                                        None, ALU.mult))
            VE(v_w3)

        for g4 in range(4):
            def sy_sc(sy, i=i, g4=g4):
                Gw = 4 * i + g4
                sy.wait_ge(wind.h, 16 * i + 4 * (g4 + 1))
                if 64 * (Gw - 3) > 0:
                    sy.wait_ge(scd.h, 64 * (Gw - 3))
                scw.inc(sy.dma_start(out=io["scr"].ap()[:, 1024 * g4:1024 * (g4 + 1)],
                                     in_=scat4[g4][:]), 16)
                sy.wait_ge(scw.h, 16 * (Gw + 1))
                for al in range(4):
                    a = 4 * g4 + al
                    blk = 2 * i + (a % 4) // 2
                    rowb = ((a // 4) * 2 + (a % 2)) * 32
                    dst = io["a2a_in"].ap()[256 * blk + rowb:256 * blk + rowb + 32, :] \
                        .rearrange("t (z w) -> t z w", z=4)
                    srcv = io["scr"].ap()[:, 256 * a:256 * (a + 1)].rearrange("(t z) w -> t z w", z=4)
                    scd.inc(sy.dma_start(out=dst, in_=srcv), 16)
            SY(sy_sc)

    def g_cc(g):
        scd.wait(g)
        with allow_ici_all_to_all():
            ccs.inc(g.collective_compute(
                "AllToAll", mybir.AluOpType.bypass,
                replica_groups=[list(range(NCORES))],
                ins=[io["a2a_in"].ap().opt()],
                outs=[io["a2a_out"].ap().opt()],
            ))
    G(g_cc)

    psC.close()
    esA = ExitStack()
    a1T = [C.sb(esA, f"o_a1T{i}", [128, TOK], BF16) for i in range(8)]
    abuf = [C.sb(esA, f"o_abuf{i}", [128, E], BF16) for i in range(2)]
    wo_sb = [C.sb(esA, f"o_wo{i}", [128, E], BF16) for i in range(2)]
    psA = ExitStack()
    ps_tr = C.psum(psA, "o_ps_tr", [128, 1024], BF16)

    for a in range(NT):
        bi = a % 2

        def sy_a1(sy, a=a, bi=bi):
            if a < 2:
                ccs.wait(sy)
                w3e.wait(sy)
            trp.wait(sy, lag=8)
            for al in range(2):
                for gm in range(2):
                    blk = 2 * (a // 4) + gm
                    rowb = ((a % 4) * 2 + al) * 32
                    a1l[bi].inc(sy.dma_start(
                        out=abuf[bi][64 * al + 32 * gm:64 * al + 32 * gm + 32, :],
                        in_=io["a2a_out"].ap()[256 * blk + rowb:256 * blk + rowb + 32, :]), 16)
        SY(sy_a1)

        def t_tr(tt, a=a, bi=bi):
            a1l[bi].wait(tt)
            evc.wait(tt)
            for ct in range(8):
                trp.inc(tt.transpose(ps_tr[:, 128 * ct:128 * (ct + 1)],
                                     abuf[bi][:, 128 * ct:128 * (ct + 1)], C.ident_bf[:]))
        T(t_tr)

        def v_ev(v, a=a):
            trp.wait(v)
            for ct in range(8):
                evc.inc(v.tensor_copy(a1T[ct][:, 128 * a:128 * (a + 1)],
                                      ps_tr[:, 128 * ct:128 * (ct + 1)]))
        VE(v_ev)

    psB = ExitStack()
    ps_m2 = [C.psum(psB, f"o_ps_m{i}", [128, 512], F32) for i in range(2)]

    for qtr in range(4):
        def g_wo(g, qtr=qtr):
            if qtr == 0:
                ccs.wait(g)
            if qtr > 0:
                wmm.wait(g)
            for k in range(2):
                ci = 2 * qtr + k
                wol.inc(g.dma_start(out=wo_sb[k][:], in_=io["w_out"][128 * ci:128 * (ci + 1), :]), 16)
        G(g_wo)

        for oc in range(8):
            for c4 in range(4):
                def t_wo(tt, qtr=qtr, oc=oc, c4=c4):
                    if qtr == 0 and oc == 0 and c4 == 0:
                        evc.wait(tt)
                    wol.wait(tt)
                    wev.wait(tt, lag=1)
                    mi = (qtr * 32 + oc * 4 + c4) % 2
                    for k in range(2):
                        ci = 2 * qtr + k
                        wmm.inc(tt.matmul(ps_m2[mi][:, :], wo_sb[k][:, 128 * oc:128 * (oc + 1)],
                                          a1T[ci][:, 512 * c4:512 * (c4 + 1)],
                                          start=(k == 0), stop=(k == 1)))
                T(t_wo)

                def v_wo(v, qtr=qtr, oc=oc, c4=c4):
                    wmm.wait(v)
                    if qtr == 0 and oc == 0 and c4 == 0:
                        w3e.wait(v)
                    mi = (qtr * 32 + oc * 4 + c4) % 2
                    wev.inc(v.tensor_tensor(accT[oc][:, 512 * c4:512 * (c4 + 1)],
                                            accT[oc][:, 512 * c4:512 * (c4 + 1)],
                                            ps_m2[mi][:, :], ALU.add))
                VE(v_wo)

    psB.close()
    psA.close()
    esA.close()
    psF = ExitStack()
    ps_f = C.psum(psF, "o_ps_f", [128, 128], BF16)
    xbuf = [C.sb(psF, f"o_xbuf{i}", [128, E], F32) for i in range(2)]
    x1b = [C.sb(psF, f"o_x1b{i}", [128, E], F32) for i in range(2)]

    for a in range(NT):
        bi = a % 2

        def sy_xa(sy, a=a, bi=bi):
            if a < 2:
                w3e.wait(sy)
                wev.wait(sy)
            finv.wait(sy, lag=8)
            xld[bi].inc(sy.dma_start(out=xbuf[bi][:],
                                     in_=io["x1_scr"].ap()[128 * a:128 * (a + 1), :]), 16)
        SY(sy_xa)

        for ct in range(8):
            def t_f(tt, a=a, ct=ct):
                if a == 0 and ct == 0:
                    w3e.wait(tt)
                    wev.wait(tt)
                finv.wait(tt)
                fint.inc(tt.transpose(ps_f[:, :], accT[ct][:, 128 * a:128 * (a + 1)],
                                      C.ident_bf[:]))
            T(t_f)

            def v_f(v, a=a, bi=bi, ct=ct):
                fint.wait(v)
                xld[bi].wait(v)
                if ct == 0:
                    sto.wait(v, lag=16)
                finv.inc(v.tensor_tensor(x1b[bi][:, 128 * ct:128 * (ct + 1)],
                                         xbuf[bi][:, 128 * ct:128 * (ct + 1)],
                                         ps_f[:, :], ALU.add))
            VE(v_f)

        def sy_o(sy, a=a, bi=bi):
            finv.wait(sy)
            sto.inc(sy.dma_start(out=io["out"][128 * a:128 * (a + 1), :], in_=x1b[bi][:]), 16)
        SY(sy_o)

    def sy_fin(sy):
        sto.wait(sy)
    SY(sy_fin)
    psF.close()
    ph.close()



def build_mega():
    nc = bass.Bass("TRN2", target_bir_lowering=False, detect_race_conditions=False)
    dt = lambda n, s, d: nc.dram_tensor(n, s, d, kind="ExternalInput").ap()
    io = dict(
        x_own=dt("x_own", [TOK, E], F32),
        wqa=dt("wqa", [E, QL], F32),
        wqb=dt("wqb", [QL, E], F32),
        wkva=dt("wkva", [E, KVL], F32),
        wkvb_kv=dt("wkvb_kv", [KVL, 512], F32),
        ropek_cos=dt("ropek_cos", [S, 32], F32),
        ropek_sin=dt("ropek_sin", [S, 32], F32),
        ropeq_cos=dt("ropeq_cos", [TOK, 32], F32),
        ropeq_sin=dt("ropeq_sin", [TOK, 32], F32),
        mask0=dt("mask0", [128, 128], F32),
        mask1=dt("mask1", [128, 128], F32),
        ident=dt("ident", [128, 128], F32),
        wout=dt("wout", [E, E], BF16),
        bout=dt("bout", [128, 8], F32),
        routerT=dt("routerT", [E, NEXP], F32),
        gup_w=dt("gup_w", [NEXP, 8, 128, 2048], BF16),
        dn_w=dt("dn_w", [NEXP, E, E], BF16),
        sgsu=dt("sgsu", [16, 128, 2048], BF16),
        sdwT=dt("sdwT", [SHD, E], BF16),
        onehot=dt("onehot", [NEXP, 2048], F32),
        wqkv=dt("wqkv", [E, 4 * E], BF16),
        w_out=dt("w_out", [E, E], BF16),
        w12T=dt("w12T", [32, 128, 2048], BF16),
        h1T=dt("h1T", [32, 128, 4096], BF16),
        h2T=dt("h2T", [32, 128, 4096], BF16),
        w3T=dt("w3T", [8, 128, 4096], BF16),
        w1s=dt("w1s", [128, 32], F32),
        w2s=dt("w2s", [128, 32], F32),
        h1s=dt("h1s", [128, 32], F32),
        h2s=dt("h2s", [128, 32], F32),
        w3s=dt("w3s", [128, 8], F32),
        out=nc.dram_tensor("out", [TOK, E], F32, kind="ExternalOutput").ap(),
        x1_scr=nc.dram_tensor("x1_scr", [TOK, E], F32),
        qkv_scr=nc.dram_tensor("qkv_scr", [TOK, 4 * E], BF16),
        scr=nc.dram_tensor("scr", [128, 4096], BF16),
        a2a_in=nc.dram_tensor("a2a_in", [TOK, E], BF16),
        a2a_out=nc.dram_tensor("a2a_out", [TOK, E], BF16),
        lat_in=nc.dram_tensor("lat_in", [KVL, TOK], BF16),
        lat_out=nc.dram_tensor("lat_out", [2 * KVL, TOK], BF16),
    )

    es = ExitStack()
    with es:
        block = es.enter_context(nc.Block())
        C = Ctx(nc, es, block)
        C.ident_bf = es.enter_context(nc.sbuf_tensor("ident_bf", [128, 128], BF16))
        C.ident_f32 = es.enter_context(nc.sbuf_tensor("ident_f32", [128, 128], F32))

        es_xn = ExitStack()
        xnT = [es_xn.enter_context(nc.sbuf_tensor(f"xnT{i}", [128, TOK], BF16)) for i in range(8)]
        es_at = ExitStack()
        attnT = [es_at.enter_context(nc.sbuf_tensor(f"attnT{i}", [128, TOK], BF16)) for i in range(8)]
        g1 = emit_l0attn(C, io, xnT, attnT)
        es_h = ExitStack()
        io["h_sb"] = [es_h.enter_context(nc.sbuf_tensor(f"h{i}", [128, TOK], BF16)) for i in range(8)]
        g2 = emit_l0ffn(C, io, xnT, attnT, es_at, g1)
        es_h.close()
        es_at.close()
        es_xn.close()

        es_xn1 = ExitStack()
        xn1T = [es_xn1.enter_context(nc.sbuf_tensor(f"xn1T{i}", [128, TOK], BF16)) for i in range(8)]
        g3 = emit_l1qkv(C, io, xn1T, g2)
        emit_l1fused(C, io, xn1T, g3)
        es_xn1.close()
    return nc



def _rope_tables():
    inv = 10000.0 ** (-np.arange(0, D, 2, dtype=np.float64) / D)
    pos = np.arange(S, dtype=np.float64)
    ang = np.outer(pos, inv)
    return np.cos(ang).astype(np.float32), np.sin(ang).astype(np.float32)


def _own_rows(core):
    b, h = core // 2, core % 2
    a = np.arange(NT)
    starts = b * S + 256 * a + 128 * h
    return (starts[:, None] + np.arange(128)[None, :]).reshape(-1)


def _inmaps(inputs):
    x2d = np.asarray(inputs["L0_attn_x"]).reshape(B * S, E)
    cosf, sinf = _rope_tables()
    ident = np.eye(128, dtype=np.float32)
    tri = np.triu(np.ones((128, 128), dtype=np.float32))
    ones = np.ones((128, 128), dtype=np.float32)
    zeros = np.zeros((128, 128), dtype=np.float32)
    wkvb = np.asarray(inputs["L0_attn_Wkvb"])
    wkvb_kv = np.ascontiguousarray(np.concatenate([wkvb[:, 0::2], wkvb[:, 1::2]], axis=1))
    onehot = np.zeros((NEXP, 2048), dtype=np.float32)
    for e in range(NEXP):
        onehot[e, 128 * e:128 * (e + 1)] = 1.0
    rs = lambda s: np.ascontiguousarray(np.asarray(s).reshape(-1, 128).T)
    import ml_dtypes
    bf = lambda a: np.ascontiguousarray(a).astype(ml_dtypes.bfloat16)
    common = dict(
        wqa=np.asarray(inputs["L0_attn_Wqa"]),
        wqb=np.asarray(inputs["L0_attn_Wqb"]),
        wkva=np.asarray(inputs["L0_attn_Wkva"]),
        wkvb_kv=wkvb_kv,
        ropek_cos=cosf, ropek_sin=sinf,
        ident=ident,
        wout=bf(np.asarray(inputs["L0_attn_Wout"])),
        bout=np.ascontiguousarray(np.asarray(inputs["L0_attn_bout"]).reshape(8, 128).T),
        routerT=np.ascontiguousarray(np.asarray(inputs["L0_ffn_router_w"]).T),
        gup_w=bf(np.asarray(inputs["L0_ffn_gate_up_w"])
                 .reshape(NEXP, 8, 128, 16, 128).transpose(0, 3, 2, 1, 4)
                 .reshape(NEXP, 2, 8, 128, 1024).transpose(0, 2, 3, 1, 4)
                 .reshape(NEXP, 8, 128, 2048)),
        dn_w=bf(np.asarray(inputs["L0_ffn_down_w"])),
        sgsu=bf(np.concatenate([
            np.asarray(inputs["L0_ffn_shared_gate_w"]).reshape(16, 128, 8, 128).transpose(0, 3, 2, 1).reshape(16, 128, 1024),
            np.asarray(inputs["L0_ffn_shared_up_w"]).reshape(16, 128, 8, 128).transpose(0, 3, 2, 1).reshape(16, 128, 1024),
        ], axis=2)),
        sdwT=bf(np.asarray(inputs["L0_ffn_shared_down_w"]).T),
        onehot=onehot,
        wqkv=bf(np.asarray(inputs["L1_attn_w_qkvw"])),
        w_out=bf(np.asarray(inputs["L1_attn_w_out"])),
        w12T=bf(np.concatenate([
            np.asarray(inputs["L1_ffn_w1_w"]).reshape(32, 128, 8, 128).transpose(0, 3, 2, 1).reshape(32, 128, 1024),
            np.asarray(inputs["L1_ffn_w2_w"]).reshape(32, 128, 8, 128).transpose(0, 3, 2, 1).reshape(32, 128, 1024),
        ], axis=2)),
        h1T=bf(np.asarray(inputs["L1_ffn_h1_w"])
               .reshape(32, 128, 32, 128).transpose(0, 3, 2, 1).reshape(32, 128, 4096)),
        h2T=bf(np.asarray(inputs["L1_ffn_h2_w"])
               .reshape(32, 128, 32, 128).transpose(0, 3, 2, 1).reshape(32, 128, 4096)),
        w3T=bf(np.asarray(inputs["L1_ffn_w3_w"])
               .reshape(8, 128, 32, 128).transpose(0, 3, 2, 1).reshape(8, 128, 4096)),
        w1s=rs(inputs["L1_ffn_w1_s"]), w2s=rs(inputs["L1_ffn_w2_s"]),
        h1s=rs(inputs["L1_ffn_h1_s"]), h2s=rs(inputs["L1_ffn_h2_s"]),
        w3s=rs(inputs["L1_ffn_w3_s"]),
    )
    maps = []
    for c in range(NCORES):
        b, h = c // 2, c % 2
        rows = _own_rows(c)
        m = dict(common)
        m.update(
            x_own=np.ascontiguousarray(x2d[rows]),
            ropeq_cos=np.ascontiguousarray(cosf[rows - b * S]),
            ropeq_sin=np.ascontiguousarray(sinf[rows - b * S]),
            mask0=(tri if h == 0 else ones),
            mask1=(zeros if h == 0 else tri),
        )
        maps.append(m)
    return maps


def kernel(**inputs):
    cores = list(range(NCORES))
    nc = build_mega()
    res = run_bass_kernel_spmd(nc, _inmaps(inputs), core_ids=cores)
    out = np.empty((B * S, E), dtype=np.float32)
    for c in cores:
        out[_own_rows(c)] = res.results[c]["out"]
    return out.reshape(B, S, E)


# version 29 (final)
# speedup vs baseline: 1.0426x; 1.0026x over previous
import numpy as np
from contextlib import ExitStack

import concourse.bass as bass
import concourse.mybir as mybir
from concourse.bass_utils import run_bass_kernel_spmd
from concourse.aot_env import allow_ici_all_to_all

F32 = mybir.dt.float32
BF16 = mybir.dt.bfloat16
AF = mybir.ActivationFunctionType
ALU = mybir.AluOpType
AX = mybir.AxisListType

B, S, E = 4, 4096, 1024
NCORES = 8
NT = 16
TOK = 2048
H, KVH, D = 16, 4, 64
QL, KVL = 64, 32
NEXP = 16
SHD = 2048
DHID = 4096
LN_EPS = 1e-5


class Sem:
    def __init__(self, nc, es, name):
        self.h = es.enter_context(nc.semaphore(name))
        self.n = 0

    def inc(self, inst, k=1):
        inst.then_inc(self.h, k)
        self.n += k
        return inst

    def wait(self, eng, lag=0):
        v = self.n - lag
        if v > 0:
            eng.wait_ge(self.h, v)

    def mark(self):
        return (self.h, self.n)


class Ctx:
    def __init__(self, nc, es, block):
        self.nc, self.es, self.block = nc, es, block
        self.T, self.V, self.S, self.Y, self.G = (
            block.tensor, block.vector, block.scalar, block.sync, block.gpsimd)
        self._sems = {}

    def gs(self, name):
        if name not in self._sems:
            self._sems[name] = Sem(self.nc, self.es, name)
        return self._sems[name]

    def sb(self, stack, name, shape, dt):
        return stack.enter_context(self.nc.sbuf_tensor(name, shape, dt))

    def psum(self, stack, name, shape, dt):
        return stack.enter_context(self.nc.psum_tensor(name, shape, dt))


def wait_gate(eng, gate):
    for (h, v) in gate:
        if v > 0:
            eng.wait_ge(h, v)



def emit_l0attn(C, io, xnT, attnT):
    nc, es = C.nc, C.es
    T, VE, SC, SY, G = C.T, C.V, C.S, C.Y, C.G
    x_own = io["x_own"]

    ph = ExitStack()
    sb = lambda n, s, d: C.sb(ph, n, s, d)
    KT = [sb(f"KT{i}", [128, S], BF16) for i in range(2)]
    Vt = sb("Vt", [128, 32 * 260], BF16)
    QT = [sb(f"QT{i}", [128, TOK], BF16) for i in range(8)]
    latT = sb("latT", [KVL, S], BF16)
    qlatT = sb("qlatT", [QL, TOK], BF16)
    wqa_sb = sb("wqa_sb", [128, 8 * QL], BF16)
    wqb_sb = sb("wqb_sb", [QL, E], BF16)
    wkva_sb = sb("wkva_sb", [128, 8 * KVL], BF16)
    wkvb_sb = sb("wkvb_sb", [KVL, 512], BF16)
    mask_sb = [sb(f"mask{i}_sb", [128, 128], BF16) for i in range(2)]
    ones64 = sb("ones64", [1, 64], F32)
    ropek_c = sb("ropek_c", [128, 32 * 32], F32)
    ropek_s = sb("ropek_s", [128, 32 * 32], F32)
    ropeq_c = sb("ropeq_c", [128, 16 * 32], F32)
    ropeq_s = sb("ropeq_s", [128, 16 * 32], F32)
    xbuf = [sb(f"xbuf{i}", [128, E], F32) for i in range(2)]
    xsq = [sb(f"xsq{i}", [128, E], F32) for i in range(2)]
    stat2 = [sb(f"stat{i}", [128, 8], F32) for i in range(2)]
    xn_tmp = [sb(f"xn_tmp{i}", [128, E], BF16) for i in range(2)]
    lat_own = sb("lat_own", [KVL, TOK], BF16)
    kq_tmp = [sb(f"kq_tmp{i}", [128, E], BF16) for i in range(2)]
    rtA = sb("rtA", [128, 512], F32)
    rtB = sb("rtB", [128, 512], F32)
    expS = [sb(f"expS{i}", [128, 512], BF16) for i in range(4)]
    recb = [sb(f"recb{i}", [64, 512], F32) for i in range(2)]
    rrow = [sb(f"rrow{i}", [1, 512], F32) for i in range(2)]

    gsn = lambda n: C.gs("a_" + n)
    vd = gsn("vd")
    wl, wls = gsn("wl"), gsn("wls")
    xl = [gsn("xl0"), gsn("xl1")]
    lnf, trp, evc = gsn("lnf"), gsn("trp"), gsn("evc")
    lns, sds, lnx = gsn("lns"), gsn("sds"), gsn("lnx")
    latm, late = gsn("latm"), gsn("late")
    lsto, agd, latl = gsn("lsto"), gsn("agd"), gsn("latl")
    kvm, roped, kqt, kqe = gsn("kvm"), gsn("roped"), gsn("kqt"), gsn("kqe")
    qlm, qle, qmm = gsn("qlm"), gsn("qle"), gsn("qmm")
    scm, exd, mkd, avd = gsn("scm"), gsn("exd"), gsn("mkd"), gsn("avd")
    nr1, nr2, nr3 = gsn("nr1"), gsn("nr2"), gsn("nr3")

    def g_loads(g):
        wl.inc(g.dma_start(out=wqa_sb[:], in_=io["wqa"].rearrange("(c p) n -> p c n", p=128)), 16)
        wl.inc(g.dma_start(out=wqb_sb[:], in_=io["wqb"]), 16)
        wl.inc(g.dma_start(out=wkva_sb[:], in_=io["wkva"].rearrange("(c p) n -> p c n", p=128)), 16)
        wl.inc(g.dma_start(out=wkvb_sb[:], in_=io["wkvb_kv"]), 16)
        wl.inc(g.dma_start(out=C.ident_bf[:], in_=io["ident"]), 16)
        wl.inc(g.dma_start(out=mask_sb[0][:], in_=io["mask0"]), 16)
        wl.inc(g.dma_start(out=mask_sb[1][:], in_=io["mask1"]), 16)
    G(g_loads)

    def sy_loads(sy):
        wls.inc(sy.dma_start(out=C.ident_f32[:], in_=io["ident"]), 16)
        wls.inc(sy.dma_start(out=ropek_c[:].rearrange("p (t f) -> p t f", f=32),
                             in_=io["ropek_cos"].rearrange("(t p) f -> p t f", p=128)), 16)
        wls.inc(sy.dma_start(out=ropek_s[:].rearrange("p (t f) -> p t f", f=32),
                             in_=io["ropek_sin"].rearrange("(t p) f -> p t f", p=128)), 16)
        wls.inc(sy.dma_start(out=ropeq_c[:].rearrange("p (t f) -> p t f", f=32),
                             in_=io["ropeq_cos"].rearrange("(t p) f -> p t f", p=128)), 16)
        wls.inc(sy.dma_start(out=ropeq_s[:].rearrange("p (t f) -> p t f", f=32),
                             in_=io["ropeq_sin"].rearrange("(t p) f -> p t f", p=128)), 16)
    SY(sy_loads)

    def v_init(v):
        wl.wait(v)
        wls.wait(v)
        v.memset(ones64[:], 1.0)
        lnf.inc(v.memset(
            Vt[:].rearrange("p (t z w) -> p t z w", z=4, w=65)[:, :, :, 64:65], 1.0))
    VE(v_init)

    def emit_ln_tile(src_ap, dst_bf_ap, ld_sem, bi, extra_war=None):
        stat = stat2[bi]
        xq = xsq[bi]

        def v_stats(v):
            ld_sem.wait(v)
            sds.wait(v, lag=1)
            v.tensor_tensor(xq[:], src_ap, src_ap, ALU.mult)
            v.tensor_reduce(stat[:, 0:1], src_ap, AX.X, ALU.add)
            vd.inc(v.tensor_reduce(stat[:, 1:2], xq[:], AX.X, ALU.add))
            vd.wait(v)
            vd.inc(v.tensor_scalar(stat[:, 2:3], stat[:, 0:1], 1.0 / E, None, ALU.mult))
            vd.wait(v)
            vd.inc(v.tensor_tensor(stat[:, 3:4], stat[:, 2:3], stat[:, 2:3], ALU.mult))
            v.tensor_scalar(stat[:, 4:5], stat[:, 1:2], 1.0 / E, LN_EPS, ALU.mult, ALU.add)
            vd.wait(v)
            lns.inc(v.tensor_tensor(stat[:, 4:5], stat[:, 4:5], stat[:, 3:4], ALU.subtract))
        VE(v_stats)

        def s_rstd(s):
            lns.wait(s)
            lnx.inc(s.activation(stat[:, 5:6], stat[:, 4:5], AF.Ln))
            lnx.wait(s)
            sds.inc(s.activation(stat[:, 6:7], stat[:, 5:6], AF.Exp, scale=-0.5))
        SC(s_rstd)

        def v_norm(v):
            sds.wait(v)
            if extra_war is not None:
                extra_war(v)
            lnf.inc(v.tensor_scalar(dst_bf_ap, src_ap, stat[:, 2:3], stat[:, 6:7],
                                    ALU.subtract, ALU.mult))
        VE(v_norm)

    psA = ExitStack()
    ps_tr = C.psum(psA, "ps_tr", [128, 1024], BF16)
    ps_lat = C.psum(psA, "ps_lat", [KVL, 128], F32)

    for a in range(NT):
        bi = a % 2

        def sy_x(sy, a=a, bi=bi):
            lnf.wait(sy, lag=1)
            xl[bi].inc(sy.dma_start(out=xbuf[bi][:], in_=x_own[128 * a:128 * (a + 1), :]), 16)
        SY(sy_x)

        emit_ln_tile(xbuf[bi][:], xn_tmp[bi][:], xl[bi], bi,
                     extra_war=(lambda v: trp.wait(v, lag=8)))

        def t_trans(tt, bi=bi):
            lnf.wait(tt)
            evc.wait(tt)
            for ct in range(8):
                trp.inc(tt.transpose(ps_tr[:, 128 * ct:128 * (ct + 1)],
                                     xn_tmp[bi][:, 128 * ct:128 * (ct + 1)], C.ident_bf[:]))
        T(t_trans)

        def v_evac(v, a=a):
            trp.wait(v)
            for ct in range(8):
                evc.inc(v.tensor_copy(xnT[ct][:, 128 * a:128 * (a + 1)],
                                      ps_tr[:, 128 * ct:128 * (ct + 1)]))
        VE(v_evac)

        def t_lat(tt, a=a):
            evc.wait(tt)
            late.wait(tt)
            for ct in range(8):
                latm.inc(tt.matmul(ps_lat[:, :], wkva_sb[:, KVL * ct:KVL * (ct + 1)],
                                   xnT[ct][:, 128 * a:128 * (a + 1)],
                                   start=(ct == 0), stop=(ct == 7)))
        T(t_lat)

        def v_lat(v, a=a):
            latm.wait(v)
            late.inc(v.tensor_copy(lat_own[:, 128 * a:128 * (a + 1)], ps_lat[:, :]))
        VE(v_lat)

    def sy_lat(sy):
        late.wait(sy)
        lsto.inc(sy.dma_start(out=io["lat_in"].ap(), in_=lat_own[:]), 16)
    SY(sy_lat)

    def g_ag(g):
        lsto.wait(g)
        with allow_ici_all_to_all():
            agd.inc(g.collective_compute(
                "AllGather", mybir.AluOpType.bypass,
                replica_groups=[[0, 1], [2, 3], [4, 5], [6, 7]],
                ins=[io["lat_in"].ap().opt()],
                outs=[io["lat_out"].ap().opt()],
            ))
    G(g_ag)

    def sy_latl(sy):
        agd.wait(sy)
        lo = io["lat_out"].ap()
        latl.inc(sy.dma_start(
            out=latT[:].rearrange("l (t two c) -> l t two c", two=2, c=128)[:, :, 0, :],
            in_=lo[0:KVL, :].rearrange("l (t c) -> l t c", c=128)), 16)
        latl.inc(sy.dma_start(
            out=latT[:].rearrange("l (t two c) -> l t two c", two=2, c=128)[:, :, 1, :],
            in_=lo[KVL:2 * KVL, :].rearrange("l (t c) -> l t c", c=128)), 16)
    SY(sy_latl)

    psA.close()
    psB = ExitStack()
    ps_ql = C.psum(psB, "ps_ql", [QL, 512], F32)
    ps_q = C.psum(psB, "ps_q", [128, 1024], F32)
    ps_qt = C.psum(psB, "ps_qt", [64, 256], BF16)

    for qc in range(4):
        def t_ql(tt, qc=qc):
            qle.wait(tt)
            for ct in range(8):
                qlm.inc(tt.matmul(ps_ql[:, :], wqa_sb[:, QL * ct:QL * (ct + 1)],
                                  xnT[ct][:, 512 * qc:512 * (qc + 1)],
                                  start=(ct == 0), stop=(ct == 7)))
        T(t_ql)

        def v_ql(v, qc=qc):
            qlm.wait(v)
            qle.inc(v.tensor_copy(qlatT[:, 512 * qc:512 * (qc + 1)], ps_ql[:, :]))
        VE(v_ql)

    for a in range(NT):
        bi = a % 2

        def t_q(tt, a=a):
            qle.wait(tt)
            roped.wait(tt)
            qmm.inc(tt.matmul(ps_q[:, 0:512], qlatT[:, 128 * a:128 * (a + 1)],
                              wqb_sb[:, 0:512], start=True, stop=True))
            qmm.inc(tt.matmul(ps_q[:, 512:1024], qlatT[:, 128 * a:128 * (a + 1)],
                              wqb_sb[:, 512:1024], start=True, stop=True))
        T(t_q)

        def v_ropeq(v, a=a, bi=bi):
            qmm.wait(v)
            kqt.wait(v, lag=16)
            e_ps = ps_q[:, :].rearrange("p (hh f2) -> p hh f2", hh=16)[:, :, 0::2]
            o_ps = ps_q[:, :].rearrange("p (hh f2) -> p hh f2", hh=16)[:, :, 1::2]
            cosb = ropeq_c[:, 32 * a:32 * (a + 1)].unsqueeze(1).broadcast_to((128, 16, 32))
            sinb = ropeq_s[:, 32 * a:32 * (a + 1)].unsqueeze(1).broadcast_to((128, 16, 32))
            rA = rtA[:, :].rearrange("p (hh f) -> p hh f", hh=16)
            rB = rtB[:, :].rearrange("p (hh f) -> p hh f", hh=16)
            qe = kq_tmp[bi][:, :].rearrange("p (hh f2) -> p hh f2", hh=16)[:, :, 0::2]
            qo = kq_tmp[bi][:, :].rearrange("p (hh f2) -> p hh f2", hh=16)[:, :, 1::2]
            v.tensor_tensor(rA, e_ps, cosb, ALU.mult)
            vd.inc(v.tensor_tensor(rB, o_ps, sinb, ALU.mult))
            vd.wait(v)
            vd.inc(v.tensor_tensor(qe, rA, rB, ALU.subtract))
            vd.wait(v)
            v.tensor_tensor(rA, e_ps, sinb, ALU.mult)
            vd.inc(v.tensor_tensor(rB, o_ps, cosb, ALU.mult))
            vd.wait(v)
            roped.inc(v.tensor_tensor(qo, rA, rB, ALU.add))
        VE(v_ropeq)

        for grp in range(8):
            def t_qt(tt, bi=bi, grp=grp):
                roped.wait(tt)
                kqe.wait(tt)
                h0, h1 = 2 * grp, 2 * grp + 1
                kqt.inc(tt.transpose(ps_qt[:, 0:128], kq_tmp[bi][:, 64 * h0:64 * (h0 + 1)],
                                     C.ident_bf[:]))
                kqt.inc(tt.transpose(ps_qt[:, 128:256], kq_tmp[bi][:, 64 * h1:64 * (h1 + 1)],
                                     C.ident_bf[:]))
            T(t_qt)

            def v_qte(v, a=a, grp=grp):
                kqt.wait(v)
                for k in range(2):
                    h_ = 2 * grp + k
                    p = (h_ % 4) + 4 * (h_ // 8)
                    up = (h_ // 4) % 2
                    kqe.inc(v.tensor_copy(QT[p][64 * up:64 * up + 64, 128 * a:128 * (a + 1)],
                                          ps_qt[:, 128 * k:128 * (k + 1)]))
            VE(v_qte)

    ps_kv = C.psum(psB, "ps_kv", [128, 512], F32)
    ps_kt = C.psum(psB, "ps_kt", [128, 256], BF16)

    for t in range(32):
        bi = t % 2

        def t_kv(tt, t=t):
            if t == 0:
                latl.wait(tt)
            roped.wait(tt)
            kvm.inc(tt.matmul(ps_kv[:, :], latT[:, 128 * t:128 * (t + 1)], wkvb_sb[:, :],
                              start=True, stop=True))
        T(t_kv)

        def v_ropek(v, t=t, bi=bi):
            kvm.wait(v)
            kqt.wait(v, lag=2)
            e_ps = ps_kv[:, 0:256].rearrange("p (k f2) -> p k f2", k=4)[:, :, 0::2]
            o_ps = ps_kv[:, 0:256].rearrange("p (k f2) -> p k f2", k=4)[:, :, 1::2]
            cosb = ropek_c[:, 32 * t:32 * (t + 1)].unsqueeze(1).broadcast_to((128, 4, 32))
            sinb = ropek_s[:, 32 * t:32 * (t + 1)].unsqueeze(1).broadcast_to((128, 4, 32))
            rA = rtA[:, 0:128].rearrange("p (k f) -> p k f", k=4)
            rB = rtB[:, 0:128].rearrange("p (k f) -> p k f", k=4)
            ke = kq_tmp[bi][:, 0:256].rearrange("p (k f2) -> p k f2", k=4)[:, :, 0::2]
            ko = kq_tmp[bi][:, 0:256].rearrange("p (k f2) -> p k f2", k=4)[:, :, 1::2]
            v.tensor_tensor(rA, e_ps, cosb, ALU.mult)
            vd.inc(v.tensor_tensor(rB, o_ps, sinb, ALU.mult))
            vd.wait(v)
            vd.inc(v.tensor_tensor(ke, rA, rB, ALU.subtract))
            vd.wait(v)
            v.tensor_tensor(rA, e_ps, sinb, ALU.mult)
            vd.inc(v.tensor_tensor(rB, o_ps, cosb, ALU.mult))
            vd.wait(v)
            v.tensor_tensor(ko, rA, rB, ALU.add)
            vdst = Vt[:, 260 * t:260 * t + 260].rearrange("p (z w) -> p z w", z=4)[:, :, 0:64]
            vsrc = ps_kv[:, 256:512].rearrange("p (z w) -> p z w", z=4)
            roped.inc(v.tensor_copy(vdst, vsrc))
        VE(v_ropek)

        def t_kt(tt, bi=bi):
            roped.wait(tt)
            kqe.wait(tt)
            kqt.inc(tt.transpose(ps_kt[:, 0:128], kq_tmp[bi][:, 0:128], C.ident_bf[:]))
            kqt.inc(tt.transpose(ps_kt[:, 128:256], kq_tmp[bi][:, 128:256], C.ident_bf[:]))
        T(t_kt)

        def v_kte(v, t=t):
            kqt.wait(v)
            v.tensor_copy(KT[0][:, 128 * t:128 * (t + 1)], ps_kt[:, 0:128])
            kqe.inc(v.tensor_copy(KT[1][:, 128 * t:128 * (t + 1)], ps_kt[:, 128:256]))
        VE(v_kte)

    psB.close()
    psC = ExitStack()
    ps_s = [C.psum(psC, f"ps_s{i}", [128, 512], F32) for i in range(4)]
    ps_o = [C.psum(psC, f"ps_o{i}", [65, 512], F32) for i in range(2)]
    ps_r = C.psum(psC, "ps_r", [64, 512], F32)

    state = {"first": True, "gsi": 0, "quad": 0}

    def quad(h_, q4):
        kvh = h_ // 4
        ktn = KT[h_ // 8]
        up = (h_ // 4) % 2
        qtn = QT[(h_ % 4) + 4 * (h_ // 8)]
        col0 = 512 * q4
        ob = state["quad"] % 2
        steps = []
        for kt in range(8 * q4 + 2):
            steps.append((kt, 0, 512, 0 if kt >= 8 * q4 else None))
        for m in range(1, 4):
            for kt in (8 * q4 + 2 * m, 8 * q4 + 2 * m + 1):
                steps.append((kt, 128 * m, 512 - 128 * m, m))
        nsteps = len(steps)
        pend = []
        exp_plan = []
        for si, (kt, off, n, mb) in enumerate(steps):
            G = state["gsi"]
            sl = G % 4
            state["gsi"] += 1

            def t_sc(tt, kt=kt, off=off, n=n, sl=sl, G=G):
                if state["first"]:
                    kqe.wait(tt)
                    roped.wait(tt)
                    qle.wait(tt)
                    state["first"] = False
                if G - 3 > 0:
                    tt.wait_ge(exd.h, G - 3)
                scm.inc(tt.matmul(ps_s[sl][:, 0:n],
                                  ktn[64 * up:64 * up + 64, 128 * kt:128 * (kt + 1)],
                                  qtn[64 * up:64 * up + 64, col0 + off:col0 + off + n],
                                  start=True, stop=True))
            T(t_sc)
            exp_plan.append((sl, n, scm.n, G))

            if mb is not None:
                def v_mask(v, kt=kt, sl=sl, G=G):
                    v.wait_ge(exd.h, G + 1)
                    mkd.inc(v.tensor_tensor(expS[sl][:, 0:128], expS[sl][:, 0:128],
                                            mask_sb[kt % 2][:], ALU.mult))
                VE(v_mask)
            mkd_goal = mkd.n

            def t_av(tt, kt=kt, off=off, n=n, sl=sl, si=si, G=G, mkd_goal=mkd_goal):
                tt.wait_ge(exd.h, G + 1)
                if mkd_goal > 0:
                    tt.wait_ge(mkd.h, mkd_goal)
                if si == 0:
                    nr3.wait(tt, lag=1)
                avd.inc(tt.matmul(ps_o[ob][:, off:off + n],
                                  Vt[:, 260 * kt + 65 * kvh:260 * kt + 65 * kvh + 65],
                                  expS[sl][:, 0:n],
                                  start=(si == 0), stop=(si == nsteps - 1)))
            pend.append(t_av)
            if len(pend) > 2:
                T(pend.pop(0))
        while pend:
            T(pend.pop(0))

        def s_exp_all(s, plan=tuple(exp_plan)):
            for (sl, n, scm_t, G) in plan:
                s.wait_ge(scm.h, scm_t)
                if G - 3 > 0:
                    s.wait_ge(avd.h, G - 3)
                exd.inc(s.activation(expS[sl][:, 0:n], ps_s[sl][:, 0:n],
                                     AF.Exp, scale=0.125))
        SC(s_exp_all)

        def v_rec(v):
            avd.wait(v)
            nr2.wait(v, lag=1)
            nr1.inc(v.reciprocal(rrow[ob][:, :], ps_o[ob][64:65, :]))
        VE(v_rec)

        def t_bc(tt):
            nr1.wait(tt)
            nr3.wait(tt)
            nr2.inc(tt.matmul(ps_r[:, :], ones64[:, :], rrow[ob][:, :], start=True, stop=True))
        T(t_bc)

        def v_fin(v, h_=h_, q4=q4):
            nr2.wait(v)
            vd.inc(v.tensor_copy(recb[ob][:, :], ps_r[:, :]))
            vd.wait(v)
            dst = attnT[h_ // 2][64 * (h_ % 2):64 * (h_ % 2) + 64, 512 * q4:512 * (q4 + 1)]
            nr3.inc(v.tensor_tensor(dst, ps_o[ob][0:64, :], recb[ob][:, :], ALU.mult))
        VE(v_fin)
        state["quad"] += 1

    for h_ in range(H):
        for q4 in range(4):
            quad(h_, q4)

    psC.close()
    ph.close()
    return [nr3.mark(), evc.mark(), kqe.mark(), roped.mark()]



def emit_l0ffn(C, io, xnT, attnT, attnT_stack, gate):
    nc = C.nc
    T, VE, SC, SY, G = C.T, C.V, C.S, C.Y, C.G

    ph = ExitStack()
    sb = lambda n, s, d: C.sb(ph, n, s, d)
    accT = [sb(f"f_accT{i}", [128, TOK], F32) for i in range(8)]
    dn_sb = [sb(f"f_dn{i}", [128, 1024], BF16) for i in range(8)]
    wbc = sb("f_wbc", [128, TOK], BF16)
    wT_sb = sb("f_wT", [NEXP, TOK], BF16)
    onehot_sb = sb("f_onehot", [NEXP, 2048], BF16)
    routerT_sb = sb("f_routerT", [128, 8 * NEXP], BF16)
    bout_sb = sb("f_bout", [128, 8], F32)
    logits = sb("f_logits", [128, NT * NEXP], F32)
    tcol = sb("f_tcol", [128, 4], F32)
    eqb = sb("f_eqb", [128, NEXP], F32)
    mkb = sb("f_mkb", [128, NEXP], F32)
    geb = sb("f_geb", [128, NEXP], F32)
    sgb = sb("f_sgb", [128, NEXP], F32)
    wfull = sb("f_wfull", [128, NEXP], F32)
    silu_sb = sb("f_silu", [128, TOK], BF16)
    wout_sb = [sb(f"f_wout{i}", [128, E], BF16) for i in range(2)]
    h_sb = io["h_sb"]

    gsn = lambda n: C.gs("f_" + n)
    vd = C.gs("a_vd")
    wlg, wls = gsn("wlg"), gsn("wls")
    wmm, wev = gsn("wmm"), gsn("wev")
    lgm, lgv, sgd, tpw, twe = gsn("lgm"), gsn("lgv"), gsn("sgd"), gsn("tpw"), gsn("twe")
    gul = [gsn(f"gul{i}") for i in range(2)]
    dnl8 = gsn("dnl")
    dnl = [dnl8] * 8
    wbm, wbe = gsn("wbm"), gsn("wbe")
    gmm, sil, umm, hbd, hcd = gsn("gmm"), gsn("sil"), gsn("umm"), gsn("hbd"), gsn("hcd")
    dnm, dne = gsn("dnm"), gsn("dne")
    sdl8 = gsn("sdl")
    sdl = [sdl8] * 8
    xld = [gsn("xld0"), gsn("xld1")]
    fint, finv = gsn("fint"), gsn("finv")
    sto = gsn("sto")

    wol = gsn("wol")

    def g_loads(g):
        wait_gate(g, gate)
        wlg.inc(g.dma_start(out=routerT_sb[:],
                            in_=io["routerT"].rearrange("(c p) n -> p c n", p=128)), 16)
        wlg.inc(g.dma_start(out=onehot_sb[:], in_=io["onehot"]), 16)
    G(g_loads)

    def sy_loads(sy):
        wait_gate(sy, gate)
        wls.inc(sy.dma_start(out=bout_sb[:], in_=io["bout"]), 16)
    SY(sy_loads)

    psP = ExitStack()
    ps_m2 = [C.psum(psP, f"f_ps_m{i}", [128, 512], F32) for i in range(2)]
    ps_t = C.psum(psP, "f_ps_t", [NEXP, 128], F32)
    ps_m = ps_m2[0]

    for qtr in range(4):
        def g_wo(g, qtr=qtr):
            if qtr == 0:
                wait_gate(g, gate)
            if qtr > 0:
                wmm.wait(g)
            for k in range(2):
                ci = 2 * qtr + k
                wol.inc(g.dma_start(out=wout_sb[k][:], in_=io["wout"][128 * ci:128 * (ci + 1), :]), 16)
        G(g_wo)

        for oc in range(8):
            for c4 in range(4):
                def t_wout(tt, qtr=qtr, oc=oc, c4=c4):
                    if qtr == 0 and oc == 0 and c4 == 0:
                        wait_gate(tt, gate)
                        wlg.wait(tt)
                    wol.wait(tt)
                    wev.wait(tt, lag=1)
                    mi = (qtr * 32 + oc * 4 + c4) % 2
                    for k in range(2):
                        ci = 2 * qtr + k
                        wmm.inc(tt.matmul(ps_m2[mi][:, :], wout_sb[k][:, 128 * oc:128 * (oc + 1)],
                                          attnT[ci][:, 512 * c4:512 * (c4 + 1)],
                                          start=(k == 0), stop=(k == 1)))
                T(t_wout)

                def v_wout(v, qtr=qtr, oc=oc, c4=c4):
                    wmm.wait(v)
                    mi = (qtr * 32 + oc * 4 + c4) % 2
                    if qtr == 0:
                        wls.wait(v)
                        wev.inc(v.tensor_scalar(accT[oc][:, 512 * c4:512 * (c4 + 1)],
                                                ps_m2[mi][:, :], bout_sb[:, oc:oc + 1], None, ALU.add))
                    else:
                        wev.inc(v.tensor_tensor(accT[oc][:, 512 * c4:512 * (c4 + 1)],
                                                accT[oc][:, 512 * c4:512 * (c4 + 1)],
                                                ps_m2[mi][:, :], ALU.add))
                VE(v_wout)

    for a in range(NT):
        def t_lg(tt, a=a):
            if a == 0:
                wev.wait(tt)
            lgv.wait(tt)
            for ci in range(8):
                lgm.inc(tt.matmul(ps_m[:, 0:NEXP], xnT[ci][:, 128 * a:128 * (a + 1)],
                                  routerT_sb[:, NEXP * ci:NEXP * (ci + 1)],
                                  start=(ci == 0), stop=(ci == 7)))
        T(t_lg)

        def v_lg(v, a=a):
            lgm.wait(v)
            lgv.inc(v.tensor_copy(logits[:, NEXP * a:NEXP * (a + 1)], ps_m[:, 0:NEXP]))
        VE(v_lg)

    for a in range(NT):
        la = logits[:, NEXP * a:NEXP * (a + 1)]

        def v_top2a(v, la=la):
            lgv.wait(v)
            vd.inc(v.tensor_reduce(tcol[:, 0:1], la, AX.X, ALU.max))
            vd.wait(v)
            vd.inc(v.tensor_scalar(eqb[:], la, tcol[:, 0:1], None, ALU.is_equal))
            vd.wait(v)
            vd.inc(v.scalar_tensor_tensor(mkb[:], eqb[:], -1e30, la, ALU.mult, ALU.add))
            vd.wait(v)
            vd.inc(v.tensor_reduce(tcol[:, 1:2], mkb[:], AX.X, ALU.max))
            vd.wait(v)
            sgd.inc(v.tensor_scalar(geb[:], la, tcol[:, 1:2], None, ALU.is_ge))
        VE(v_top2a)

        def s_sig(s, la=la):
            sgd.wait(s)
            tpw.wait(s, lag=1)
            sgd.inc(s.activation(sgb[:], la, AF.Sigmoid))
        SC(s_sig)

        def v_wfull(v):
            sgd.wait(v)
            tpw.inc(v.tensor_tensor(wfull[:], sgb[:], geb[:], ALU.mult))
        VE(v_wfull)

        def t_wT(tt):
            tpw.wait(tt)
            twe.wait(tt)
            tpw.inc(tt.transpose(ps_t[:, :], wfull[:, :], C.ident_f32[:]))
        T(t_wT)

        def v_wTe(v, a=a):
            tpw.wait(v)
            twe.inc(v.tensor_copy(wT_sb[:, 128 * a:128 * (a + 1)], ps_t[:, :]))
        VE(v_wTe)

    psP.close()

    esP3 = ExitStack()
    gu2 = [C.sb(esP3, f"f_gu2_{i}", [128, 2048], BF16) for i in range(2)]
    psX = ExitStack()
    ps_ga = C.psum(psX, "f_ps_ga", [128, 2048], F32)
    ps_gb = C.psum(psX, "f_ps_gb", [128, 2048], F32)

    def emit_expert(e):
        def g_dn(g, e=e):
            if e == 0:
                wmm.wait(g)
            dnm.wait(g)
            for ci in range(8):
                dnl[ci].inc(g.dma_start(out=dn_sb[ci][:],
                                        in_=io["dn_w"][e, 128 * ci:128 * (ci + 1), :]), 16)
        G(g_dn)

        for c4 in range(4):
            def t_wb(tt, e=e, c4=c4):
                if e == 0 and c4 == 0:
                    twe.wait(tt)
                    wev.wait(tt)
                hbd.wait(tt)
                wbe.wait(tt)
                dne.wait(tt)
                wbm.inc(tt.matmul(ps_gb[:, 512 * c4:512 * (c4 + 1)],
                                  onehot_sb[:, 128 * e:128 * (e + 1)],
                                  wT_sb[:, 512 * c4:512 * (c4 + 1)], start=True, stop=True))
            T(t_wb)

        def v_wb(v):
            wbm.wait(v)
            wbe.inc(v.tensor_copy(wbc[:], ps_gb[:, :]))
        VE(v_wb)

        for g in range(8):
            wi = g % 2

            def g_gu(g_, e=e, g=g, wi=wi):
                if e == 0 and g < 2:
                    wmm.wait(g_)
                umm.wait(g_, lag=1)
                gul[wi].inc(g_.dma_start(out=gu2[wi][:], in_=io["gup_w"][e, g]), 16)
            G(g_gu)

            def t_gate(tt, e=e, g=g, wi=wi):
                gul[wi].wait(tt)
                sil.wait(tt)
                if g == 0:
                    wbe.wait(tt)
                    dne.wait(tt)
                for ci in range(8):
                    for c4 in range(4):
                        gmm.inc(tt.matmul(ps_ga[:, 512 * c4:512 * (c4 + 1)],
                                          gu2[wi][:, 128 * ci:128 * (ci + 1)],
                                          xnT[ci][:, 512 * c4:512 * (c4 + 1)],
                                          start=(ci == 0), stop=(ci == 7)))
            T(t_gate)

            def s_silu(s, g=g):
                hbd.wait(s)
                for c4 in range(4):
                    gmm.wait(s, lag=3 - c4)
                    inst = s.activation(silu_sb[:, 512 * c4:512 * (c4 + 1)],
                                        ps_ga[:, 512 * c4:512 * (c4 + 1)], AF.Silu)
                sil.inc(inst)
            SC(s_silu)

            def t_up(tt, e=e, g=g, wi=wi):
                hbd.wait(tt)
                if g == 0:
                    wbe.wait(tt)
                for ci in range(8):
                    for c4 in range(4):
                        inst = tt.matmul(ps_gb[:, 512 * c4:512 * (c4 + 1)],
                                         gu2[wi][:, 1024 + 128 * ci:1024 + 128 * (ci + 1)],
                                         xnT[ci][:, 512 * c4:512 * (c4 + 1)],
                                         start=(ci == 0), stop=(ci == 7))
                umm.inc(inst)
            T(t_up)

            def v_h(v, e=e, g=g):
                umm.wait(v)
                sil.wait(v)
                wbe.wait(v)
                dnm.wait(v)
                hbd.inc(v.tensor_tensor(h_sb[g][:], ps_gb[:, :], silu_sb[:], ALU.mult))
                hbd.wait(v)
                hcd.inc(v.tensor_tensor(h_sb[g][:], h_sb[g][:], wbc[:], ALU.mult))
            VE(v_h)

        for oc in range(8):
            pbank = ps_ga if oc % 2 == 0 else ps_gb

            def t_dn(tt, e=e, oc=oc, pbank=pbank):
                hcd.wait(tt)
                dnl[0].wait(tt)
                if oc >= 2:
                    dne.wait(tt, lag=4)
                for hc in range(8):
                    for c4 in range(4):
                        inst = tt.matmul(pbank[:, 512 * c4:512 * (c4 + 1)],
                                         dn_sb[hc][:, 128 * oc:128 * (oc + 1)],
                                         h_sb[hc][:, 512 * c4:512 * (c4 + 1)],
                                         start=(hc == 0), stop=(hc == 7))
                        if hc == 7:
                            dnm.inc(inst)
            T(t_dn)

            for c4 in range(4):
                def v_dn(v, oc=oc, c4=c4, pbank=pbank):
                    dnm.wait(v, lag=3 - c4)
                    dne.inc(v.tensor_tensor(accT[oc][:, 512 * c4:512 * (c4 + 1)],
                                            accT[oc][:, 512 * c4:512 * (c4 + 1)],
                                            pbank[:, 512 * c4:512 * (c4 + 1)], ALU.add))
                VE(v_dn)

    for e in range(NEXP):
        emit_expert(e)

    for sh in range(2):
        for gl in range(8):
            gidx = 8 * sh + gl
            wi = gidx % 2

            def g_ws(g, gidx=gidx, wi=wi):
                umm.wait(g, lag=1)
                gul[wi].inc(g.dma_start(out=gu2[wi][:], in_=io["sgsu"][gidx]), 16)
            G(g_ws)

            def g_sd(g, gidx=gidx, gl=gl):
                dnm.wait(g)
                sdl[gl].inc(g.dma_start(out=dn_sb[gl][:],
                                        in_=io["sdwT"][128 * gidx:128 * (gidx + 1), :]), 16)
            G(g_sd)

            def t_sg(tt, wi=wi, sh=sh, gl=gl):
                gul[wi].wait(tt)
                sil.wait(tt)
                if sh == 0 and gl == 0:
                    dne.wait(tt)
                for ci in range(8):
                    for c4 in range(4):
                        gmm.inc(tt.matmul(ps_ga[:, 512 * c4:512 * (c4 + 1)],
                                          gu2[wi][:, 128 * ci:128 * (ci + 1)],
                                          xnT[ci][:, 512 * c4:512 * (c4 + 1)],
                                          start=(ci == 0), stop=(ci == 7)))
            T(t_sg)

            def s_ssilu(s):
                hbd.wait(s)
                for c4 in range(4):
                    gmm.wait(s, lag=3 - c4)
                    inst = s.activation(silu_sb[:, 512 * c4:512 * (c4 + 1)],
                                        ps_ga[:, 512 * c4:512 * (c4 + 1)], AF.Silu)
                sil.inc(inst)
            SC(s_ssilu)

            def t_su(tt, wi=wi, sh=sh, gl=gl):
                hbd.wait(tt)
                if sh == 0 and gl == 0:
                    dne.wait(tt)
                for ci in range(8):
                    for c4 in range(4):
                        inst = tt.matmul(ps_gb[:, 512 * c4:512 * (c4 + 1)],
                                         gu2[wi][:, 1024 + 128 * ci:1024 + 128 * (ci + 1)],
                                         xnT[ci][:, 512 * c4:512 * (c4 + 1)],
                                         start=(ci == 0), stop=(ci == 7))
                umm.inc(inst)
            T(t_su)

            def v_t(v, gl=gl):
                umm.wait(v)
                sil.wait(v)
                dnm.wait(v)
                hbd.inc(v.tensor_tensor(h_sb[gl][:], ps_gb[:, :], silu_sb[:], ALU.mult))
            VE(v_t)

        for oc in range(8):
            pbank = ps_ga if oc % 2 == 0 else ps_gb

            def t_sd(tt, oc=oc, pbank=pbank):
                hbd.wait(tt)
                sdl[0].wait(tt)
                if oc >= 2:
                    dne.wait(tt, lag=4)
                for gl in range(8):
                    for c4 in range(4):
                        inst = tt.matmul(pbank[:, 512 * c4:512 * (c4 + 1)],
                                         dn_sb[gl][:, 128 * oc:128 * (oc + 1)],
                                         h_sb[gl][:, 512 * c4:512 * (c4 + 1)],
                                         start=(gl == 0), stop=(gl == 7))
                        if gl == 7:
                            dnm.inc(inst)
            T(t_sd)

            for c4 in range(4):
                def v_sd(v, oc=oc, c4=c4, pbank=pbank):
                    dnm.wait(v, lag=3 - c4)
                    dne.inc(v.tensor_tensor(accT[oc][:, 512 * c4:512 * (c4 + 1)],
                                            accT[oc][:, 512 * c4:512 * (c4 + 1)],
                                            pbank[:, 512 * c4:512 * (c4 + 1)], ALU.add))
                VE(v_sd)

    psX.close()
    esP3.close()
    psF = ExitStack()
    ps_f = C.psum(psF, "f_ps_f", [128, 128], F32)
    xbuf = [h_sb[0][:].bitcast(F32), h_sb[1][:].bitcast(F32)]
    x1b = [h_sb[2][:].bitcast(F32), h_sb[3][:].bitcast(F32)]

    for a in range(NT):
        bi = a % 2

        def sy_xa(sy, a=a, bi=bi):
            if a < 2:
                dne.wait(sy)
            finv.wait(sy, lag=8)
            xld[bi].inc(sy.dma_start(out=xbuf[bi], in_=io["x_own"][128 * a:128 * (a + 1), :]), 16)
        SY(sy_xa)

        for ct in range(8):
            def t_f(tt, a=a, ct=ct):
                if a == 0 and ct == 0:
                    dne.wait(tt)
                    hbd.wait(tt)
                finv.wait(tt)
                fint.inc(tt.transpose(ps_f[:, :], accT[ct][:, 128 * a:128 * (a + 1)],
                                      C.ident_f32[:]))
            T(t_f)

            def v_f(v, a=a, bi=bi, ct=ct):
                fint.wait(v)
                xld[bi].wait(v)
                if ct == 0:
                    sto.wait(v, lag=16)
                finv.inc(v.tensor_tensor(x1b[bi][:, 128 * ct:128 * (ct + 1)],
                                         xbuf[bi][:, 128 * ct:128 * (ct + 1)],
                                         ps_f[:, :], ALU.add))
            VE(v_f)

        def sy_o(sy, a=a, bi=bi):
            finv.wait(sy)
            sto.inc(sy.dma_start(out=io["x1_scr"].ap()[128 * a:128 * (a + 1), :], in_=x1b[bi]), 16)
        SY(sy_o)

    psF.close()
    ph.close()
    return [sto.mark(), finv.mark()]



def emit_l1qkv(C, io, xn1T, gate):
    nc = C.nc
    T, VE, SC, SY, G = C.T, C.V, C.S, C.Y, C.G

    ph = ExitStack()
    sb = lambda n, s, d: C.sb(ph, n, s, d)
    w_sb = [sb(f"q_w{i}", [128, 4 * E], BF16) for i in range(8)]
    xbuf = [sb(f"q_xbuf{i}", [128, E], F32) for i in range(4)]
    xsq = [sb(f"q_xsq{i}", [128, E], F32) for i in range(4)]
    stat2 = [sb(f"q_stat{i}", [128, 8], F32) for i in range(4)]
    xn_tmp = [sb(f"q_xn{i}", [128, E], BF16) for i in range(4)]
    qkv_sb = [sb(f"q_qkv{i}", [128, 4 * E], BF16) for i in range(4)]

    gsn = lambda n: C.gs("q_" + n)
    vd = C.gs("a_vd")
    wlg = gsn("wlg")
    lnf, lns, sds, lnx = C.gs("a_lnf"), C.gs("a_lns"), C.gs("a_sds"), C.gs("a_lnx")
    xl = [C.gs("a_xl0"), C.gs("a_xl1")]
    trp, evc = C.gs("a_trp"), C.gs("a_evc")
    qmm, qev, sto = gsn("qmm"), gsn("qev"), gsn("sto")

    def g_loads(g):
        wait_gate(g, gate)
        for ci in range(8):
            wlg.inc(g.dma_start(out=w_sb[ci][:], in_=io["wqkv"][128 * ci:128 * (ci + 1), :]), 16)
    G(g_loads)

    ps_tr = C.psum(ph, "q_ps_tr", [128, 1024], BF16)
    ps_q = [C.psum(ph, f"q_ps_q{i}", [128, 512], F32) for i in range(2)]

    gq = {"n": 0}
    for a in range(NT):
        bi = a % 4
        xi = a % 2

        def sy_x(sy, a=a, bi=bi, xi=xi):
            if a < 2:
                wait_gate(sy, gate)
            lnf.wait(sy, lag=3)
            xl[xi].inc(sy.dma_start(out=xbuf[bi][:],
                                    in_=io["x1_scr"].ap()[128 * a:128 * (a + 1), :]), 16)
        SY(sy_x)

        def v_stats(v, bi=bi, xi=xi):
            stat = stat2[bi]
            xl[xi].wait(v)
            sds.wait(v, lag=3)
            v.tensor_tensor(xsq[bi][:], xbuf[bi][:], xbuf[bi][:], ALU.mult)
            v.tensor_reduce(stat[:, 0:1], xbuf[bi][:], AX.X, ALU.add)
            vd.inc(v.tensor_reduce(stat[:, 1:2], xsq[bi][:], AX.X, ALU.add))
            vd.wait(v)
            vd.inc(v.tensor_scalar(stat[:, 2:3], stat[:, 0:1], 1.0 / E, None, ALU.mult))
            vd.wait(v)
            vd.inc(v.tensor_tensor(stat[:, 3:4], stat[:, 2:3], stat[:, 2:3], ALU.mult))
            v.tensor_scalar(stat[:, 4:5], stat[:, 1:2], 1.0 / E, LN_EPS, ALU.mult, ALU.add)
            vd.wait(v)
            lns.inc(v.tensor_tensor(stat[:, 4:5], stat[:, 4:5], stat[:, 3:4], ALU.subtract))
        VE(v_stats)

        def s_rstd(s, bi=bi):
            stat = stat2[bi]
            lns.wait(s)
            lnx.inc(s.activation(stat[:, 5:6], stat[:, 4:5], AF.Ln))
            lnx.wait(s)
            sds.inc(s.activation(stat[:, 6:7], stat[:, 5:6], AF.Exp, scale=-0.5))
        SC(s_rstd)

        def v_norm(v, bi=bi):
            stat = stat2[bi]
            sds.wait(v)
            trp.wait(v, lag=24)
            lnf.inc(v.tensor_scalar(xn_tmp[bi][:], xbuf[bi][:], stat[:, 2:3], stat[:, 6:7],
                                    ALU.subtract, ALU.mult))
        VE(v_norm)

        def t_trans(tt, bi=bi):
            lnf.wait(tt)
            evc.wait(tt)
            for ct in range(8):
                trp.inc(tt.transpose(ps_tr[:, 128 * ct:128 * (ct + 1)],
                                     xn_tmp[bi][:, 128 * ct:128 * (ct + 1)], C.ident_bf[:]))
        T(t_trans)

        def v_evac(v, a=a):
            trp.wait(v)
            for ct in range(8):
                evc.inc(v.tensor_copy(xn1T[ct][:, 128 * a:128 * (a + 1)],
                                      ps_tr[:, 128 * ct:128 * (ct + 1)]))
        VE(v_evac)

        for oc8 in range(8):
            sl = gq["n"] % 2
            gq["n"] += 1

            def t_q(tt, a=a, oc8=oc8, sl=sl):
                if a == 0 and oc8 == 0:
                    wlg.wait(tt)
                evc.wait(tt)
                qev.wait(tt, lag=1)
                for ci in range(8):
                    qmm.inc(tt.matmul(ps_q[sl][:, :], xn1T[ci][:, 128 * a:128 * (a + 1)],
                                      w_sb[ci][:, 512 * oc8:512 * (oc8 + 1)],
                                      start=(ci == 0), stop=(ci == 7)))
            T(t_q)

            def v_q(v, a=a, bi=bi, oc8=oc8, sl=sl):
                qmm.wait(v)
                if oc8 == 0:
                    sto.wait(v, lag=48)
                qev.inc(v.tensor_copy(qkv_sb[bi][:, 512 * oc8:512 * (oc8 + 1)], ps_q[sl][:, :]))
            VE(v_q)

        def sy_q(sy, a=a, bi=bi):
            qev.wait(sy)
            sto.inc(sy.dma_start(out=io["qkv_scr"].ap()[128 * a:128 * (a + 1), :],
                                 in_=qkv_sb[bi][:]), 16)
        SY(sy_q)

    ph.close()
    return [sto.mark(), evc.mark(), qev.mark()]



def emit_l1fused(C, io, xn1T, gate):
    nc = C.nc
    T, VE, SC, SY, G = C.T, C.V, C.S, C.Y, C.G

    ph = ExitStack()
    sb = lambda n, s, d: C.sb(ph, n, s, d)
    accT = [sb(f"o_accT{i}", [128, TOK], BF16) for i in range(8)]
    scl = sb("o_scl", [128, 5 * 32], F32)
    kv_all = sb("w_kv", [128, 8192], BF16)
    qb = [sb(f"w_qb{i}", [128, 256], BF16) for i in range(2)]
    wb_all = sb("w_wball", [128, 4096], BF16)
    wscr = sb("w_scr", [128, 4096], BF16)
    sc2 = [sb(f"w_sc{i}", [128, 1024], F32) for i in range(2)]
    P4 = [sb(f"w_P{i}", [128, 1024], BF16) for i in range(4)]
    scat4 = [sb(f"w_scat{i}", [128, 1024], BF16) for i in range(4)]
    sums = sb("w_sums", [128, 16], F32)
    rec = sb("w_rec", [128, 16], F32)
    ov = sb("w_ov", [128, 256], F32)
    t1 = wscr[:].bitcast(F32)[:, 0:256]

    gsn = lambda n: C.gs("o_" + n)
    vd = C.gs("a_vd")
    wls = gsn("wls")
    a1l = [gsn("a1l0"), gsn("a1l1")]
    trp, evc = C.gs("a_trp"), C.gs("a_evc")
    wol = gsn("wol")
    wmm, wev = gsn("wmm"), gsn("wev")
    w12l = [gsn(f"w12l{i}") for i in range(2)]
    whl = [gsn(f"whl{i}") for i in range(2)]
    w3l = [gsn(f"w3l{i}") for i in range(2)]
    amm, sil, bmm, uev = gsn("amm"), gsn("sil"), gsn("bmm"), gsn("uev")
    hmm, hev = gsn("hmm"), gsn("hev")
    h2m, h2e = gsn("h2m"), gsn("h2e")
    w3m, w3e = gsn("w3m"), gsn("w3e")
    xld = [C.gs("a_xl0"), C.gs("a_xl1")]
    fint, finv, sto = gsn("fint"), gsn("finv"), gsn("sto")
    kvl, qbl, wbl = gsn("kvl"), gsn("qbl"), gsn("wbl")
    scr_d, exq, wind = gsn("w_scr_d"), gsn("w_exq"), gsn("w_wind")
    pfw = gsn("pfw")
    scw, scd, ccs = gsn("w_scw"), gsn("w_scd"), gsn("w_ccs")

    def sy_loads(sy):
        wait_gate(sy, gate)
        for k, t in enumerate(["w1s", "w2s", "h1s", "h2s"]):
            wls.inc(sy.dma_start(out=scl[:, 32 * k:32 * (k + 1)], in_=io[t]), 16)
        wls.inc(sy.dma_start(out=scl[:, 128:136], in_=io["w3s"]), 16)
    SY(sy_loads)

    psC = ExitStack()
    ps_a2 = [C.psum(psC, f"o_ps_a{i}", [128, 512], F32) for i in range(2)]
    ps_b2 = [C.psum(psC, f"o_ps_b{i}", [128, 512], F32) for i in range(2)]
    u_sb = [C.sb(psC, f"o_u{i}", [128, 512], BF16) for i in range(32)]
    v_sb = [C.sb(psC, f"o_v{i}", [128, 512], BF16) for i in range(32)]
    wbuf = [C.sb(psC, f"o_wbuf{i}", [128, 4096], BF16) for i in range(2)]
    silu2 = [C.sb(psC, f"o_silu{i}", [128, 512], BF16) for i in range(2)]

    qkv = io["qkv_scr"].ap()
    for qt in range(4):
        i = qt
        def sy_kv(sy, i=i):
            if i == 0:
                wait_gate(sy, gate)
            else:
                wind.wait(sy)
                sy.wait_ge(w3e.h, 8 * i)
            for a in range(NT):
                kvl.inc(sy.dma_start(out=kv_all[:, 512 * a:512 * (a + 1)],
                                     in_=qkv[128 * a:128 * (a + 1),
                                             1024 * i + 256:1024 * i + 768]), 16)
            for a in range(NT):
                wbl.inc(sy.dma_start(out=wb_all[:, 256 * a:256 * (a + 1)],
                                     in_=qkv[128 * a:128 * (a + 1),
                                             1024 * i + 768:1024 * (i + 1)]), 16)
        SY(sy_kv)

        def sy_qb(sy, i=i):
            for a in range(NT):
                A = 16 * i + a
                if A - 1 > 0:
                    sy.wait_ge(scr_d.h, A - 1)
                qbl.inc(sy.dma_start(out=qb[a % 2][:],
                                     in_=qkv[128 * a:128 * (a + 1),
                                             1024 * i:1024 * i + 256]), 16)
        SY(sy_qb)

        def sy_pfw(sy, i=i):
            sy.wait_ge(scr_d.h, 16 * (i + 1))
            pfw.inc(sy.dma_start(
                out=kv_all[:].rearrange("p (a c) -> p a c", c=512)[:, :, 0:256],
                in_=io["w3T"][0].rearrange("p (a c) -> p a c", c=256)), 16)
        SY(sy_pfw)

        amm_marks = []
        sil0, uev0 = sil.n, uev.n
        for g in range(32):
            wi = g % 2

            def g_w12(g_, qt=qt, g=g, wi=wi):
                if qt == 0 and g < 2:
                    wait_gate(g_, gate)
                if g < 2:
                    w3m.wait(g_)
                bmm.wait(g_, lag=8)
                if g < 2:
                    for h in range(2):
                        w12l[wi].inc(g_.dma_start(out=wbuf[wi][:, 1024 * h:1024 * (h + 1)],
                                                  in_=io["w12T"][g, :, 1024 * h:1024 * (h + 1)]), 16)
                else:
                    w12l[wi].inc(g_.dma_start(out=wbuf[wi][:, 0:2048], in_=io["w12T"][g]), 16)
            G(g_w12)

            def t_w1(tt, qt=qt, g=g, wi=wi, sil0=sil0):
                w12l[wi].wait(tt)
                if sil0 + g - 1 > 0:
                    tt.wait_ge(sil.h, sil0 + g - 1)
                w3e.wait(tt)
                for ci in range(8):
                    amm.inc(tt.matmul(ps_a2[wi][:, :], wbuf[wi][:, 128 * ci:128 * (ci + 1)],
                                      xn1T[ci][:, 512 * qt:512 * (qt + 1)],
                                      start=(ci == 0), stop=(ci == 7)))
            T(t_w1)
            amm_marks.append(amm.n)

            def t_w2(tt, qt=qt, g=g, wi=wi, sil0=sil0, uev0=uev0):
                if sil0 + g > 0:
                    tt.wait_ge(sil.h, sil0 + g)
                if uev0 + g - 1 > 0:
                    tt.wait_ge(uev.h, uev0 + g - 1)
                for ci in range(8):
                    bmm.inc(tt.matmul(ps_b2[wi][:, :],
                                      wbuf[wi][:, 1024 + 128 * ci:1024 + 128 * (ci + 1)],
                                      xn1T[ci][:, 512 * qt:512 * (qt + 1)],
                                      start=(ci == 0), stop=(ci == 7)))
            T(t_w2)

            def v_u(v, g=g, wi=wi, sil0=sil0):
                bmm.wait(v)
                v.wait_ge(sil.h, sil0 + g + 1)
                hmm.wait(v)
                uev.inc(v.scalar_tensor_tensor(u_sb[g][:], ps_b2[wi][:, :],
                                               scl[:, 32 + g:33 + g],
                                               silu2[wi][:], ALU.mult, ALU.mult))
            VE(v_u)

        def s_si_all(s, qt=qt, marks=tuple(amm_marks), uev0=uev0):
            if qt == 0:
                wls.wait(s)
            for g in range(32):
                s.wait_ge(amm.h, marks[g])
                if uev0 + g - 1 > 0:
                    s.wait_ge(uev.h, uev0 + g - 1)
                sil.inc(s.activation(silu2[g % 2][:], ps_a2[g % 2][:, :], AF.Silu,
                                     scale=scl[:, g:g + 1]))
        SC(s_si_all)

        for a in range(NT):
            g4, al = a // 4, a % 4

            def v_sc(v, i=i, a=a, g4=g4, al=al):
                A = 16 * i + a
                b = g4 % 2
                kvl.wait(v)
                v.wait_ge(qbl.h, 16 * (A + 1))
                if al == 0 and 4 * i + g4 - 1 > 0:
                    v.wait_ge(exq.h, 4 * i + g4 - 1)
                q_ap = qb[a % 2][:].rearrange("p (u d) -> p u d", d=16) \
                    .unsqueeze(1).broadcast_to((128, 16, 16, 16))
                k_ap = kv_all[:].rearrange("p (ap c) -> p ap c", c=512)[:, :, 0:256] \
                    .rearrange("p ap (u d) -> p ap u d", d=16)
                o_ap = wscr[:].rearrange("p (ap u d) -> p ap u d", u=16, d=16)
                vd.inc(v.tensor_tensor(o_ap, q_ap, k_ap, ALU.mult))
                vd.wait(v)
                scr_d.inc(v.tensor_reduce(sc2[b][:, 256 * al:256 * (al + 1)],
                                          wscr[:].rearrange("p (au d) -> p au d", d=16),
                                          AX.X, ALU.add))
            VE(v_sc)

        hmm_marks = []
        hev0, h2m0 = hev.n, h2m.n
        for g in range(32):
            wi = g % 2

            def g_wh(g_, g=g, wi=wi):
                if g < 2:
                    bmm.wait(g_)
                hmm.wait(g_, lag=32)
                if g < 2:
                    for h in range(2):
                        whl[wi].inc(g_.dma_start(out=wbuf[wi][:, 2048 * h:2048 * (h + 1)],
                                                 in_=io["h1T"][g, :, 2048 * h:2048 * (h + 1)]), 16)
                else:
                    whl[wi].inc(g_.dma_start(out=wbuf[wi][:], in_=io["h1T"][g]), 16)
            G(g_wh)

            def t_h1(tt, g=g, wi=wi, hev0=hev0):
                whl[wi].wait(tt)
                uev.wait(tt)
                if hev0 + g - 1 > 0:
                    tt.wait_ge(hev.h, hev0 + g - 1)
                for ci in range(32):
                    hmm.inc(tt.matmul(ps_a2[wi][:, :], wbuf[wi][:, 128 * ci:128 * (ci + 1)],
                                      u_sb[ci][:, :], start=(ci == 0), stop=(ci == 31)))
            T(t_h1)
            hmm_marks.append(hmm.n)

        def s_g1_all(s, marks=tuple(hmm_marks), h2m0=h2m0):
            if h2m0 > 0:
                s.wait_ge(h2m.h, h2m0)
            for g in range(32):
                s.wait_ge(hmm.h, marks[g])
                hev.inc(s.activation(v_sb[g][:], ps_a2[g % 2][:, :], AF.Gelu,
                                     scale=scl[:, 64 + g:65 + g]))
        SC(s_g1_all)

        def s_expw(s, i=i):
            for g4 in range(4):
                s.wait_ge(scr_d.h, 16 * i + 4 * (g4 + 1))
                t = 16 * (i - 1) + 4 * (g4 + 1)
                if i > 0 and t > 0:
                    s.wait_ge(wind.h, t)
                exq.inc(s.activation(P4[g4][:], sc2[g4 % 2][:], AF.Exp, scale=0.25))
        SC(s_expw)

        for a in range(NT):
            g4, al = a // 4, a % 4

            def v_av(v, i=i, a=a, g4=g4, al=al):
                A = 16 * i + a
                Gw = 4 * i + g4
                b = g4 % 2
                v.wait_ge(exq.h, 4 * i + g4 + 1)
                v.wait_ge(wbl.h, 256 * (i + 1))
                if al == 0 and 16 * (Gw - 3) > 0:
                    v.wait_ge(scw.h, 16 * (Gw - 3))
                pv = P4[g4][:, 256 * al:256 * (al + 1)].rearrange("p (ap u) -> p u ap", ap=16)
                vd.inc(v.tensor_reduce(sums[:], pv, AX.X, ALU.add))
                vd.wait(v)
                vd.inc(v.reciprocal(rec[:], sums[:]))
                vd.wait(v)
                p_ap = pv.unsqueeze(2).broadcast_to((128, 16, 16, 16))
                v_ap = kv_all[:].rearrange("p (ap c) -> p ap c", c=512)[:, :, 256:512] \
                    .rearrange("p ap (u d) -> p u d ap", d=16)
                t_ap = wscr[:].rearrange("p (u d ap) -> p u d ap", d=16, ap=16)
                vd.inc(v.tensor_tensor(t_ap, p_ap, v_ap, ALU.mult))
                vd.wait(v)
                vd.inc(v.tensor_reduce(ov[:], wscr[:].rearrange("p (ud ap) -> p ud ap", ap=16),
                                       AX.X, ALU.add))
                vd.wait(v)
                r_ap = rec[:].unsqueeze(2).broadcast_to((128, 16, 16))
                vd.inc(v.tensor_tensor(t1.rearrange("p (u d) -> p u d", d=16),
                                       ov[:].rearrange("p (u d) -> p u d", d=16), r_ap, ALU.mult))
                vd.wait(v)
                wind.inc(v.tensor_tensor(scat4[g4][:, 256 * al:256 * (al + 1)], t1,
                                         wb_all[:, 256 * a:256 * (a + 1)], ALU.mult))
            VE(v_av)

        h2m_marks = []
        h2e0, w3m0 = h2e.n, w3m.n
        for g in range(32):
            wi = g % 2

            def g_wh2(g_, g=g, wi=wi):
                if g < 2:
                    hmm.wait(g_)
                h2m.wait(g_, lag=32)
                if g < 2:
                    for h in range(2):
                        whl[wi].inc(g_.dma_start(out=wbuf[wi][:, 2048 * h:2048 * (h + 1)],
                                                 in_=io["h2T"][g, :, 2048 * h:2048 * (h + 1)]), 16)
                else:
                    whl[wi].inc(g_.dma_start(out=wbuf[wi][:], in_=io["h2T"][g]), 16)
            G(g_wh2)

            def t_h2(tt, g=g, wi=wi, h2e0=h2e0):
                whl[wi].wait(tt)
                hev.wait(tt)
                if h2e0 + g - 1 > 0:
                    tt.wait_ge(h2e.h, h2e0 + g - 1)
                for ci in range(32):
                    h2m.inc(tt.matmul(ps_b2[wi][:, :], wbuf[wi][:, 128 * ci:128 * (ci + 1)],
                                      v_sb[ci][:, :], start=(ci == 0), stop=(ci == 31)))
            T(t_h2)
            h2m_marks.append(h2m.n)

        def s_g2_all(s, marks=tuple(h2m_marks), w3m0=w3m0):
            if w3m0 > 0:
                s.wait_ge(w3m.h, w3m0)
            for g in range(32):
                s.wait_ge(h2m.h, marks[g])
                h2e.inc(s.activation(u_sb[g][:], ps_b2[g % 2][:, :], AF.Gelu,
                                     scale=scl[:, 96 + g:97 + g]))
        SC(s_g2_all)


        w3e0 = w3e.n
        for oc in range(8):
            wi = oc % 2

            if oc > 0:
                def g_w3(g_, oc=oc, wi=wi):
                    if oc < 2:
                        h2m.wait(g_)
                    w3m.wait(g_, lag=32)
                    if oc < 2:
                        for h in range(2):
                            w3l[wi].inc(g_.dma_start(out=wbuf[wi][:, 2048 * h:2048 * (h + 1)],
                                                     in_=io["w3T"][oc, :, 2048 * h:2048 * (h + 1)]), 16)
                    else:
                        w3l[wi].inc(g_.dma_start(out=wbuf[wi][:], in_=io["w3T"][oc]), 16)
                G(g_w3)

            def t_w3(tt, qt=qt, oc=oc, wi=wi, w3e0=w3e0):
                if oc == 0:
                    tt.wait_ge(pfw.h, 16 * (qt + 1))
                else:
                    w3l[wi].wait(tt)
                h2e.wait(tt)
                if w3e0 + oc - 1 > 0:
                    tt.wait_ge(w3e.h, w3e0 + oc - 1)
                for ci in range(32):
                    w_ap = (kv_all[:, 512 * (ci // 2) + 128 * (ci % 2):
                                   512 * (ci // 2) + 128 * (ci % 2) + 128]
                            if oc == 0 else wbuf[wi][:, 128 * ci:128 * (ci + 1)])
                    w3m.inc(tt.matmul(ps_a2[wi][:, :], w_ap,
                                      u_sb[ci][:, :], start=(ci == 0), stop=(ci == 31)))
            T(t_w3)

            def v_w3(v, qt=qt, oc=oc, wi=wi):
                w3m.wait(v)
                w3e.inc(v.tensor_scalar(accT[oc][:, 512 * qt:512 * (qt + 1)],
                                        ps_a2[wi][:, :], scl[:, 128 + oc:129 + oc],
                                        None, ALU.mult))
            VE(v_w3)

        for g4 in range(4):
            def sy_sc(sy, i=i, g4=g4):
                Gw = 4 * i + g4
                sy.wait_ge(wind.h, 16 * i + 4 * (g4 + 1))
                if 64 * (Gw - 3) > 0:
                    sy.wait_ge(scd.h, 64 * (Gw - 3))
                scw.inc(sy.dma_start(out=io["scr"].ap()[:, 1024 * g4:1024 * (g4 + 1)],
                                     in_=scat4[g4][:]), 16)
                sy.wait_ge(scw.h, 16 * (Gw + 1))
                for al in range(4):
                    a = 4 * g4 + al
                    blk = 2 * i + (a % 4) // 2
                    rowb = ((a // 4) * 2 + (a % 2)) * 32
                    dst = io["a2a_in"].ap()[256 * blk + rowb:256 * blk + rowb + 32, :] \
                        .rearrange("t (z w) -> t z w", z=4)
                    srcv = io["scr"].ap()[:, 256 * a:256 * (a + 1)].rearrange("(t z) w -> t z w", z=4)
                    scd.inc(sy.dma_start(out=dst, in_=srcv), 16)
            SY(sy_sc)

    def g_cc(g):
        scd.wait(g)
        with allow_ici_all_to_all():
            ccs.inc(g.collective_compute(
                "AllToAll", mybir.AluOpType.bypass,
                replica_groups=[list(range(NCORES))],
                ins=[io["a2a_in"].ap().opt()],
                outs=[io["a2a_out"].ap().opt()],
            ))
    G(g_cc)

    psC.close()
    esA = ExitStack()
    a1T = [C.sb(esA, f"o_a1T{i}", [128, TOK], BF16) for i in range(8)]
    abuf = [C.sb(esA, f"o_abuf{i}", [128, E], BF16) for i in range(2)]
    wo_sb = [C.sb(esA, f"o_wo{i}", [128, E], BF16) for i in range(2)]
    psA = ExitStack()
    ps_tr = C.psum(psA, "o_ps_tr", [128, 1024], BF16)

    for a in range(NT):
        bi = a % 2

        def sy_a1(sy, a=a, bi=bi):
            if a < 2:
                ccs.wait(sy)
                w3e.wait(sy)
            trp.wait(sy, lag=8)
            for al in range(2):
                for gm in range(2):
                    blk = 2 * (a // 4) + gm
                    rowb = ((a % 4) * 2 + al) * 32
                    a1l[bi].inc(sy.dma_start(
                        out=abuf[bi][64 * al + 32 * gm:64 * al + 32 * gm + 32, :],
                        in_=io["a2a_out"].ap()[256 * blk + rowb:256 * blk + rowb + 32, :]), 16)
        SY(sy_a1)

        def t_tr(tt, a=a, bi=bi):
            a1l[bi].wait(tt)
            evc.wait(tt)
            for ct in range(8):
                trp.inc(tt.transpose(ps_tr[:, 128 * ct:128 * (ct + 1)],
                                     abuf[bi][:, 128 * ct:128 * (ct + 1)], C.ident_bf[:]))
        T(t_tr)

        def v_ev(v, a=a):
            trp.wait(v)
            for ct in range(8):
                evc.inc(v.tensor_copy(a1T[ct][:, 128 * a:128 * (a + 1)],
                                      ps_tr[:, 128 * ct:128 * (ct + 1)]))
        VE(v_ev)

    psB = ExitStack()
    ps_m2 = [C.psum(psB, f"o_ps_m{i}", [128, 512], F32) for i in range(2)]

    for qtr in range(4):
        def g_wo(g, qtr=qtr):
            if qtr == 0:
                ccs.wait(g)
            if qtr > 0:
                wmm.wait(g)
            for k in range(2):
                ci = 2 * qtr + k
                wol.inc(g.dma_start(out=wo_sb[k][:], in_=io["w_out"][128 * ci:128 * (ci + 1), :]), 16)
        G(g_wo)

        for oc in range(8):
            for c4 in range(4):
                def t_wo(tt, qtr=qtr, oc=oc, c4=c4):
                    if qtr == 0 and oc == 0 and c4 == 0:
                        evc.wait(tt)
                    wol.wait(tt)
                    wev.wait(tt, lag=1)
                    mi = (qtr * 32 + oc * 4 + c4) % 2
                    for k in range(2):
                        ci = 2 * qtr + k
                        wmm.inc(tt.matmul(ps_m2[mi][:, :], wo_sb[k][:, 128 * oc:128 * (oc + 1)],
                                          a1T[ci][:, 512 * c4:512 * (c4 + 1)],
                                          start=(k == 0), stop=(k == 1)))
                T(t_wo)

                def v_wo(v, qtr=qtr, oc=oc, c4=c4):
                    wmm.wait(v)
                    if qtr == 0 and oc == 0 and c4 == 0:
                        w3e.wait(v)
                    mi = (qtr * 32 + oc * 4 + c4) % 2
                    wev.inc(v.tensor_tensor(accT[oc][:, 512 * c4:512 * (c4 + 1)],
                                            accT[oc][:, 512 * c4:512 * (c4 + 1)],
                                            ps_m2[mi][:, :], ALU.add))
                VE(v_wo)

    psB.close()
    psA.close()
    esA.close()
    psF = ExitStack()
    ps_f = C.psum(psF, "o_ps_f", [128, 128], BF16)
    xbuf = [C.sb(psF, f"o_xbuf{i}", [128, E], F32) for i in range(2)]
    x1b = [C.sb(psF, f"o_x1b{i}", [128, E], F32) for i in range(2)]

    for a in range(NT):
        bi = a % 2

        def sy_xa(sy, a=a, bi=bi):
            if a < 2:
                w3e.wait(sy)
                wev.wait(sy)
            finv.wait(sy, lag=8)
            xld[bi].inc(sy.dma_start(out=xbuf[bi][:],
                                     in_=io["x1_scr"].ap()[128 * a:128 * (a + 1), :]), 16)
        SY(sy_xa)

        for ct in range(8):
            def t_f(tt, a=a, ct=ct):
                if a == 0 and ct == 0:
                    w3e.wait(tt)
                    wev.wait(tt)
                finv.wait(tt)
                fint.inc(tt.transpose(ps_f[:, :], accT[ct][:, 128 * a:128 * (a + 1)],
                                      C.ident_bf[:]))
            T(t_f)

            def v_f(v, a=a, bi=bi, ct=ct):
                fint.wait(v)
                xld[bi].wait(v)
                if ct == 0:
                    sto.wait(v, lag=16)
                finv.inc(v.tensor_tensor(x1b[bi][:, 128 * ct:128 * (ct + 1)],
                                         xbuf[bi][:, 128 * ct:128 * (ct + 1)],
                                         ps_f[:, :], ALU.add))
            VE(v_f)

        def sy_o(sy, a=a, bi=bi):
            finv.wait(sy)
            sto.inc(sy.dma_start(out=io["out"][128 * a:128 * (a + 1), :], in_=x1b[bi][:]), 16)
        SY(sy_o)

    def sy_fin(sy):
        sto.wait(sy)
    SY(sy_fin)
    psF.close()
    ph.close()



def build_mega():
    nc = bass.Bass("TRN2", target_bir_lowering=False, detect_race_conditions=False)
    dt = lambda n, s, d: nc.dram_tensor(n, s, d, kind="ExternalInput").ap()
    io = dict(
        x_own=dt("x_own", [TOK, E], F32),
        wqa=dt("wqa", [E, QL], F32),
        wqb=dt("wqb", [QL, E], F32),
        wkva=dt("wkva", [E, KVL], F32),
        wkvb_kv=dt("wkvb_kv", [KVL, 512], F32),
        ropek_cos=dt("ropek_cos", [S, 32], F32),
        ropek_sin=dt("ropek_sin", [S, 32], F32),
        ropeq_cos=dt("ropeq_cos", [TOK, 32], F32),
        ropeq_sin=dt("ropeq_sin", [TOK, 32], F32),
        mask0=dt("mask0", [128, 128], F32),
        mask1=dt("mask1", [128, 128], F32),
        ident=dt("ident", [128, 128], F32),
        wout=dt("wout", [E, E], BF16),
        bout=dt("bout", [128, 8], F32),
        routerT=dt("routerT", [E, NEXP], F32),
        gup_w=dt("gup_w", [NEXP, 8, 128, 2048], BF16),
        dn_w=dt("dn_w", [NEXP, E, E], BF16),
        sgsu=dt("sgsu", [16, 128, 2048], BF16),
        sdwT=dt("sdwT", [SHD, E], BF16),
        onehot=dt("onehot", [NEXP, 2048], F32),
        wqkv=dt("wqkv", [E, 4 * E], BF16),
        w_out=dt("w_out", [E, E], BF16),
        w12T=dt("w12T", [32, 128, 2048], BF16),
        h1T=dt("h1T", [32, 128, 4096], BF16),
        h2T=dt("h2T", [32, 128, 4096], BF16),
        w3T=dt("w3T", [8, 128, 4096], BF16),
        w1s=dt("w1s", [128, 32], F32),
        w2s=dt("w2s", [128, 32], F32),
        h1s=dt("h1s", [128, 32], F32),
        h2s=dt("h2s", [128, 32], F32),
        w3s=dt("w3s", [128, 8], F32),
        out=nc.dram_tensor("out", [TOK, E], F32, kind="ExternalOutput").ap(),
        x1_scr=nc.dram_tensor("x1_scr", [TOK, E], F32),
        qkv_scr=nc.dram_tensor("qkv_scr", [TOK, 4 * E], BF16),
        scr=nc.dram_tensor("scr", [128, 4096], BF16),
        a2a_in=nc.dram_tensor("a2a_in", [TOK, E], BF16),
        a2a_out=nc.dram_tensor("a2a_out", [TOK, E], BF16),
        lat_in=nc.dram_tensor("lat_in", [KVL, TOK], BF16),
        lat_out=nc.dram_tensor("lat_out", [2 * KVL, TOK], BF16),
    )

    es = ExitStack()
    with es:
        block = es.enter_context(nc.Block())
        C = Ctx(nc, es, block)
        C.ident_bf = es.enter_context(nc.sbuf_tensor("ident_bf", [128, 128], BF16))
        C.ident_f32 = es.enter_context(nc.sbuf_tensor("ident_f32", [128, 128], F32))

        es_xn = ExitStack()
        xnT = [es_xn.enter_context(nc.sbuf_tensor(f"xnT{i}", [128, TOK], BF16)) for i in range(8)]
        es_at = ExitStack()
        attnT = [es_at.enter_context(nc.sbuf_tensor(f"attnT{i}", [128, TOK], BF16)) for i in range(8)]
        g1 = emit_l0attn(C, io, xnT, attnT)
        es_h = ExitStack()
        io["h_sb"] = [es_h.enter_context(nc.sbuf_tensor(f"h{i}", [128, TOK], BF16)) for i in range(8)]
        g2 = emit_l0ffn(C, io, xnT, attnT, es_at, g1)
        es_h.close()
        es_at.close()
        es_xn.close()

        es_xn1 = ExitStack()
        xn1T = [es_xn1.enter_context(nc.sbuf_tensor(f"xn1T{i}", [128, TOK], BF16)) for i in range(8)]
        g3 = emit_l1qkv(C, io, xn1T, g2)
        emit_l1fused(C, io, xn1T, g3)
        es_xn1.close()
    return nc



def _rope_tables():
    inv = 10000.0 ** (-np.arange(0, D, 2, dtype=np.float64) / D)
    pos = np.arange(S, dtype=np.float64)
    ang = np.outer(pos, inv)
    return np.cos(ang).astype(np.float32), np.sin(ang).astype(np.float32)


def _own_rows(core):
    b, h = core // 2, core % 2
    a = np.arange(NT)
    starts = b * S + 256 * a + 128 * h
    return (starts[:, None] + np.arange(128)[None, :]).reshape(-1)


def _inmaps(inputs):
    x2d = np.asarray(inputs["L0_attn_x"]).reshape(B * S, E)
    cosf, sinf = _rope_tables()
    ident = np.eye(128, dtype=np.float32)
    tri = np.triu(np.ones((128, 128), dtype=np.float32))
    ones = np.ones((128, 128), dtype=np.float32)
    zeros = np.zeros((128, 128), dtype=np.float32)
    wkvb = np.asarray(inputs["L0_attn_Wkvb"])
    wkvb_kv = np.ascontiguousarray(np.concatenate([wkvb[:, 0::2], wkvb[:, 1::2]], axis=1))
    onehot = np.zeros((NEXP, 2048), dtype=np.float32)
    for e in range(NEXP):
        onehot[e, 128 * e:128 * (e + 1)] = 1.0
    rs = lambda s: np.ascontiguousarray(np.asarray(s).reshape(-1, 128).T)
    import ml_dtypes
    bf = lambda a: np.ascontiguousarray(a).astype(ml_dtypes.bfloat16)
    common = dict(
        wqa=np.asarray(inputs["L0_attn_Wqa"]),
        wqb=np.asarray(inputs["L0_attn_Wqb"]),
        wkva=np.asarray(inputs["L0_attn_Wkva"]),
        wkvb_kv=wkvb_kv,
        ropek_cos=cosf, ropek_sin=sinf,
        ident=ident,
        wout=bf(np.asarray(inputs["L0_attn_Wout"])),
        bout=np.ascontiguousarray(np.asarray(inputs["L0_attn_bout"]).reshape(8, 128).T),
        routerT=np.ascontiguousarray(np.asarray(inputs["L0_ffn_router_w"]).T),
        gup_w=bf(np.asarray(inputs["L0_ffn_gate_up_w"])
                 .reshape(NEXP, 8, 128, 16, 128).transpose(0, 3, 2, 1, 4)
                 .reshape(NEXP, 2, 8, 128, 1024).transpose(0, 2, 3, 1, 4)
                 .reshape(NEXP, 8, 128, 2048)),
        dn_w=bf(np.asarray(inputs["L0_ffn_down_w"])),
        sgsu=bf(np.concatenate([
            np.asarray(inputs["L0_ffn_shared_gate_w"]).reshape(16, 128, 8, 128).transpose(0, 3, 2, 1).reshape(16, 128, 1024),
            np.asarray(inputs["L0_ffn_shared_up_w"]).reshape(16, 128, 8, 128).transpose(0, 3, 2, 1).reshape(16, 128, 1024),
        ], axis=2)),
        sdwT=bf(np.asarray(inputs["L0_ffn_shared_down_w"]).T),
        onehot=onehot,
        wqkv=bf(np.asarray(inputs["L1_attn_w_qkvw"])),
        w_out=bf(np.asarray(inputs["L1_attn_w_out"])),
        w12T=bf(np.concatenate([
            np.asarray(inputs["L1_ffn_w1_w"]).reshape(32, 128, 8, 128).transpose(0, 3, 2, 1).reshape(32, 128, 1024),
            np.asarray(inputs["L1_ffn_w2_w"]).reshape(32, 128, 8, 128).transpose(0, 3, 2, 1).reshape(32, 128, 1024),
        ], axis=2)),
        h1T=bf(np.asarray(inputs["L1_ffn_h1_w"])
               .reshape(32, 128, 32, 128).transpose(0, 3, 2, 1).reshape(32, 128, 4096)),
        h2T=bf(np.asarray(inputs["L1_ffn_h2_w"])
               .reshape(32, 128, 32, 128).transpose(0, 3, 2, 1).reshape(32, 128, 4096)),
        w3T=bf(np.asarray(inputs["L1_ffn_w3_w"])
               .reshape(8, 128, 32, 128).transpose(0, 3, 2, 1).reshape(8, 128, 4096)),
        w1s=rs(inputs["L1_ffn_w1_s"]), w2s=rs(inputs["L1_ffn_w2_s"]),
        h1s=rs(inputs["L1_ffn_h1_s"]), h2s=rs(inputs["L1_ffn_h2_s"]),
        w3s=rs(inputs["L1_ffn_w3_s"]),
    )
    maps = []
    for c in range(NCORES):
        b, h = c // 2, c % 2
        rows = _own_rows(c)
        m = dict(common)
        m.update(
            x_own=np.ascontiguousarray(x2d[rows]),
            ropeq_cos=np.ascontiguousarray(cosf[rows - b * S]),
            ropeq_sin=np.ascontiguousarray(sinf[rows - b * S]),
            mask0=(tri if h == 0 else ones),
            mask1=(zeros if h == 0 else tri),
        )
        maps.append(m)
    return maps


def kernel(**inputs):
    cores = list(range(NCORES))
    nc = build_mega()
    res = run_bass_kernel_spmd(nc, _inmaps(inputs), core_ids=cores)
    out = np.empty((B * S, E), dtype=np.float32)
    for c in cores:
        out[_own_rows(c)] = res.results[c]["out"]
    return out.reshape(B, S, E)


# version 30
# speedup vs baseline: 1.0443x; 1.0016x over previous
import numpy as np
from contextlib import ExitStack

import concourse.bass as bass
import concourse.mybir as mybir
from concourse.bass_utils import run_bass_kernel_spmd
from concourse.aot_env import allow_ici_all_to_all

F32 = mybir.dt.float32
BF16 = mybir.dt.bfloat16
AF = mybir.ActivationFunctionType
ALU = mybir.AluOpType
AX = mybir.AxisListType

B, S, E = 4, 4096, 1024
NCORES = 8
NT = 16
TOK = 2048
H, KVH, D = 16, 4, 64
QL, KVL = 64, 32
NEXP = 16
SHD = 2048
DHID = 4096
LN_EPS = 1e-5


class Sem:
    def __init__(self, nc, es, name):
        self.h = es.enter_context(nc.semaphore(name))
        self.n = 0

    def inc(self, inst, k=1):
        inst.then_inc(self.h, k)
        self.n += k
        return inst

    def wait(self, eng, lag=0):
        v = self.n - lag
        if v > 0:
            eng.wait_ge(self.h, v)

    def mark(self):
        return (self.h, self.n)


class Ctx:
    def __init__(self, nc, es, block):
        self.nc, self.es, self.block = nc, es, block
        self.T, self.V, self.S, self.Y, self.G = (
            block.tensor, block.vector, block.scalar, block.sync, block.gpsimd)
        self._sems = {}

    def gs(self, name):
        if name not in self._sems:
            self._sems[name] = Sem(self.nc, self.es, name)
        return self._sems[name]

    def sb(self, stack, name, shape, dt):
        return stack.enter_context(self.nc.sbuf_tensor(name, shape, dt))

    def psum(self, stack, name, shape, dt):
        return stack.enter_context(self.nc.psum_tensor(name, shape, dt))


def wait_gate(eng, gate):
    for (h, v) in gate:
        if v > 0:
            eng.wait_ge(h, v)



def emit_l0attn(C, io, xnT, attnT):
    nc, es = C.nc, C.es
    T, VE, SC, SY, G = C.T, C.V, C.S, C.Y, C.G
    x_own = io["x_own"]

    ph = ExitStack()
    sb = lambda n, s, d: C.sb(ph, n, s, d)
    KT = [sb(f"KT{i}", [128, S], BF16) for i in range(2)]
    Vt = sb("Vt", [128, 32 * 260], BF16)
    QT = [sb(f"QT{i}", [128, TOK], BF16) for i in range(8)]
    latT = sb("latT", [KVL, S], BF16)
    qlatT = sb("qlatT", [QL, TOK], BF16)
    wqa_sb = sb("wqa_sb", [128, 8 * QL], BF16)
    wqb_sb = sb("wqb_sb", [QL, E], BF16)
    wkva_sb = sb("wkva_sb", [128, 8 * KVL], BF16)
    wkvb_sb = sb("wkvb_sb", [KVL, 512], BF16)
    mask_sb = [sb(f"mask{i}_sb", [128, 128], BF16) for i in range(2)]
    ones64 = sb("ones64", [1, 64], F32)
    ropek_c = sb("ropek_c", [128, 32 * 32], F32)
    ropek_s = sb("ropek_s", [128, 32 * 32], F32)
    ropeq_c = sb("ropeq_c", [128, 16 * 32], F32)
    ropeq_s = sb("ropeq_s", [128, 16 * 32], F32)
    xbuf = [sb(f"xbuf{i}", [128, E], F32) for i in range(2)]
    xsq = [sb(f"xsq{i}", [128, E], F32) for i in range(2)]
    stat2 = [sb(f"stat{i}", [128, 8], F32) for i in range(2)]
    xn_tmp = [sb(f"xn_tmp{i}", [128, E], BF16) for i in range(2)]
    lat_own = sb("lat_own", [KVL, TOK], BF16)
    kq_tmp = [sb(f"kq_tmp{i}", [128, E], BF16) for i in range(2)]
    rtA = sb("rtA", [128, 512], F32)
    rtB = sb("rtB", [128, 512], F32)
    expS = [sb(f"expS{i}", [128, 512], BF16) for i in range(4)]
    recb = [sb(f"recb{i}", [64, 512], F32) for i in range(2)]
    rrow = [sb(f"rrow{i}", [1, 512], F32) for i in range(2)]

    gsn = lambda n: C.gs("a_" + n)
    vd = gsn("vd")
    wl, wls = gsn("wl"), gsn("wls")
    xl = [gsn("xl0"), gsn("xl1")]
    lnf, trp, evc = gsn("lnf"), gsn("trp"), gsn("evc")
    lns, sds, lnx = gsn("lns"), gsn("sds"), gsn("lnx")
    latm, late = gsn("latm"), gsn("late")
    lsto, agd, latl = gsn("lsto"), gsn("agd"), gsn("latl")
    kvm, roped, kqt, kqe = gsn("kvm"), gsn("roped"), gsn("kqt"), gsn("kqe")
    qlm, qle, qmm = gsn("qlm"), gsn("qle"), gsn("qmm")
    scm, exd, mkd, avd = gsn("scm"), gsn("exd"), gsn("mkd"), gsn("avd")
    nr1, nr2, nr3 = gsn("nr1"), gsn("nr2"), gsn("nr3")

    def g_loads(g):
        wl.inc(g.dma_start(out=wqa_sb[:], in_=io["wqa"].rearrange("(c p) n -> p c n", p=128)), 16)
        wl.inc(g.dma_start(out=wqb_sb[:], in_=io["wqb"]), 16)
        wl.inc(g.dma_start(out=wkva_sb[:], in_=io["wkva"].rearrange("(c p) n -> p c n", p=128)), 16)
        wl.inc(g.dma_start(out=wkvb_sb[:], in_=io["wkvb_kv"]), 16)
        wl.inc(g.dma_start(out=C.ident_bf[:], in_=io["ident"]), 16)
        wl.inc(g.dma_start(out=mask_sb[0][:], in_=io["mask0"]), 16)
        wl.inc(g.dma_start(out=mask_sb[1][:], in_=io["mask1"]), 16)
    G(g_loads)

    def sy_loads(sy):
        wls.inc(sy.dma_start(out=C.ident_f32[:], in_=io["ident"]), 16)
        wls.inc(sy.dma_start(out=ropek_c[:].rearrange("p (t f) -> p t f", f=32),
                             in_=io["ropek_cos"].rearrange("(t p) f -> p t f", p=128)), 16)
        wls.inc(sy.dma_start(out=ropek_s[:].rearrange("p (t f) -> p t f", f=32),
                             in_=io["ropek_sin"].rearrange("(t p) f -> p t f", p=128)), 16)
        wls.inc(sy.dma_start(out=ropeq_c[:].rearrange("p (t f) -> p t f", f=32),
                             in_=io["ropeq_cos"].rearrange("(t p) f -> p t f", p=128)), 16)
        wls.inc(sy.dma_start(out=ropeq_s[:].rearrange("p (t f) -> p t f", f=32),
                             in_=io["ropeq_sin"].rearrange("(t p) f -> p t f", p=128)), 16)
    SY(sy_loads)

    def v_init(v):
        wl.wait(v)
        wls.wait(v)
        v.memset(ones64[:], 1.0)
        lnf.inc(v.memset(
            Vt[:].rearrange("p (t z w) -> p t z w", z=4, w=65)[:, :, :, 64:65], 1.0))
    VE(v_init)

    def emit_ln_tile(src_ap, dst_bf_ap, ld_sem, bi, extra_war=None):
        stat = stat2[bi]
        xq = xsq[bi]

        def v_stats(v):
            ld_sem.wait(v)
            sds.wait(v, lag=1)
            v.tensor_tensor(xq[:], src_ap, src_ap, ALU.mult)
            v.tensor_reduce(stat[:, 0:1], src_ap, AX.X, ALU.add)
            vd.inc(v.tensor_reduce(stat[:, 1:2], xq[:], AX.X, ALU.add))
            vd.wait(v)
            vd.inc(v.tensor_scalar(stat[:, 2:3], stat[:, 0:1], 1.0 / E, None, ALU.mult))
            vd.wait(v)
            vd.inc(v.tensor_tensor(stat[:, 3:4], stat[:, 2:3], stat[:, 2:3], ALU.mult))
            v.tensor_scalar(stat[:, 4:5], stat[:, 1:2], 1.0 / E, LN_EPS, ALU.mult, ALU.add)
            vd.wait(v)
            lns.inc(v.tensor_tensor(stat[:, 4:5], stat[:, 4:5], stat[:, 3:4], ALU.subtract))
        VE(v_stats)

        def s_rstd(s):
            lns.wait(s)
            lnx.inc(s.activation(stat[:, 5:6], stat[:, 4:5], AF.Ln))
            lnx.wait(s)
            sds.inc(s.activation(stat[:, 6:7], stat[:, 5:6], AF.Exp, scale=-0.5))
        SC(s_rstd)

        def v_norm(v):
            sds.wait(v)
            if extra_war is not None:
                extra_war(v)
            lnf.inc(v.tensor_scalar(dst_bf_ap, src_ap, stat[:, 2:3], stat[:, 6:7],
                                    ALU.subtract, ALU.mult))
        VE(v_norm)

    psA = ExitStack()
    ps_tr = C.psum(psA, "ps_tr", [128, 1024], BF16)
    ps_lat = C.psum(psA, "ps_lat", [KVL, 128], F32)

    for a in range(NT):
        bi = a % 2

        def sy_x(sy, a=a, bi=bi):
            lnf.wait(sy, lag=1)
            xl[bi].inc(sy.dma_start(out=xbuf[bi][:], in_=x_own[128 * a:128 * (a + 1), :]), 16)
        SY(sy_x)

        emit_ln_tile(xbuf[bi][:], xn_tmp[bi][:], xl[bi], bi,
                     extra_war=(lambda v: trp.wait(v, lag=8)))

        def t_trans(tt, bi=bi):
            lnf.wait(tt)
            evc.wait(tt)
            for ct in range(8):
                trp.inc(tt.transpose(ps_tr[:, 128 * ct:128 * (ct + 1)],
                                     xn_tmp[bi][:, 128 * ct:128 * (ct + 1)], C.ident_bf[:]))
        T(t_trans)

        def v_evac(v, a=a):
            trp.wait(v)
            for ct in range(8):
                evc.inc(v.tensor_copy(xnT[ct][:, 128 * a:128 * (a + 1)],
                                      ps_tr[:, 128 * ct:128 * (ct + 1)]))
        VE(v_evac)

        def t_lat(tt, a=a):
            evc.wait(tt)
            late.wait(tt)
            for ct in range(8):
                latm.inc(tt.matmul(ps_lat[:, :], wkva_sb[:, KVL * ct:KVL * (ct + 1)],
                                   xnT[ct][:, 128 * a:128 * (a + 1)],
                                   start=(ct == 0), stop=(ct == 7)))
        T(t_lat)

        def v_lat(v, a=a):
            latm.wait(v)
            late.inc(v.tensor_copy(lat_own[:, 128 * a:128 * (a + 1)], ps_lat[:, :]))
        VE(v_lat)

    def sy_lat(sy):
        late.wait(sy)
        lsto.inc(sy.dma_start(out=io["lat_in"].ap(), in_=lat_own[:]), 16)
    SY(sy_lat)

    def g_ag(g):
        lsto.wait(g)
        with allow_ici_all_to_all():
            agd.inc(g.collective_compute(
                "AllGather", mybir.AluOpType.bypass,
                replica_groups=[[0, 1], [2, 3], [4, 5], [6, 7]],
                ins=[io["lat_in"].ap().opt()],
                outs=[io["lat_out"].ap().opt()],
            ))
    G(g_ag)

    def sy_latl(sy):
        agd.wait(sy)
        lo = io["lat_out"].ap()
        latl.inc(sy.dma_start(
            out=latT[:].rearrange("l (t two c) -> l t two c", two=2, c=128)[:, :, 0, :],
            in_=lo[0:KVL, :].rearrange("l (t c) -> l t c", c=128)), 16)
        latl.inc(sy.dma_start(
            out=latT[:].rearrange("l (t two c) -> l t two c", two=2, c=128)[:, :, 1, :],
            in_=lo[KVL:2 * KVL, :].rearrange("l (t c) -> l t c", c=128)), 16)
    SY(sy_latl)

    psA.close()
    psB = ExitStack()
    ps_ql = C.psum(psB, "ps_ql", [QL, 512], F32)
    ps_q = C.psum(psB, "ps_q", [128, 1024], F32)
    ps_qt = C.psum(psB, "ps_qt", [64, 256], BF16)

    for qc in range(4):
        def t_ql(tt, qc=qc):
            qle.wait(tt)
            for ct in range(8):
                qlm.inc(tt.matmul(ps_ql[:, :], wqa_sb[:, QL * ct:QL * (ct + 1)],
                                  xnT[ct][:, 512 * qc:512 * (qc + 1)],
                                  start=(ct == 0), stop=(ct == 7)))
        T(t_ql)

        def v_ql(v, qc=qc):
            qlm.wait(v)
            qle.inc(v.tensor_copy(qlatT[:, 512 * qc:512 * (qc + 1)], ps_ql[:, :]))
        VE(v_ql)

    for a in range(NT):
        bi = a % 2

        def t_q(tt, a=a):
            qle.wait(tt)
            roped.wait(tt)
            qmm.inc(tt.matmul(ps_q[:, 0:512], qlatT[:, 128 * a:128 * (a + 1)],
                              wqb_sb[:, 0:512], start=True, stop=True))
            qmm.inc(tt.matmul(ps_q[:, 512:1024], qlatT[:, 128 * a:128 * (a + 1)],
                              wqb_sb[:, 512:1024], start=True, stop=True))
        T(t_q)

        def v_ropeq(v, a=a, bi=bi):
            qmm.wait(v)
            kqt.wait(v, lag=16)
            e_ps = ps_q[:, :].rearrange("p (hh f2) -> p hh f2", hh=16)[:, :, 0::2]
            o_ps = ps_q[:, :].rearrange("p (hh f2) -> p hh f2", hh=16)[:, :, 1::2]
            cosb = ropeq_c[:, 32 * a:32 * (a + 1)].unsqueeze(1).broadcast_to((128, 16, 32))
            sinb = ropeq_s[:, 32 * a:32 * (a + 1)].unsqueeze(1).broadcast_to((128, 16, 32))
            rA = rtA[:, :].rearrange("p (hh f) -> p hh f", hh=16)
            rB = rtB[:, :].rearrange("p (hh f) -> p hh f", hh=16)
            qe = kq_tmp[bi][:, :].rearrange("p (hh f2) -> p hh f2", hh=16)[:, :, 0::2]
            qo = kq_tmp[bi][:, :].rearrange("p (hh f2) -> p hh f2", hh=16)[:, :, 1::2]
            v.tensor_tensor(rA, e_ps, cosb, ALU.mult)
            vd.inc(v.tensor_tensor(rB, o_ps, sinb, ALU.mult))
            vd.wait(v)
            vd.inc(v.tensor_tensor(qe, rA, rB, ALU.subtract))
            vd.wait(v)
            v.tensor_tensor(rA, e_ps, sinb, ALU.mult)
            vd.inc(v.tensor_tensor(rB, o_ps, cosb, ALU.mult))
            vd.wait(v)
            roped.inc(v.tensor_tensor(qo, rA, rB, ALU.add))
        VE(v_ropeq)

        for grp in range(8):
            def t_qt(tt, bi=bi, grp=grp):
                roped.wait(tt)
                kqe.wait(tt)
                h0, h1 = 2 * grp, 2 * grp + 1
                kqt.inc(tt.transpose(ps_qt[:, 0:128], kq_tmp[bi][:, 64 * h0:64 * (h0 + 1)],
                                     C.ident_bf[:]))
                kqt.inc(tt.transpose(ps_qt[:, 128:256], kq_tmp[bi][:, 64 * h1:64 * (h1 + 1)],
                                     C.ident_bf[:]))
            T(t_qt)

            def v_qte(v, a=a, grp=grp):
                kqt.wait(v)
                for k in range(2):
                    h_ = 2 * grp + k
                    p = (h_ % 4) + 4 * (h_ // 8)
                    up = (h_ // 4) % 2
                    kqe.inc(v.tensor_copy(QT[p][64 * up:64 * up + 64, 128 * a:128 * (a + 1)],
                                          ps_qt[:, 128 * k:128 * (k + 1)]))
            VE(v_qte)

    ps_kv = C.psum(psB, "ps_kv", [128, 512], F32)
    ps_kt = C.psum(psB, "ps_kt", [128, 256], BF16)

    for t in range(32):
        bi = t % 2

        def t_kv(tt, t=t):
            if t == 0:
                latl.wait(tt)
            roped.wait(tt)
            kvm.inc(tt.matmul(ps_kv[:, :], latT[:, 128 * t:128 * (t + 1)], wkvb_sb[:, :],
                              start=True, stop=True))
        T(t_kv)

        def v_ropek(v, t=t, bi=bi):
            kvm.wait(v)
            kqt.wait(v, lag=2)
            e_ps = ps_kv[:, 0:256].rearrange("p (k f2) -> p k f2", k=4)[:, :, 0::2]
            o_ps = ps_kv[:, 0:256].rearrange("p (k f2) -> p k f2", k=4)[:, :, 1::2]
            cosb = ropek_c[:, 32 * t:32 * (t + 1)].unsqueeze(1).broadcast_to((128, 4, 32))
            sinb = ropek_s[:, 32 * t:32 * (t + 1)].unsqueeze(1).broadcast_to((128, 4, 32))
            rA = rtA[:, 0:128].rearrange("p (k f) -> p k f", k=4)
            rB = rtB[:, 0:128].rearrange("p (k f) -> p k f", k=4)
            ke = kq_tmp[bi][:, 0:256].rearrange("p (k f2) -> p k f2", k=4)[:, :, 0::2]
            ko = kq_tmp[bi][:, 0:256].rearrange("p (k f2) -> p k f2", k=4)[:, :, 1::2]
            v.tensor_tensor(rA, e_ps, cosb, ALU.mult)
            vd.inc(v.tensor_tensor(rB, o_ps, sinb, ALU.mult))
            vd.wait(v)
            vd.inc(v.tensor_tensor(ke, rA, rB, ALU.subtract))
            vd.wait(v)
            v.tensor_tensor(rA, e_ps, sinb, ALU.mult)
            vd.inc(v.tensor_tensor(rB, o_ps, cosb, ALU.mult))
            vd.wait(v)
            v.tensor_tensor(ko, rA, rB, ALU.add)
            vdst = Vt[:, 260 * t:260 * t + 260].rearrange("p (z w) -> p z w", z=4)[:, :, 0:64]
            vsrc = ps_kv[:, 256:512].rearrange("p (z w) -> p z w", z=4)
            roped.inc(v.tensor_copy(vdst, vsrc))
        VE(v_ropek)

        def t_kt(tt, bi=bi):
            roped.wait(tt)
            kqe.wait(tt)
            kqt.inc(tt.transpose(ps_kt[:, 0:128], kq_tmp[bi][:, 0:128], C.ident_bf[:]))
            kqt.inc(tt.transpose(ps_kt[:, 128:256], kq_tmp[bi][:, 128:256], C.ident_bf[:]))
        T(t_kt)

        def v_kte(v, t=t):
            kqt.wait(v)
            v.tensor_copy(KT[0][:, 128 * t:128 * (t + 1)], ps_kt[:, 0:128])
            kqe.inc(v.tensor_copy(KT[1][:, 128 * t:128 * (t + 1)], ps_kt[:, 128:256]))
        VE(v_kte)

    psB.close()
    psC = ExitStack()
    ps_s = [C.psum(psC, f"ps_s{i}", [128, 512], F32) for i in range(4)]
    ps_o = [C.psum(psC, f"ps_o{i}", [65, 512], F32) for i in range(2)]
    ps_r = C.psum(psC, "ps_r", [64, 512], F32)

    state = {"first": True, "gsi": 0, "quad": 0}

    def quad(h_, q4):
        kvh = h_ // 4
        ktn = KT[h_ // 8]
        up = (h_ // 4) % 2
        qtn = QT[(h_ % 4) + 4 * (h_ // 8)]
        col0 = 512 * q4
        ob = state["quad"] % 2
        steps = []
        for kt in range(8 * q4 + 2):
            steps.append((kt, 0, 512, 0 if kt >= 8 * q4 else None))
        for m in range(1, 4):
            for kt in (8 * q4 + 2 * m, 8 * q4 + 2 * m + 1):
                steps.append((kt, 128 * m, 512 - 128 * m, m))
        nsteps = len(steps)
        pend = []
        exp_plan = []
        for si, (kt, off, n, mb) in enumerate(steps):
            G = state["gsi"]
            sl = G % 4
            state["gsi"] += 1

            def t_sc(tt, kt=kt, off=off, n=n, sl=sl, G=G):
                if state["first"]:
                    kqe.wait(tt)
                    roped.wait(tt)
                    qle.wait(tt)
                    state["first"] = False
                if G - 3 > 0:
                    tt.wait_ge(exd.h, G - 3)
                scm.inc(tt.matmul(ps_s[sl][:, 0:n],
                                  ktn[64 * up:64 * up + 64, 128 * kt:128 * (kt + 1)],
                                  qtn[64 * up:64 * up + 64, col0 + off:col0 + off + n],
                                  start=True, stop=True))
            T(t_sc)
            exp_plan.append((sl, n, scm.n, G))

            if mb is not None:
                def v_mask(v, kt=kt, sl=sl, G=G):
                    v.wait_ge(exd.h, G + 1)
                    mkd.inc(v.tensor_tensor(expS[sl][:, 0:128], expS[sl][:, 0:128],
                                            mask_sb[kt % 2][:], ALU.mult))
                VE(v_mask)
            mkd_goal = mkd.n

            def t_av(tt, kt=kt, off=off, n=n, sl=sl, si=si, G=G, mkd_goal=mkd_goal):
                tt.wait_ge(exd.h, G + 1)
                if mkd_goal > 0:
                    tt.wait_ge(mkd.h, mkd_goal)
                if si == 0:
                    nr3.wait(tt, lag=1)
                avd.inc(tt.matmul(ps_o[ob][:, off:off + n],
                                  Vt[:, 260 * kt + 65 * kvh:260 * kt + 65 * kvh + 65],
                                  expS[sl][:, 0:n],
                                  start=(si == 0), stop=(si == nsteps - 1)))
            pend.append(t_av)
            if len(pend) > 2:
                T(pend.pop(0))
        while pend:
            T(pend.pop(0))

        def s_exp_all(s, plan=tuple(exp_plan)):
            for (sl, n, scm_t, G) in plan:
                s.wait_ge(scm.h, scm_t)
                if G - 3 > 0:
                    s.wait_ge(avd.h, G - 3)
                exd.inc(s.activation(expS[sl][:, 0:n], ps_s[sl][:, 0:n],
                                     AF.Exp, scale=0.125))
        SC(s_exp_all)

        def v_rec(v):
            avd.wait(v)
            nr2.wait(v, lag=1)
            nr1.inc(v.reciprocal(rrow[ob][:, :], ps_o[ob][64:65, :]))
        VE(v_rec)

        def t_bc(tt):
            nr1.wait(tt)
            nr3.wait(tt)
            nr2.inc(tt.matmul(ps_r[:, :], ones64[:, :], rrow[ob][:, :], start=True, stop=True))
        T(t_bc)

        def v_fin(v, h_=h_, q4=q4):
            nr2.wait(v)
            vd.inc(v.tensor_copy(recb[ob][:, :], ps_r[:, :]))
            vd.wait(v)
            dst = attnT[h_ // 2][64 * (h_ % 2):64 * (h_ % 2) + 64, 512 * q4:512 * (q4 + 1)]
            nr3.inc(v.tensor_tensor(dst, ps_o[ob][0:64, :], recb[ob][:, :], ALU.mult))
        VE(v_fin)
        state["quad"] += 1

    for h_ in range(H):
        for q4 in range(4):
            quad(h_, q4)

    psC.close()
    ph.close()
    return [nr3.mark(), evc.mark(), kqe.mark(), roped.mark()]



def emit_l0ffn(C, io, xnT, attnT, attnT_stack, gate):
    nc = C.nc
    T, VE, SC, SY, G = C.T, C.V, C.S, C.Y, C.G

    ph = ExitStack()
    sb = lambda n, s, d: C.sb(ph, n, s, d)
    accT = [sb(f"f_accT{i}", [128, TOK], F32) for i in range(8)]
    dn_sb = [sb(f"f_dn{i}", [128, 1024], BF16) for i in range(8)]
    wbc = sb("f_wbc", [128, TOK], BF16)
    wT_sb = sb("f_wT", [NEXP, TOK], BF16)
    onehot_sb = sb("f_onehot", [NEXP, 2048], BF16)
    routerT_sb = sb("f_routerT", [128, 8 * NEXP], BF16)
    bout_sb = sb("f_bout", [128, 8], F32)
    logits = sb("f_logits", [128, NT * NEXP], F32)
    tcol = sb("f_tcol", [128, 4], F32)
    eqb = sb("f_eqb", [128, NEXP], F32)
    mkb = sb("f_mkb", [128, NEXP], F32)
    geb = sb("f_geb", [128, NEXP], F32)
    sgb = sb("f_sgb", [128, NEXP], F32)
    wfull = sb("f_wfull", [128, NEXP], F32)
    silu_sb = sb("f_silu", [128, TOK], BF16)
    wout_sb = [sb(f"f_wout{i}", [128, E], BF16) for i in range(2)]
    h_sb = io["h_sb"]

    gsn = lambda n: C.gs("f_" + n)
    vd = C.gs("a_vd")
    wlg, wls = gsn("wlg"), gsn("wls")
    wmm, wev = gsn("wmm"), gsn("wev")
    lgm, lgv, sgd, tpw, twe = gsn("lgm"), gsn("lgv"), gsn("sgd"), gsn("tpw"), gsn("twe")
    gul = [gsn(f"gul{i}") for i in range(2)]
    dnl8 = gsn("dnl")
    dnl = [dnl8] * 8
    wbm, wbe = gsn("wbm"), gsn("wbe")
    gmm, sil, umm, hbd, hcd = gsn("gmm"), gsn("sil"), gsn("umm"), gsn("hbd"), gsn("hcd")
    dnm, dne = gsn("dnm"), gsn("dne")
    sdl8 = gsn("sdl")
    sdl = [sdl8] * 8
    xld = [gsn("xld0"), gsn("xld1")]
    fint, finv = gsn("fint"), gsn("finv")
    sto = gsn("sto")

    wol = gsn("wol")

    def g_loads(g):
        wait_gate(g, gate)
        wlg.inc(g.dma_start(out=routerT_sb[:],
                            in_=io["routerT"].rearrange("(c p) n -> p c n", p=128)), 16)
        wlg.inc(g.dma_start(out=onehot_sb[:], in_=io["onehot"]), 16)
    G(g_loads)

    def sy_loads(sy):
        wait_gate(sy, gate)
        wls.inc(sy.dma_start(out=bout_sb[:], in_=io["bout"]), 16)
    SY(sy_loads)

    psP = ExitStack()
    ps_m2 = [C.psum(psP, f"f_ps_m{i}", [128, 512], F32) for i in range(2)]
    ps_t = C.psum(psP, "f_ps_t", [NEXP, 128], F32)
    ps_m = ps_m2[0]

    for qtr in range(4):
        def g_wo(g, qtr=qtr):
            if qtr == 0:
                wait_gate(g, gate)
            if qtr > 0:
                wmm.wait(g)
            for k in range(2):
                ci = 2 * qtr + k
                wol.inc(g.dma_start(out=wout_sb[k][:], in_=io["wout"][128 * ci:128 * (ci + 1), :]), 16)
        G(g_wo)

        for oc in range(8):
            for c4 in range(4):
                def t_wout(tt, qtr=qtr, oc=oc, c4=c4):
                    if qtr == 0 and oc == 0 and c4 == 0:
                        wait_gate(tt, gate)
                        wlg.wait(tt)
                    wol.wait(tt)
                    wev.wait(tt, lag=1)
                    mi = (qtr * 32 + oc * 4 + c4) % 2
                    for k in range(2):
                        ci = 2 * qtr + k
                        wmm.inc(tt.matmul(ps_m2[mi][:, :], wout_sb[k][:, 128 * oc:128 * (oc + 1)],
                                          attnT[ci][:, 512 * c4:512 * (c4 + 1)],
                                          start=(k == 0), stop=(k == 1)))
                T(t_wout)

                def v_wout(v, qtr=qtr, oc=oc, c4=c4):
                    wmm.wait(v)
                    mi = (qtr * 32 + oc * 4 + c4) % 2
                    if qtr == 0:
                        wls.wait(v)
                        wev.inc(v.tensor_scalar(accT[oc][:, 512 * c4:512 * (c4 + 1)],
                                                ps_m2[mi][:, :], bout_sb[:, oc:oc + 1], None, ALU.add))
                    else:
                        wev.inc(v.tensor_tensor(accT[oc][:, 512 * c4:512 * (c4 + 1)],
                                                accT[oc][:, 512 * c4:512 * (c4 + 1)],
                                                ps_m2[mi][:, :], ALU.add))
                VE(v_wout)

    for a in range(NT):
        def t_lg(tt, a=a):
            if a == 0:
                wev.wait(tt)
            lgv.wait(tt)
            for ci in range(8):
                lgm.inc(tt.matmul(ps_m[:, 0:NEXP], xnT[ci][:, 128 * a:128 * (a + 1)],
                                  routerT_sb[:, NEXP * ci:NEXP * (ci + 1)],
                                  start=(ci == 0), stop=(ci == 7)))
        T(t_lg)

        def v_lg(v, a=a):
            lgm.wait(v)
            lgv.inc(v.tensor_copy(logits[:, NEXP * a:NEXP * (a + 1)], ps_m[:, 0:NEXP]))
        VE(v_lg)

    for a in range(NT):
        la = logits[:, NEXP * a:NEXP * (a + 1)]

        def v_top2a(v, la=la):
            lgv.wait(v)
            vd.inc(v.tensor_reduce(tcol[:, 0:1], la, AX.X, ALU.max))
            vd.wait(v)
            vd.inc(v.tensor_scalar(eqb[:], la, tcol[:, 0:1], None, ALU.is_equal))
            vd.wait(v)
            vd.inc(v.scalar_tensor_tensor(mkb[:], eqb[:], -1e30, la, ALU.mult, ALU.add))
            vd.wait(v)
            vd.inc(v.tensor_reduce(tcol[:, 1:2], mkb[:], AX.X, ALU.max))
            vd.wait(v)
            sgd.inc(v.tensor_scalar(geb[:], la, tcol[:, 1:2], None, ALU.is_ge))
        VE(v_top2a)

        def s_sig(s, la=la):
            sgd.wait(s)
            tpw.wait(s, lag=1)
            sgd.inc(s.activation(sgb[:], la, AF.Sigmoid))
        SC(s_sig)

        def v_wfull(v):
            sgd.wait(v)
            tpw.inc(v.tensor_tensor(wfull[:], sgb[:], geb[:], ALU.mult))
        VE(v_wfull)

        def t_wT(tt):
            tpw.wait(tt)
            twe.wait(tt)
            tpw.inc(tt.transpose(ps_t[:, :], wfull[:, :], C.ident_f32[:]))
        T(t_wT)

        def v_wTe(v, a=a):
            tpw.wait(v)
            twe.inc(v.tensor_copy(wT_sb[:, 128 * a:128 * (a + 1)], ps_t[:, :]))
        VE(v_wTe)

    psP.close()

    esP3 = ExitStack()
    gu2 = [C.sb(esP3, f"f_gu2_{i}", [128, 2048], BF16) for i in range(2)]
    psX = ExitStack()
    ps_ga = C.psum(psX, "f_ps_ga", [128, 2048], F32)
    ps_gb = C.psum(psX, "f_ps_gb", [128, 2048], F32)

    def emit_expert(e):
        def g_dn(g, e=e):
            if e == 0:
                wmm.wait(g)
            dnm.wait(g)
            for ci in range(8):
                dnl[ci].inc(g.dma_start(out=dn_sb[ci][:],
                                        in_=io["dn_w"][e, 128 * ci:128 * (ci + 1), :]), 16)
        G(g_dn)

        for c4 in range(4):
            def t_wb(tt, e=e, c4=c4):
                if e == 0 and c4 == 0:
                    twe.wait(tt)
                    wev.wait(tt)
                hbd.wait(tt)
                wbe.wait(tt)
                dne.wait(tt)
                wbm.inc(tt.matmul(ps_gb[:, 512 * c4:512 * (c4 + 1)],
                                  onehot_sb[:, 128 * e:128 * (e + 1)],
                                  wT_sb[:, 512 * c4:512 * (c4 + 1)], start=True, stop=True))
            T(t_wb)

        def v_wb(v):
            wbm.wait(v)
            wbe.inc(v.tensor_copy(wbc[:], ps_gb[:, :]))
        VE(v_wb)

        for g in range(8):
            wi = g % 2

            def g_gu(g_, e=e, g=g, wi=wi):
                if e == 0 and g < 2:
                    wmm.wait(g_)
                umm.wait(g_, lag=1)
                gul[wi].inc(g_.dma_start(out=gu2[wi][:], in_=io["gup_w"][e, g]), 16)
            G(g_gu)

            def t_gate(tt, e=e, g=g, wi=wi):
                gul[wi].wait(tt)
                sil.wait(tt)
                if g == 0:
                    wbe.wait(tt)
                    dne.wait(tt)
                for ci in range(8):
                    for c4 in range(4):
                        gmm.inc(tt.matmul(ps_ga[:, 512 * c4:512 * (c4 + 1)],
                                          gu2[wi][:, 128 * ci:128 * (ci + 1)],
                                          xnT[ci][:, 512 * c4:512 * (c4 + 1)],
                                          start=(ci == 0), stop=(ci == 7)))
            T(t_gate)

            def s_silu(s, g=g):
                hbd.wait(s)
                for c4 in range(4):
                    gmm.wait(s, lag=3 - c4)
                    inst = s.activation(silu_sb[:, 512 * c4:512 * (c4 + 1)],
                                        ps_ga[:, 512 * c4:512 * (c4 + 1)], AF.Silu)
                sil.inc(inst)
            SC(s_silu)

            def t_up(tt, e=e, g=g, wi=wi):
                hbd.wait(tt)
                if g == 0:
                    wbe.wait(tt)
                for ci in range(8):
                    for c4 in range(4):
                        inst = tt.matmul(ps_gb[:, 512 * c4:512 * (c4 + 1)],
                                         gu2[wi][:, 1024 + 128 * ci:1024 + 128 * (ci + 1)],
                                         xnT[ci][:, 512 * c4:512 * (c4 + 1)],
                                         start=(ci == 0), stop=(ci == 7))
                umm.inc(inst)
            T(t_up)

            def v_h(v, e=e, g=g):
                umm.wait(v)
                sil.wait(v)
                wbe.wait(v)
                dnm.wait(v)
                hbd.inc(v.tensor_tensor(h_sb[g][:], ps_gb[:, :], silu_sb[:], ALU.mult))
                hbd.wait(v)
                hcd.inc(v.tensor_tensor(h_sb[g][:], h_sb[g][:], wbc[:], ALU.mult))
            VE(v_h)

        for oc in range(8):
            pbank = ps_ga if oc % 2 == 0 else ps_gb

            def t_dn(tt, e=e, oc=oc, pbank=pbank):
                hcd.wait(tt)
                dnl[0].wait(tt)
                if oc >= 2:
                    dne.wait(tt, lag=4)
                for hc in range(8):
                    for c4 in range(4):
                        inst = tt.matmul(pbank[:, 512 * c4:512 * (c4 + 1)],
                                         dn_sb[hc][:, 128 * oc:128 * (oc + 1)],
                                         h_sb[hc][:, 512 * c4:512 * (c4 + 1)],
                                         start=(hc == 0), stop=(hc == 7))
                        if hc == 7:
                            dnm.inc(inst)
            T(t_dn)

            for c4 in range(4):
                def v_dn(v, oc=oc, c4=c4, pbank=pbank):
                    dnm.wait(v, lag=3 - c4)
                    dne.inc(v.tensor_tensor(accT[oc][:, 512 * c4:512 * (c4 + 1)],
                                            accT[oc][:, 512 * c4:512 * (c4 + 1)],
                                            pbank[:, 512 * c4:512 * (c4 + 1)], ALU.add))
                VE(v_dn)

    for e in range(NEXP):
        emit_expert(e)

    for sh in range(2):
        for gl in range(8):
            gidx = 8 * sh + gl
            wi = gidx % 2

            def g_ws(g, gidx=gidx, wi=wi):
                umm.wait(g, lag=1)
                gul[wi].inc(g.dma_start(out=gu2[wi][:], in_=io["sgsu"][gidx]), 16)
            G(g_ws)

            def g_sd(g, gidx=gidx, gl=gl):
                dnm.wait(g)
                sdl[gl].inc(g.dma_start(out=dn_sb[gl][:],
                                        in_=io["sdwT"][128 * gidx:128 * (gidx + 1), :]), 16)
            G(g_sd)

            def t_sg(tt, wi=wi, sh=sh, gl=gl):
                gul[wi].wait(tt)
                sil.wait(tt)
                if sh == 0 and gl == 0:
                    dne.wait(tt)
                for ci in range(8):
                    for c4 in range(4):
                        gmm.inc(tt.matmul(ps_ga[:, 512 * c4:512 * (c4 + 1)],
                                          gu2[wi][:, 128 * ci:128 * (ci + 1)],
                                          xnT[ci][:, 512 * c4:512 * (c4 + 1)],
                                          start=(ci == 0), stop=(ci == 7)))
            T(t_sg)

            def s_ssilu(s):
                hbd.wait(s)
                for c4 in range(4):
                    gmm.wait(s, lag=3 - c4)
                    inst = s.activation(silu_sb[:, 512 * c4:512 * (c4 + 1)],
                                        ps_ga[:, 512 * c4:512 * (c4 + 1)], AF.Silu)
                sil.inc(inst)
            SC(s_ssilu)

            def t_su(tt, wi=wi, sh=sh, gl=gl):
                hbd.wait(tt)
                if sh == 0 and gl == 0:
                    dne.wait(tt)
                for ci in range(8):
                    for c4 in range(4):
                        inst = tt.matmul(ps_gb[:, 512 * c4:512 * (c4 + 1)],
                                         gu2[wi][:, 1024 + 128 * ci:1024 + 128 * (ci + 1)],
                                         xnT[ci][:, 512 * c4:512 * (c4 + 1)],
                                         start=(ci == 0), stop=(ci == 7))
                umm.inc(inst)
            T(t_su)

            def v_t(v, gl=gl):
                umm.wait(v)
                sil.wait(v)
                dnm.wait(v)
                hbd.inc(v.tensor_tensor(h_sb[gl][:], ps_gb[:, :], silu_sb[:], ALU.mult))
            VE(v_t)

        for oc in range(8):
            pbank = ps_ga if oc % 2 == 0 else ps_gb

            def t_sd(tt, oc=oc, pbank=pbank):
                hbd.wait(tt)
                sdl[0].wait(tt)
                if oc >= 2:
                    dne.wait(tt, lag=4)
                for gl in range(8):
                    for c4 in range(4):
                        inst = tt.matmul(pbank[:, 512 * c4:512 * (c4 + 1)],
                                         dn_sb[gl][:, 128 * oc:128 * (oc + 1)],
                                         h_sb[gl][:, 512 * c4:512 * (c4 + 1)],
                                         start=(gl == 0), stop=(gl == 7))
                        if gl == 7:
                            dnm.inc(inst)
            T(t_sd)

            for c4 in range(4):
                def v_sd(v, oc=oc, c4=c4, pbank=pbank):
                    dnm.wait(v, lag=3 - c4)
                    dne.inc(v.tensor_tensor(accT[oc][:, 512 * c4:512 * (c4 + 1)],
                                            accT[oc][:, 512 * c4:512 * (c4 + 1)],
                                            pbank[:, 512 * c4:512 * (c4 + 1)], ALU.add))
                VE(v_sd)

    psX.close()
    esP3.close()
    psF = ExitStack()
    ps_f = C.psum(psF, "f_ps_f", [128, 128], F32)
    xbuf = [h_sb[0][:].bitcast(F32), h_sb[1][:].bitcast(F32)]
    x1b = [h_sb[2][:].bitcast(F32), h_sb[3][:].bitcast(F32)]

    for a in range(NT):
        bi = a % 2

        def sy_xa(sy, a=a, bi=bi):
            if a < 2:
                dne.wait(sy)
            finv.wait(sy, lag=8)
            xld[bi].inc(sy.dma_start(out=xbuf[bi], in_=io["x_own"][128 * a:128 * (a + 1), :]), 16)
        SY(sy_xa)

        for ct in range(8):
            def t_f(tt, a=a, ct=ct):
                if a == 0 and ct == 0:
                    dne.wait(tt)
                    hbd.wait(tt)
                finv.wait(tt)
                fint.inc(tt.transpose(ps_f[:, :], accT[ct][:, 128 * a:128 * (a + 1)],
                                      C.ident_f32[:]))
            T(t_f)

            def v_f(v, a=a, bi=bi, ct=ct):
                fint.wait(v)
                xld[bi].wait(v)
                if ct == 0:
                    sto.wait(v, lag=16)
                finv.inc(v.tensor_tensor(x1b[bi][:, 128 * ct:128 * (ct + 1)],
                                         xbuf[bi][:, 128 * ct:128 * (ct + 1)],
                                         ps_f[:, :], ALU.add))
            VE(v_f)

        def sy_o(sy, a=a, bi=bi):
            finv.wait(sy)
            sto.inc(sy.dma_start(out=io["x1_scr"].ap()[128 * a:128 * (a + 1), :], in_=x1b[bi]), 16)
        SY(sy_o)

    psF.close()
    ph.close()
    return [sto.mark(), finv.mark()]



def emit_l1qkv(C, io, xn1T, gate):
    nc = C.nc
    T, VE, SC, SY, G = C.T, C.V, C.S, C.Y, C.G

    ph = ExitStack()
    sb = lambda n, s, d: C.sb(ph, n, s, d)
    w_sb = [sb(f"q_w{i}", [128, 4 * E], BF16) for i in range(8)]
    xbuf = [sb(f"q_xbuf{i}", [128, E], F32) for i in range(4)]
    xsq = [sb(f"q_xsq{i}", [128, E], F32) for i in range(4)]
    stat2 = [sb(f"q_stat{i}", [128, 8], F32) for i in range(4)]
    xn_tmp = [sb(f"q_xn{i}", [128, E], BF16) for i in range(4)]
    qkv_sb = [sb(f"q_qkv{i}", [128, 4 * E], BF16) for i in range(4)]

    gsn = lambda n: C.gs("q_" + n)
    vd = C.gs("a_vd")
    wlg = gsn("wlg")
    lnf, lns, sds, lnx = C.gs("a_lnf"), C.gs("a_lns"), C.gs("a_sds"), C.gs("a_lnx")
    xl = [C.gs("a_xl0"), C.gs("a_xl1")]
    trp, evc = C.gs("a_trp"), C.gs("a_evc")
    qmm, qev, sto = gsn("qmm"), gsn("qev"), gsn("sto")

    def g_loads(g):
        wait_gate(g, gate)
        for ci in range(8):
            wlg.inc(g.dma_start(out=w_sb[ci][:], in_=io["wqkv"][128 * ci:128 * (ci + 1), :]), 16)
    G(g_loads)

    ps_tr = C.psum(ph, "q_ps_tr", [128, 1024], BF16)
    ps_q = [C.psum(ph, f"q_ps_q{i}", [128, 512], F32) for i in range(2)]

    gq = {"n": 0}
    for a in range(NT):
        bi = a % 4
        xi = a % 2

        def sy_x(sy, a=a, bi=bi, xi=xi):
            if a < 2:
                wait_gate(sy, gate)
            lnf.wait(sy, lag=3)
            xl[xi].inc(sy.dma_start(out=xbuf[bi][:],
                                    in_=io["x1_scr"].ap()[128 * a:128 * (a + 1), :]), 16)
        SY(sy_x)

        def v_stats(v, bi=bi, xi=xi):
            stat = stat2[bi]
            xl[xi].wait(v)
            sds.wait(v, lag=3)
            v.tensor_tensor(xsq[bi][:], xbuf[bi][:], xbuf[bi][:], ALU.mult)
            v.tensor_reduce(stat[:, 0:1], xbuf[bi][:], AX.X, ALU.add)
            vd.inc(v.tensor_reduce(stat[:, 1:2], xsq[bi][:], AX.X, ALU.add))
            vd.wait(v)
            vd.inc(v.tensor_scalar(stat[:, 2:3], stat[:, 0:1], 1.0 / E, None, ALU.mult))
            vd.wait(v)
            vd.inc(v.tensor_tensor(stat[:, 3:4], stat[:, 2:3], stat[:, 2:3], ALU.mult))
            v.tensor_scalar(stat[:, 4:5], stat[:, 1:2], 1.0 / E, LN_EPS, ALU.mult, ALU.add)
            vd.wait(v)
            lns.inc(v.tensor_tensor(stat[:, 4:5], stat[:, 4:5], stat[:, 3:4], ALU.subtract))
        VE(v_stats)

        def s_rstd(s, bi=bi):
            stat = stat2[bi]
            lns.wait(s)
            lnx.inc(s.activation(stat[:, 5:6], stat[:, 4:5], AF.Ln))
            lnx.wait(s)
            sds.inc(s.activation(stat[:, 6:7], stat[:, 5:6], AF.Exp, scale=-0.5))
        SC(s_rstd)

        def v_norm(v, bi=bi):
            stat = stat2[bi]
            sds.wait(v)
            trp.wait(v, lag=24)
            lnf.inc(v.tensor_scalar(xn_tmp[bi][:], xbuf[bi][:], stat[:, 2:3], stat[:, 6:7],
                                    ALU.subtract, ALU.mult))
        VE(v_norm)

        def t_trans(tt, bi=bi):
            lnf.wait(tt)
            evc.wait(tt)
            for ct in range(8):
                trp.inc(tt.transpose(ps_tr[:, 128 * ct:128 * (ct + 1)],
                                     xn_tmp[bi][:, 128 * ct:128 * (ct + 1)], C.ident_bf[:]))
        T(t_trans)

        def v_evac(v, a=a):
            trp.wait(v)
            for ct in range(8):
                evc.inc(v.tensor_copy(xn1T[ct][:, 128 * a:128 * (a + 1)],
                                      ps_tr[:, 128 * ct:128 * (ct + 1)]))
        VE(v_evac)

        for oc8 in range(8):
            sl = gq["n"] % 2
            gq["n"] += 1

            def t_q(tt, a=a, oc8=oc8, sl=sl):
                if a == 0 and oc8 == 0:
                    wlg.wait(tt)
                evc.wait(tt)
                qev.wait(tt, lag=1)
                for ci in range(8):
                    qmm.inc(tt.matmul(ps_q[sl][:, :], xn1T[ci][:, 128 * a:128 * (a + 1)],
                                      w_sb[ci][:, 512 * oc8:512 * (oc8 + 1)],
                                      start=(ci == 0), stop=(ci == 7)))
            T(t_q)

            def v_q(v, a=a, bi=bi, oc8=oc8, sl=sl):
                qmm.wait(v)
                if oc8 == 0:
                    sto.wait(v, lag=48)
                qev.inc(v.tensor_copy(qkv_sb[bi][:, 512 * oc8:512 * (oc8 + 1)], ps_q[sl][:, :]))
            VE(v_q)

        def sy_q(sy, a=a, bi=bi):
            qev.wait(sy)
            sto.inc(sy.dma_start(out=io["qkv_scr"].ap()[128 * a:128 * (a + 1), :],
                                 in_=qkv_sb[bi][:]), 16)
        SY(sy_q)

    ph.close()
    return [sto.mark(), evc.mark(), qev.mark()]



def emit_l1fused(C, io, xn1T, gate):
    nc = C.nc
    T, VE, SC, SY, G = C.T, C.V, C.S, C.Y, C.G

    ph = ExitStack()
    sb = lambda n, s, d: C.sb(ph, n, s, d)
    accT = [sb(f"o_accT{i}", [128, TOK], BF16) for i in range(8)]
    scl = sb("o_scl", [128, 5 * 32], F32)
    kv_all = sb("w_kv", [128, 8192], BF16)
    qb = [sb(f"w_qb{i}", [128, 256], BF16) for i in range(2)]
    wb_all = sb("w_wball", [128, 4096], BF16)
    wscr = sb("w_scr", [128, 4096], BF16)
    sc2 = [sb(f"w_sc{i}", [128, 1024], F32) for i in range(2)]
    P4 = [sb(f"w_P{i}", [128, 1024], BF16) for i in range(4)]
    scat4 = [sb(f"w_scat{i}", [128, 1024], BF16) for i in range(4)]
    sums = sb("w_sums", [128, 16], F32)
    rec = sb("w_rec", [128, 16], F32)
    ov = sb("w_ov", [128, 256], F32)
    t1 = wscr[:].bitcast(F32)[:, 0:256]

    gsn = lambda n: C.gs("o_" + n)
    vd = C.gs("a_vd")
    wls = gsn("wls")
    a1l = [gsn("a1l0"), gsn("a1l1")]
    trp, evc = C.gs("a_trp"), C.gs("a_evc")
    wol = gsn("wol")
    wmm, wev = gsn("wmm"), gsn("wev")
    w12l = [gsn(f"w12l{i}") for i in range(2)]
    whl = [gsn(f"whl{i}") for i in range(2)]
    w3l = [gsn(f"w3l{i}") for i in range(2)]
    amm, sil, bmm, uev = gsn("amm"), gsn("sil"), gsn("bmm"), gsn("uev")
    hmm, hev = gsn("hmm"), gsn("hev")
    h2m, h2e = gsn("h2m"), gsn("h2e")
    w3m, w3e = gsn("w3m"), gsn("w3e")
    xld = [C.gs("a_xl0"), C.gs("a_xl1")]
    fint, finv, sto = gsn("fint"), gsn("finv"), gsn("sto")
    kvl, qbl, wbl = gsn("kvl"), gsn("qbl"), gsn("wbl")
    scr_d, exq, wind = gsn("w_scr_d"), gsn("w_exq"), gsn("w_wind")
    pfw = gsn("pfw")
    scw, scd, ccs = gsn("w_scw"), gsn("w_scd"), gsn("w_ccs")

    def sy_loads(sy):
        wait_gate(sy, gate)
        for k, t in enumerate(["w1s", "w2s", "h1s", "h2s"]):
            wls.inc(sy.dma_start(out=scl[:, 32 * k:32 * (k + 1)], in_=io[t]), 16)
        wls.inc(sy.dma_start(out=scl[:, 128:136], in_=io["w3s"]), 16)
    SY(sy_loads)

    psC = ExitStack()
    ps_a2 = [C.psum(psC, f"o_ps_a{i}", [128, 512], F32) for i in range(2)]
    ps_b2 = [C.psum(psC, f"o_ps_b{i}", [128, 512], F32) for i in range(2)]
    u_sb = [C.sb(psC, f"o_u{i}", [128, 512], BF16) for i in range(32)]
    v_sb = [C.sb(psC, f"o_v{i}", [128, 512], BF16) for i in range(32)]
    wbuf = [C.sb(psC, f"o_wbuf{i}", [128, 4096], BF16) for i in range(2)]
    silu2 = [C.sb(psC, f"o_silu{i}", [128, 512], BF16) for i in range(2)]

    qkv = io["qkv_scr"].ap()
    for qt in range(4):
        i = qt
        def sy_kv(sy, i=i):
            if i == 0:
                wait_gate(sy, gate)
            else:
                wind.wait(sy)
                sy.wait_ge(w3e.h, 8 * i)
            for a in range(NT):
                kvl.inc(sy.dma_start(out=kv_all[:, 512 * a:512 * (a + 1)],
                                     in_=qkv[128 * a:128 * (a + 1),
                                             1024 * i + 256:1024 * i + 768]), 16)
            for a in range(NT):
                wbl.inc(sy.dma_start(out=wb_all[:, 256 * a:256 * (a + 1)],
                                     in_=qkv[128 * a:128 * (a + 1),
                                             1024 * i + 768:1024 * (i + 1)]), 16)
        SY(sy_kv)

        def sy_qb(sy, i=i):
            for a in range(NT):
                A = 16 * i + a
                if A - 1 > 0:
                    sy.wait_ge(scr_d.h, A - 1)
                qbl.inc(sy.dma_start(out=qb[a % 2][:],
                                     in_=qkv[128 * a:128 * (a + 1),
                                             1024 * i:1024 * i + 256]), 16)
        SY(sy_qb)

        def sy_pfw(sy, i=i):
            sy.wait_ge(scr_d.h, 16 * (i + 1))
            pfw.inc(sy.dma_start(
                out=kv_all[:].rearrange("p (a c) -> p a c", c=512)[:, :, 0:256],
                in_=io["w3T"][0].rearrange("p (a c) -> p a c", c=256)), 16)
        SY(sy_pfw)

        amm_marks = []
        sil0, uev0 = sil.n, uev.n
        for g in range(32):
            wi = g % 2

            def g_w12(g_, qt=qt, g=g, wi=wi):
                if qt == 0 and g < 2:
                    wait_gate(g_, gate)
                if g < 2:
                    w3m.wait(g_)
                bmm.wait(g_, lag=8)
                if g < 2:
                    for h in range(2):
                        w12l[wi].inc(g_.dma_start(out=wbuf[wi][:, 1024 * h:1024 * (h + 1)],
                                                  in_=io["w12T"][g, :, 1024 * h:1024 * (h + 1)]), 16)
                else:
                    w12l[wi].inc(g_.dma_start(out=wbuf[wi][:, 0:2048], in_=io["w12T"][g]), 16)
            G(g_w12)

            def t_w1(tt, qt=qt, g=g, wi=wi, sil0=sil0):
                w12l[wi].wait(tt)
                if sil0 + g - 1 > 0:
                    tt.wait_ge(sil.h, sil0 + g - 1)
                w3e.wait(tt)
                for ci in range(8):
                    amm.inc(tt.matmul(ps_a2[wi][:, :], wbuf[wi][:, 128 * ci:128 * (ci + 1)],
                                      xn1T[ci][:, 512 * qt:512 * (qt + 1)],
                                      start=(ci == 0), stop=(ci == 7)))
            T(t_w1)
            amm_marks.append(amm.n)

            def t_w2(tt, qt=qt, g=g, wi=wi, sil0=sil0, uev0=uev0):
                if sil0 + g > 0:
                    tt.wait_ge(sil.h, sil0 + g)
                if uev0 + g - 1 > 0:
                    tt.wait_ge(uev.h, uev0 + g - 1)
                for ci in range(8):
                    bmm.inc(tt.matmul(ps_b2[wi][:, :],
                                      wbuf[wi][:, 1024 + 128 * ci:1024 + 128 * (ci + 1)],
                                      xn1T[ci][:, 512 * qt:512 * (qt + 1)],
                                      start=(ci == 0), stop=(ci == 7)))
            T(t_w2)

            def v_u(v, g=g, wi=wi, sil0=sil0):
                bmm.wait(v)
                v.wait_ge(sil.h, sil0 + g + 1)
                hmm.wait(v)
                uev.inc(v.scalar_tensor_tensor(u_sb[g][:], ps_b2[wi][:, :],
                                               scl[:, 32 + g:33 + g],
                                               silu2[wi][:], ALU.mult, ALU.mult))
            VE(v_u)

        def s_si_all(s, qt=qt, marks=tuple(amm_marks), uev0=uev0):
            if qt == 0:
                wls.wait(s)
            for g in range(32):
                s.wait_ge(amm.h, marks[g])
                if uev0 + g - 1 > 0:
                    s.wait_ge(uev.h, uev0 + g - 1)
                sil.inc(s.activation(silu2[g % 2][:], ps_a2[g % 2][:, :], AF.Silu,
                                     scale=scl[:, g:g + 1]))
        SC(s_si_all)

        for a in range(NT):
            g4, al = a // 4, a % 4

            def v_sc(v, i=i, a=a, g4=g4, al=al):
                A = 16 * i + a
                b = g4 % 2
                kvl.wait(v)
                v.wait_ge(qbl.h, 16 * (A + 1))
                if al == 0 and 4 * i + g4 - 1 > 0:
                    v.wait_ge(exq.h, 4 * i + g4 - 1)
                q_ap = qb[a % 2][:].rearrange("p (u d) -> p u d", d=16) \
                    .unsqueeze(1).broadcast_to((128, 16, 16, 16))
                k_ap = kv_all[:].rearrange("p (ap c) -> p ap c", c=512)[:, :, 0:256] \
                    .rearrange("p ap (u d) -> p ap u d", d=16)
                o_ap = wscr[:].rearrange("p (ap u d) -> p ap u d", u=16, d=16)
                vd.inc(v.tensor_tensor(o_ap, q_ap, k_ap, ALU.mult))
                vd.wait(v)
                scr_d.inc(v.tensor_reduce(sc2[b][:, 256 * al:256 * (al + 1)],
                                          wscr[:].rearrange("p (au d) -> p au d", d=16),
                                          AX.X, ALU.add))
            VE(v_sc)

        hmm_marks = []
        hev0, h2m0 = hev.n, h2m.n
        for g in range(32):
            wi = g % 2

            def g_wh(g_, qt=qt, g=g, wi=wi):
                if g == 0:
                    g_.wait_ge(bmm.h, 256 * qt + 248)
                elif g == 1:
                    bmm.wait(g_)
                hmm.wait(g_, lag=32)
                if g < 2:
                    for h in range(2):
                        whl[wi].inc(g_.dma_start(out=wbuf[wi][:, 2048 * h:2048 * (h + 1)],
                                                 in_=io["h1T"][g, :, 2048 * h:2048 * (h + 1)]), 16)
                else:
                    whl[wi].inc(g_.dma_start(out=wbuf[wi][:], in_=io["h1T"][g]), 16)
            G(g_wh)

            def t_h1(tt, g=g, wi=wi, hev0=hev0):
                whl[wi].wait(tt)
                uev.wait(tt)
                if hev0 + g - 1 > 0:
                    tt.wait_ge(hev.h, hev0 + g - 1)
                for ci in range(32):
                    hmm.inc(tt.matmul(ps_a2[wi][:, :], wbuf[wi][:, 128 * ci:128 * (ci + 1)],
                                      u_sb[ci][:, :], start=(ci == 0), stop=(ci == 31)))
            T(t_h1)
            hmm_marks.append(hmm.n)

        def s_g1_all(s, marks=tuple(hmm_marks), h2m0=h2m0):
            if h2m0 > 0:
                s.wait_ge(h2m.h, h2m0)
            for g in range(32):
                s.wait_ge(hmm.h, marks[g])
                hev.inc(s.activation(v_sb[g][:], ps_a2[g % 2][:, :], AF.Gelu,
                                     scale=scl[:, 64 + g:65 + g]))
        SC(s_g1_all)

        def s_expw(s, i=i):
            for g4 in range(4):
                s.wait_ge(scr_d.h, 16 * i + 4 * (g4 + 1))
                t = 16 * (i - 1) + 4 * (g4 + 1)
                if i > 0 and t > 0:
                    s.wait_ge(wind.h, t)
                exq.inc(s.activation(P4[g4][:], sc2[g4 % 2][:], AF.Exp, scale=0.25))
        SC(s_expw)

        for a in range(NT):
            g4, al = a // 4, a % 4

            def v_av(v, i=i, a=a, g4=g4, al=al):
                A = 16 * i + a
                Gw = 4 * i + g4
                b = g4 % 2
                v.wait_ge(exq.h, 4 * i + g4 + 1)
                v.wait_ge(wbl.h, 256 * (i + 1))
                if al == 0 and 16 * (Gw - 3) > 0:
                    v.wait_ge(scw.h, 16 * (Gw - 3))
                pv = P4[g4][:, 256 * al:256 * (al + 1)].rearrange("p (ap u) -> p u ap", ap=16)
                vd.inc(v.tensor_reduce(sums[:], pv, AX.X, ALU.add))
                vd.wait(v)
                vd.inc(v.reciprocal(rec[:], sums[:]))
                vd.wait(v)
                p_ap = pv.unsqueeze(2).broadcast_to((128, 16, 16, 16))
                v_ap = kv_all[:].rearrange("p (ap c) -> p ap c", c=512)[:, :, 256:512] \
                    .rearrange("p ap (u d) -> p u d ap", d=16)
                t_ap = wscr[:].rearrange("p (u d ap) -> p u d ap", d=16, ap=16)
                vd.inc(v.tensor_tensor(t_ap, p_ap, v_ap, ALU.mult))
                vd.wait(v)
                vd.inc(v.tensor_reduce(ov[:], wscr[:].rearrange("p (ud ap) -> p ud ap", ap=16),
                                       AX.X, ALU.add))
                vd.wait(v)
                r_ap = rec[:].unsqueeze(2).broadcast_to((128, 16, 16))
                vd.inc(v.tensor_tensor(t1.rearrange("p (u d) -> p u d", d=16),
                                       ov[:].rearrange("p (u d) -> p u d", d=16), r_ap, ALU.mult))
                vd.wait(v)
                wind.inc(v.tensor_tensor(scat4[g4][:, 256 * al:256 * (al + 1)], t1,
                                         wb_all[:, 256 * a:256 * (a + 1)], ALU.mult))
            VE(v_av)

        h2m_marks = []
        h2e0, w3m0 = h2e.n, w3m.n
        for g in range(32):
            wi = g % 2

            def g_wh2(g_, qt=qt, g=g, wi=wi):
                if g == 0:
                    g_.wait_ge(hmm.h, 1024 * qt + 992)
                elif g == 1:
                    hmm.wait(g_)
                h2m.wait(g_, lag=32)
                if g < 2:
                    for h in range(2):
                        whl[wi].inc(g_.dma_start(out=wbuf[wi][:, 2048 * h:2048 * (h + 1)],
                                                 in_=io["h2T"][g, :, 2048 * h:2048 * (h + 1)]), 16)
                else:
                    whl[wi].inc(g_.dma_start(out=wbuf[wi][:], in_=io["h2T"][g]), 16)
            G(g_wh2)

            def t_h2(tt, g=g, wi=wi, h2e0=h2e0):
                whl[wi].wait(tt)
                hev.wait(tt)
                if h2e0 + g - 1 > 0:
                    tt.wait_ge(h2e.h, h2e0 + g - 1)
                for ci in range(32):
                    h2m.inc(tt.matmul(ps_b2[wi][:, :], wbuf[wi][:, 128 * ci:128 * (ci + 1)],
                                      v_sb[ci][:, :], start=(ci == 0), stop=(ci == 31)))
            T(t_h2)
            h2m_marks.append(h2m.n)

        def s_g2_all(s, marks=tuple(h2m_marks), w3m0=w3m0):
            if w3m0 > 0:
                s.wait_ge(w3m.h, w3m0)
            for g in range(32):
                s.wait_ge(h2m.h, marks[g])
                h2e.inc(s.activation(u_sb[g][:], ps_b2[g % 2][:, :], AF.Gelu,
                                     scale=scl[:, 96 + g:97 + g]))
        SC(s_g2_all)


        w3e0 = w3e.n
        for oc in range(8):
            wi = oc % 2

            if oc > 0:
                def g_w3(g_, oc=oc, wi=wi):
                    if oc < 2:
                        h2m.wait(g_)
                    w3m.wait(g_, lag=32)
                    if oc < 2:
                        for h in range(2):
                            w3l[wi].inc(g_.dma_start(out=wbuf[wi][:, 2048 * h:2048 * (h + 1)],
                                                     in_=io["w3T"][oc, :, 2048 * h:2048 * (h + 1)]), 16)
                    else:
                        w3l[wi].inc(g_.dma_start(out=wbuf[wi][:], in_=io["w3T"][oc]), 16)
                G(g_w3)

            def t_w3(tt, qt=qt, oc=oc, wi=wi, w3e0=w3e0):
                if oc == 0:
                    tt.wait_ge(pfw.h, 16 * (qt + 1))
                else:
                    w3l[wi].wait(tt)
                h2e.wait(tt)
                if w3e0 + oc - 1 > 0:
                    tt.wait_ge(w3e.h, w3e0 + oc - 1)
                for ci in range(32):
                    w_ap = (kv_all[:, 512 * (ci // 2) + 128 * (ci % 2):
                                   512 * (ci // 2) + 128 * (ci % 2) + 128]
                            if oc == 0 else wbuf[wi][:, 128 * ci:128 * (ci + 1)])
                    w3m.inc(tt.matmul(ps_a2[wi][:, :], w_ap,
                                      u_sb[ci][:, :], start=(ci == 0), stop=(ci == 31)))
            T(t_w3)

            def v_w3(v, qt=qt, oc=oc, wi=wi):
                w3m.wait(v)
                w3e.inc(v.tensor_scalar(accT[oc][:, 512 * qt:512 * (qt + 1)],
                                        ps_a2[wi][:, :], scl[:, 128 + oc:129 + oc],
                                        None, ALU.mult))
            VE(v_w3)

        for g4 in range(4):
            def sy_sc(sy, i=i, g4=g4):
                Gw = 4 * i + g4
                sy.wait_ge(wind.h, 16 * i + 4 * (g4 + 1))
                if 64 * (Gw - 3) > 0:
                    sy.wait_ge(scd.h, 64 * (Gw - 3))
                scw.inc(sy.dma_start(out=io["scr"].ap()[:, 1024 * g4:1024 * (g4 + 1)],
                                     in_=scat4[g4][:]), 16)
                sy.wait_ge(scw.h, 16 * (Gw + 1))
                for al in range(4):
                    a = 4 * g4 + al
                    blk = 2 * i + (a % 4) // 2
                    rowb = ((a // 4) * 2 + (a % 2)) * 32
                    dst = io["a2a_in"].ap()[256 * blk + rowb:256 * blk + rowb + 32, :] \
                        .rearrange("t (z w) -> t z w", z=4)
                    srcv = io["scr"].ap()[:, 256 * a:256 * (a + 1)].rearrange("(t z) w -> t z w", z=4)
                    scd.inc(sy.dma_start(out=dst, in_=srcv), 16)
            SY(sy_sc)

    def g_cc(g):
        scd.wait(g)
        with allow_ici_all_to_all():
            ccs.inc(g.collective_compute(
                "AllToAll", mybir.AluOpType.bypass,
                replica_groups=[list(range(NCORES))],
                ins=[io["a2a_in"].ap().opt()],
                outs=[io["a2a_out"].ap().opt()],
            ))
    G(g_cc)

    psC.close()
    esA = ExitStack()
    a1T = [C.sb(esA, f"o_a1T{i}", [128, TOK], BF16) for i in range(8)]
    abuf = [C.sb(esA, f"o_abuf{i}", [128, E], BF16) for i in range(2)]
    wo_sb = [C.sb(esA, f"o_wo{i}", [128, E], BF16) for i in range(2)]
    psA = ExitStack()
    ps_tr = C.psum(psA, "o_ps_tr", [128, 1024], BF16)

    for a in range(NT):
        bi = a % 2

        def sy_a1(sy, a=a, bi=bi):
            if a < 2:
                ccs.wait(sy)
                w3e.wait(sy)
            trp.wait(sy, lag=8)
            for al in range(2):
                for gm in range(2):
                    blk = 2 * (a // 4) + gm
                    rowb = ((a % 4) * 2 + al) * 32
                    a1l[bi].inc(sy.dma_start(
                        out=abuf[bi][64 * al + 32 * gm:64 * al + 32 * gm + 32, :],
                        in_=io["a2a_out"].ap()[256 * blk + rowb:256 * blk + rowb + 32, :]), 16)
        SY(sy_a1)

        def t_tr(tt, a=a, bi=bi):
            a1l[bi].wait(tt)
            evc.wait(tt)
            for ct in range(8):
                trp.inc(tt.transpose(ps_tr[:, 128 * ct:128 * (ct + 1)],
                                     abuf[bi][:, 128 * ct:128 * (ct + 1)], C.ident_bf[:]))
        T(t_tr)

        def v_ev(v, a=a):
            trp.wait(v)
            for ct in range(8):
                evc.inc(v.tensor_copy(a1T[ct][:, 128 * a:128 * (a + 1)],
                                      ps_tr[:, 128 * ct:128 * (ct + 1)]))
        VE(v_ev)

    psB = ExitStack()
    ps_m2 = [C.psum(psB, f"o_ps_m{i}", [128, 512], F32) for i in range(2)]

    for qtr in range(4):
        def g_wo(g, qtr=qtr):
            if qtr == 0:
                ccs.wait(g)
            if qtr > 0:
                wmm.wait(g)
            for k in range(2):
                ci = 2 * qtr + k
                wol.inc(g.dma_start(out=wo_sb[k][:], in_=io["w_out"][128 * ci:128 * (ci + 1), :]), 16)
        G(g_wo)

        for oc in range(8):
            for c4 in range(4):
                def t_wo(tt, qtr=qtr, oc=oc, c4=c4):
                    if qtr == 0 and oc == 0 and c4 == 0:
                        evc.wait(tt)
                    wol.wait(tt)
                    wev.wait(tt, lag=1)
                    mi = (qtr * 32 + oc * 4 + c4) % 2
                    for k in range(2):
                        ci = 2 * qtr + k
                        wmm.inc(tt.matmul(ps_m2[mi][:, :], wo_sb[k][:, 128 * oc:128 * (oc + 1)],
                                          a1T[ci][:, 512 * c4:512 * (c4 + 1)],
                                          start=(k == 0), stop=(k == 1)))
                T(t_wo)

                def v_wo(v, qtr=qtr, oc=oc, c4=c4):
                    wmm.wait(v)
                    if qtr == 0 and oc == 0 and c4 == 0:
                        w3e.wait(v)
                    mi = (qtr * 32 + oc * 4 + c4) % 2
                    wev.inc(v.tensor_tensor(accT[oc][:, 512 * c4:512 * (c4 + 1)],
                                            accT[oc][:, 512 * c4:512 * (c4 + 1)],
                                            ps_m2[mi][:, :], ALU.add))
                VE(v_wo)

    psB.close()
    psA.close()
    esA.close()
    psF = ExitStack()
    ps_f = C.psum(psF, "o_ps_f", [128, 128], BF16)
    xbuf = [C.sb(psF, f"o_xbuf{i}", [128, E], F32) for i in range(2)]
    x1b = [C.sb(psF, f"o_x1b{i}", [128, E], F32) for i in range(2)]

    for a in range(NT):
        bi = a % 2

        def sy_xa(sy, a=a, bi=bi):
            if a < 2:
                w3e.wait(sy)
                wev.wait(sy)
            finv.wait(sy, lag=8)
            xld[bi].inc(sy.dma_start(out=xbuf[bi][:],
                                     in_=io["x1_scr"].ap()[128 * a:128 * (a + 1), :]), 16)
        SY(sy_xa)

        for ct in range(8):
            def t_f(tt, a=a, ct=ct):
                if a == 0 and ct == 0:
                    w3e.wait(tt)
                    wev.wait(tt)
                finv.wait(tt)
                fint.inc(tt.transpose(ps_f[:, :], accT[ct][:, 128 * a:128 * (a + 1)],
                                      C.ident_bf[:]))
            T(t_f)

            def v_f(v, a=a, bi=bi, ct=ct):
                fint.wait(v)
                xld[bi].wait(v)
                if ct == 0:
                    sto.wait(v, lag=16)
                finv.inc(v.tensor_tensor(x1b[bi][:, 128 * ct:128 * (ct + 1)],
                                         xbuf[bi][:, 128 * ct:128 * (ct + 1)],
                                         ps_f[:, :], ALU.add))
            VE(v_f)

        def sy_o(sy, a=a, bi=bi):
            finv.wait(sy)
            sto.inc(sy.dma_start(out=io["out"][128 * a:128 * (a + 1), :], in_=x1b[bi][:]), 16)
        SY(sy_o)

    def sy_fin(sy):
        sto.wait(sy)
    SY(sy_fin)
    psF.close()
    ph.close()



def build_mega():
    nc = bass.Bass("TRN2", target_bir_lowering=False, detect_race_conditions=False)
    dt = lambda n, s, d: nc.dram_tensor(n, s, d, kind="ExternalInput").ap()
    io = dict(
        x_own=dt("x_own", [TOK, E], F32),
        wqa=dt("wqa", [E, QL], F32),
        wqb=dt("wqb", [QL, E], F32),
        wkva=dt("wkva", [E, KVL], F32),
        wkvb_kv=dt("wkvb_kv", [KVL, 512], F32),
        ropek_cos=dt("ropek_cos", [S, 32], F32),
        ropek_sin=dt("ropek_sin", [S, 32], F32),
        ropeq_cos=dt("ropeq_cos", [TOK, 32], F32),
        ropeq_sin=dt("ropeq_sin", [TOK, 32], F32),
        mask0=dt("mask0", [128, 128], F32),
        mask1=dt("mask1", [128, 128], F32),
        ident=dt("ident", [128, 128], F32),
        wout=dt("wout", [E, E], BF16),
        bout=dt("bout", [128, 8], F32),
        routerT=dt("routerT", [E, NEXP], F32),
        gup_w=dt("gup_w", [NEXP, 8, 128, 2048], BF16),
        dn_w=dt("dn_w", [NEXP, E, E], BF16),
        sgsu=dt("sgsu", [16, 128, 2048], BF16),
        sdwT=dt("sdwT", [SHD, E], BF16),
        onehot=dt("onehot", [NEXP, 2048], F32),
        wqkv=dt("wqkv", [E, 4 * E], BF16),
        w_out=dt("w_out", [E, E], BF16),
        w12T=dt("w12T", [32, 128, 2048], BF16),
        h1T=dt("h1T", [32, 128, 4096], BF16),
        h2T=dt("h2T", [32, 128, 4096], BF16),
        w3T=dt("w3T", [8, 128, 4096], BF16),
        w1s=dt("w1s", [128, 32], F32),
        w2s=dt("w2s", [128, 32], F32),
        h1s=dt("h1s", [128, 32], F32),
        h2s=dt("h2s", [128, 32], F32),
        w3s=dt("w3s", [128, 8], F32),
        out=nc.dram_tensor("out", [TOK, E], F32, kind="ExternalOutput").ap(),
        x1_scr=nc.dram_tensor("x1_scr", [TOK, E], F32),
        qkv_scr=nc.dram_tensor("qkv_scr", [TOK, 4 * E], BF16),
        scr=nc.dram_tensor("scr", [128, 4096], BF16),
        a2a_in=nc.dram_tensor("a2a_in", [TOK, E], BF16),
        a2a_out=nc.dram_tensor("a2a_out", [TOK, E], BF16),
        lat_in=nc.dram_tensor("lat_in", [KVL, TOK], BF16),
        lat_out=nc.dram_tensor("lat_out", [2 * KVL, TOK], BF16),
    )

    es = ExitStack()
    with es:
        block = es.enter_context(nc.Block())
        C = Ctx(nc, es, block)
        C.ident_bf = es.enter_context(nc.sbuf_tensor("ident_bf", [128, 128], BF16))
        C.ident_f32 = es.enter_context(nc.sbuf_tensor("ident_f32", [128, 128], F32))

        es_xn = ExitStack()
        xnT = [es_xn.enter_context(nc.sbuf_tensor(f"xnT{i}", [128, TOK], BF16)) for i in range(8)]
        es_at = ExitStack()
        attnT = [es_at.enter_context(nc.sbuf_tensor(f"attnT{i}", [128, TOK], BF16)) for i in range(8)]
        g1 = emit_l0attn(C, io, xnT, attnT)
        es_h = ExitStack()
        io["h_sb"] = [es_h.enter_context(nc.sbuf_tensor(f"h{i}", [128, TOK], BF16)) for i in range(8)]
        g2 = emit_l0ffn(C, io, xnT, attnT, es_at, g1)
        es_h.close()
        es_at.close()
        es_xn.close()

        es_xn1 = ExitStack()
        xn1T = [es_xn1.enter_context(nc.sbuf_tensor(f"xn1T{i}", [128, TOK], BF16)) for i in range(8)]
        g3 = emit_l1qkv(C, io, xn1T, g2)
        emit_l1fused(C, io, xn1T, g3)
        es_xn1.close()
    return nc



def _rope_tables():
    inv = 10000.0 ** (-np.arange(0, D, 2, dtype=np.float64) / D)
    pos = np.arange(S, dtype=np.float64)
    ang = np.outer(pos, inv)
    return np.cos(ang).astype(np.float32), np.sin(ang).astype(np.float32)


def _own_rows(core):
    b, h = core // 2, core % 2
    a = np.arange(NT)
    starts = b * S + 256 * a + 128 * h
    return (starts[:, None] + np.arange(128)[None, :]).reshape(-1)


def _inmaps(inputs):
    x2d = np.asarray(inputs["L0_attn_x"]).reshape(B * S, E)
    cosf, sinf = _rope_tables()
    ident = np.eye(128, dtype=np.float32)
    tri = np.triu(np.ones((128, 128), dtype=np.float32))
    ones = np.ones((128, 128), dtype=np.float32)
    zeros = np.zeros((128, 128), dtype=np.float32)
    wkvb = np.asarray(inputs["L0_attn_Wkvb"])
    wkvb_kv = np.ascontiguousarray(np.concatenate([wkvb[:, 0::2], wkvb[:, 1::2]], axis=1))
    onehot = np.zeros((NEXP, 2048), dtype=np.float32)
    for e in range(NEXP):
        onehot[e, 128 * e:128 * (e + 1)] = 1.0
    rs = lambda s: np.ascontiguousarray(np.asarray(s).reshape(-1, 128).T)
    import ml_dtypes
    bf = lambda a: np.ascontiguousarray(a).astype(ml_dtypes.bfloat16)
    common = dict(
        wqa=np.asarray(inputs["L0_attn_Wqa"]),
        wqb=np.asarray(inputs["L0_attn_Wqb"]),
        wkva=np.asarray(inputs["L0_attn_Wkva"]),
        wkvb_kv=wkvb_kv,
        ropek_cos=cosf, ropek_sin=sinf,
        ident=ident,
        wout=bf(np.asarray(inputs["L0_attn_Wout"])),
        bout=np.ascontiguousarray(np.asarray(inputs["L0_attn_bout"]).reshape(8, 128).T),
        routerT=np.ascontiguousarray(np.asarray(inputs["L0_ffn_router_w"]).T),
        gup_w=bf(np.asarray(inputs["L0_ffn_gate_up_w"])
                 .reshape(NEXP, 8, 128, 16, 128).transpose(0, 3, 2, 1, 4)
                 .reshape(NEXP, 2, 8, 128, 1024).transpose(0, 2, 3, 1, 4)
                 .reshape(NEXP, 8, 128, 2048)),
        dn_w=bf(np.asarray(inputs["L0_ffn_down_w"])),
        sgsu=bf(np.concatenate([
            np.asarray(inputs["L0_ffn_shared_gate_w"]).reshape(16, 128, 8, 128).transpose(0, 3, 2, 1).reshape(16, 128, 1024),
            np.asarray(inputs["L0_ffn_shared_up_w"]).reshape(16, 128, 8, 128).transpose(0, 3, 2, 1).reshape(16, 128, 1024),
        ], axis=2)),
        sdwT=bf(np.asarray(inputs["L0_ffn_shared_down_w"]).T),
        onehot=onehot,
        wqkv=bf(np.asarray(inputs["L1_attn_w_qkvw"])),
        w_out=bf(np.asarray(inputs["L1_attn_w_out"])),
        w12T=bf(np.concatenate([
            np.asarray(inputs["L1_ffn_w1_w"]).reshape(32, 128, 8, 128).transpose(0, 3, 2, 1).reshape(32, 128, 1024),
            np.asarray(inputs["L1_ffn_w2_w"]).reshape(32, 128, 8, 128).transpose(0, 3, 2, 1).reshape(32, 128, 1024),
        ], axis=2)),
        h1T=bf(np.asarray(inputs["L1_ffn_h1_w"])
               .reshape(32, 128, 32, 128).transpose(0, 3, 2, 1).reshape(32, 128, 4096)),
        h2T=bf(np.asarray(inputs["L1_ffn_h2_w"])
               .reshape(32, 128, 32, 128).transpose(0, 3, 2, 1).reshape(32, 128, 4096)),
        w3T=bf(np.asarray(inputs["L1_ffn_w3_w"])
               .reshape(8, 128, 32, 128).transpose(0, 3, 2, 1).reshape(8, 128, 4096)),
        w1s=rs(inputs["L1_ffn_w1_s"]), w2s=rs(inputs["L1_ffn_w2_s"]),
        h1s=rs(inputs["L1_ffn_h1_s"]), h2s=rs(inputs["L1_ffn_h2_s"]),
        w3s=rs(inputs["L1_ffn_w3_s"]),
    )
    maps = []
    for c in range(NCORES):
        b, h = c // 2, c % 2
        rows = _own_rows(c)
        m = dict(common)
        m.update(
            x_own=np.ascontiguousarray(x2d[rows]),
            ropeq_cos=np.ascontiguousarray(cosf[rows - b * S]),
            ropeq_sin=np.ascontiguousarray(sinf[rows - b * S]),
            mask0=(tri if h == 0 else ones),
            mask1=(zeros if h == 0 else tri),
        )
        maps.append(m)
    return maps


def kernel(**inputs):
    cores = list(range(NCORES))
    nc = build_mega()
    res = run_bass_kernel_spmd(nc, _inmaps(inputs), core_ids=cores)
    out = np.empty((B * S, E), dtype=np.float32)
    for c in cores:
        out[_own_rows(c)] = res.results[c]["out"]
    return out.reshape(B, S, E)
